# version 13
# baseline (speedup 1.0000x reference)
import numpy as np
import ml_dtypes
from contextlib import ExitStack
import concourse.bass as bass
import concourse.mybir as mybir
from concourse.bass_utils import run_bass_kernel_spmd

F32 = mybir.dt.float32
BF16 = mybir.dt.bfloat16
AF = mybir.ActivationFunctionType
ALU = mybir.AluOpType
AX = mybir.AxisListType

T, D = 4096, 1024
NT = T // 128
L1A_NQ = 4
L1A_DBG = None
L1A_SPILL = False
F_NT = 8
F_DBG = False
F_NL = 2
EPS_LN = 1e-5
NEG = np.float32(-1e30)
BF16NP = ml_dtypes.bfloat16



def host_consts():
    c = {}
    c["ident"] = np.eye(128, dtype=np.float32)
    c["constcols"] = np.tile(np.array([[0.25, 8.0, 2.0, 0.5, 1e-5, -0.5]], np.float32), (128, 1))
    c["mask_su"] = np.triu(np.ones((128, 128), np.float32))
    c["cums128"] = np.triu(np.ones((128, 128), np.float32))
    NEGC = np.float32(-1e30)
    c["amask_i128"] = np.where(np.triu(np.ones((128, 128), np.float32)) > 0, np.float32(0), NEGC)
    c["amask_sL128"] = np.where(np.tril(np.ones((128, 128), np.float32), -1) > 0, np.float32(0), NEGC)
    half = np.arange(0, 64, 2, dtype=np.float64)
    inv = 1.0 / 10000.0 ** (half / 64.0)
    tpos = np.arange(T, dtype=np.float64)
    fr = tpos[:, None] * inv[None, :]
    fr = np.concatenate([fr, fr], -1)
    base = (half + 0.4 * 64) / (1.4 * 64)
    power = (tpos - T // 2) / 256.0
    sc = base[None, :] ** power[:, None]
    sc = np.concatenate([sc, sc], -1)
    cos, sin = np.cos(fr), np.sin(fr)
    sgn = np.where(np.arange(64) < 32, -1.0, 1.0)[:, None]

    def tile2(a):
        return np.ascontiguousarray(np.tile(a.astype(np.float32), (2, 1)))
    c["rot_csq"] = tile2((cos * sc).T)
    c["rot_ssq"] = tile2((sin * sc).T * sgn)
    c["rot_csk"] = tile2((cos / sc).T)
    c["rot_ssk"] = tile2((sin / sc).T * sgn)
    jj = np.arange(128)[:, None]
    qq = np.arange(512)[None, :]
    fm = np.zeros((128, 8 * 512), np.float32)
    for a in range(8):
        if a < 4:
            m = (jj >= qq - 128 * a)
        else:
            m = (jj <= qq - 128 * (a - 4))
        fm[:, 512 * a:512 * (a + 1)] = m
    c["fmask"] = fm.astype(BF16NP)
    return c


def prep_weights(ins):
    w = {}
    w["l0_wq"] = np.ascontiguousarray(ins["L0_attn_Wq"])
    w["l0_wk"] = np.ascontiguousarray(ins["L0_attn_Wk"])
    w["l0_wv"] = np.ascontiguousarray(ins["L0_attn_Wv"])
    w["l0_wo"] = np.ascontiguousarray(ins["L0_attn_Wo"])
    w["l0_rw"] = np.ascontiguousarray(ins["L0_ffn_router_w"])
    w["l0_rb"] = np.ascontiguousarray(ins["L0_ffn_router_b"]).reshape(1, 8)
    w["l0_w1"] = np.ascontiguousarray(ins["L0_ffn_W1"])
    w["l0_b1"] = np.ascontiguousarray(ins["L0_ffn_b1"]).reshape(1, 8 * 512)
    w["l0_w2"] = np.ascontiguousarray(ins["L0_ffn_W2"])
    w["l0_b2"] = np.ascontiguousarray(ins["L0_ffn_b2"]).reshape(1, 8 * 1024)
    w["l1_wq"] = np.ascontiguousarray(ins["L1_attn_Wq"])
    w["l1_wk"] = np.ascontiguousarray(ins["L1_attn_Wk"])
    w["l1_wv"] = np.ascontiguousarray(ins["L1_attn_Wv"])
    w["l1_wg"] = np.ascontiguousarray(ins["L1_attn_Wg"])
    w["l1_wo"] = np.ascontiguousarray(ins["L1_attn_Wo"])
    w["l1_wab"] = np.concatenate([ins["L1_attn_Wa"], ins["L1_attn_Wb"]], axis=1)
    w["l1_alog_dt"] = np.concatenate(
        [np.asarray(ins["L1_attn_A_log"]).reshape(1, 8), np.asarray(ins["L1_attn_dt_bias"]).reshape(1, 8)], axis=1)
    w["l1_convq"] = np.ascontiguousarray(ins["L1_attn_conv_q"])
    w["l1_convk"] = np.ascontiguousarray(ins["L1_attn_conv_k"])
    w["l1_convv"] = np.ascontiguousarray(ins["L1_attn_conv_v"])
    w["l1_onw"] = np.ascontiguousarray(np.asarray(ins["L1_attn_o_norm_w"], np.float32).reshape(2, 128).T)
    w["alogdt_bf"] = np.ascontiguousarray(
        np.concatenate([np.asarray(ins["L1_attn_A_log"]).reshape(1, 8),
                        np.asarray(ins["L1_attn_dt_bias"]).reshape(1, 8)], axis=1).astype(BF16NP))
    wqkv = np.asarray(ins["L1_ffn_wqkv"], np.float32)
    w["f_wqkv"] = np.ascontiguousarray(wqkv.astype(BF16NP))
    perm = np.concatenate([64 * h + (np.arange(64) + 32) % 64 for h in range(8)])
    w["f_wqkv2"] = np.ascontiguousarray(np.concatenate(
        [wqkv[:, :, 0:512][:, :, perm], wqkv[:, :, 512:1024][:, :, perm]], axis=2).astype(BF16NP))
    qs = np.asarray(ins["L1_ffn_q_scale"], np.float32)
    ks = np.asarray(ins["L1_ffn_k_scale"], np.float32)
    w["f_qs"] = np.ascontiguousarray(np.tile(qs, (1, 2)).reshape(2, 128, 1))
    w["f_qs_sw"] = np.ascontiguousarray(np.tile(np.roll(qs, -32, axis=1), (1, 2)).reshape(2, 128, 1))
    w["f_ks"] = np.ascontiguousarray(np.tile(ks, (1, 2)).reshape(2, 128, 1))
    w["f_ks_sw"] = np.ascontiguousarray(np.tile(np.roll(ks, -32, axis=1), (1, 2)).reshape(2, 128, 1))
    w["f_wo"] = np.ascontiguousarray(np.asarray(ins["L1_ffn_wo"], np.float32).astype(BF16NP))
    w["f_lng"] = np.ascontiguousarray(
        np.asarray(ins["L1_ffn_ln_g"], np.float32).reshape(2, 8, 128).transpose(0, 2, 1))
    w["f_lnb"] = np.ascontiguousarray(
        np.asarray(ins["L1_ffn_ln_b"], np.float32).reshape(2, 8, 128).transpose(0, 2, 1))
    HID = 2730
    w1 = np.asarray(ins["L1_ffn_w1"], np.float32); b1 = np.asarray(ins["L1_ffn_b1"], np.float32)
    w1p = np.zeros((2, 1024, 5632), np.float32)
    w1p[:, :, 0:HID] = w1[:, :, 0:HID]
    w1p[:, :, 2816:2816 + HID] = w1[:, :, HID:2 * HID]
    b1p = np.zeros((2, 1, 5632), np.float32)
    b1p[:, 0, 0:HID] = b1[:, 0:HID]
    b1p[:, 0, 2816:2816 + HID] = b1[:, HID:2 * HID]
    w["f_w1"] = np.ascontiguousarray(w1p.astype(BF16NP))
    w["f_b1"] = np.ascontiguousarray(b1p.reshape(2, 44, 128).transpose(0, 2, 1).astype(np.float32))
    w2 = np.asarray(ins["L1_ffn_w2"], np.float32)
    w2p = np.zeros((2, 2816, 1024), np.float32)
    w2p[:, 0:HID, :] = w2
    w["f_w2"] = np.ascontiguousarray(w2p.astype(BF16NP))
    w["f_b2"] = np.ascontiguousarray(
        np.asarray(ins["L1_ffn_b2"], np.float32).reshape(2, 8, 128).transpose(0, 2, 1))
    return w



class Cnt:
    __slots__ = ("sem", "n")

    def __init__(self, sem):
        self.sem = sem
        self.n = 0


class SlotDma:

    def __init__(self, b, name, nslots):
        self.cnts = [b.cnt(f"{name}{j}") for j in range(nslots)]

    def issue(self, eng, j, out, in_, **kw):
        c = self.cnts[j]
        eng.dma_start(out, in_, **kw).then_inc(c.sem, 16)
        c.n += 16
        return c.n

    def wait_total(self, eng, j):
        c = self.cnts[j]
        eng.wait_ge(c.sem, c.n)


class Bld:
    def __init__(self, nc, blk, ctx):
        self.nc = nc
        self.blk = blk
        self.ctx = ctx
        self.sem_ctx = ctx
        self.phase = 0
        self.sems = {}
        self.dram = {}
        self.sb = {}

    def cnt(self, name):
        if name not in self.sems:
            sem = self.sem_ctx.enter_context(self.nc.semaphore(name))
            self.sems[name] = Cnt(sem)
        return self.sems[name]

    def sbuf(self, name, shape, dtype, ctx=None):
        t = (ctx or self.ctx).enter_context(self.nc.sbuf_tensor(name, shape, dtype))
        return t

    def psum2(self, name, shape, dtype=F32, ctx=None):
        return (ctx or self.ctx).enter_context(self.nc.psum_tensor(name, shape, dtype))

    def psum(self, name, shape, dtype=F32):
        return self.ctx.enter_context(self.nc.psum_tensor(name, shape, dtype))

    def ce(self, key, inst, k=1):
        c = self.cnt("chain_" + key)
        inst.then_inc(c.sem, k)
        c.n += k
        return c.n

    def cew(self, eng, key):
        c = self.cnt("chain_" + key)
        eng.wait_ge(c.sem, c.n)

    def fence(self, eng, key, inst):
        self.ce(key, inst)
        self.cew(eng, key)
        return inst

    def barrier(self):
        self.phase += 1
        ph = self.cnt("ph")
        target = 5 * self.phase
        for name in ("sync", "tensor", "vector", "scalar", "gpsimd"):
            sec = getattr(self.blk, name)

            def body(eng, target=target, ph=ph):
                eng.nop().then_inc(ph.sem, 1)
                eng.wait_ge(ph.sem, target)

            sec(body)
        ph.n = target



ENG_SEMNAME = {"tensor": "sq_t", "vector": "sq_v", "scalar": "sq_s", "gpsimd": "sq_g"}
DMA_QUEUES = {"sync": "sq_dS", "gpsimd_dma": "sq_dG", "scalar_dma": "sq_dA"}
DMA_POOL = 10


class Step:
    __slots__ = ("sem", "val")

    def __init__(self, sem, val):
        self.sem = sem
        self.val = val


class Seq:

    def __init__(self, b):
        self.b = b
        self.steps = []
        self.dma_pool = {}
        self.dma_rr = {}

    def step(self, engkey, thunk, deps=(), ninc=1):
        c = self.b.cnt(ENG_SEMNAME[engkey])
        c.n += ninc
        st = Step(c.sem, c.n)
        self.steps.append((engkey, tuple(d for d in deps if d is not None), thunk, st, ninc))
        return st

    def dma(self, queue, thunk, deps=(), n=1):
        if queue not in self.dma_pool:
            self.dma_pool[queue] = [self.b.cnt(f"{DMA_QUEUES[queue]}{j}") for j in range(DMA_POOL)]
            self.dma_rr[queue] = 0
        j = self.dma_rr[queue] % DMA_POOL
        self.dma_rr[queue] += 1
        c = self.dma_pool[queue][j]
        prev_total = c.n
        c.n += 16 * n
        st = Step(c.sem, c.n)
        self.steps.append(("__dma_" + queue, tuple(d for d in deps if d is not None) + (("__slot", c, prev_total),),
                           thunk, st, 16 * n))
        return st

    def emit(self):
        by_sec = {}
        for engkey, deps, thunk, st, ninc in self.steps:
            if engkey.startswith("__dma_"):
                queue = engkey[len("__dma_"):]
                secname = "sync" if queue == "sync" else ("gpsimd" if queue == "gpsimd_dma" else "scalar")
            else:
                secname = engkey
            by_sec.setdefault(secname, []).append((engkey, deps, thunk, st, ninc))
        blk = self.b.blk
        for secname, items in by_sec.items():
            sec = getattr(blk, secname)

            def body(eng, items=items):
                last_wait = {}
                for engkey, deps, thunk, st, ninc in items:
                    for d in deps:
                        if isinstance(d, tuple) and d and d[0] == "__slot":
                            _, c, prev_total = d
                            if prev_total > 0:
                                eng.wait_ge(c.sem, prev_total)
                            continue
                        if last_wait.get(d.sem, -1) >= d.val:
                            continue
                        eng.wait_ge(d.sem, d.val)
                        last_wait[d.sem] = d.val
                    inst = thunk(eng)
                    if engkey.startswith("__dma_"):
                        insts = inst if isinstance(inst, (list, tuple)) else [inst]
                        assert 16 * len(insts) == ninc
                        for ii in insts:
                            ii.then_inc(st.sem, 16)
                    else:
                        inst.then_inc(st.sem, ninc)

            sec(body)
        self.steps = []

    def drain_dmas(self):
        blk = self.b.blk
        for queue, cnts in self.dma_pool.items():
            secname = "sync" if queue == "sync" else ("gpsimd" if queue == "gpsimd_dma" else "scalar")
            sec = getattr(blk, secname)

            def body(eng, cnts=list(cnts)):
                for c in cnts:
                    if c.n > 0:
                        eng.wait_ge(c.sem, c.n)

            sec(body)

    def barrier_all(self):
        self.emit()
        self.drain_dmas()
        self.b.barrier()



def emit_ln(b: Bld, src, dst_bf, dst_f32=None, tag="ln"):
    nc = b.nc
    NB = 3
    lds = [b.cnt(tag + f"_ld{j}") for j in range(NB)]
    sts = [b.cnt(tag + f"_st{j}") for j in range(NB)]
    vv = b.cnt(tag + "_v"); sv = b.cnt(tag + "_s")
    per = 3 if dst_f32 is not None else 2
    nst = 2 if dst_f32 is not None else 1
    xa = [b.sbuf(f"{tag}_x{i}", [128, D], F32) for i in range(NB)]
    xc = [b.sbuf(f"{tag}_c{i}", [128, D], F32) for i in range(NB)]
    ob = [b.sbuf(f"{tag}_o{i}", [128, D], BF16) for i in range(NB)]
    of = [b.sbuf(f"{tag}_f{i}", [128, D], F32) for i in range(NB)] if dst_f32 is not None else None
    sq = b.sbuf(tag + "_sq", [128, D], F32)
    stat = b.sbuf(tag + "_stat", [128, 8], F32)
    junk = b.sbuf(tag + "_junk", [128, 2], F32)
    vv0, sv0 = vv.n, sv.n
    ld0 = [c.n for c in lds]
    st0 = [c.n for c in sts]

    @b.blk.sync
    def _(eng):
        for i in range(NT):
            j = i % NB
            if i >= NB:
                eng.wait_ge(sts[j].sem, st0[j] + 16 * nst * (i // NB))
            eng.dma_start(xa[j][:], src[128 * i:128 * (i + 1), :]).then_inc(lds[j].sem, 16)
    for j in range(NB):
        lds[j].n += 16 * len(range(j, NT, NB))

    @b.blk.vector
    def _(eng):
        for i in range(NT):
            j = i % NB
            eng.wait_ge(lds[j].sem, ld0[j] + 16 * (i // NB + 1))
            b.fence(eng, "v", eng.reduce_sum(stat[:, 0:1], xa[j][:], axis=AX.X))
            b.fence(eng, "v", eng.tensor_scalar(stat[:, 1:2], stat[:, 0:1], 1.0 / D, None, ALU.mult))
            b.fence(eng, "v", eng.tensor_scalar(xc[j][:], xa[j][:], stat[:, 1:2], None, ALU.subtract))
            b.fence(eng, "v", eng.tensor_tensor(sq[:], xc[j][:], xc[j][:], ALU.mult))
            b.fence(eng, "v", eng.reduce_sum(stat[:, 2:3], sq[:], axis=AX.X))
            b.fence(eng, "v", eng.tensor_scalar(stat[:, 3:4], stat[:, 2:3], 1.0 / D, EPS_LN, ALU.mult, ALU.add))
            eng.nop().then_inc(vv.sem, 1)
            eng.wait_ge(sv.sem, sv0 + i + 1)
            b.fence(eng, "v", eng.reciprocal(stat[:, 5:6], stat[:, 4:5]))
            eng.tensor_scalar(ob[j][:], xc[j][:], stat[:, 5:6], None, ALU.mult).then_inc(vv.sem, 1)
            if dst_f32 is not None:
                eng.tensor_scalar(of[j][:], xc[j][:], stat[:, 5:6], None, ALU.mult).then_inc(vv.sem, 1)
    vv.n += per * NT

    @b.blk.scalar
    def _(eng):
        for i in range(NT):
            eng.wait_ge(vv.sem, vv0 + per * i + 1)
            eng.activation(stat[:, 4:5], stat[:, 3:4], AF.Sqrt).then_inc(sv.sem, 1)
    sv.n += NT

    @b.blk.gpsimd
    def _(eng):
        for i in range(NT):
            j = i % NB
            eng.wait_ge(vv.sem, vv0 + per * i + 2)
            eng.dma_start(dst_bf[128 * i:128 * (i + 1), :], ob[j][:]).then_inc(sts[j].sem, 16)
            if dst_f32 is not None:
                eng.wait_ge(vv.sem, vv0 + per * i + 3)
                eng.dma_start(dst_f32[128 * i:128 * (i + 1), :], of[j][:]).then_inc(sts[j].sem, 16)
        for j in range(NB):
            eng.wait_ge(sts[j].sem, st0[j] + 16 * nst * len(range(j, NT, NB)))
    for j in range(NB):
        sts[j].n += 16 * nst * len(range(j, NT, NB))


def emit_moe_combine(b: Bld, xn_bf, x_src, s_src, x1_dst):
    nc = b.nc
    W = b.dram
    TH = 1024
    NTH = TH // 128
    wls = b.cnt("moe_wls")
    wle = [b.cnt(f"moe_wle{j}") for j in range(2)]
    lslot = [b.cnt(f"moe_l{j}") for j in range(2)]
    sslot = [b.cnt(f"moe_s{j}") for j in range(2)]
    tv = b.cnt("moe_t"); vv = b.cnt("moe_v"); sv = b.cnt("moe_sc")
    xfc = b.cnt("moe_xfc")

    xf = b.sbuf("moe_xf", [128, 8 * TH], BF16)
    w1sb = [b.sbuf(f"moe_w1_{i}", [128, 8 * 512], BF16) for i in range(2)]
    w2sb = [b.sbuf(f"moe_w2_{i}", [128, 4 * 1024], BF16) for i in range(2)]
    rwsb = b.sbuf("moe_rw", [128, 8 * 8], BF16)
    rbsb = b.sbuf("moe_rb", [1, 8], BF16)
    b1sb = b.sbuf("moe_b1", [1, 8 * 512], BF16)
    b2sb = b.sbuf("moe_b2", [1, 8 * 1024], BF16)
    ones1 = b.sbuf("moe_ones1", [1, 512], BF16)
    masktok = b.sbuf("moe_masktok", [128, NTH * 8], F32)
    rstat = b.sbuf("moe_rstat", [128, 16], F32)
    junkv = b.sbuf("moe_junkv", [128, 2], F32)
    rtile = b.sbuf("moe_rtile", [128, 8], F32)
    h1sb = [b.sbuf(f"moe_h1_{i}", [128, 4 * TH], BF16) for i in range(2)]
    yacc = b.sbuf("moe_yacc", [128, NTH * 1024], F32)
    xsb = [b.sbuf(f"moe_xs{i}", [128, D], F32) for i in range(2)]
    ssb = [b.sbuf(f"moe_ss{i}", [128, D], F32) for i in range(2)]
    gsb = [b.sbuf(f"moe_gs{i}", [128, D], F32) for i in range(2)]
    x1sb = [b.sbuf(f"moe_x1{i}", [128, D], F32) for i in range(2)]

    psA = b.psum("moe_psA", [128, 512])
    psB = b.psum("moe_psB", [128, 512])
    psC = b.psum("moe_psC", [128, 512])
    psD = b.psum("moe_psD", [128, 512])
    psR = b.psum("moe_psR", [128, 512])

    @b.blk.gpsimd
    def _(eng):
        eng.dma_start(rwsb[:].rearrange("p (kc e) -> p kc e", kc=8),
                      W["l0_rw"].rearrange("(kc p) e -> p kc e", p=128)).then_inc(wls.sem, 16)
        eng.dma_start(rbsb[:], W["l0_rb"]).then_inc(wls.sem, 16)
        eng.dma_start(b1sb[:], W["l0_b1"]).then_inc(wls.sem, 16)
        eng.dma_start(b2sb[:], W["l0_b2"]).then_inc(wls.sem, 16)
    wls.n += 16 * 4
    wl_static = wls.n

    @b.blk.vector
    def _(eng):
        eng.memset(ones1[:], 1.0).then_inc(vv.sem, 1)
    vv.n += 1
    ones_done = vv.n

    for half in range(4):
        t0 = half * TH
        xf0 = xfc.n

        @b.blk.sync
        def _(eng, t0=t0):
            for r in range(8):
                eng.dma_start(xf[:, r * TH:(r + 1) * TH],
                              xn_bf[t0:t0 + TH, r * 128:(r + 1) * 128], transpose=True).then_inc(xfc.sem, 16)
        xfc.n += 16 * 8

        tvr = tv.n; tv.n += NTH
        svr = sv.n; sv.n += NTH
        vvr = vv.n; vv.n += 3 * NTH

        @b.blk.tensor
        def _(eng, xf0=xf0, tvr=tvr, vvr=vvr, half=half):
            eng.wait_ge(xfc.sem, xf0 + 16 * 8)
            if half == 0:
                eng.wait_ge(wls.sem, wl_static)
                eng.wait_ge(vv.sem, ones_done)
            for i in range(NTH):
                if i > 0:
                    eng.wait_ge(vv.sem, vvr + 3 * (i - 1) + 1)
                for kc in range(8):
                    nc.tensor.matmul(psR[:, 0:8], xf[:, kc * TH + 128 * i: kc * TH + 128 * (i + 1)],
                                     rwsb[:, kc * 8:(kc + 1) * 8], start=(kc == 0), stop=False)
                nc.tensor.matmul(psR[:, 0:8], ones1[0:1, 0:128], rbsb[0:1, :],
                                 start=False, stop=True).then_inc(tv.sem, 1)

        @b.blk.vector
        def _(eng, tvr=tvr, vvr=vvr, svr=svr):
            for i in range(NTH):
                eng.wait_ge(tv.sem, tvr + (i + 1))
                b.fence(eng, "v", eng.reduce_max(rstat[:, 0:1], psR[:, 0:8], axis=AX.X))
                b.fence(eng, "v", eng.tensor_scalar(rtile[:], psR[:, 0:8], rstat[:, 0:1], None, ALU.subtract))
                eng.nop().then_inc(vv.sem, 1)
                eng.wait_ge(sv.sem, svr + i + 1)
                b.fence(eng, "v", eng.reduce_sum(rstat[:, 1:2], rtile[:], axis=AX.X))
                b.fence(eng, "v", eng.reciprocal(rstat[:, 2:3], rstat[:, 1:2]))
                b.fence(eng, "v", eng.tensor_scalar(rtile[:], rtile[:], rstat[:, 2:3], None, ALU.mult))
                b.fence(eng, "v", eng.reduce_max(rstat[:, 3:4], rtile[:], axis=AX.X))
                b.fence(eng, "v", eng.tensor_scalar(rstat[:, 8:16], rtile[:], rstat[:, 3:4], None, ALU.is_ge))
                b.fence(eng, "v", eng.scalar_tensor_tensor(rstat[:, 8:16], rstat[:, 8:16], -1e30, rtile[:], ALU.mult, ALU.add))
                b.fence(eng, "v", eng.reduce_max(rstat[:, 4:5], rstat[:, 8:16], axis=AX.X))
                b.fence(eng, "v", eng.tensor_tensor(rstat[:, 5:6], rstat[:, 3:4], rstat[:, 4:5], ALU.add))
                b.fence(eng, "v", eng.reciprocal(rstat[:, 6:7], rstat[:, 5:6]))
                b.fence(eng, "v", eng.tensor_scalar(rstat[:, 8:16], rtile[:], rstat[:, 4:5], None, ALU.is_ge))
                b.fence(eng, "v", eng.tensor_tensor(rstat[:, 8:16], rstat[:, 8:16], rtile[:], ALU.mult))
                eng.tensor_scalar(masktok[:, 8 * i:8 * (i + 1)], rstat[:, 8:16], rstat[:, 6:7],
                                  None, ALU.mult).then_inc(vv.sem, 2)

        @b.blk.scalar
        def _(eng, vvr=vvr, svr=svr):
            for i in range(NTH):
                eng.wait_ge(vv.sem, vvr + 3 * i + 1)
                eng.activation(rtile[:], rtile[:], AF.Exp).then_inc(sv.sem, 1)

        @b.blk.vector
        def _(eng):
            for i in range(NTH):
                inst = eng.memset(yacc[:, 1024 * i:1024 * (i + 1)], 0.0)
            b.fence(eng, "v", inst)
            eng.nop().then_inc(vv.sem, 1)
        vv.n += 1

        y_done_tv = {}
        NT512 = TH // 512
        n_h1 = 4 * NT512
        n_y = NTH * 2
        for n in range(8):
            buf = n % 2
            wl_tag = wle[buf].n; wle[buf].n += 32
            tvh = tv.n; tv.n += n_h1 + n_y
            svh = sv.n; sv.n += n_h1
            vvh = vv.n; vv.n += n_y

            @b.blk.gpsimd
            def _(eng, n=n, buf=buf, gate=(y_done_tv.get(n - 2))):
                if gate is not None:
                    eng.wait_ge(tv.sem, gate)
                eng.dma_start(w1sb[buf][:].rearrange("p (kc f) -> p kc f", kc=8),
                              W["l0_w1"][n].rearrange("(kc p) f -> p kc f", p=128)).then_inc(wle[buf].sem, 16)
                eng.dma_start(w2sb[buf][:].rearrange("p (kc f) -> p kc f", kc=4),
                              W["l0_w2"][n].rearrange("(kc p) f -> p kc f", p=128)).then_inc(wle[buf].sem, 16)

            @b.blk.tensor
            def _(eng, n=n, buf=buf, wl_tag=wl_tag, tvh=tvh, svh=svh, vvh=vvh):
                eng.wait_ge(wle[buf].sem, wl_tag + 32)
                cnt = 0
                for m in range(4):
                    for nt4 in range(TH // 512):
                        if svh + cnt - 1 > 0:
                            eng.wait_ge(sv.sem, svh + cnt - 1)
                        ps = psA if cnt % 2 == 0 else psB
                        for kc in range(8):
                            nc.tensor.matmul(ps[:, 0:512],
                                             w1sb[buf][:, kc * 512 + 128 * m: kc * 512 + 128 * (m + 1)],
                                             xf[:, kc * TH + 512 * nt4: kc * TH + 512 * (nt4 + 1)],
                                             start=(kc == 0), stop=False)
                        nc.tensor.matmul(ps[:, 0:512],
                                         b1sb[0:1, n * 512 + 128 * m: n * 512 + 128 * (m + 1)],
                                         ones1[0:1, 0:512],
                                         start=False, stop=True).then_inc(tv.sem, 1)
                        cnt += 1
                eng.wait_ge(sv.sem, svh + 4 * (TH // 512))
                cnt2 = 0
                for i in range(NTH):
                    for dh in range(2):
                        if vvh + cnt2 - 1 > 0:
                            eng.wait_ge(vv.sem, vvh + cnt2 - 1)
                        ps = psC if cnt2 % 2 == 0 else psD
                        for kc in range(4):
                            nc.tensor.matmul(ps[:, 0:512],
                                             h1sb[buf][:, kc * TH + 128 * i: kc * TH + 128 * (i + 1)],
                                             w2sb[buf][:, kc * 1024 + 512 * dh: kc * 1024 + 512 * (dh + 1)],
                                             start=(kc == 0), stop=False)
                        nc.tensor.matmul(ps[:, 0:512], ones1[0:1, 0:128],
                                         b2sb[0:1, n * 1024 + 512 * dh: n * 1024 + 512 * (dh + 1)],
                                         start=False, stop=True).then_inc(tv.sem, 1)
                        cnt2 += 1
            y_done_tv[n] = tv.n

            @b.blk.scalar
            def _(eng, buf=buf, tvh=tvh):
                for cnt in range(4 * (TH // 512)):
                    eng.wait_ge(tv.sem, tvh + cnt + 1)
                    m, nt4 = divmod(cnt, TH // 512)
                    ps = psA if cnt % 2 == 0 else psB
                    eng.activation(h1sb[buf][:, m * TH + 512 * nt4: m * TH + 512 * (nt4 + 1)],
                                   ps[:, 0:512], AF.Gelu).then_inc(sv.sem, 1)

            @b.blk.vector
            def _(eng, n=n, tvh=tvh):
                for cnt2 in range(NTH * 2):
                    eng.wait_ge(tv.sem, tvh + 4 * (TH // 512) + cnt2 + 1)
                    i, dh = divmod(cnt2, 2)
                    ps = psC if cnt2 % 2 == 0 else psD
                    inst = eng.scalar_tensor_tensor(
                        yacc[:, 1024 * i + 512 * dh: 1024 * i + 512 * (dh + 1)],
                        ps[:, 0:512], masktok[:, 8 * i + n: 8 * i + n + 1],
                        yacc[:, 1024 * i + 512 * dh: 1024 * i + 512 * (dh + 1)],
                        ALU.mult, ALU.add)
                    inst.then_inc(vv.sem, 1)
                    if cnt2 == NTH * 2 - 1:
                        eng.wait_ge(vv.sem, vvh + NTH * 2)

        b.barrier()

        vvc = vv.n; vv.n += 2 * NTH
        svc = sv.n; sv.n += NTH
        ld0c = [c.n for c in lslot]
        st0c = [c.n for c in sslot]

        @b.blk.sync
        def _(eng, t0=t0, ld0c=ld0c, st0c=st0c):
            for i in range(NTH):
                j = i % 2
                if i >= 2:
                    eng.wait_ge(sslot[j].sem, st0c[j] + 16 * (i // 2))
                eng.dma_start(xsb[j][:], x_src[t0 + 128 * i: t0 + 128 * (i + 1), :]).then_inc(lslot[j].sem, 16)
                eng.dma_start(ssb[j][:], s_src[t0 + 128 * i: t0 + 128 * (i + 1), :]).then_inc(lslot[j].sem, 16)

        @b.blk.scalar
        def _(eng, svc=svc, vvc=vvc):
            for i in range(NTH):
                j = i % 2
                if i >= 2:
                    eng.wait_ge(vv.sem, vvc + 2 * (i - 2) + 2)
                eng.activation(gsb[j][:], yacc[:, 1024 * i:1024 * (i + 1)], AF.Silu).then_inc(sv.sem, 1)

        @b.blk.vector
        def _(eng, ld0c=ld0c, svc=svc):
            for i in range(NTH):
                j = i % 2
                eng.wait_ge(lslot[j].sem, ld0c[j] + 32 * (i // 2 + 1))
                eng.wait_ge(sv.sem, svc + i + 1)
                eng.tensor_tensor(gsb[j][:], gsb[j][:], ssb[j][:], ALU.mult).then_inc(vv.sem, 1)
                eng.wait_ge(vv.sem, vvc + 2 * i + 1)
                eng.tensor_tensor(x1sb[j][:], gsb[j][:], xsb[j][:], ALU.add).then_inc(vv.sem, 1)

        @b.blk.gpsimd
        def _(eng, t0=t0, vvc=vvc, st0c=st0c):
            for i in range(NTH):
                j = i % 2
                eng.wait_ge(vv.sem, vvc + 2 * i + 2)
                eng.dma_start(x1_dst[t0 + 128 * i: t0 + 128 * (i + 1), :], x1sb[j][:]).then_inc(sslot[j].sem, 16)
            for j in range(2):
                eng.wait_ge(sslot[j].sem, st0c[j] + 16 * len(range(j, NTH, 2)))
        for j in range(2):
            lslot[j].n += 32 * len(range(j, NTH, 2))
            sslot[j].n += 16 * len(range(j, NTH, 2))

        b.barrier()


def emit_l0attn(b: Bld, xn_bf, s0_dst):
    nc = b.nc
    W = b.dram
    SEG = 1024
    NSEG = T // SEG
    NTS = SEG // 128
    H = 16

    wl = b.cnt("a0_wl")
    ml = b.cnt("a0_ml")
    xfc = b.cnt("a0_xfc")
    tv = b.cnt("a0_t")
    vv = b.cnt("a0_v")
    sv = b.cnt("a0_sc")
    gv = b.cnt("a0_g")
    qkst = b.cnt("a0_qkst")
    qktr = [b.cnt(f"a0_qktr{j}") for j in range(2)]
    pst = b.cnt("a0_pst")
    ptr = [b.cnt(f"a0_ptr{j}") for j in range(2)]
    sst = [b.cnt(f"a0_sst{j}") for j in range(2)]

    phq_d = nc.dram_tensor("a0_phq", [SEG, 256], BF16, kind="Internal").ap()
    qk_d = nc.dram_tensor("a0_qkd", [SEG, 1152], BF16, kind="Internal").ap()

    xf = b.sbuf("a0_xf", [128, 8 * SEG], BF16)
    qk_tok = b.sbuf("a0_qk", [128, NTS * 1024], BF16)
    qf = b.sbuf("a0_qf", [128, 8 * SEG], BF16)
    kf = b.sbuf("a0_kf", [128, 8 * SEG], BF16)
    v_tok = b.sbuf("a0_v", [128, NTS * H * 65], BF16)
    phs = b.sbuf("a0_phs", [128, NTS * 256], BF16)
    phk_tok = b.sbuf("a0_phkt", [128, NTS * 256], BF16)
    phqf = b.sbuf("a0_phqf", [128, 2 * SEG], BF16)
    o_fm = b.sbuf("a0_ofm", [128, 8 * SEG], BF16)
    st32 = b.sbuf("a0_st32", [128, H * 130], F32)
    stbf = b.sbuf("a0_stbf", [128, H * 130], BF16)
    mask_su = b.sbuf("a0_mask", [128, 128], F32)
    ones1 = b.sbuf("a0_ones", [1, 128], BF16)
    rrow = b.sbuf("a0_rrow", [1, 128], F32)
    rrow_bf = b.sbuf("a0_rrowbf", [1, 128], BF16)
    obsb = b.sbuf("a0_obsb", [128, 128], F32)
    mtf = b.sbuf("a0_mtf", [128, 128], F32)
    mt_sb = b.sbuf("a0_mt", [128, 128], BF16)
    junkg = b.sbuf("a0_junkg", [128, 2], F32)
    ztile = b.sbuf("a0_ztile", [128, 128], BF16)
    s0sb = [b.sbuf(f"a0_s0{j}", [128, D], F32) for j in range(2)]
    wq = b.sbuf("a0_wq", [128, 8 * 256], BF16)
    wk = b.sbuf("a0_wk", [128, 8 * 256], BF16)
    wv = b.sbuf("a0_wv", [128, 8 * 1024], BF16)
    wo = b.sbuf("a0_wo", [128, 8 * 1024], BF16)

    psA = b.psum("a0_psA", [128, 512])
    psB = b.psum("a0_psB", [128, 512])
    psM = b.psum("a0_psM", [128, 128])
    psO = b.psum("a0_psO", [128, 128])
    psBC = b.psum("a0_psBC", [128, 128])
    psS = b.psum("a0_psS", [128, 130])

    EPS = 1e-5

    @b.blk.gpsimd
    def _(eng):
        eng.dma_start(wq[:].rearrange("p (kc f) -> p kc f", kc=8),
                      W["l0_wq"].rearrange("(kc p) f -> p kc f", p=128)).then_inc(wl.sem, 16)
        eng.dma_start(wk[:].rearrange("p (kc f) -> p kc f", kc=8),
                      W["l0_wk"].rearrange("(kc p) f -> p kc f", p=128)).then_inc(wl.sem, 16)
        eng.dma_start(wv[:].rearrange("p (kc f) -> p kc f", kc=8),
                      W["l0_wv"].rearrange("(kc p) f -> p kc f", p=128)).then_inc(wl.sem, 16)
        eng.dma_start(wo[:].rearrange("p (kc f) -> p kc f", kc=8),
                      W["l0_wo"].rearrange("(kc p) f -> p kc f", p=128)).then_inc(wl.sem, 16)
    wl.n += 64

    @b.blk.sync
    def _(eng):
        eng.dma_start(mask_su[:], W["mask_su"]).then_inc(ml.sem, 16)
        eng.wait_ge(b.cnt("a0_v").sem, 1)
        for t in range(SEG // 128):
            eng.dma_start(qk_d[128 * t:128 * (t + 1), 1024:1152], ztile[:]).then_inc(ml.sem, 16)
    ml.n += 16 * (1 + SEG // 128)

    @b.blk.vector
    def _(eng):
        eng.wait_ge(wl.sem, wl.n)
        eng.wait_ge(ml.sem, 16)
        eng.memset(st32[:], 0.0)
        eng.memset(stbf[:], 0.0)
        eng.memset(ones1[:], 1.0)
        eng.memset(qk_tok[:], 0.0)
        eng.memset(ztile[:], 0.0)
        b.fence(eng, "v", eng.memset(
            v_tok[:].rearrange("p (t h c) -> p t h c", t=NTS, h=H)[:, :, :, 64:65], 1.0))
        eng.nop().then_inc(vv.sem, 1)
    vv.n += 1
    b.barrier()

    for seg in range(NSEG):
        t0 = seg * SEG
        xf0 = xfc.n

        @b.blk.sync
        def _(eng, t0=t0):
            for r in range(8):
                eng.dma_start(xf[:, r * SEG:(r + 1) * SEG],
                              xn_bf[t0:t0 + SEG, r * 128:(r + 1) * 128], transpose=True).then_inc(xfc.sem, 16)
        xfc.n += 16 * 8

        tv0 = tv.n
        sv0 = sv.n

        @b.blk.tensor
        def _(eng, xf0=xf0, tv0=tv0, sv0=sv0):
            eng.wait_ge(xfc.sem, xf0 + 16 * 8)
            cnt = 0
            for i in range(NTS):
                ps = psA if cnt % 2 == 0 else psB
                if cnt >= 2:
                    eng.wait_ge(sv.sem, sv0 + 2 * (cnt - 1))
                for kc in range(8):
                    nc.tensor.matmul(ps[:, 0:256], xf[:, kc * SEG + 128 * i: kc * SEG + 128 * (i + 1)],
                                     wq[:, kc * 256:(kc + 1) * 256], start=(kc == 0), stop=False)
                for kc in range(8):
                    inst = nc.tensor.matmul(ps[:, 256:512], xf[:, kc * SEG + 128 * i: kc * SEG + 128 * (i + 1)],
                                            wk[:, kc * 256:(kc + 1) * 256], start=False, stop=(kc == 7))
                inst.then_inc(tv.sem, 1)
                cnt += 1
            for i in range(NTS):
                for nh in range(2):
                    ps = psA if cnt % 2 == 0 else psB
                    if cnt >= 2:
                        eng.wait_ge(sv.sem, sv0 + 2 * (cnt - 1))
                    for kc in range(8):
                        inst = nc.tensor.matmul(ps[:, 0:512], xf[:, kc * SEG + 128 * i: kc * SEG + 128 * (i + 1)],
                                                wv[:, kc * 1024 + 512 * nh: kc * 1024 + 512 * (nh + 1)],
                                                start=(kc == 0), stop=(kc == 7))
                    inst.then_inc(tv.sem, 1)
                    cnt += 1
        tv.n += 3 * NTS

        @b.blk.scalar
        def _(eng, tv0=tv0, sv0=sv0):
            cnt = 0
            for i in range(NTS):
                ps = psA if cnt % 2 == 0 else psB
                eng.wait_ge(tv.sem, tv0 + cnt + 1)
                qdst = qk_tok[:, 1024 * i: 1024 * i + 512].rearrange("p (h s) -> p h s", s=32)[:, :, 0:16]
                kdst = qk_tok[:, 1024 * i + 512: 1024 * (i + 1)].rearrange("p (h s) -> p h s", s=32)[:, :, 0:16]
                eng.activation(qdst, ps[:, 0:256].rearrange("p (h f) -> p h f", h=16), AF.Copy, scale=0.25)
                eng.copy(kdst, ps[:, 256:512].rearrange("p (h f) -> p h f", h=16)).then_inc(sv.sem, 2)
                cnt += 1
            for i in range(NTS):
                for nh in range(2):
                    ps = psA if cnt % 2 == 0 else psB
                    eng.wait_ge(tv.sem, tv0 + cnt + 1)
                    dst = v_tok[:, i * H * 65 + nh * 8 * 65: i * H * 65 + (nh * 8 + 8) * 65]
                    eng.copy(dst.rearrange("p (h c) -> p h c", h=8)[:, :, 0:64],
                             ps[:, 0:512].rearrange("p (h c) -> p h c", h=8)).then_inc(sv.sem, 2)
                    cnt += 1
        sv.n += 2 * 3 * NTS

        b.barrier()

        q0 = qkst.n
        qt0 = [c.n for c in qktr]

        @b.blk.sync
        def _(eng, q0=q0, qt0=qt0):
            eng.dma_start(qk_d[:, 0:1024].rearrange("(t p) f -> p t f", p=128),
                          qk_tok[:].rearrange("p (t f) -> p t f", t=NTS)).then_inc(qkst.sem, 16)
            eng.wait_ge(qkst.sem, q0 + 16)
            eng.wait_ge(ml.sem, 16 * (1 + SEG // 128))
            for g in range(8):
                srcq = 64 * g
                srck = 512 + 64 * g
                eng.dma_start(qf[:, g * SEG:(g + 1) * SEG],
                              qk_d[:, srcq:srcq + 128], transpose=True).then_inc(qktr[0].sem, 16)
                eng.dma_start(kf[:, g * SEG:(g + 1) * SEG],
                              qk_d[:, srck:srck + 128], transpose=True).then_inc(qktr[1].sem, 16)
        qkst.n += 16
        for c in qktr:
            c.n += 128
        qk_targets = [c.n for c in qktr]

        b.barrier()

        for h in range(H):
            vv0 = vv.n
            gv0 = gv.n

            @b.blk.vector
            def _(eng, h=h):
                for i in range(NTS):
                    qsl = qk_tok[:, 1024 * i + 32 * h: 1024 * i + 32 * h + 16]
                    in0 = qsl.rearrange("p (i one) -> p i one", one=1).broadcast_to([128, 16, 16])
                    in1 = qsl.rearrange("p (one j) -> p one j", one=1).broadcast_to([128, 16, 16])
                    inst = eng.tensor_tensor(phs[:, 256 * i:256 * (i + 1)].rearrange("p (i j) -> p i j", i=16),
                                             in0, in1, ALU.mult)
                    if i < NTS - 1:
                        b.ce("v", inst)
                inst.then_inc(vv.sem, 1)
                b.sems["chain_v"].n += 0
            vv.n += 1

            @b.blk.gpsimd
            def _(eng, h=h):
                for i in range(NTS):
                    ksl = qk_tok[:, 1024 * i + 512 + 32 * h: 1024 * i + 512 + 32 * h + 16]
                    in0 = ksl.rearrange("p (i one) -> p i one", one=1).broadcast_to([128, 16, 16])
                    in1 = ksl.rearrange("p (one j) -> p one j", one=1).broadcast_to([128, 16, 16])
                    inst = eng.tensor_tensor(phk_tok[:, 256 * i:256 * (i + 1)].rearrange("p (i j) -> p i j", i=16),
                                              in0, in1, ALU.mult)
                inst.then_inc(gv.sem, 1)
            gv.n += 1

            p0q = pst.n
            pt0 = [c.n for c in ptr]

            @b.blk.sync
            def _(eng, vv0=vv0, p0q=p0q):
                eng.wait_ge(vv.sem, vv0 + 1)
                eng.dma_start(phq_d.rearrange("(t p) f -> p t f", p=128),
                              phs[:].rearrange("p (t f) -> p t f", t=NTS)).then_inc(pst.sem, 16)
                eng.wait_ge(pst.sem, p0q + 16)
                eng.dma_start(phqf[:, 0:SEG], phq_d[:, 0:128], transpose=True).then_inc(ptr[0].sem, 16)
                eng.dma_start(phqf[:, SEG:2 * SEG], phq_d[:, 128:256], transpose=True).then_inc(ptr[1].sem, 16)
            pst.n += 16
            for c in ptr:
                c.n += 16
            ptr_targets = [c.n for c in ptr]
            gv_target = gv.n

            b.barrier()

            tvh = tv.n
            vvh = vv.n
            svh = sv.n
            n_c = NTS
            hb, ho = 32 * (h % 2), (h // 2) * SEG

            @b.blk.tensor
            def _(eng, h=h, tvh=tvh, vvh=vvh, svh=svh, ptr_targets=ptr_targets,
                  qk_targets=qk_targets, hb=hb, ho=ho):
                for j in range(2):
                    eng.wait_ge(ptr[j].sem, ptr_targets[j])
                    eng.wait_ge(qktr[j].sem, qk_targets[j])
                for c in range(n_c):
                    if c > 0:
                        eng.wait_ge(sv.sem, svh + 2 * (c - 1) + 1)
                    nc.tensor.matmul(psM[:, 0:128],
                                     kf[hb:hb + 16, ho + 128 * c: ho + 128 * (c + 1)],
                                     qf[hb:hb + 16, ho + 128 * c: ho + 128 * (c + 1)],
                                     start=True, stop=True).then_inc(tv.sem, 1)
                    eng.wait_ge(vv.sem, vvh + 4 * c + 2)
                    if c > 0:
                        eng.wait_ge(sv.sem, svh + 2 * (c - 1) + 2)
                        eng.wait_ge(vv.sem, vvh + 4 * (c - 1) + 3)
                        eng.wait_ge(vv.sem, vvh + 4 * (c - 1) + 4)
                    vt = v_tok[:, (c * H + h) * 65:(c * H + h) * 65 + 65]
                    nc.tensor.matmul(psO[0:65, 0:128], vt, mt_sb[:, 0:128], start=True, stop=False)
                    for fh in range(2):
                        inst = nc.tensor.matmul(psO[0:65, 0:128], stbf[:, h * 130 + 65 * fh: h * 130 + 65 * (fh + 1)],
                                                phqf[:, fh * SEG + 128 * c: fh * SEG + 128 * (c + 1)],
                                                start=False, stop=(fh == 1))
                    inst.then_inc(tv.sem, 1)
                    for fh in range(2):
                        inst = nc.tensor.matmul(psS[:, 65 * fh:65 * (fh + 1)],
                                                phk_tok[:, 256 * c + 128 * fh: 256 * c + 128 * (fh + 1)],
                                                vt, start=True, stop=True)
                    inst.then_inc(tv.sem, 1)
                    eng.wait_ge(vv.sem, vvh + 4 * c + 3)
                    nc.tensor.matmul(psBC[0:64, 0:128], ones1[0:1, 0:64], rrow_bf[0:1, 0:128],
                                     start=True, stop=True).then_inc(tv.sem, 1)
            tv.n += 4 * n_c

            @b.blk.scalar
            def _(eng, tvh=tvh, svh=svh, vvh=vvh, gv_target=gv_target):
                eng.wait_ge(gv.sem, gv_target)
                for c in range(n_c):
                    eng.wait_ge(tv.sem, tvh + 4 * c + 1)
                    if c > 0:
                        eng.wait_ge(vv.sem, vvh + 4 * (c - 1) + 1)
                    eng.activation(mtf[:, 0:128], psM[:, 0:128], AF.Square).then_inc(sv.sem, 1)
                    eng.wait_ge(tv.sem, tvh + 4 * c + 2)
                    if c > 0:
                        eng.wait_ge(vv.sem, vvh + 4 * (c - 1) + 4)
                    eng.copy(obsb[0:64, 0:128], psO[0:64, 0:128]).then_inc(sv.sem, 1)
            sv.n += 2 * n_c

            @b.blk.vector
            def _(eng, h=h, tvh=tvh, vvh=vvh, svh=svh):
                for c in range(n_c):
                    eng.wait_ge(sv.sem, svh + 2 * c + 1)
                    eng.tensor_tensor(mt_sb[:, 0:128], mtf[:, 0:128], mask_su[:], ALU.mult).then_inc(vv.sem, 2)
                    eng.wait_ge(tv.sem, tvh + 4 * c + 2)
                    b.fence(eng, "v", eng.tensor_scalar(rrow[0:1, 0:128], psO[64:65, 0:128],
                                                        EPS, EPS, ALU.add, ALU.max))
                    b.fence(eng, "v", eng.reciprocal(rrow[0:1, 0:128], rrow[0:1, 0:128]))
                    eng.tensor_copy(rrow_bf[0:1, 0:128], rrow[0:1, 0:128]).then_inc(vv.sem, 1)
                    eng.wait_ge(sv.sem, svh + 2 * c + 2)
                    eng.wait_ge(tv.sem, tvh + 4 * c + 4)
                    tile = h // 2
                    row0 = (h % 2) * 64
                    eng.tensor_tensor(o_fm[row0:row0 + 64, tile * SEG + 128 * c: tile * SEG + 128 * (c + 1)],
                                      obsb[0:64, 0:128], psBC[0:64, 0:128], ALU.mult)
                    eng.wait_ge(tv.sem, tvh + 4 * c + 3)
                    b.fence(eng, "v", eng.tensor_tensor(st32[:, h * 130:(h + 1) * 130],
                                                        st32[:, h * 130:(h + 1) * 130],
                                                        psS[:, 0:130], ALU.add))
                    eng.tensor_copy(stbf[:, h * 130:(h + 1) * 130],
                                    st32[:, h * 130:(h + 1) * 130]).then_inc(vv.sem, 1)
            vv.n += 4 * n_c

            b.barrier()

        tvp = tv.n
        vvp = vv.n
        st0s = [c.n for c in sst]

        @b.blk.tensor
        def _(eng, tvp=tvp, vvp=vvp):
            cnt = 0
            for i in range(NTS):
                for nh in range(2):
                    ps = psA if cnt % 2 == 0 else psB
                    if cnt >= 2:
                        eng.wait_ge(vv.sem, vvp + cnt - 1)
                    for kc in range(8):
                        inst = nc.tensor.matmul(ps[:, 0:512], o_fm[:, kc * SEG + 128 * i: kc * SEG + 128 * (i + 1)],
                                                wo[:, kc * 1024 + 512 * nh: kc * 1024 + 512 * (nh + 1)],
                                                start=(kc == 0), stop=(kc == 7))
                    inst.then_inc(tv.sem, 1)
                    cnt += 1
        tv.n += 2 * NTS

        @b.blk.vector
        def _(eng, tvp=tvp, st0s=st0s):
            cnt = 0
            for i in range(NTS):
                for nh in range(2):
                    j = i % 2
                    ps = psA if cnt % 2 == 0 else psB
                    eng.wait_ge(tv.sem, tvp + cnt + 1)
                    if cnt >= 4:
                        eng.wait_ge(sst[j].sem, st0s[j] + 16 * ((cnt - 4) // 4 + 1))
                    eng.tensor_copy(s0sb[j][:, 512 * nh:512 * (nh + 1)], ps[:, 0:512]).then_inc(vv.sem, 1)
                    cnt += 1
        vv.n += 2 * NTS

        @b.blk.gpsimd
        def _(eng, t0=t0, vvp=vvp, st0s=st0s):
            for i in range(NTS):
                j = i % 2
                eng.wait_ge(vv.sem, vvp + 2 * (i + 1))
                eng.dma_start(s0_dst[t0 + 128 * i:t0 + 128 * (i + 1), :], s0sb[j][:]).then_inc(sst[j].sem, 16)
            for j in range(2):
                eng.wait_ge(sst[j].sem, st0s[j] + 16 * len(range(j, NTS, 2)))
        for j in range(2):
            sst[j].n += 16 * len(range(j, NTS, 2))

        b.barrier()


def emit_l1attn(b: Bld, xn_bf, s1_dst):
    import contextlib
    nc = b.nc
    W = b.dram
    QT = 1024
    NQ = T // QT
    NTQ = QT // 128
    H, KD, VD = 8, 128, 256
    EPSR = 1e-5

    q_d = nc.dram_tensor("a1_q", [1024, T], BF16, kind="Internal").ap()
    k_d = nc.dram_tensor("a1_k", [1024, T], BF16, kind="Internal").ap()
    v_d = nc.dram_tensor("a1_v", [2048, T], BF16, kind="Internal").ap()
    gs_d = nc.dram_tensor("a1_gs", [2048, T], BF16, kind="Internal").ap()
    og_d = nc.dram_tensor("a1_og", [T, 2048], BF16, kind="Internal").ap()
    if L1A_SPILL:
        dbg_d = nc.dram_tensor("a1_dbg", [32 * 8, 4, 128, 256], F32, kind="Internal").ap()

    seq = Seq(b)
    wl = b.cnt("a1_wl")
    fl = b.cnt("a1_fl")

    bbuf = b.sbuf("a1_bbuf", [128, 32 * 8], F32)
    gbuf = b.sbuf("a1_gbuf", [128, 32 * 8], F32)
    negA = b.sbuf("a1_negA", [128, 8], F32)
    alog = b.sbuf("a1_alog", [1, 16], F32)
    onesc = b.sbuf("a1_onesc", [128, 1], F32)
    ones1 = b.sbuf("a1_ones1", [1, 128], BF16)
    onesf1 = b.sbuf("a1_onesf1", [1, 128], F32)
    cums = b.sbuf("a1_cums", [128, 128], F32)
    am_i = b.sbuf("a1_am_i", [128, 128], F32)
    am_sL = b.sbuf("a1_am_sL", [128, 128], F32)
    identf = b.sbuf("a1_identf", [128, 128], F32)
    bt0 = b.sbuf("a1_bt0", [128, 8], F32)
    bt1 = b.sbuf("a1_bt1", [128, 8], F32)
    bt2 = b.sbuf("a1_bt2", [128, 8], F32)
    bt3 = b.sbuf("a1_bt3", [128, 8], F32)
    albf = b.sbuf("a1_albf", [1, 16], BF16)

    ld0 = seq.dma("sync", lambda eng: [
        eng.dma_start(albf[:], W["alogdt_bf"]),
        eng.dma_start(alog[:], W["l1_alog_dt"]),
        eng.dma_start(cums[:], W["cums128"]),
        eng.dma_start(am_i[:], W["amask_i128"]),
        eng.dma_start(am_sL[:], W["amask_sL128"]),
        eng.dma_start(identf[:], W["ident"]),
    ], n=6)
    s_exp = seq.step("scalar", lambda eng: eng.activation(alog[0:1, 0:8], alog[0:1, 0:8], AF.Exp), deps=[ld0])
    v_ini = seq.step("vector", lambda eng: (
        eng.tensor_scalar(alog[0:1, 0:8], alog[0:1, 0:8], -1.0, None, ALU.mult),
        eng.memset(onesc[:], 1.0),
        eng.memset(ones1[:], 1.0),
        eng.memset(onesf1[:], 1.0))[-1], deps=[s_exp])
    ctxIN = contextlib.ExitStack()
    psIN = b.psum2("a1_psIN", [128, 128], F32, ctxIN)
    t_na = seq.step("tensor", lambda eng: nc.tensor.matmul(
        psIN[:, 0:8], onesf1[0:1, :], alog[0:1, 0:8], start=True, stop=True), deps=[v_ini])
    v_na = seq.step("vector", lambda eng: eng.tensor_copy(negA[:], psIN[:, 0:8]), deps=[t_na])

    ctxA = contextlib.ExitStack()
    xf = b.sbuf("a1_xf", [128, 8 * QT], BF16, ctxA)
    wq = b.sbuf("a1_wq", [128, 8 * 1024], BF16, ctxA)
    wk = b.sbuf("a1_wk", [128, 8 * 1024], BF16, ctxA)
    wv = b.sbuf("a1_wv", [128, 8 * 2048], BF16, ctxA)
    wg = b.sbuf("a1_wg", [128, 8 * 2048], BF16, ctxA)
    wab = b.sbuf("a1_wab", [128, 8 * 16], BF16, ctxA)
    cwq = b.sbuf("a1_cwq", [128, 8 * 4], F32, ctxA)
    cwk = b.sbuf("a1_cwk", [128, 8 * 4], F32, ctxA)
    cwv = b.sbuf("a1_cwv", [128, 16 * 4], F32, ctxA)
    raw = b.sbuf("a1_raw", [128, 1032], F32, ctxA)
    ysb = b.sbuf("a1_ysb", [128, 1024], F32, ctxA)
    sqb = b.sbuf("a1_sqb", [128, 1024], F32, ctxA)
    ssrow = b.sbuf("a1_ssrow", [1, 1024], F32, ctxA)
    rrow = b.sbuf("a1_rrow", [1, 1024], F32, ctxA)
    rrbf = b.sbuf("a1_rrbf", [1, 1024], BF16, ctxA)
    bnd = b.sbuf("a1_bnd", [128, 32 * 3], F32, ctxA)
    stg = b.sbuf("a1_stg", [128, 1024], BF16, ctxA)
    psQ = b.psum2("a1_psQ", [128, 512], F32, ctxA)
    psW = b.psum2("a1_psW", [128, 512], F32, ctxA)
    psE = b.psum2("a1_psE", [128, 512], F32, ctxA)
    psE2 = b.psum2("a1_psE2", [128, 512], F32, ctxA)
    psH8 = b.psum2("a1_psH8", [128, 16], F32, ctxA)

    wld = seq.dma("gpsimd_dma", lambda eng: [
        eng.dma_start(wq[:].rearrange("p (kc f) -> p kc f", kc=8),
                      W["l1_wq"].rearrange("(kc p) f -> p kc f", p=128)),
        eng.dma_start(wk[:].rearrange("p (kc f) -> p kc f", kc=8),
                      W["l1_wk"].rearrange("(kc p) f -> p kc f", p=128)),
        eng.dma_start(wv[:].rearrange("p (kc f) -> p kc f", kc=8),
                      W["l1_wv"].rearrange("(kc p) f -> p kc f", p=128)),
        eng.dma_start(wg[:].rearrange("p (kc f) -> p kc f", kc=8),
                      W["l1_wg"].rearrange("(kc p) f -> p kc f", p=128)),
        eng.dma_start(wab[:].rearrange("p (kc f) -> p kc f", kc=8),
                      W["l1_wab"].rearrange("(kc p) f -> p kc f", p=128)),
    ], n=5)
    cwl = seq.dma("sync", lambda eng: [
        eng.dma_start(cwq[:].rearrange("p (kc s) -> p kc s", kc=8),
                      W["l1_convq"].rearrange("(kc p) s -> p kc s", p=128)),
        eng.dma_start(cwk[:].rearrange("p (kc s) -> p kc s", kc=8),
                      W["l1_convk"].rearrange("(kc p) s -> p kc s", p=128)),
        eng.dma_start(cwv[:].rearrange("p (kc s) -> p kc s", kc=16),
                      W["l1_convv"].rearrange("(kc p) s -> p kc s", p=128)),
    ], n=3)
    v_bnd = seq.step("vector", lambda eng: eng.memset(bnd[:], 0.0), deps=[cwl])

    prev_projmm = None
    for qi in range(L1A_NQ):
        t0 = qi * QT
        xfl = seq.dma("sync", lambda eng, t0=t0: [
            eng.dma_start(xf[:, r * QT:(r + 1) * QT],
                          xn_bf[t0:t0 + QT, r * 128:(r + 1) * 128], transpose=True)
            for r in range(8)], deps=[wld, prev_projmm], n=8)

        prev_gv3 = None
        prev_gs2 = None
        prev_gs1 = None
        for i in range(NTQ):
            gcol = (qi * NTQ + i) * 8

            def abmm(eng, i=i):
                for kc in range(8):
                    nc.tensor.matmul(psH8[:, 0:16], xf[:, kc * QT + 128 * i: kc * QT + 128 * (i + 1)],
                                     wab[:, kc * 16:(kc + 1) * 16], start=(kc == 0), stop=False)
                return nc.tensor.matmul(psH8[:, 0:8], ones1[0:1, 0:128], albf[0:1, 8:16],
                                        start=False, stop=True)
            ab = seq.step("tensor", abmm, deps=[xfl, v_na, prev_gv3, prev_gs2, ld0, wld])
            gv1 = seq.step("vector", lambda eng: (
                b.fence(eng, "v", eng.tensor_scalar(bt0[:], psH8[:, 0:8], -1.0, None, ALU.mult)),
                eng.tensor_tensor(bt1[:], psH8[:, 0:8], bt0[:], ALU.max))[-1],
                deps=[ab, prev_gs1])
            gs1 = seq.step("scalar", lambda eng: (
                b.fence(eng, "s", eng.activation(bt2[:], bt1[:], AF.Exp, scale=-1.0)),
                eng.activation(bt2[:], bt2[:], AF.Ln, bias=1.0))[-1], deps=[gv1])
            prev_gs1 = gs1
            gv2 = seq.step("vector", lambda eng, gcol=gcol: (
                b.fence(eng, "v", eng.tensor_scalar(bt1[:], psH8[:, 0:8], 0.0, None, ALU.max)),
                b.fence(eng, "v", eng.tensor_tensor(bt2[:], bt2[:], bt1[:], ALU.add)),
                eng.tensor_tensor(gbuf[:, gcol:gcol + 8], bt2[:], negA[:], ALU.mult))[-1], deps=[gs1])
            gs2 = seq.step("scalar", lambda eng: eng.activation(bt3[:], psH8[:, 8:16], AF.Sigmoid), deps=[gv2])
            gv3 = seq.step("vector", lambda eng, gcol=gcol: eng.tensor_scalar(
                bbuf[:, gcol:gcol + 8], bt3[:], 2.0, None, ALU.mult), deps=[gs2])
            prev_gv3, prev_gs2 = gv3, gs2

        specs = [("q", 8, wq, cwq, True, q_d, 0), ("k", 8, wk, cwk, True, k_d, 8),
                 ("v", 16, wv, cwv, False, v_d, 16), ("g", 16, wg, None, False, gs_d, 0)]
        prev_ev = {}
        last_store = None
        for name, nm, wsb, cw, isqk, dst, bo in specs:
            for m in range(nm):
                def pjmm(eng, nm=nm, wsb=wsb, m=m):
                    for kc in range(8):
                        nc.tensor.matmul(psQ[:, 0:512],
                                         wsb[:, kc * (nm * 128) + 128 * m: kc * (nm * 128) + 128 * (m + 1)],
                                         xf[:, kc * QT: kc * QT + 512],
                                         start=(kc == 0), stop=(kc == 7))
                    for kc in range(8):
                        inst = nc.tensor.matmul(psQ[:, 0:512] if False else psW[:, 0:512],
                                                wsb[:, kc * (nm * 128) + 128 * m: kc * (nm * 128) + 128 * (m + 1)],
                                                xf[:, kc * QT + 512: kc * QT + 1024],
                                                start=(kc == 0), stop=(kc == 7))
                    return inst
                pj = seq.step("tensor", pjmm,
                              deps=[xfl, prev_ev.get("free_psQW"), (v_na if qi == 0 else None)])
                if name == "g":
                    sil = seq.step("scalar", lambda eng: (
                        eng.activation(stg[:, 0:512], psQ[:, 0:512], AF.Silu),
                        eng.activation(stg[:, 512:1024], psW[:, 0:512], AF.Silu))[-1],
                        deps=[pj, prev_ev.get("stg")])
                    prev_ev["free_psQW"] = sil
                    st = seq.dma("gpsimd_dma", lambda eng, dst=dst, m=m, t0=t0: [
                        eng.dma_start(dst[128 * m:128 * (m + 1), t0:t0 + QT], stg[:])], deps=[sil], n=1)
                    prev_ev["stg"] = st
                    last_store = st
                    continue
                bslot = bo + m

                def evconv(eng, cw=cw, m=m, bslot=bslot):
                    eng.tensor_copy(raw[:, 8:520], psQ[:, 0:512])
                    eng.tensor_copy(raw[:, 520:1032], psW[:, 0:512])
                    b.fence(eng, "v", eng.tensor_copy(raw[:, 5:8], bnd[:, 3 * bslot:3 * bslot + 3]))
                    b.fence(eng, "v", eng.tensor_scalar(ysb[:], raw[:, 8:1032], cw[:, 4 * m + 3:4 * m + 4], None, ALU.mult))
                    b.fence(eng, "v", eng.scalar_tensor_tensor(ysb[:], raw[:, 7:1031], cw[:, 4 * m + 2:4 * m + 3], ysb[:], ALU.mult, ALU.add))
                    b.fence(eng, "v", eng.scalar_tensor_tensor(ysb[:], raw[:, 6:1030], cw[:, 4 * m + 1:4 * m + 2], ysb[:], ALU.mult, ALU.add))
                    b.fence(eng, "v", eng.scalar_tensor_tensor(ysb[:], raw[:, 5:1029], cw[:, 4 * m + 0:4 * m + 1], ysb[:], ALU.mult, ALU.add))
                    return eng.tensor_copy(bnd[:, 3 * bslot:3 * bslot + 3], raw[:, 1029:1032])
                ev = seq.step("vector", evconv, deps=[pj, prev_ev.get("raw"), prev_ev.get("ysb")])
                prev_ev["raw"] = ev
                prev_ev["free_psQW"] = ev
                if not isqk:
                    silv = seq.step("scalar", lambda eng: eng.activation(stg[:], ysb[:], AF.Silu),
                                    deps=[ev, prev_ev.get("stg")])
                    prev_ev["ysb"] = silv
                    st = seq.dma("gpsimd_dma", lambda eng, dst=dst, m=m, t0=t0: [
                        eng.dma_start(dst[128 * m:128 * (m + 1), t0:t0 + QT], stg[:])], deps=[silv], n=1)
                    prev_ev["stg"] = st
                    last_store = st
                    continue
                sil = seq.step("scalar", lambda eng: (
                    b.fence(eng, "s", eng.activation(ysb[:], ysb[:], AF.Silu)),
                    eng.activation(sqb[:], ysb[:], AF.Square))[-1],
                    deps=[ev, prev_ev.get("sqb")])
                prev_ev["ysb"] = sil
                ssm = seq.step("tensor", lambda eng: (
                    nc.tensor.matmul(psE[0:1, 0:512], onesc[:], sqb[:, 0:512], start=True, stop=True),
                    nc.tensor.matmul(psE2[0:1, 0:512], onesc[:], sqb[:, 512:1024], start=True, stop=True))[-1],
                    deps=[sil, prev_ev.get("psE")])
                prev_ev["sqb"] = ssm
                nr1 = seq.step("vector", lambda eng: (
                    eng.tensor_copy(ssrow[0:1, 0:512], psE[0:1, 0:512]),
                    eng.tensor_copy(ssrow[0:1, 512:1024], psE2[0:1, 0:512]))[-1],
                    deps=[ssm, prev_ev.get("nr2")])
                nr2 = seq.step("scalar", lambda eng: (
                    b.fence(eng, "s", eng.activation(rrow[:], ssrow[:], AF.Ln)),
                    eng.activation(rrow[:], rrow[:], AF.Exp, scale=-0.5))[-1],
                    deps=[nr1, prev_ev.get("nr3")])
                nr3 = seq.step("vector", lambda eng: (
                    b.fence(eng, "v", eng.tensor_scalar(rrow[:], rrow[:], 1e6, None, ALU.min)),
                    eng.tensor_copy(rrbf[:], rrow[:]))[-1], deps=[nr2, prev_ev.get("bcm")])
                prev_ev["psE"] = nr1
                prev_ev["nr2"] = nr2
                prev_ev["nr3"] = nr3
                bcm = seq.step("tensor", lambda eng: (
                    nc.tensor.matmul(psQ[:, 0:512], ones1[0:1, :], rrbf[0:1, 0:512], start=True, stop=True),
                    nc.tensor.matmul(psW[:, 0:512], ones1[0:1, :], rrbf[0:1, 512:1024], start=True, stop=True))[-1],
                    deps=[nr3, ev])
                prev_ev["bcm"] = bcm
                fin = seq.step("vector", lambda eng: (
                    eng.tensor_tensor(stg[:, 0:512], ysb[:, 0:512], psQ[:, 0:512], ALU.mult),
                    eng.tensor_tensor(stg[:, 512:1024], ysb[:, 512:1024], psW[:, 0:512], ALU.mult))[-1],
                    deps=[bcm, sil, prev_ev.get("stg")])
                prev_ev["free_psQW"] = fin
                prev_ev["ysb"] = fin
                st = seq.dma("gpsimd_dma", lambda eng, dst=dst, m=m, t0=t0: [
                    eng.dma_start(dst[128 * m:128 * (m + 1), t0:t0 + QT], stg[:])], deps=[fin], n=1)
                prev_ev["stg"] = st
                last_store = st
        prev_projmm = last_store

    seq.barrier_all()
    ctxA.close()

    ctxIN.close()
    ctxB = contextlib.ExitStack()
    qfq = b.sbuf("a1_qfq", [128, 8 * QT], BF16, ctxB)
    kfq = b.sbuf("a1_kfq", [128, 8 * QT], BF16, ctxB)
    vtok = b.sbuf("a1_vtok", [128, 8 * 2048], BF16, ctxB)
    ktok = b.sbuf("a1_ktok", [128, 8 * 1024], BF16, ctxB)
    gstok = b.sbuf("a1_gstok", [128, 8 * 2048], BF16, ctxB)
    Sk = b.sbuf("a1_Sk", [128, 8 * 256], F32, ctxB)
    Skb = b.sbuf("a1_Skb", [128, 8 * 256], BF16, ctxB)
    lgsb = b.sbuf("a1_lgsb", [128, 8], F32, ctxB)
    gCrow = b.sbuf("a1_gCrow", [1, 8], F32, ctxB)
    gCcol = b.sbuf("a1_gCcol", [128, 8], F32, ctxB)
    gcol8 = b.sbuf("a1_gcol8", [128, 8], F32, ctxB)
    Jsb = b.sbuf("a1_Jsb", [128, 128], F32, ctxB)
    tE1 = b.sbuf("a1_tE1", [128, 128], F32, ctxB)
    tE2 = b.sbuf("a1_tE2", [128, 128], F32, ctxB)
    Ep = b.sbuf("a1_Ep", [128, 128], F32, ctxB)
    En = b.sbuf("a1_En", [128, 128], F32, ctxB)
    LamS = b.sbuf("a1_LamS", [128, 128], F32, ctxB)
    MpB = b.sbuf("a1_MpB", [128, 128], BF16, ctxB)
    Gp = b.sbuf("a1_Gp", [128, 128], F32, ctxB)
    Hp = b.sbuf("a1_Hp", [128, 128], F32, ctxB)
    Zs = b.sbuf("a1_Zs", [128, 128], F32, ctxB)
    Tpt = b.sbuf("a1_Tpt", [128, 128], F32, ctxB)
    xa_sb = b.sbuf("a1_xa", [128, 64], F32, ctxB)
    t7 = b.sbuf("a1_t7", [128, 256], F32, ctxB)
    RhsS = b.sbuf("a1_Rhs", [128, 256], F32, ctxB)
    Ub = b.sbuf("a1_Ub", [128, 256], BF16, ctxB)
    Osb = b.sbuf("a1_Osb", [128, 256], F32, ctxB)
    sqO = b.sbuf("a1_sqO", [128, 256], F32, ctxB)
    rmsc = b.sbuf("a1_rmsc", [128, 2], F32, ctxB)
    bgc = b.sbuf("a1_bgc", [128, 1], F32, ctxB)
    ktsc = b.sbuf("a1_ktsc", [128, 128], BF16, ctxB)
    OgB = b.sbuf("a1_OgB", [128, 256], BF16, ctxB)

    pM = b.psum2("a1_pM", [128, 128], F32, ctxB)
    pQ2 = b.psum2("a1_pQ2", [128, 128], F32, ctxB)
    pJ = b.psum2("a1_pJ", [128, 128], F32, ctxB)
    pJ2 = b.psum2("a1_pJ2", [128, 128], F32, ctxB)
    pZ = b.psum2("a1_pZ", [128, 128], F32, ctxB)
    pU = b.psum2("a1_pU", [128, 256], F32, ctxB)
    pP = b.psum2("a1_pP", [128, 256], F32, ctxB)
    pG8 = b.psum2("a1_pG8", [128, 16], F32, ctxB)

    vz = seq.step("vector", lambda eng: (
        eng.memset(Sk[:], 0.0), eng.memset(Skb[:], 0.0), eng.memset(Gp[:], 0.0),
        eng.memset(pJ[:], 0.0))[-1], deps=[])

    hold = {"skb": vz, "zero": vz}
    prev = {}
    for qi in range(L1A_NQ):
        t0 = qi * QT
        bload = seq.dma("sync", lambda eng, t0=t0: [
            eng.dma_start(qfq[:].rearrange("p (r t) -> p r t", r=8),
                          q_d.rearrange("(r p) t -> p r t", p=128)[:, :, t0:t0 + QT]),
            eng.dma_start(kfq[:].rearrange("p (r t) -> p r t", r=8),
                          k_d.rearrange("(r p) t -> p r t", p=128)[:, :, t0:t0 + QT]),
        ], deps=[prev.get("lastA2") or None, prev.get("q_done")], n=2)
        btr = seq.dma("sync", lambda eng, t0=t0: [
            eng.dma_start(vtok[:, tt * 2048:(tt + 1) * 2048],
                          v_d[:, t0 + 128 * tt: t0 + 128 * (tt + 1)], transpose=True)
            for tt in range(8)] + [
            eng.dma_start(ktok[:, tt * 1024:(tt + 1) * 1024],
                          k_d[:, t0 + 128 * tt: t0 + 128 * (tt + 1)], transpose=True)
            for tt in range(8)] + [
            eng.dma_start(gstok[:, tt * 2048:(tt + 1) * 2048],
                          gs_d[:, t0 + 128 * tt: t0 + 128 * (tt + 1)], transpose=True)
            for tt in range(8)], deps=[prev.get("tok_done")], n=24)
        og_last = []
        ncc_l = NTQ if L1A_DBG is None else L1A_DBG[0]
        for cc in range(ncc_l):
            gtile = qi * NTQ + cc
            gcol0 = gtile * 8
            lgm = seq.step("tensor", lambda eng, gcol0=gcol0: (
                nc.tensor.matmul(pG8[:, 0:8], cums[:], gbuf[:, gcol0:gcol0 + 8], start=True, stop=True),
                nc.tensor.matmul(pG8[0:1, 8:16], onesc[:], gbuf[:, gcol0:gcol0 + 8], start=True, stop=True))[-1],
                deps=[prev.get("pG8"), (v_na if qi == 0 and cc == 0 else None)])
            lgv = seq.step("vector", lambda eng: eng.tensor_copy(lgsb[:], pG8[:, 0:8]), deps=[lgm])
            gce = seq.step("scalar", lambda eng: (
                eng.activation(gCrow[:], pG8[0:1, 8:16], AF.Exp),
                eng.activation(gcol8[:], lgsb[:], AF.Exp))[-1], deps=[lgv, lgm])
            gcb = seq.step("tensor", lambda eng: nc.tensor.matmul(
                pG8[:, 8:16], onesf1[0:1, :], gCrow[0:1, :], start=True, stop=True), deps=[gce])
            gcv = seq.step("vector", lambda eng: eng.tensor_copy(gCcol[:], pG8[:, 8:16]), deps=[gcb])
            prev["pG8"] = gcv
            nh_l = H if L1A_DBG is None else L1A_DBG[1]
            for h in range(nh_l):
                ksl = kfq[:, h * QT + 128 * cc: h * QT + 128 * (cc + 1)]
                qsl = qfq[:, h * QT + 128 * cc: h * QT + 128 * (cc + 1)]
                vsl = vtok[:, cc * 2048 + 256 * h: cc * 2048 + 256 * (h + 1)]
                ktsl = ktok[:, cc * 1024 + 128 * h: cc * 1024 + 128 * (h + 1)]
                gssl = gstok[:, cc * 2048 + 256 * h: cc * 2048 + 256 * (h + 1)]
                bcol = bbuf[:, gcol0 + h: gcol0 + h + 1]
                skslab = slice(h * 256, (h + 1) * 256)

                akk = seq.step("tensor", lambda eng, ksl=ksl: nc.tensor.matmul(
                    pM[:], ksl, ksl, start=True, stop=True), deps=[bload, prev.get("pM")])
                akq = seq.step("tensor", lambda eng, ksl=ksl, qsl=qsl: nc.tensor.matmul(
                    pQ2[:], ksl, qsl, start=True, stop=True), deps=[bload, prev.get("pQ2")])
                jt = seq.step("tensor", lambda eng, h=h: nc.tensor.transpose(
                    pJ[:], lgsb[:, h:h + 1].broadcast_to([128, 128]), identf[:]),
                    deps=[lgv, prev.get("pJ"), hold.get("zero")])
                jev = seq.step("vector", lambda eng: eng.tensor_copy(Jsb[:], pJ[:]),
                               deps=[jt, prev.get("Jsb")])
                j2t = seq.step("tensor", lambda eng: nc.tensor.transpose(pJ2[:], Jsb[:], identf[:]),
                               deps=[jev, prev.get("pJ2")])
                ep1 = seq.step("vector", lambda eng: eng.tensor_tensor(tE1[:], Jsb[:], am_i[:], ALU.add),
                               deps=[jev, prev.get("tE1")])
                ep2 = seq.step("vector", lambda eng: eng.tensor_tensor(tE2[:], tE1[:], pJ2[:], ALU.subtract),
                               deps=[ep1, j2t, prev.get("tE2")])
                epx = seq.step("scalar", lambda eng: eng.activation(Ep[:], tE2[:], AF.Exp),
                               deps=[ep2, prev.get("Ep")])
                en1 = seq.step("vector", lambda eng: eng.tensor_tensor(tE1[:], pJ2[:], am_sL[:], ALU.add),
                               deps=[j2t, ep2])
                en2 = seq.step("vector", lambda eng: eng.tensor_tensor(tE2[:], tE1[:], Jsb[:], ALU.subtract),
                               deps=[en1, epx])
                enx = seq.step("scalar", lambda eng: eng.activation(En[:], tE2[:], AF.Exp),
                               deps=[en2, prev.get("En")])
                lam1 = seq.step("vector", lambda eng: eng.tensor_tensor(tE1[:], En[:], pM[:], ALU.mult),
                                deps=[enx, akk, en2])
                lam2 = seq.step("vector", lambda eng, bcol=bcol: eng.tensor_scalar(
                    LamS[:], tE1[:], bcol, None, ALU.mult), deps=[lam1, prev.get("LamS")])
                mp1 = seq.step("vector", lambda eng: eng.tensor_tensor(tE2[:], Ep[:], pQ2[:], ALU.mult),
                               deps=[epx, akq, enx])
                mp2 = seq.step("vector", lambda eng: eng.tensor_copy(MpB[:], tE2[:]),
                               deps=[mp1, prev.get("MpB")])
                gp1 = seq.step("vector", lambda eng: (
                    eng.tensor_copy(Gp[0:64, 0:64], LamS[0:64, 0:64]),
                    eng.tensor_copy(Gp[64:128, 64:128], LamS[64:128, 64:128]))[-1],
                    deps=[lam2, prev.get("Gp")])
                h0t = seq.step("tensor", lambda eng: nc.tensor.transpose(pZ[:], Gp[:], identf[:]),
                               deps=[gp1, prev.get("pZ")])
                h0e = seq.step("vector", lambda eng: eng.tensor_copy(Hp[:], pZ[:]), deps=[h0t, prev.get("Hp")])
                z0 = seq.step("vector", lambda eng: eng.tensor_tensor(Zs[:], identf[:], Gp[:], ALU.subtract),
                              deps=[gp1, prev.get("Zs")])
                gj, hj, zj = gp1, h0e, z0
                for lev in range(5):
                    sq1 = seq.step("tensor", lambda eng: nc.tensor.matmul(
                        pZ[:], Hp[:], Gp[:], start=True, stop=True), deps=[hj, gj, h0e, en1])
                    sq2 = seq.step("tensor", lambda eng: nc.tensor.matmul(
                        pJ2[:], Gp[:], Hp[:], start=True, stop=True), deps=[hj, gj, en1])
                    ge = seq.step("vector", lambda eng: eng.tensor_copy(Gp[:], pZ[:]),
                                  deps=[sq1, sq2, z0])
                    he = seq.step("vector", lambda eng: eng.tensor_copy(Hp[:], pJ2[:]), deps=[sq2])
                    zm = seq.step("tensor", lambda eng: nc.tensor.matmul(
                        pU[:, 0:128], Hp[:], Zs[:], start=True, stop=True), deps=[he, zj, prev.get("pU")])
                    za = seq.step("vector", lambda eng: eng.tensor_tensor(Zs[:], Zs[:], pU[:, 0:128], ALU.add),
                                  deps=[zm])
                    gj, hj, zj = ge, he, za
                    prev["pU"] = za
                tpt = seq.step("tensor", lambda eng: nc.tensor.transpose(pZ[:], Zs[:], identf[:]),
                               deps=[zj, gj])
                tpe = seq.step("vector", lambda eng: eng.tensor_copy(Tpt[:], pZ[:]), deps=[tpt, prev.get("Tpt")])
                sa = seq.step("tensor", lambda eng: nc.tensor.matmul(
                    pU[0:64, 0:64], LamS[64:128, 0:64], Tpt[64:128, 64:128], start=True, stop=True),
                    deps=[tpe, lam2, prev.get("pU")])
                sae = seq.step("vector", lambda eng: eng.tensor_copy(xa_sb[0:64, 0:64], pU[0:64, 0:64]),
                               deps=[sa])
                sb_ = seq.step("tensor", lambda eng: nc.tensor.matmul(
                    pU[0:64, 64:128], Zs[0:64, 0:64], xa_sb[0:64, 0:64], start=True, stop=True),
                    deps=[sae, zj])
                sbe = seq.step("vector", lambda eng: eng.tensor_scalar(
                    Tpt[0:64, 64:128], pU[0:64, 64:128], -1.0, None, ALU.mult), deps=[sb_, tpe])
                p3 = seq.step("tensor", lambda eng, ksl=ksl, skslab=skslab: nc.tensor.matmul(
                    pP[:], ksl, Skb[:, skslab], start=True, stop=True),
                    deps=[bload, hold.get("skb"), prev.get("pP")])
                bg = seq.step("vector", lambda eng, h=h, bcol=bcol: eng.tensor_tensor(
                    bgc[:], bcol, gcol8[:, h:h + 1], ALU.mult), deps=[gce, prev.get("bgc")])
                r1 = seq.step("vector", lambda eng: eng.tensor_scalar(t7[:], pP[:], bgc[:], None, ALU.mult),
                              deps=[p3, bg, prev.get("t7")])
                r2 = seq.step("vector", lambda eng, vsl=vsl, bcol=bcol: eng.scalar_tensor_tensor(
                    RhsS[:], vsl, bcol, t7[:], ALU.mult, ALU.subtract), deps=[r1, btr, prev.get("Rhs")])
                um = seq.step("tensor", lambda eng: nc.tensor.matmul(
                    pU[:, 0:256], Tpt[:], RhsS[:], start=True, stop=True), deps=[sbe, r2, sbe])
                ue = seq.step("vector", lambda eng: eng.tensor_copy(Ub[:], pU[:, 0:256]),
                              deps=[um, prev.get("Ub")])
                om = seq.step("tensor", lambda eng: nc.tensor.matmul(
                    pP[:], MpB[:], Ub[:], start=True, stop=True), deps=[ue, mp2, r1])
                p2 = seq.step("tensor", lambda eng, qsl=qsl, skslab=skslab: nc.tensor.matmul(
                    pM[:] if False else pU[:, 0:256], qsl, Skb[:, skslab], start=True, stop=True),
                    deps=[bload, hold.get("skb"), ue])
                oa1 = seq.step("vector", lambda eng, h=h: eng.tensor_scalar(
                    t7[:], pU[:, 0:256], gcol8[:, h:h + 1], None, ALU.mult), deps=[p2, r2])
                oa2 = seq.step("vector", lambda eng: eng.tensor_tensor(Osb[:], t7[:], pP[:], ALU.add),
                               deps=[oa1, om, prev.get("Osb")])
                prev["pP"] = oa2
                prev["pU"] = oa1
                dec = seq.step("vector", lambda eng, ktsl=ktsl: eng.tensor_scalar(
                    ktsc[:], ktsl, Ep[:, 127:128], None, ALU.mult), deps=[epx, btr, prev.get("ktsc")])
                sup = seq.step("tensor", lambda eng: nc.tensor.matmul(
                    pP[:], ktsc[:], Ub[:], start=True, stop=True), deps=[dec, ue, oa2])
                skbu = seq.step("vector", lambda eng, h=h, skslab=skslab: (
                    b.fence(eng, "v", eng.scalar_tensor_tensor(
                        Sk[:, skslab], Sk[:, skslab], gCcol[:, h:h + 1],
                        pP[:], ALU.mult, ALU.add)),
                    eng.tensor_copy(Skb[:, skslab], Sk[:, skslab]))[-1], deps=[sup, gcv])
                hold["skb"] = skbu
                prev["pP" ] = skbu
                rm1 = seq.step("scalar", lambda eng: eng.activation(sqO[:], Osb[:], AF.Square),
                               deps=[oa2, prev.get("sqO")])
                rm2 = seq.step("vector", lambda eng: (
                    b.fence(eng, "v", eng.reduce_sum(rmsc[:, 0:1], sqO[:], axis=AX.X)),
                    eng.tensor_scalar(rmsc[:, 0:1], rmsc[:, 0:1], 1.0 / 256, EPSR, ALU.mult, ALU.add))[-1],
                    deps=[rm1])
                rm3 = seq.step("scalar", lambda eng: eng.activation(rmsc[:, 1:2], rmsc[:, 0:1], AF.Sqrt),
                               deps=[rm2])
                rm4 = seq.step("vector", lambda eng: eng.reciprocal(rmsc[:, 0:1], rmsc[:, 1:2]), deps=[rm3])
                gat = seq.step("vector", lambda eng, gssl=gssl: (
                    b.fence(eng, "v", eng.tensor_scalar(t7[:], Osb[:], rmsc[:, 0:1], None, ALU.mult)),
                    eng.tensor_tensor(OgB[:], t7[:], gssl, ALU.mult))[-1],
                    deps=[rm4, btr, prev.get("OgB")])
                prev["t7"] = gat
                if L1A_SPILL:
                    idx = (qi * NTQ + cc) * 8 + h
                    sp0 = seq.dma("gpsimd_dma", lambda eng, idx=idx: [
                        eng.dma_start(dbg_d[idx, 0], Ub[:])], deps=[ue, om], n=1)
                    sp1 = seq.dma("gpsimd_dma", lambda eng, idx=idx: [
                        eng.dma_start(dbg_d[idx, 1], Osb[:])], deps=[oa2, gat], n=1)
                    sp2 = seq.dma("gpsimd_dma", lambda eng, idx=idx: [
                        eng.dma_start(dbg_d[idx, 2][:, 0:128], LamS[:])], deps=[lam2, sb_], n=1)
                    sp3 = seq.dma("gpsimd_dma", lambda eng, idx=idx: [
                        eng.dma_start(dbg_d[idx, 3], RhsS[:])], deps=[r2, um], n=1)
                    prev["_sp"] = (sp0, sp1, sp2, sp3)
                ogst = seq.dma("gpsimd_dma", lambda eng, t0=t0, cc=cc, h=h: [
                    eng.dma_start(og_d[t0 + 128 * cc: t0 + 128 * (cc + 1), 256 * h:256 * (h + 1)], OgB[:])],
                    deps=[gat], n=1)
                og_last.append(ogst)
                prev["OgB"] = ogst
                prev["pM"] = lam1
                prev["pQ2"] = mp1
                prev["pJ"] = ep2 if False else jev
                prev["pJ2"] = he
                prev["tE1"] = lam2
                prev["tE2"] = mp2
                prev["Ep"] = dec
                prev["En"] = lam1
                prev["LamS"] = sb_ if not L1A_SPILL else prev["_sp"][2]
                prev["MpB"] = om
                prev["Gp"] = sq1
                prev["Hp"] = sq2 if False else zm
                prev["Zs"] = sb_
                prev["Tpt"] = um
                prev["bgc"] = r1
                prev["Rhs"] = um if not L1A_SPILL else prev["_sp"][3]
                prev["Ub"] = sup if not L1A_SPILL else prev["_sp"][0]
                prev["Osb"] = gat if not L1A_SPILL else prev["_sp"][1]
                prev["sqO"] = rm2
                prev["ktsc"] = sup
                prev["pZ"] = tpe
                prev["Hp"] = he
                prev["Gp"] = ge
                prev["pJ"] = jev
                prev["Jsb"] = en2
        prev["q_done"] = og_last[-1]
        prev["tok_done"] = og_last[-1]
        prev["lastA2"] = og_last[-1]
    seq.barrier_all()
    ctxB.close()

    ctxC = contextlib.ExitStack()
    woS = b.sbuf("a1_woS", [128, 16 * 1024], BF16, ctxC)
    wof = b.sbuf("a1_wof", [128, 1024], F32, ctxC)
    onwc = b.sbuf("a1_onw", [128, 2], F32, ctxC)
    ogf = b.sbuf("a1_ogf", [128, 16 * 1024], BF16, ctxC)
    s1sb = b.sbuf("a1_s1sb", [128, 1024], F32, ctxC)
    psQ3 = b.psum2("a1_psQ3", [128, 512], F32, ctxC)
    psW3 = b.psum2("a1_psW3", [128, 512], F32, ctxC)

    onl = seq.dma("sync", lambda eng: [eng.dma_start(onwc[:], W["l1_onw"])], n=1)
    prev_sc = None
    for kc in range(16):
        wld2 = seq.dma("sync", lambda eng, kc=kc: [
            eng.dma_start(wof[:], W["l1_wo"][128 * kc:128 * (kc + 1), :])], deps=[prev_sc], n=1)
        sc = seq.step("vector", lambda eng, kc=kc: eng.tensor_scalar(
            woS[:, 1024 * kc:1024 * (kc + 1)], wof[:], onwc[:, kc % 2:kc % 2 + 1], None, ALU.mult),
            deps=[wld2, onl])
        prev_sc = sc
    wo_ready = prev_sc

    prev_pj = None
    prev_ev = None
    for qi in (range(L1A_NQ) if L1A_DBG is None else []):
        t0 = qi * QT
        ogl = seq.dma("sync", lambda eng, t0=t0: [
            eng.dma_start(ogf[:, kc * 1024:(kc + 1) * 1024],
                          og_d[t0:t0 + QT, 128 * kc:128 * (kc + 1)], transpose=True)
            for kc in range(16)], deps=[prev_pj], n=16)
        for i in range(NTQ):
            def womm(eng, i=i):
                for nh in range(2):
                    ps = psQ3 if nh == 0 else psW3
                    for kc in range(16):
                        inst = nc.tensor.matmul(ps[:, 0:512],
                                                ogf[:, kc * QT + 128 * i: kc * QT + 128 * (i + 1)],
                                                woS[:, kc * 1024 + 512 * nh: kc * 1024 + 512 * (nh + 1)],
                                                start=(kc == 0), stop=(kc == 15))
                return inst
            pj = seq.step("tensor", womm, deps=[ogl, wo_ready, prev_ev])
            ev = seq.step("vector", lambda eng: (
                eng.tensor_copy(s1sb[:, 0:512], psQ3[:, 0:512]),
                eng.tensor_copy(s1sb[:, 512:1024], psW3[:, 0:512]))[-1], deps=[pj])
            st = seq.dma("gpsimd_dma", lambda eng, t0=t0, i=i: [
                eng.dma_start(s1_dst[t0 + 128 * i: t0 + 128 * (i + 1), :], s1sb[:])], deps=[ev], n=1)
            prev_ev = st
            prev_pj = pj
    seq.barrier_all()
    ctxC.close()




def emit_l1ffn(b: Bld, xn1_bf, silg_d):
    nc = b.nc
    W = b.dram
    QT = 512
    NQ = F_NT
    knd = "ExternalOutput" if F_DBG else "Internal"
    if F_DBG:
        odbg_d = nc.dram_tensor("f_odbg", [512, T], BF16, kind=knd).ap()
        ddbg_d = nc.dram_tensor("f_ddbg", [8, T], F32, kind=knd).ap()
        pdbg_d = nc.dram_tensor("f_pdbg", [8 * 128, 512], BF16, kind=knd).ap()
        vdbg_d = nc.dram_tensor("f_vdbg", [128, 8 * 128], BF16, kind=knd).ap()
        podbg_d = nc.dram_tensor("f_podbg", [128, 512], F32, kind=knd).ap()
    qr_d = nc.dram_tensor("f_qr", [512, T], BF16, kind=knd).ap()
    kr_d = nc.dram_tensor("f_kr", [512, T], BF16, kind=knd).ap()
    v_d = nc.dram_tensor("f_vd", [512, T], BF16, kind=knd).ap()
    xatt_d = nc.dram_tensor("f_xatt", [1024, T], F32, kind=knd).ap()
    x2f_d = nc.dram_tensor("f_x2f", [1024, T], F32, kind=knd).ap()
    x2bf_d = nc.dram_tensor("f_x2bf", [1024, T], BF16, kind=knd).ap()

    gctx = ExitStack()
    onesc = b.sbuf("ff_onesc", [128, 1], F32, ctx=gctx)
    onesr = b.sbuf("ff_onesr", [1, 128], F32, ctx=gctx)

    for l in range(F_NL):
        ctxA = ExitStack()
        wq_sb = b.sbuf(f"ffa{l}_wqkv", [128, 8 * 1536], BF16, ctx=ctxA)
        wq2_sb = b.sbuf(f"ffa{l}_wqkv2", [128, 8 * 1024], BF16, ctx=ctxA)
        qscol = b.sbuf(f"ffa{l}_qs", [128, 4], F32, ctx=ctxA)
        tbl = [b.sbuf(f"ffa{l}_tbl{i}", [128, 4 * 512], F32, ctx=ctxA) for i in range(2)]
        xcur = [b.sbuf(f"ffa{l}_x{i}", [128, 8 * 512], BF16, ctx=ctxA) for i in range(2)]
        xsh = [b.sbuf(f"ffa{l}_xs{i}", [128, 4 * 512], BF16, ctx=ctxA) for i in range(2)]
        sqsb = [b.sbuf(f"ffa{l}_sq{i}", [128, 512], F32, ctx=ctxA) for i in range(2)]
        aob = [b.sbuf(f"ffa{l}_ob{i}", [128, 512], BF16, ctx=ctxA) for i in range(2)]
        tw = [b.sbuf(f"ffa{l}_t{i}", [128, 512], F32, ctx=ctxA) for i in range(4)]
        rsA = b.sbuf(f"ffa{l}_rA", [1, 512], F32, ctx=ctxA)
        rsB = b.sbuf(f"ffa{l}_rB", [1, 512], F32, ctx=ctxA)
        rsA2 = b.sbuf(f"ffa{l}_rA2", [1, 512], F32, ctx=ctxA)
        rsB2 = b.sbuf(f"ffa{l}_rB2", [1, 512], F32, ctx=ctxA)
        rsA3 = b.sbuf(f"ffa{l}_rA3", [1, 512], F32, ctx=ctxA)
        rsB3 = b.sbuf(f"ffa{l}_rB3", [1, 512], F32, ctx=ctxA)
        bbS = b.sbuf(f"ffa{l}_bbS", [128, 512], F32, ctx=ctxA)
        psP = [b.psum2(f"ffa{l}_pP{i}", [128, 512], ctx=ctxA) for i in range(2)]
        psP2 = [b.psum2(f"ffa{l}_pQ{i}", [128, 512], ctx=ctxA) for i in range(2)]
        psN = b.psum2(f"ffa{l}_pN", [1, 512], ctx=ctxA)
        psN2 = b.psum2(f"ffa{l}_pN2", [1, 512], ctx=ctxA)
        psBB = b.psum2(f"ffa{l}_pB", [128, 512], ctx=ctxA)

        seq = Seq(b)
        ini = None
        if l == 0:
            ini = seq.step("vector", lambda eng: (
                eng.memset(onesc[:], 1.0),
                eng.memset(onesr[:], 1.0))[-1])
        wld = seq.dma("sync", lambda eng, l=l: [
            eng.dma_start(wq_sb[:], W["f_wqkv"][l].rearrange("(kc p) f -> p kc f", p=128)),
            eng.dma_start(wq2_sb[:], W["f_wqkv2"][l].rearrange("(kc p) f -> p kc f", p=128))], n=2)
        qsld = seq.dma("sync", lambda eng, l=l: [
            eng.dma_start(qscol[:, 0:1], W["f_qs"][l]),
            eng.dma_start(qscol[:, 1:2], W["f_qs_sw"][l]),
            eng.dma_start(qscol[:, 2:3], W["f_ks"][l]),
            eng.dma_start(qscol[:, 3:4], W["f_ks_sw"][l])], n=4)

        pr_rhs = [None, None]
        pr_sh = None
        pr_tblread = [None, None]
        pr_ppc = [(), ()]
        pr_pp2c = [(), ()]
        pr_sqc = [(), ()]
        pr_rw = None
        pr_rv = None
        pr_bc = None
        pr_rotbb = None
        pr_bbread = None
        pr_aost = [None, None]
        gp = gp2 = gsq = gao = 0

        for t in range(NQ):
            t0 = QT * t
            tb = t % 2
            if l == 0:
                xld = seq.dma("sync", lambda eng, tb=tb, t0=t0: [
                    eng.dma_start(xcur[tb][:, 512 * c:512 * (c + 1)],
                                  xn1_bf[t0:t0 + QT, 128 * c:128 * (c + 1)], transpose=True)
                    for c in range(8)], deps=[pr_rhs[tb], pr_sh], n=8)

                def shmk(eng, tb=tb, pb=(t - 1) % 2, first=(t == 0)):
                    ii = None
                    for ci in range(4):
                        c = ci + 4
                        eng.tensor_copy(xsh[tb][:, 512 * ci + 1:512 * ci + 512],
                                        xcur[tb][:, 512 * c:512 * c + 511])
                        if first:
                            ii = eng.memset(xsh[tb][:, 512 * ci:512 * ci + 1], 0.0)
                        else:
                            ii = eng.tensor_copy(xsh[tb][:, 512 * ci:512 * ci + 1],
                                                 xcur[pb][:, 512 * c + 511:512 * c + 512])
                    return ii
                sh = seq.step("vector", shmk, deps=[xld, pr_rhs[tb]])
                pr_sh = sh
            else:
                sh = None
                if t == 0:
                    z0 = seq.step("vector", lambda eng, tb=tb: [
                        eng.memset(xcur[tb][:, 512 * c:512 * c + 1], 0.0) for c in range(4, 8)][-1],
                        deps=[pr_rhs[tb]])

                    def x0ld(eng, tb=tb):
                        out = []
                        for c in range(4):
                            out.append(eng.dma_start(xcur[tb][:, 512 * c:512 * (c + 1)],
                                                     x2bf_d[128 * c:128 * (c + 1), 0:QT]))
                        for c in range(4, 8):
                            out.append(eng.dma_start(xcur[tb][:, 512 * c + 1:512 * (c + 1)],
                                                     x2bf_d[128 * c:128 * (c + 1), 0:QT - 1]))
                        return out
                    xld = seq.dma("sync", x0ld, deps=[pr_rhs[tb], z0], n=8)
                else:
                    def xnld(eng, tb=tb, t0=t0):
                        out = []
                        for c in range(4):
                            out.append(eng.dma_start(xcur[tb][:, 512 * c:512 * (c + 1)],
                                                     x2bf_d[128 * c:128 * (c + 1), t0:t0 + QT]))
                        for c in range(4, 8):
                            out.append(eng.dma_start(xcur[tb][:, 512 * c:512 * (c + 1)],
                                                     x2bf_d[128 * c:128 * (c + 1), t0 - 1:t0 + QT - 1]))
                        return out
                    xld = seq.dma("sync", xnld, deps=[pr_rhs[tb]], n=8)
            tld = seq.dma("scalar_dma", lambda eng, tb=tb, t0=t0: [
                eng.dma_start(tbl[tb][:, 0:512], W["rot_csq"][:, t0:t0 + QT]),
                eng.dma_start(tbl[tb][:, 512:1024], W["rot_ssq"][:, t0:t0 + QT]),
                eng.dma_start(tbl[tb][:, 1024:1536], W["rot_csk"][:, t0:t0 + QT]),
                eng.dma_start(tbl[tb][:, 1536:2048], W["rot_ssk"][:, t0:t0 + QT])],
                deps=[pr_tblread[tb]], n=4)

            def rhs_c(kc, tb=tb):
                if l == 0 and kc >= 4:
                    return xsh[tb][:, 512 * (kc - 4):512 * (kc - 3)]
                return xcur[tb][:, 512 * kc:512 * (kc + 1)]

            last_mm = None
            last_rot = None
            for kind in range(3):
                for oc in range(4):
                    pp = gp % 2; gp += 1
                    wcol = 512 * kind + 128 * oc

                    def mmR(eng, pp=pp, wcol=wcol, tb=tb):
                        ii = None
                        for kc in range(8):
                            ii = nc.tensor.matmul(psP[pp][:], wq_sb[:, 1536 * kc + wcol:1536 * kc + wcol + 128],
                                                  rhs_c(kc, tb), start=(kc == 0), stop=(kc == 7))
                        return ii
                    mm_r = seq.step("tensor", mmR, deps=[xld, sh, wld] + list(pr_ppc[pp]))
                    last_mm = mm_r
                    if kind == 2:
                        ao = gao % 2; gao += 1
                        cpv = seq.step("scalar", lambda eng, ao=ao, pp=pp: eng.activation(
                            aob[ao][:], psP[pp][:], AF.Copy), deps=[mm_r, pr_aost[ao]])
                        sto = seq.dma("gpsimd_dma", lambda eng, ao=ao, oc=oc, t0=t0: eng.dma_start(
                            v_d[128 * oc:128 * (oc + 1), t0:t0 + QT], aob[ao][:]), deps=[cpv], n=1)
                        pr_ppc[pp] = (cpv,)
                        pr_aost[ao] = sto
                        continue
                    pp2 = gp2 % 2; gp2 += 1
                    w2col = 512 * kind + 128 * oc

                    def mmS(eng, pp2=pp2, w2col=w2col, tb=tb):
                        ii = None
                        for kc in range(8):
                            ii = nc.tensor.matmul(psP2[pp2][:], wq2_sb[:, 1024 * kc + w2col:1024 * kc + w2col + 128],
                                                  rhs_c(kc, tb), start=(kc == 0), stop=(kc == 7))
                        return ii
                    mm_s = seq.step("tensor", mmS, deps=[xld, sh, wld] + list(pr_pp2c[pp2]))
                    sb2 = gsq % 2; gsq += 1
                    sqs = seq.step("scalar", lambda eng, sb2=sb2, pp=pp: eng.activation(
                        sqsb[sb2][:], psP[pp][:], AF.Square), deps=[mm_r] + list(pr_sqc[sb2]))
                    nm1 = seq.step("tensor", lambda eng, sb2=sb2: nc.tensor.matmul(
                        psN[0:1, :], onesc[0:64, :], sqsb[sb2][0:64, :], start=True, stop=True),
                        deps=[sqs, pr_rw])
                    nm2 = seq.step("tensor", lambda eng, sb2=sb2: nc.tensor.matmul(
                        psN2[0:1, :], onesc[64:128, :], sqsb[sb2][64:128, :], start=True, stop=True),
                        deps=[sqs, pr_rw])
                    rw = seq.step("scalar", lambda eng: (
                        b.fence(eng, "s", eng.activation(rsA[:], psN[0:1, :], AF.Ln)),
                        b.fence(eng, "s", eng.activation(rsB[:], psN2[0:1, :], AF.Ln)),
                        eng.activation(rsA3[:], rsA[:], AF.Exp, scale=-0.5),
                        eng.activation(rsB3[:], rsB[:], AF.Exp, scale=-0.5))[-1],
                        deps=[nm1, nm2, pr_rv, pr_rw])
                    rv = seq.step("vector", lambda eng: (
                        eng.tensor_scalar(rsA2[:], rsA3[:], 1e12, None, ALU.min),
                        eng.tensor_scalar(rsB2[:], rsB3[:], 1e12, None, ALU.min))[-1], deps=[rw, pr_bc])
                    bc = seq.step("tensor", lambda eng: (
                        nc.tensor.matmul(psBB[0:64, :], onesr[0:1, 0:64], rsA2[:], start=True, stop=True),
                        nc.tensor.matmul(psBB[64:128, :], onesr[0:1, 0:64], rsB2[:], start=True, stop=True))[-1],
                        deps=[rv, pr_rotbb])
                    bbe = seq.step("scalar", lambda eng: eng.activation(
                        bbS[:], psBB[:], AF.Copy), deps=[bc, pr_bbread])
                    ao = gao % 2; gao += 1
                    scol = 2 * kind
                    toff = 1024 * kind

                    def rotk(eng, pp=pp, pp2=pp2, ao=ao, tb=tb, scol=scol, toff=toff):
                        b.fence(eng, "v", eng.scalar_tensor_tensor(
                            tw[0][:], psP[pp][:], qscol[:, scol:scol + 1], bbS[:], ALU.mult, ALU.mult))
                        b.fence(eng, "v", eng.scalar_tensor_tensor(
                            tw[1][:], psP2[pp2][:], qscol[:, scol + 1:scol + 2], bbS[:], ALU.mult, ALU.mult))
                        b.fence(eng, "v", eng.tensor_tensor(
                            tw[2][:], tw[0][:], tbl[tb][:, toff:toff + 512], ALU.mult))
                        b.fence(eng, "v", eng.tensor_tensor(
                            tw[3][:], tw[1][:], tbl[tb][:, toff + 512:toff + 1024], ALU.mult))
                        return eng.tensor_tensor(aob[ao][:], tw[2][:], tw[3][:], ALU.add)
                    rot = seq.step("vector", rotk, deps=[bbe, mm_r, mm_s, tld, qsld, ini, pr_aost[ao]])
                    last_rot = rot
                    dst = qr_d if kind == 0 else kr_d
                    sto = seq.dma("gpsimd_dma", lambda eng, ao=ao, oc=oc, t0=t0, dst=dst: eng.dma_start(
                        dst[128 * oc:128 * (oc + 1), t0:t0 + QT], aob[ao][:]), deps=[rot], n=1)
                    pr_ppc[pp] = (sqs, rot)
                    pr_pp2c[pp2] = (rot,)
                    pr_sqc[sb2] = (nm1, nm2)
                    pr_rw = rw
                    pr_rv = rv
                    pr_bc = bc
                    pr_rotbb = bbe
                    pr_bbread = rot
                    pr_aost[ao] = sto
            pr_rhs[tb] = last_mm
            pr_tblread[tb] = last_rot
        seq.barrier_all()
        ctxA.close()

        ctxB = ExitStack()
        maskb = b.sbuf(f"ffb{l}_mask", [128, 8 * 512], BF16, ctx=ctxB)
        wo_sb = b.sbuf(f"ffb{l}_wo", [128, 4 * 1024], BF16, ctx=ctxB)
        qt = [b.sbuf(f"ffb{l}_q{i}", [64, 512], BF16, ctx=ctxB) for i in range(2)]
        kt = [b.sbuf(f"ffb{l}_k{i}", [64, 1024], BF16, ctx=ctxB) for i in range(2)]
        vpt = [b.sbuf(f"ffb{l}_v{i}", [128, 8 * 128], BF16, ctx=ctxB) for i in range(2)]
        onescb = b.sbuf(f"ffb{l}_ones", [128, 1], BF16, ctx=ctxB)
        clsb = [b.sbuf(f"ffb{l}_cl{i}", [128, 512], F32, ctx=ctxB) for i in range(4)]
        ebf = [b.sbuf(f"ffb{l}_e{i}", [128, 512], BF16, ctx=ctxB) for i in range(4)]
        psb = [b.sbuf(f"ffb{l}_p{i}", [128, 512], BF16, ctx=ctxB) for i in range(4)]
        rc = [b.sbuf(f"ffb{l}_rcv{i}", [1, 512], F32, ctx=ctxB) for i in range(2)]
        rc0 = b.sbuf(f"ffb{l}_rc0", [1, 512], F32, ctx=ctxB)
        rbS = b.sbuf(f"ffb{l}_rb", [64, 512], F32, ctx=ctxB)
        if F_DBG:
            podbg_sb = b.sbuf(f"ffb{l}_podbg", [128, 512], F32, ctx=ctxB)
        o_f = [b.sbuf(f"ffb{l}_of{i}", [128, 4 * 512], BF16, ctx=ctxB) for i in range(2)]
        if l == 0:
            xres = [b.sbuf(f"ffb{l}_xr{i}", [128, 8 * 512], BF16, ctx=ctxB) for i in range(2)]
        else:
            xres = [b.sbuf(f"ffb{l}_xr{i}", [128, 8 * 512], F32, ctx=ctxB) for i in range(2)]
        xo_sb = [b.sbuf(f"ffb{l}_xo{i}", [128, 512], F32, ctx=ctxB) for i in range(2)]
        psS = [b.psum2(f"ffb{l}_pS{i}", [128, 512], ctx=ctxB) for i in range(2)]
        psO = [b.psum2(f"ffb{l}_pO{i}", [128, 512], ctx=ctxB) for i in range(2)]
        psD = b.psum2(f"ffb{l}_pD", [1, 512], ctx=ctxB)
        psB = b.psum2(f"ffb{l}_pBc", [128, 512], ctx=ctxB)
        psX = [b.psum2(f"ffb{l}_pX{i}", [128, 512], ctx=ctxB) for i in range(2)]

        seq = Seq(b)
        vini = seq.step("vector", lambda eng: eng.memset(onescb[:], 1.0))
        mld = seq.dma("sync", lambda eng: eng.dma_start(maskb[:], W["fmask"]), n=1)
        wold = seq.dma("sync", lambda eng, l=l: eng.dma_start(
            wo_sb[:], W["f_wo"][l].rearrange("(kc p) f -> p kc f", p=128)), n=1)

        pr_sim_hb = [None, None]
        pr_vpt = [None, None]
        pr_clamp = [None, None]
        pr_exp = [None] * 4
        pr_mask = [None] * 4
        pr_mm2b = [None] * 4
        pr_divO = [None, None]
        pr_bcmO = [None, None]
        pr_bcm = None
        pr_bce = None
        pr_rbread = None
        pr_dnst = None
        pr_rec = None
        pr_pdbg = [None, None]
        pr_odbg = [None, None]
        pr_ofuse = [None, None]
        pr_resread = [None, None]
        pr_resp = [None, None]
        pr_stoxo = [None, None]
        ga = 0
        gam = 0
        gfo = 0
        for w in range(NQ):
            q0 = QT * w
            na = 4 if w == 0 else 8
            kw0 = max(0, q0 - 512)
            wb = w % 2
            last_div = None
            pend = None

            def emit_tail(ob, h, mm2, rec, pb, cur_vld):
                nonlocal pr_bcm, pr_bce, pr_rbread, last_div
                bcm = seq.step("tensor", lambda eng: nc.tensor.matmul(
                    psB[0:64, :], onesr[0:1, 0:64], rc[ob][:], start=True, stop=True),
                    deps=[rec, pr_bce])
                bce = seq.step("scalar", lambda eng: eng.activation(
                    rbS[:], psB[0:64, :], AF.Copy), deps=[bcm, pr_rbread])
                div = seq.step("vector", lambda eng, h=h, wb=wb, ob=ob: eng.tensor_tensor(
                    o_f[wb][64 * (h % 2):64 * (h % 2) + 64, 512 * (h // 2):512 * (h // 2 + 1)],
                    psO[ob][0:64, :], rbS[:], ALU.mult), deps=[bce, mm2, pr_ofuse[wb], pr_odbg[wb]])
                pr_divO[ob] = div
                pr_bcmO[ob] = bcm
                pr_bcm = bcm
                pr_bce = bce
                pr_rbread = div
                last_div = div

            for h in range(8):
                hb = (w * 8 + h) % 2
                gpr = h // 2
                pb = (w * 4 + gpr) % 2
                ob = h % 2
                qld = seq.dma("scalar_dma", lambda eng, hb=hb, h=h, q0=q0: eng.dma_start(
                    qt[hb][:], qr_d[64 * h:64 * (h + 1), q0:q0 + QT]), deps=[pr_sim_hb[hb]], n=1)
                kld = seq.dma("scalar_dma", lambda eng, hb=hb, h=h, kw0=kw0, na=na: eng.dma_start(
                    kt[hb][:, 0:128 * na], kr_d[64 * h:64 * (h + 1), kw0:kw0 + 128 * na]),
                    deps=[pr_sim_hb[hb]], n=1)
                if h % 2 == 0:
                    cur_vld = seq.dma("sync", lambda eng, pb=pb, g=gpr, kw0=kw0, na=na: [
                        eng.dma_start(vpt[pb][:, 128 * a:128 * (a + 1)],
                                      v_d[128 * g:128 * (g + 1), kw0 + 128 * a:kw0 + 128 * (a + 1)],
                                      transpose=True) for a in range(na)],
                        deps=[pr_vpt[pb]], n=na)
                mm2 = None
                mmq = []

                def emit_mm2(a, am, msk, pb=pb, ob=ob, na=na, hoff=64 * (h % 2)):
                    def mm2k(eng, am=am, pb=pb, a=a, na=na, hoff=hoff, ob=ob):
                        nc.tensor.matmul(psO[ob][0:64, :], vpt[pb][:, 128 * a + hoff:128 * a + hoff + 64],
                                         psb[am][:], start=(a == 0), stop=(a == na - 1))
                        return nc.tensor.matmul(psD[0:1, :], onescb[:], psb[am][:],
                                                start=(a == 0), stop=(a == na - 1))
                    st = seq.step("tensor", mm2k,
                                  deps=[msk, cur_vld, vini]
                                  + ([pr_divO[ob], pr_rec] if a == 0 else []))
                    pr_mm2b[am] = st
                    return st

                for a in range(na):
                    ma = a + 4 if w == 0 else a
                    ab = ga % 2; ga += 1
                    am = gam % 4; gam += 1
                    sim = seq.step("tensor", lambda eng, ab=ab, hb=hb, a=a: nc.tensor.matmul(
                        psS[ab][:], kt[hb][:, 128 * a:128 * (a + 1)], qt[hb][:],
                        start=True, stop=True), deps=[qld, kld, pr_clamp[ab]])
                    if w == 0 or a >= 3:
                        clamp = seq.step("vector", lambda eng, ab=ab, am=am: eng.tensor_scalar(
                            clsb[am][:], psS[ab][:], 10.0, None, ALU.min), deps=[sim, pr_exp[am]])
                        exps = seq.step("scalar", lambda eng, am=am: eng.activation(
                            ebf[am][:], clsb[am][:], AF.Exp, scale=8.0), deps=[clamp, pr_mask[am]])
                    else:
                        exps = seq.step("scalar", lambda eng, ab=ab, am=am: eng.activation(
                            ebf[am][:], psS[ab][:], AF.Exp, scale=8.0), deps=[sim, pr_mask[am]])
                        clamp = exps
                    msk = seq.step("gpsimd", lambda eng, am=am, ma=ma: eng.tensor_tensor(
                        psb[am][:], ebf[am][:], maskb[:, 512 * ma:512 * (ma + 1)], ALU.mult),
                        deps=[exps, mld, pr_mm2b[am], pr_pdbg[am % 2]])
                    if F_DBG and l == 0 and w == 0 and h == 0:
                        pr_pdbg[am % 2] = seq.dma("gpsimd_dma", lambda eng, am=am, a=a: eng.dma_start(
                            pdbg_d[128 * a:128 * (a + 1), :], psb[am][:]), deps=[msk], n=1)
                    mmq.append((a, am, msk))
                    if len(mmq) > 3:
                        mm2 = emit_mm2(*mmq.pop(0))
                    pr_clamp[ab] = clamp
                    pr_exp[am] = exps
                    pr_mask[am] = msk
                while mmq:
                    mm2 = emit_mm2(*mmq.pop(0))
                pr_sim_hb[hb] = sim
                pr_vpt[pb] = mm2
                pr_po = None
                if F_DBG and l == 0 and w == 0 and h == 0:
                    poev = seq.step("scalar", lambda eng, ob=ob: eng.activation(
                        podbg_sb[:], psO[ob][:], AF.Copy), deps=[mm2])
                    pr_po = seq.dma("gpsimd_dma", lambda eng, pb=pb: [
                        eng.dma_start(vdbg_d, vpt[pb][:]),
                        eng.dma_start(podbg_d, podbg_sb[:])], deps=[poev, cur_vld], n=2)
                rec = seq.step("scalar", lambda eng, ob=ob: (
                    b.fence(eng, "s", eng.activation(rc0[:], psD[0:1, :], AF.Ln)),
                    eng.activation(rc[ob][:], rc0[:], AF.Exp, scale=-1.0))[-1],
                    deps=[mm2, pr_bcmO[ob], pr_dnst, pr_po, pr_rec])
                if F_DBG and l == 0:
                    pr_dnst = seq.dma("gpsimd_dma", lambda eng, h=h, q0=q0, ob=ob: eng.dma_start(
                        ddbg_d[h:h + 1, q0:q0 + QT], rc[ob][:]), deps=[rec], n=1)
                pr_rec = rec
                if pend is not None:
                    emit_tail(*pend)
                pend = (ob, h, mm2, rec, pb, cur_vld)
            emit_tail(*pend)
            pend = None
            if l == 0:
                xrl = seq.dma("sync", lambda eng, wb=wb, q0=q0: [
                    eng.dma_start(xres[wb][:, 512 * c:512 * (c + 1)],
                                  xn1_bf[q0:q0 + QT, 128 * c:128 * (c + 1)], transpose=True)
                    for c in range(8)], deps=[pr_resread[wb]], n=8)
            else:
                xrl = seq.dma("sync", lambda eng, wb=wb, q0=q0: [
                    eng.dma_start(xres[wb][:, 512 * c:512 * (c + 1)],
                                  x2f_d[128 * c:128 * (c + 1), q0:q0 + QT])
                    for c in range(8)], deps=[pr_resread[wb]], n=8)
            if F_DBG and l == 0:
                pr_odbg[wb] = seq.dma("gpsimd_dma", lambda eng, wb=wb, q0=q0: [
                    eng.dma_start(odbg_d[128 * ic:128 * (ic + 1), q0:q0 + QT],
                                  o_f[wb][:, 512 * ic:512 * (ic + 1)]) for ic in range(4)],
                    deps=[last_div], n=4)
            last_wom = None
            last_res = None
            for fo in range(8):
                fb = gfo % 2; gfo += 1

                def womk(eng, fb=fb, fo=fo, wb=wb):
                    ii = None
                    for ic in range(4):
                        ii = nc.tensor.matmul(psX[fb][:], wo_sb[:, 1024 * ic + 128 * fo:1024 * ic + 128 * fo + 128],
                                              o_f[wb][:, 512 * ic:512 * (ic + 1)],
                                              start=(ic == 0), stop=(ic == 3))
                    return ii
                wom = seq.step("tensor", womk, deps=[last_div, wold, pr_resp[fb]])
                res = seq.step("vector", lambda eng, fb=fb, fo=fo, wb=wb: eng.tensor_tensor(
                    xo_sb[fb][:], psX[fb][:], xres[wb][:, 512 * fo:512 * (fo + 1)], ALU.add),
                    deps=[wom, xrl, pr_stoxo[fb]])
                sto = seq.dma("gpsimd_dma", lambda eng, fb=fb, fo=fo, q0=q0: eng.dma_start(
                    xatt_d[128 * fo:128 * (fo + 1), q0:q0 + QT], xo_sb[fb][:]), deps=[res], n=1)
                pr_resp[fb] = res
                pr_stoxo[fb] = sto
                last_wom = wom
                last_res = res
            pr_ofuse[wb] = last_wom
            pr_resread[wb] = last_res
        seq.barrier_all()
        ctxB.close()

        ctxC = ExitStack()
        w1sb = b.sbuf(f"ffc{l}_w1", [128, 8 * 5632], BF16, ctx=ctxC)
        w2sb = b.sbuf(f"ffc{l}_w2", [128, 22 * 1024], BF16, ctx=ctxC)
        b1c = b.sbuf(f"ffc{l}_b1", [128, 44], F32, ctx=ctxC)
        b2c = b.sbuf(f"ffc{l}_b2", [128, 8], F32, ctx=ctxC)
        lgc = b.sbuf(f"ffc{l}_lg", [128, 8], F32, ctx=ctxC)
        lbc = b.sbuf(f"ffc{l}_lb", [128, 8], F32, ctx=ctxC)
        xa03 = b.sbuf(f"ffc{l}_x03", [128, 4 * 512], F32, ctx=ctxC)
        xa47 = b.sbuf(f"ffc{l}_x47", [128, 4 * 513], F32, ctx=ctxC)
        hn = b.sbuf(f"ffc{l}_hn", [128, 8 * 512], BF16, ctx=ctxC)
        msb = b.sbuf(f"ffc{l}_m", [128, 22 * 512], BF16, ctx=ctxC)
        sqc = [b.sbuf(f"ffc{l}_sq{i}", [128, 512], F32, ctx=ctxC) for i in range(2)]
        glb = [b.sbuf(f"ffc{l}_gl{i}", [128, 512], BF16, ctx=ctxC) for i in range(2)]
        x2sb = [b.sbuf(f"ffc{l}_x2{i}", [128, 512], F32, ctx=ctxC) for i in range(2)]
        obf = [b.sbuf(f"ffc{l}_ob{i}", [128, 512], BF16, ctx=ctxC) for i in range(2)]
        tc1 = b.sbuf(f"ffc{l}_tc1", [128, 512], F32, ctx=ctxC)
        tc2 = b.sbuf(f"ffc{l}_tc2", [128, 512], F32, ctx=ctxC)
        muR = b.sbuf(f"ffc{l}_mu", [1, 512], F32, ctx=ctxC)
        mu2R = b.sbuf(f"ffc{l}_mu2", [1, 512], F32, ctx=ctxC)
        vrR = b.sbuf(f"ffc{l}_vr", [1, 512], F32, ctx=ctxC)
        sdR = b.sbuf(f"ffc{l}_sd", [1, 512], F32, ctx=ctxC)
        rsR = b.sbuf(f"ffc{l}_rs", [1, 512], F32, ctx=ctxC)
        psA = [b.psum2(f"ffc{l}_pA{i}", [128, 512], ctx=ctxC) for i in range(2)]
        psG = [b.psum2(f"ffc{l}_pG{i}", [128, 512], ctx=ctxC) for i in range(2)]
        psY = [b.psum2(f"ffc{l}_pY{i}", [128, 512], ctx=ctxC) for i in range(2)]
        psMuT = b.psum2(f"ffc{l}_pM", [128, 512], ctx=ctxC)
        psSqT = b.psum2(f"ffc{l}_pQ", [128, 512], ctx=ctxC)

        seq = Seq(b)
        wld1 = seq.dma("sync", lambda eng, l=l: eng.dma_start(
            w1sb[:], W["f_w1"][l].rearrange("(kc p) f -> p kc f", p=128)), n=1)
        wld2 = seq.dma("sync", lambda eng, l=l: eng.dma_start(
            w2sb[:], W["f_w2"][l].rearrange("(kc p) f -> p kc f", p=128)), n=1)
        bld = seq.dma("sync", lambda eng, l=l: [
            eng.dma_start(b1c[:], W["f_b1"][l]),
            eng.dma_start(b2c[:], W["f_b2"][l])], n=2)
        lld = seq.dma("sync", lambda eng, l=l: [
            eng.dma_start(lgc[:], W["f_lng"][l]),
            eng.dma_start(lbc[:], W["f_lnb"][l])], n=2)

        pr_res_last = None
        pr_gel_last = None
        pr_mmY_last = None
        pr_hn_last = None
        pr_st2c = [None, None]
        pr_mTTp = [None, None]
        pr_gelp = [None, None]
        pr_resy = [None, None]
        pr_sty = [None, None]
        pr_sty2 = [None, None]
        pr_bcrd = None
        pr_bc2rd = None

        def uview(c):
            if c < 4:
                return xa03[:, 512 * c:512 * (c + 1)]
            return xa47[:, 513 * (c - 4) + 1:513 * (c - 4) + 513]

        def sview(c):
            if c < 4:
                return xa03[:, 512 * c:512 * (c + 1)]
            return xa47[:, 513 * (c - 4):513 * (c - 4) + 512]

        for t in range(NQ):
            t0 = QT * t
            xdep = [pr_res_last, pr_gel_last, pr_mmY_last]
            xld1 = seq.dma("scalar_dma", lambda eng, t0=t0: [
                eng.dma_start(xa03[:, 512 * c:512 * (c + 1)],
                              xatt_d[128 * c:128 * (c + 1), t0:t0 + QT]) for c in range(4)],
                deps=xdep, n=4)
            if t == 0:
                zc = seq.step("vector", lambda eng: [
                    eng.memset(xa47[:, 513 * ci:513 * ci + 1], 0.0) for ci in range(4)][-1],
                    deps=xdep)
                xld2 = seq.dma("scalar_dma", lambda eng: [
                    eng.dma_start(xa47[:, 513 * ci + 1:513 * ci + 513],
                                  xatt_d[128 * (ci + 4):128 * (ci + 5), 0:QT]) for ci in range(4)],
                    deps=xdep + [zc], n=4)
            else:
                xld2 = seq.dma("scalar_dma", lambda eng, t0=t0: [
                    eng.dma_start(xa47[:, 513 * ci:513 * ci + 513],
                                  xatt_d[128 * (ci + 4):128 * (ci + 5), t0 - 1:t0 + QT]) for ci in range(4)],
                    deps=xdep, n=4)
            stm = st2 = None
            for c in range(8):
                stm = seq.step("tensor", lambda eng, c=c: nc.tensor.matmul(
                    psMuT[0:1, :], onesc[:], sview(c), start=(c == 0), stop=(c == 7)),
                    deps=[xld1, xld2] + ([pr_hn_last] if c == 0 else []))
                sqs = seq.step("scalar", lambda eng, c=c: eng.activation(
                    sqc[c % 2][:], sview(c), AF.Square), deps=[xld1, xld2, pr_st2c[c % 2]])
                st2 = seq.step("tensor", lambda eng, c=c: nc.tensor.matmul(
                    psSqT[0:1, :], onesc[:], sqc[c % 2][:], start=(c == 0), stop=(c == 7)),
                    deps=[sqs] + ([pr_hn_last] if c == 0 else []))
                pr_st2c[c % 2] = st2
            r1 = seq.step("vector", lambda eng: eng.tensor_scalar(
                muR[:], psMuT[0:1, :], 1.0 / 1024.0, None, ALU.mult), deps=[stm, pr_bcrd])
            r2 = seq.step("scalar", lambda eng: eng.activation(mu2R[:], muR[:], AF.Square), deps=[r1])
            r3 = seq.step("vector", lambda eng: (
                b.fence(eng, "v", eng.tensor_scalar(vrR[:], psSqT[0:1, :], 1.0 / 1024.0, None, ALU.mult)),
                eng.tensor_tensor(vrR[:], vrR[:], mu2R[:], ALU.subtract))[-1], deps=[st2, r2, pr_bc2rd])
            r4 = seq.step("scalar", lambda eng: (
                b.fence(eng, "s", eng.activation(sdR[:], vrR[:], AF.Ln, bias=1e-5)),
                eng.activation(rsR[:], sdR[:], AF.Exp, scale=-0.5))[-1], deps=[r3])
            bcm = seq.step("tensor", lambda eng: nc.tensor.matmul(
                psMuT[:], onesr[0:1, :], muR[:], start=True, stop=True), deps=[r1])
            bcs = seq.step("tensor", lambda eng: nc.tensor.matmul(
                psSqT[:], onesr[0:1, :], rsR[:], start=True, stop=True), deps=[r4])
            hns = None
            for c in range(8):
                hns = seq.step("vector", lambda eng, c=c: (
                    b.fence(eng, "v", eng.tensor_tensor(tc1[:], sview(c), psMuT[:], ALU.subtract)),
                    b.fence(eng, "v", eng.tensor_tensor(tc2[:], tc1[:], psSqT[:], ALU.mult)),
                    eng.scalar_tensor_tensor(hn[:, 512 * c:512 * (c + 1)], tc2[:], lgc[:, c:c + 1],
                                             lbc[:, c:c + 1].broadcast_to([128, 512]),
                                             ALU.mult, ALU.add))[-1],
                    deps=[bcm, bcs, lld, pr_hn_last])
            pr_bcrd = hns
            pr_bc2rd = hns
            last_mTT = None
            for pc in range(22):
                pb2 = pc % 2

                def mmAk(eng, pb2=pb2, pc=pc):
                    ii = None
                    for kc in range(8):
                        ii = nc.tensor.matmul(psA[pb2][:], w1sb[:, 5632 * kc + 128 * pc:5632 * kc + 128 * pc + 128],
                                              hn[:, 512 * kc:512 * (kc + 1)], start=(kc == 0), stop=(kc == 7))
                    return ii
                mmA_ = seq.step("tensor", mmAk, deps=[hns, wld1, pr_mTTp[pb2]])

                def mmGk(eng, pb2=pb2, pc=pc):
                    ii = None
                    for kc in range(8):
                        ii = nc.tensor.matmul(psG[pb2][:],
                                              w1sb[:, 5632 * kc + 2816 + 128 * pc:5632 * kc + 2816 + 128 * pc + 128],
                                              hn[:, 512 * kc:512 * (kc + 1)], start=(kc == 0), stop=(kc == 7))
                    return ii
                mmG_ = seq.step("tensor", mmGk, deps=[hns, wld1, pr_gelp[pb2]])
                gel = seq.step("scalar", lambda eng, pb2=pb2, pc=pc: eng.activation(
                    glb[pb2][:], psG[pb2][:], AF.Gelu, bias=b1c[:, 22 + pc:23 + pc]),
                    deps=[mmG_, bld, pr_mTTp[pb2]])
                mTT = seq.step("vector", lambda eng, pb2=pb2, pc=pc: eng.scalar_tensor_tensor(
                    msb[:, 512 * pc:512 * (pc + 1)], psA[pb2][:], b1c[:, pc:pc + 1], glb[pb2][:],
                    ALU.add, ALU.mult), deps=[gel, mmA_, bld, pr_mmY_last])
                pr_mTTp[pb2] = mTT
                pr_gelp[pb2] = gel
                last_mTT = mTT
            for yc in range(8):
                yb = yc % 2

                def mmYk(eng, yb=yb, yc=yc):
                    ii = None
                    for pc in range(22):
                        ii = nc.tensor.matmul(psY[yb][:], w2sb[:, 1024 * pc + 128 * yc:1024 * pc + 128 * yc + 128],
                                              msb[:, 512 * pc:512 * (pc + 1)], start=(pc == 0), stop=(pc == 21))
                    return ii
                mmY_ = seq.step("tensor", mmYk, deps=[last_mTT, wld2, pr_resy[yb]])
                res = seq.step("vector", lambda eng, yb=yb, yc=yc: eng.scalar_tensor_tensor(
                    x2sb[yb][:], psY[yb][:], b2c[:, yc:yc + 1], uview(yc), ALU.add, ALU.add),
                    deps=[mmY_, bld, pr_sty[yb]])
                if l == 0:
                    cpb = seq.step("scalar", lambda eng, yb=yb: eng.activation(
                        obf[yb][:], x2sb[yb][:], AF.Copy), deps=[res, pr_sty2[yb]])
                    sto = seq.dma("gpsimd_dma", lambda eng, yb=yb, yc=yc, t0=t0: [
                        eng.dma_start(x2f_d[128 * yc:128 * (yc + 1), t0:t0 + QT], x2sb[yb][:]),
                        eng.dma_start(x2bf_d[128 * yc:128 * (yc + 1), t0:t0 + QT], obf[yb][:])],
                        deps=[res, cpb], n=2)
                    pr_sty2[yb] = sto
                else:
                    cpb = seq.step("scalar", lambda eng, yb=yb: eng.activation(
                        obf[yb][:], x2sb[yb][:], AF.Silu), deps=[res, pr_sty2[yb]])
                    sto = seq.dma("gpsimd_dma", lambda eng, yb=yb, yc=yc, t0=t0: eng.dma_start(
                        silg_d[128 * yc:128 * (yc + 1), t0:t0 + QT], obf[yb][:]), deps=[cpb], n=1)
                    pr_sty2[yb] = sto
                pr_sty[yb] = sto
                pr_resy[yb] = res
                pr_res_last = res
                pr_mmY_last = mmY_
            pr_gel_last = gel
            pr_hn_last = res
        seq.barrier_all()
        ctxC.close()
    gctx.close()


def emit_final_combine(b: Bld, x1, s1, silg, out):
    ctx = ExitStack()
    sgt = [b.sbuf(f"fc_sg{i}", [128, 1024], BF16, ctx=ctx) for i in range(2)]
    x1t = [b.sbuf(f"fc_x1{i}", [128, 1024], F32, ctx=ctx) for i in range(2)]
    s1t = [b.sbuf(f"fc_s1{i}", [128, 1024], F32, ctx=ctx) for i in range(2)]
    tmp = b.sbuf("fc_tmp", [128, 1024], F32, ctx=ctx)
    ot = [b.sbuf(f"fc_o{i}", [128, 1024], F32, ctx=ctx) for i in range(2)]
    seq = Seq(b)
    pr_v1 = [None, None]
    pr_sto = [None, None]
    pr_v1_any = None
    for tt in range(32):
        tq0 = 128 * tt
        bb = tt % 2
        sgld = seq.dma("sync", lambda eng, bb=bb, tq0=tq0: eng.dma_start(
            sgt[bb][:], silg[:, tq0:tq0 + 128], transpose=True),
            deps=[pr_v1[bb]], n=1)
        xld = seq.dma("sync", lambda eng, bb=bb, tq0=tq0: [
            eng.dma_start(x1t[bb][:], x1[tq0:tq0 + 128, :]),
            eng.dma_start(s1t[bb][:], s1[tq0:tq0 + 128, :])], deps=[pr_v1[bb]], n=2)
        v1 = seq.step("vector", lambda eng, bb=bb: (
            b.fence(eng, "v", eng.tensor_tensor(tmp[:], s1t[bb][:], sgt[bb][:], ALU.mult)),
            eng.tensor_tensor(ot[bb][:], x1t[bb][:], tmp[:], ALU.add))[-1],
            deps=[sgld, xld, pr_sto[bb], pr_v1_any])
        pr_v1_any = v1
        sto = seq.dma("gpsimd_dma", lambda eng, bb=bb, tq0=tq0: eng.dma_start(
            out[tq0:tq0 + 128, :], ot[bb][:]), deps=[v1], n=1)
        pr_v1[bb] = v1
        pr_sto[bb] = sto
    seq.barrier_all()
    ctx.close()


SCRATCH = {
    "xn_bf": ([T, D], BF16), "s0": ([T, D], F32), "x1": ([T, D], F32),
    "xn1_bf": ([T, D], BF16), "xn1_f32": ([T, D], F32), "s1": ([T, D], F32),
    "silg": ([D, T], BF16), "out": ([T, D], F32),
}

PHASE_IO = {
    "ln0": (["x"], ["xn_bf"]),
    "l0attn": (["xn_bf"], ["s0"]),
    "moe": (["xn_bf", "x", "s0"], ["x1"]),
    "ln1": (["x1"], ["xn1_bf", "xn1_f32"]),
    "l1attn": (["xn1_bf"], ["s1"]),
    "l1ffn": (["xn1_bf"], ["silg"]),
    "combine": (["x1", "s1", "silg"], ["out"]),
}

PHASE_WEIGHTS = {
    "ln0": [],
    "l0attn": ["l0_wq", "l0_wk", "l0_wv", "l0_wo", "mask_su"],
    "moe": ["l0_rw", "l0_rb", "l0_w1", "l0_b1", "l0_w2", "l0_b2"],
    "ln1": [],
    "l1attn": ["l1_wq", "l1_wk", "l1_wv", "l1_wg", "l1_wo", "l1_wab", "l1_alog_dt",
               "l1_convq", "l1_convk", "l1_convv", "l1_onw", "alogdt_bf",
               "cums128", "amask_i128", "amask_sL128"],
    "l1ffn": ["f_wqkv", "f_wqkv2", "f_qs", "f_qs_sw", "f_ks", "f_ks_sw", "f_wo",
              "f_lng", "f_lnb", "f_w1", "f_b1", "f_w2", "f_b2",
              "rot_csq", "rot_ssq", "rot_csk", "rot_ssk", "fmask"],
    "combine": [],
}


def group_io(phases):
    produced = set()
    sin = set()
    needs_x = False
    for ph in phases:
        rd, wr = PHASE_IO[ph]
        for r in rd:
            if r == "x":
                needs_x = True
            elif r not in produced:
                sin.add(r)
        produced.update(wr)
    weights = ["ident", "constcols"]
    for ph in phases:
        weights.extend(PHASE_WEIGHTS[ph])
    return sorted(sin), sorted(produced), needs_x, weights


def build_program(phases, host_map, extern_in, outs):
    nc = bass.Bass("TRN2", target_bir_lowering=False)
    from concourse.bass import get_kernel_semaphore_range
    kr = get_kernel_semaphore_range()
    clear_range = range(kr.start + 3, kr.stop)
    nc.all_engine_barrier()
    nc.gpsimd.dma_reset(clear_range)
    nc.gpsimd.sem_clear(clear_range)
    nc.all_engine_barrier()
    ctx = ExitStack()
    aps = {}
    for name, arr in host_map.items():
        dt = BF16 if arr.dtype == BF16NP else F32
        aps[name] = nc.dram_tensor(name, list(arr.shape), dt, kind="ExternalInput").ap()
    for name, (shape, dt) in SCRATCH.items():
        if name in aps:
            continue
        if name in extern_in:
            kind = "ExternalInput"
        elif name in outs:
            kind = "ExternalOutput"
        else:
            kind = "Internal"
        aps[name] = nc.dram_tensor(name, shape, dt, kind=kind).ap()

    with ctx:
        blk = ctx.enter_context(nc.Block())
        b = Bld(nc, blk, ctx)
        b.dram = aps
        b.sb["ident_bf"] = b.sbuf("g_ident_bf", [128, 128], BF16)
        b.sb["ident_f32"] = b.sbuf("g_ident_f32", [128, 128], F32)
        b.sb["constcols"] = b.sbuf("g_constcols", [128, 6], F32)
        nc.const_aps.register(F32, 0.25, b.sb["constcols"][:, 0:1])
        nc.const_aps.register(F32, 8.0, b.sb["constcols"][:, 1:2])
        nc.const_aps.register(F32, 2.0, b.sb["constcols"][:, 2:3])
        nc.const_aps.register(F32, 0.5, b.sb["constcols"][:, 3:4])
        nc.const_aps.register(F32, 1e-5, b.sb["constcols"][:, 4:5])
        nc.const_aps.register(F32, -0.5, b.sb["constcols"][:, 5:6])
        ic = b.cnt("g_init")

        @blk.sync
        def _(eng):
            eng.dma_start(b.sb["ident_f32"][:], aps["ident"]).then_inc(ic.sem, 16)
            eng.dma_start(b.sb["constcols"][:], aps["constcols"]).then_inc(ic.sem, 16)
        ic.n += 32

        @blk.vector
        def _(eng):
            eng.wait_ge(ic.sem, 32)
            eng.tensor_copy(b.sb["ident_bf"][:], b.sb["ident_f32"][:]).then_inc(ic.sem, 1)
        ic.n += 1
        b.barrier()

        for ph in phases:
            pctx = ExitStack()
            outer = b.ctx
            b.ctx = pctx
            emit_phase(b, ph)
            b.barrier()
            b.ctx = outer
            pctx.close()
    return nc


def emit_phase(b, ph):
    aps = b.dram
    if ph == "ln0":
        emit_ln(b, aps["x"], aps["xn_bf"], None, tag="ln0")
    elif ph == "ln1":
        emit_ln(b, aps["x1"], aps["xn1_bf"], aps["xn1_f32"], tag="ln1")
    elif ph == "moe":
        emit_moe_combine(b, aps["xn_bf"], aps["x"], aps["s0"], aps["x1"])
    elif ph == "l0attn":
        emit_l0attn(b, aps["xn_bf"], aps["s0"])
    elif ph == "l1attn":
        emit_l1attn(b, aps["xn1_bf"], aps["s1"])
    elif ph == "l1ffn":
        emit_l1ffn(b, aps["xn1_bf"], aps["silg"])
    elif ph == "combine":
        emit_final_combine(b, aps["x1"], aps["s1"], aps["silg"], aps["out"])
    else:
        raise ValueError(ph)


LAUNCH_GROUPS = [["ln0", "l0attn", "moe", "ln1", "l1attn", "l1ffn", "combine"]]


def run_groups(x_full, inputs, groups):
    wmap = prep_weights(inputs)
    wmap.update(host_consts())
    shards = [np.ascontiguousarray(np.asarray(x_full[i], np.float32)) for i in range(8)]
    live = [dict() for _ in range(8)]
    final = None
    for group in groups:
        sin, souts, needs_x, weights = group_io(group)
        host_map = {w: wmap[w] for w in dict.fromkeys(weights)}
        if needs_x or any(ph in ("ln0", "moe") for ph in group):
            host_map["x"] = shards[0]
        nc = build_program(group, host_map, sin, souts)
        in_maps = []
        for core in range(8):
            m = {k: v for k, v in host_map.items() if k != "x"}
            if "x" in host_map:
                m["x"] = shards[core]
            for sname in sin:
                m[sname] = live[core][sname]
            in_maps.append(m)
        res = run_bass_kernel_spmd(nc, in_maps, core_ids=list(range(8)))
        for core in range(8):
            for sname in souts:
                live[core][sname] = np.ascontiguousarray(np.asarray(res.results[core][sname]))
        if "out" in souts:
            final = np.stack([live[core]["out"] for core in range(8)], axis=0)
    return final


def kernel(**inputs):
    x_full = np.ascontiguousarray(np.asarray(inputs["L0_attn_hidden_states"], dtype=np.float32))
    return run_groups(x_full, inputs, LAUNCH_GROUPS)


# version 14
# speedup vs baseline: 1.0083x; 1.0083x over previous
import numpy as np
import ml_dtypes
from contextlib import ExitStack
import concourse.bass as bass
import concourse.mybir as mybir
from concourse.bass_utils import run_bass_kernel_spmd

F32 = mybir.dt.float32
BF16 = mybir.dt.bfloat16
AF = mybir.ActivationFunctionType
ALU = mybir.AluOpType
AX = mybir.AxisListType

T, D = 4096, 1024
NT = T // 128
L1A_NQ = 4
L1A_DBG = None
L1A_SPILL = False
F_NT = 8
F_DBG = False
F_NL = 2
EPS_LN = 1e-5
NEG = np.float32(-1e30)
BF16NP = ml_dtypes.bfloat16



def host_consts():
    c = {}
    c["ident"] = np.eye(128, dtype=np.float32)
    c["constcols"] = np.tile(np.array([[0.25, 8.0, 2.0, 0.5, 1e-5, -0.5]], np.float32), (128, 1))
    c["mask_su"] = np.triu(np.ones((128, 128), np.float32))
    c["cums128"] = np.triu(np.ones((128, 128), np.float32))
    NEGC = np.float32(-1e30)
    c["amask_i128"] = np.where(np.triu(np.ones((128, 128), np.float32)) > 0, np.float32(0), NEGC)
    c["amask_sL128"] = np.where(np.tril(np.ones((128, 128), np.float32), -1) > 0, np.float32(0), NEGC)
    half = np.arange(0, 64, 2, dtype=np.float64)
    inv = 1.0 / 10000.0 ** (half / 64.0)
    tpos = np.arange(T, dtype=np.float64)
    fr = tpos[:, None] * inv[None, :]
    fr = np.concatenate([fr, fr], -1)
    base = (half + 0.4 * 64) / (1.4 * 64)
    power = (tpos - T // 2) / 256.0
    sc = base[None, :] ** power[:, None]
    sc = np.concatenate([sc, sc], -1)
    cos, sin = np.cos(fr), np.sin(fr)
    sgn = np.where(np.arange(64) < 32, -1.0, 1.0)[:, None]

    def tile2(a):
        return np.ascontiguousarray(np.tile(a.astype(np.float32), (2, 1)))
    c["rot_csq"] = tile2((cos * sc).T)
    c["rot_ssq"] = tile2((sin * sc).T * sgn)
    c["rot_csk"] = tile2((cos / sc).T)
    c["rot_ssk"] = tile2((sin / sc).T * sgn)
    jj = np.arange(128)[:, None]
    qq = np.arange(512)[None, :]
    fm = np.zeros((128, 8 * 512), np.float32)
    for a in range(8):
        if a < 4:
            m = (jj >= qq - 128 * a)
        else:
            m = (jj <= qq - 128 * (a - 4))
        fm[:, 512 * a:512 * (a + 1)] = m
    c["fmask"] = fm.astype(BF16NP)
    return c


def prep_weights(ins):
    w = {}
    w["l0_wq"] = np.ascontiguousarray(ins["L0_attn_Wq"])
    w["l0_wk"] = np.ascontiguousarray(ins["L0_attn_Wk"])
    w["l0_wv"] = np.ascontiguousarray(ins["L0_attn_Wv"])
    w["l0_wo"] = np.ascontiguousarray(ins["L0_attn_Wo"])
    w["l0_rw"] = np.ascontiguousarray(ins["L0_ffn_router_w"])
    w["l0_rb"] = np.ascontiguousarray(ins["L0_ffn_router_b"]).reshape(1, 8)
    w["l0_w1"] = np.ascontiguousarray(ins["L0_ffn_W1"])
    w["l0_b1"] = np.ascontiguousarray(ins["L0_ffn_b1"]).reshape(1, 8 * 512)
    w["l0_w2"] = np.ascontiguousarray(ins["L0_ffn_W2"])
    w["l0_b2"] = np.ascontiguousarray(ins["L0_ffn_b2"]).reshape(1, 8 * 1024)
    w["l1_wq"] = np.ascontiguousarray(ins["L1_attn_Wq"])
    w["l1_wk"] = np.ascontiguousarray(ins["L1_attn_Wk"])
    w["l1_wv"] = np.ascontiguousarray(ins["L1_attn_Wv"])
    w["l1_wg"] = np.ascontiguousarray(ins["L1_attn_Wg"])
    w["l1_wo"] = np.ascontiguousarray(ins["L1_attn_Wo"])
    w["l1_wab"] = np.concatenate([ins["L1_attn_Wa"], ins["L1_attn_Wb"]], axis=1)
    w["l1_alog_dt"] = np.concatenate(
        [np.asarray(ins["L1_attn_A_log"]).reshape(1, 8), np.asarray(ins["L1_attn_dt_bias"]).reshape(1, 8)], axis=1)
    w["l1_convq"] = np.ascontiguousarray(ins["L1_attn_conv_q"])
    w["l1_convk"] = np.ascontiguousarray(ins["L1_attn_conv_k"])
    w["l1_convv"] = np.ascontiguousarray(ins["L1_attn_conv_v"])
    w["l1_onw"] = np.ascontiguousarray(np.asarray(ins["L1_attn_o_norm_w"], np.float32).reshape(2, 128).T)
    w["alogdt_bf"] = np.ascontiguousarray(
        np.concatenate([np.asarray(ins["L1_attn_A_log"]).reshape(1, 8),
                        np.asarray(ins["L1_attn_dt_bias"]).reshape(1, 8)], axis=1).astype(BF16NP))
    wqkv = np.asarray(ins["L1_ffn_wqkv"], np.float32)
    w["f_wqkv"] = np.ascontiguousarray(wqkv.astype(BF16NP))
    perm = np.concatenate([64 * h + (np.arange(64) + 32) % 64 for h in range(8)])
    w["f_wqkv2"] = np.ascontiguousarray(np.concatenate(
        [wqkv[:, :, 0:512][:, :, perm], wqkv[:, :, 512:1024][:, :, perm]], axis=2).astype(BF16NP))
    qs = np.asarray(ins["L1_ffn_q_scale"], np.float32)
    ks = np.asarray(ins["L1_ffn_k_scale"], np.float32)
    w["f_qs"] = np.ascontiguousarray(np.tile(qs, (1, 2)).reshape(2, 128, 1))
    w["f_qs_sw"] = np.ascontiguousarray(np.tile(np.roll(qs, -32, axis=1), (1, 2)).reshape(2, 128, 1))
    w["f_ks"] = np.ascontiguousarray(np.tile(ks, (1, 2)).reshape(2, 128, 1))
    w["f_ks_sw"] = np.ascontiguousarray(np.tile(np.roll(ks, -32, axis=1), (1, 2)).reshape(2, 128, 1))
    w["f_wo"] = np.ascontiguousarray(np.asarray(ins["L1_ffn_wo"], np.float32).astype(BF16NP))
    w["f_lng"] = np.ascontiguousarray(
        np.asarray(ins["L1_ffn_ln_g"], np.float32).reshape(2, 8, 128).transpose(0, 2, 1))
    w["f_lnb"] = np.ascontiguousarray(
        np.asarray(ins["L1_ffn_ln_b"], np.float32).reshape(2, 8, 128).transpose(0, 2, 1))
    HID = 2730
    w1 = np.asarray(ins["L1_ffn_w1"], np.float32); b1 = np.asarray(ins["L1_ffn_b1"], np.float32)
    w1p = np.zeros((2, 1024, 5632), np.float32)
    w1p[:, :, 0:HID] = w1[:, :, 0:HID]
    w1p[:, :, 2816:2816 + HID] = w1[:, :, HID:2 * HID]
    b1p = np.zeros((2, 1, 5632), np.float32)
    b1p[:, 0, 0:HID] = b1[:, 0:HID]
    b1p[:, 0, 2816:2816 + HID] = b1[:, HID:2 * HID]
    w["f_w1"] = np.ascontiguousarray(w1p.astype(BF16NP))
    w["f_b1"] = np.ascontiguousarray(b1p.reshape(2, 44, 128).transpose(0, 2, 1).astype(np.float32))
    w2 = np.asarray(ins["L1_ffn_w2"], np.float32)
    w2p = np.zeros((2, 2816, 1024), np.float32)
    w2p[:, 0:HID, :] = w2
    w["f_w2"] = np.ascontiguousarray(w2p.astype(BF16NP))
    w["f_b2"] = np.ascontiguousarray(
        np.asarray(ins["L1_ffn_b2"], np.float32).reshape(2, 8, 128).transpose(0, 2, 1))
    return w



class Cnt:
    __slots__ = ("sem", "n")

    def __init__(self, sem):
        self.sem = sem
        self.n = 0


class SlotDma:

    def __init__(self, b, name, nslots):
        self.cnts = [b.cnt(f"{name}{j}") for j in range(nslots)]

    def issue(self, eng, j, out, in_, **kw):
        c = self.cnts[j]
        eng.dma_start(out, in_, **kw).then_inc(c.sem, 16)
        c.n += 16
        return c.n

    def wait_total(self, eng, j):
        c = self.cnts[j]
        eng.wait_ge(c.sem, c.n)


class Bld:
    def __init__(self, nc, blk, ctx):
        self.nc = nc
        self.blk = blk
        self.ctx = ctx
        self.sem_ctx = ctx
        self.phase = 0
        self.sems = {}
        self.dram = {}
        self.sb = {}

    def cnt(self, name):
        if name not in self.sems:
            sem = self.sem_ctx.enter_context(self.nc.semaphore(name))
            self.sems[name] = Cnt(sem)
        return self.sems[name]

    def sbuf(self, name, shape, dtype, ctx=None):
        t = (ctx or self.ctx).enter_context(self.nc.sbuf_tensor(name, shape, dtype))
        return t

    def psum2(self, name, shape, dtype=F32, ctx=None):
        return (ctx or self.ctx).enter_context(self.nc.psum_tensor(name, shape, dtype))

    def psum(self, name, shape, dtype=F32):
        return self.ctx.enter_context(self.nc.psum_tensor(name, shape, dtype))

    def ce(self, key, inst, k=1):
        c = self.cnt("chain_" + key)
        inst.then_inc(c.sem, k)
        c.n += k
        return c.n

    def cew(self, eng, key):
        c = self.cnt("chain_" + key)
        eng.wait_ge(c.sem, c.n)

    def fence(self, eng, key, inst):
        self.ce(key, inst)
        self.cew(eng, key)
        return inst

    def barrier(self):
        self.phase += 1
        ph = self.cnt("ph")
        target = 5 * self.phase
        for name in ("sync", "tensor", "vector", "scalar", "gpsimd"):
            sec = getattr(self.blk, name)

            def body(eng, target=target, ph=ph):
                eng.nop().then_inc(ph.sem, 1)
                eng.wait_ge(ph.sem, target)

            sec(body)
        ph.n = target



ENG_SEMNAME = {"tensor": "sq_t", "vector": "sq_v", "scalar": "sq_s", "gpsimd": "sq_g"}
DMA_QUEUES = {"sync": "sq_dS", "gpsimd_dma": "sq_dG", "scalar_dma": "sq_dA"}
DMA_POOL = 10


class Step:
    __slots__ = ("sem", "val")

    def __init__(self, sem, val):
        self.sem = sem
        self.val = val


class Seq:

    def __init__(self, b):
        self.b = b
        self.steps = []
        self.dma_pool = {}
        self.dma_rr = {}

    def step(self, engkey, thunk, deps=(), ninc=1):
        c = self.b.cnt(ENG_SEMNAME[engkey])
        c.n += ninc
        st = Step(c.sem, c.n)
        self.steps.append((engkey, tuple(d for d in deps if d is not None), thunk, st, ninc))
        return st

    def dma(self, queue, thunk, deps=(), n=1):
        if queue not in self.dma_pool:
            self.dma_pool[queue] = [self.b.cnt(f"{DMA_QUEUES[queue]}{j}") for j in range(DMA_POOL)]
            self.dma_rr[queue] = 0
        j = self.dma_rr[queue] % DMA_POOL
        self.dma_rr[queue] += 1
        c = self.dma_pool[queue][j]
        prev_total = c.n
        c.n += 16 * n
        st = Step(c.sem, c.n)
        self.steps.append(("__dma_" + queue, tuple(d for d in deps if d is not None) + (("__slot", c, prev_total),),
                           thunk, st, 16 * n))
        return st

    def emit(self):
        by_sec = {}
        for engkey, deps, thunk, st, ninc in self.steps:
            if engkey.startswith("__dma_"):
                queue = engkey[len("__dma_"):]
                secname = "sync" if queue == "sync" else ("gpsimd" if queue == "gpsimd_dma" else "scalar")
            else:
                secname = engkey
            by_sec.setdefault(secname, []).append((engkey, deps, thunk, st, ninc))
        blk = self.b.blk
        for secname, items in by_sec.items():
            sec = getattr(blk, secname)

            def body(eng, items=items):
                last_wait = {}
                for engkey, deps, thunk, st, ninc in items:
                    for d in deps:
                        if isinstance(d, tuple) and d and d[0] == "__slot":
                            _, c, prev_total = d
                            if prev_total > 0:
                                eng.wait_ge(c.sem, prev_total)
                            continue
                        if last_wait.get(d.sem, -1) >= d.val:
                            continue
                        eng.wait_ge(d.sem, d.val)
                        last_wait[d.sem] = d.val
                    inst = thunk(eng)
                    if engkey.startswith("__dma_"):
                        insts = inst if isinstance(inst, (list, tuple)) else [inst]
                        assert 16 * len(insts) == ninc
                        for ii in insts:
                            ii.then_inc(st.sem, 16)
                    else:
                        inst.then_inc(st.sem, ninc)

            sec(body)
        self.steps = []

    def drain_dmas(self):
        blk = self.b.blk
        for queue, cnts in self.dma_pool.items():
            secname = "sync" if queue == "sync" else ("gpsimd" if queue == "gpsimd_dma" else "scalar")
            sec = getattr(blk, secname)

            def body(eng, cnts=list(cnts)):
                for c in cnts:
                    if c.n > 0:
                        eng.wait_ge(c.sem, c.n)

            sec(body)

    def barrier_all(self):
        self.emit()
        self.drain_dmas()
        self.b.barrier()



def emit_ln(b: Bld, src, dst_bf, dst_f32=None, tag="ln"):
    nc = b.nc
    NB = 3
    lds = [b.cnt(tag + f"_ld{j}") for j in range(NB)]
    sts = [b.cnt(tag + f"_st{j}") for j in range(NB)]
    vv = b.cnt(tag + "_v"); sv = b.cnt(tag + "_s")
    per = 3 if dst_f32 is not None else 2
    nst = 2 if dst_f32 is not None else 1
    xa = [b.sbuf(f"{tag}_x{i}", [128, D], F32) for i in range(NB)]
    xc = [b.sbuf(f"{tag}_c{i}", [128, D], F32) for i in range(NB)]
    ob = [b.sbuf(f"{tag}_o{i}", [128, D], BF16) for i in range(NB)]
    of = [b.sbuf(f"{tag}_f{i}", [128, D], F32) for i in range(NB)] if dst_f32 is not None else None
    sq = b.sbuf(tag + "_sq", [128, D], F32)
    stat = b.sbuf(tag + "_stat", [128, 8], F32)
    junk = b.sbuf(tag + "_junk", [128, 2], F32)
    vv0, sv0 = vv.n, sv.n
    ld0 = [c.n for c in lds]
    st0 = [c.n for c in sts]

    @b.blk.sync
    def _(eng):
        for i in range(NT):
            j = i % NB
            if i >= NB:
                eng.wait_ge(sts[j].sem, st0[j] + 16 * nst * (i // NB))
            eng.dma_start(xa[j][:], src[128 * i:128 * (i + 1), :]).then_inc(lds[j].sem, 16)
    for j in range(NB):
        lds[j].n += 16 * len(range(j, NT, NB))

    @b.blk.vector
    def _(eng):
        for i in range(NT):
            j = i % NB
            eng.wait_ge(lds[j].sem, ld0[j] + 16 * (i // NB + 1))
            b.fence(eng, "v", eng.reduce_sum(stat[:, 0:1], xa[j][:], axis=AX.X))
            b.fence(eng, "v", eng.tensor_scalar(stat[:, 1:2], stat[:, 0:1], 1.0 / D, None, ALU.mult))
            b.fence(eng, "v", eng.tensor_scalar(xc[j][:], xa[j][:], stat[:, 1:2], None, ALU.subtract))
            b.fence(eng, "v", eng.tensor_tensor(sq[:], xc[j][:], xc[j][:], ALU.mult))
            b.fence(eng, "v", eng.reduce_sum(stat[:, 2:3], sq[:], axis=AX.X))
            b.fence(eng, "v", eng.tensor_scalar(stat[:, 3:4], stat[:, 2:3], 1.0 / D, EPS_LN, ALU.mult, ALU.add))
            eng.nop().then_inc(vv.sem, 1)
            eng.wait_ge(sv.sem, sv0 + i + 1)
            b.fence(eng, "v", eng.reciprocal(stat[:, 5:6], stat[:, 4:5]))
            eng.tensor_scalar(ob[j][:], xc[j][:], stat[:, 5:6], None, ALU.mult).then_inc(vv.sem, 1)
            if dst_f32 is not None:
                eng.tensor_scalar(of[j][:], xc[j][:], stat[:, 5:6], None, ALU.mult).then_inc(vv.sem, 1)
    vv.n += per * NT

    @b.blk.scalar
    def _(eng):
        for i in range(NT):
            eng.wait_ge(vv.sem, vv0 + per * i + 1)
            eng.activation(stat[:, 4:5], stat[:, 3:4], AF.Sqrt).then_inc(sv.sem, 1)
    sv.n += NT

    @b.blk.gpsimd
    def _(eng):
        for i in range(NT):
            j = i % NB
            eng.wait_ge(vv.sem, vv0 + per * i + 2)
            eng.dma_start(dst_bf[128 * i:128 * (i + 1), :], ob[j][:]).then_inc(sts[j].sem, 16)
            if dst_f32 is not None:
                eng.wait_ge(vv.sem, vv0 + per * i + 3)
                eng.dma_start(dst_f32[128 * i:128 * (i + 1), :], of[j][:]).then_inc(sts[j].sem, 16)
        for j in range(NB):
            eng.wait_ge(sts[j].sem, st0[j] + 16 * nst * len(range(j, NT, NB)))
    for j in range(NB):
        sts[j].n += 16 * nst * len(range(j, NT, NB))


def emit_moe_combine(b: Bld, xn_bf, x_src, s_src, x1_dst):
    nc = b.nc
    W = b.dram
    TH = 1024
    NTH = TH // 128
    wls = b.cnt("moe_wls")
    wle = [b.cnt(f"moe_wle{j}") for j in range(2)]
    lslot = [b.cnt(f"moe_l{j}") for j in range(2)]
    sslot = [b.cnt(f"moe_s{j}") for j in range(2)]
    tv = b.cnt("moe_t"); vv = b.cnt("moe_v"); sv = b.cnt("moe_sc")
    xfc = b.cnt("moe_xfc")

    xf = b.sbuf("moe_xf", [128, 8 * TH], BF16)
    w1sb = [b.sbuf(f"moe_w1_{i}", [128, 8 * 512], BF16) for i in range(2)]
    w2sb = [b.sbuf(f"moe_w2_{i}", [128, 4 * 1024], BF16) for i in range(2)]
    rwsb = b.sbuf("moe_rw", [128, 8 * 8], BF16)
    rbsb = b.sbuf("moe_rb", [1, 8], BF16)
    b1sb = b.sbuf("moe_b1", [1, 8 * 512], BF16)
    b2sb = b.sbuf("moe_b2", [1, 8 * 1024], BF16)
    ones1 = b.sbuf("moe_ones1", [1, 512], BF16)
    masktok = b.sbuf("moe_masktok", [128, NTH * 8], F32)
    rstat = b.sbuf("moe_rstat", [128, 16], F32)
    junkv = b.sbuf("moe_junkv", [128, 2], F32)
    rtile = b.sbuf("moe_rtile", [128, 8], F32)
    h1sb = [b.sbuf(f"moe_h1_{i}", [128, 4 * TH], BF16) for i in range(2)]
    yacc = b.sbuf("moe_yacc", [128, NTH * 1024], F32)
    xsb = [b.sbuf(f"moe_xs{i}", [128, D], F32) for i in range(2)]
    ssb = [b.sbuf(f"moe_ss{i}", [128, D], F32) for i in range(2)]
    gsb = [b.sbuf(f"moe_gs{i}", [128, D], F32) for i in range(2)]
    x1sb = [b.sbuf(f"moe_x1{i}", [128, D], F32) for i in range(2)]

    psA = b.psum("moe_psA", [128, 512])
    psB = b.psum("moe_psB", [128, 512])
    psC = b.psum("moe_psC", [128, 512])
    psD = b.psum("moe_psD", [128, 512])
    psR = b.psum("moe_psR", [128, 512])

    @b.blk.gpsimd
    def _(eng):
        eng.dma_start(rwsb[:].rearrange("p (kc e) -> p kc e", kc=8),
                      W["l0_rw"].rearrange("(kc p) e -> p kc e", p=128)).then_inc(wls.sem, 16)
        eng.dma_start(rbsb[:], W["l0_rb"]).then_inc(wls.sem, 16)
        eng.dma_start(b1sb[:], W["l0_b1"]).then_inc(wls.sem, 16)
        eng.dma_start(b2sb[:], W["l0_b2"]).then_inc(wls.sem, 16)
    wls.n += 16 * 4
    wl_static = wls.n

    @b.blk.vector
    def _(eng):
        eng.memset(ones1[:], 1.0).then_inc(vv.sem, 1)
    vv.n += 1
    ones_done = vv.n

    for half in range(4):
        t0 = half * TH
        xf0 = xfc.n

        @b.blk.sync
        def _(eng, t0=t0):
            for r in range(8):
                eng.dma_start(xf[:, r * TH:(r + 1) * TH],
                              xn_bf[t0:t0 + TH, r * 128:(r + 1) * 128], transpose=True).then_inc(xfc.sem, 16)
        xfc.n += 16 * 8

        tvr = tv.n; tv.n += NTH
        svr = sv.n; sv.n += NTH
        vvr = vv.n; vv.n += 3 * NTH

        @b.blk.tensor
        def _(eng, xf0=xf0, tvr=tvr, vvr=vvr, half=half):
            eng.wait_ge(xfc.sem, xf0 + 16 * 8)
            if half == 0:
                eng.wait_ge(wls.sem, wl_static)
                eng.wait_ge(vv.sem, ones_done)
            for i in range(NTH):
                if i > 0:
                    eng.wait_ge(vv.sem, vvr + 3 * (i - 1) + 1)
                for kc in range(8):
                    nc.tensor.matmul(psR[:, 0:8], xf[:, kc * TH + 128 * i: kc * TH + 128 * (i + 1)],
                                     rwsb[:, kc * 8:(kc + 1) * 8], start=(kc == 0), stop=False)
                nc.tensor.matmul(psR[:, 0:8], ones1[0:1, 0:128], rbsb[0:1, :],
                                 start=False, stop=True).then_inc(tv.sem, 1)

        @b.blk.vector
        def _(eng, tvr=tvr, vvr=vvr, svr=svr):
            for i in range(NTH):
                eng.wait_ge(tv.sem, tvr + (i + 1))
                b.fence(eng, "v", eng.reduce_max(rstat[:, 0:1], psR[:, 0:8], axis=AX.X))
                b.fence(eng, "v", eng.tensor_scalar(rtile[:], psR[:, 0:8], rstat[:, 0:1], None, ALU.subtract))
                eng.nop().then_inc(vv.sem, 1)
                eng.wait_ge(sv.sem, svr + i + 1)
                b.fence(eng, "v", eng.reduce_sum(rstat[:, 1:2], rtile[:], axis=AX.X))
                b.fence(eng, "v", eng.reciprocal(rstat[:, 2:3], rstat[:, 1:2]))
                b.fence(eng, "v", eng.tensor_scalar(rtile[:], rtile[:], rstat[:, 2:3], None, ALU.mult))
                b.fence(eng, "v", eng.reduce_max(rstat[:, 3:4], rtile[:], axis=AX.X))
                b.fence(eng, "v", eng.tensor_scalar(rstat[:, 8:16], rtile[:], rstat[:, 3:4], None, ALU.is_ge))
                b.fence(eng, "v", eng.scalar_tensor_tensor(rstat[:, 8:16], rstat[:, 8:16], -1e30, rtile[:], ALU.mult, ALU.add))
                b.fence(eng, "v", eng.reduce_max(rstat[:, 4:5], rstat[:, 8:16], axis=AX.X))
                b.fence(eng, "v", eng.tensor_tensor(rstat[:, 5:6], rstat[:, 3:4], rstat[:, 4:5], ALU.add))
                b.fence(eng, "v", eng.reciprocal(rstat[:, 6:7], rstat[:, 5:6]))
                b.fence(eng, "v", eng.tensor_scalar(rstat[:, 8:16], rtile[:], rstat[:, 4:5], None, ALU.is_ge))
                b.fence(eng, "v", eng.tensor_tensor(rstat[:, 8:16], rstat[:, 8:16], rtile[:], ALU.mult))
                eng.tensor_scalar(masktok[:, 8 * i:8 * (i + 1)], rstat[:, 8:16], rstat[:, 6:7],
                                  None, ALU.mult).then_inc(vv.sem, 2)

        @b.blk.scalar
        def _(eng, vvr=vvr, svr=svr):
            for i in range(NTH):
                eng.wait_ge(vv.sem, vvr + 3 * i + 1)
                eng.activation(rtile[:], rtile[:], AF.Exp).then_inc(sv.sem, 1)

        @b.blk.vector
        def _(eng):
            for i in range(NTH):
                inst = eng.memset(yacc[:, 1024 * i:1024 * (i + 1)], 0.0)
            b.fence(eng, "v", inst)
            eng.nop().then_inc(vv.sem, 1)
        vv.n += 1

        y_done_tv = {}
        NT512 = TH // 512
        n_h1 = 4 * NT512
        n_y = NTH * 2
        for n in range(8):
            buf = n % 2
            wl_tag = wle[buf].n; wle[buf].n += 32
            tvh = tv.n; tv.n += n_h1 + n_y
            svh = sv.n; sv.n += n_h1
            vvh = vv.n; vv.n += n_y

            @b.blk.gpsimd
            def _(eng, n=n, buf=buf, gate=(y_done_tv.get(n - 2))):
                if gate is not None:
                    eng.wait_ge(tv.sem, gate)
                eng.dma_start(w1sb[buf][:].rearrange("p (kc f) -> p kc f", kc=8),
                              W["l0_w1"][n].rearrange("(kc p) f -> p kc f", p=128)).then_inc(wle[buf].sem, 16)
                eng.dma_start(w2sb[buf][:].rearrange("p (kc f) -> p kc f", kc=4),
                              W["l0_w2"][n].rearrange("(kc p) f -> p kc f", p=128)).then_inc(wle[buf].sem, 16)

            @b.blk.tensor
            def _(eng, n=n, buf=buf, wl_tag=wl_tag, tvh=tvh, svh=svh, vvh=vvh):
                eng.wait_ge(wle[buf].sem, wl_tag + 32)
                cnt = 0
                for m in range(4):
                    for nt4 in range(TH // 512):
                        if svh + cnt - 1 > 0:
                            eng.wait_ge(sv.sem, svh + cnt - 1)
                        ps = psA if cnt % 2 == 0 else psB
                        for kc in range(8):
                            nc.tensor.matmul(ps[:, 0:512],
                                             w1sb[buf][:, kc * 512 + 128 * m: kc * 512 + 128 * (m + 1)],
                                             xf[:, kc * TH + 512 * nt4: kc * TH + 512 * (nt4 + 1)],
                                             start=(kc == 0), stop=False)
                        nc.tensor.matmul(ps[:, 0:512],
                                         b1sb[0:1, n * 512 + 128 * m: n * 512 + 128 * (m + 1)],
                                         ones1[0:1, 0:512],
                                         start=False, stop=True).then_inc(tv.sem, 1)
                        cnt += 1
                eng.wait_ge(sv.sem, svh + 4 * (TH // 512))
                cnt2 = 0
                for i in range(NTH):
                    for dh in range(2):
                        if vvh + cnt2 - 1 > 0:
                            eng.wait_ge(vv.sem, vvh + cnt2 - 1)
                        ps = psC if cnt2 % 2 == 0 else psD
                        for kc in range(4):
                            nc.tensor.matmul(ps[:, 0:512],
                                             h1sb[buf][:, kc * TH + 128 * i: kc * TH + 128 * (i + 1)],
                                             w2sb[buf][:, kc * 1024 + 512 * dh: kc * 1024 + 512 * (dh + 1)],
                                             start=(kc == 0), stop=False)
                        nc.tensor.matmul(ps[:, 0:512], ones1[0:1, 0:128],
                                         b2sb[0:1, n * 1024 + 512 * dh: n * 1024 + 512 * (dh + 1)],
                                         start=False, stop=True).then_inc(tv.sem, 1)
                        cnt2 += 1
            y_done_tv[n] = tv.n

            @b.blk.scalar
            def _(eng, buf=buf, tvh=tvh):
                for cnt in range(4 * (TH // 512)):
                    eng.wait_ge(tv.sem, tvh + cnt + 1)
                    m, nt4 = divmod(cnt, TH // 512)
                    ps = psA if cnt % 2 == 0 else psB
                    eng.activation(h1sb[buf][:, m * TH + 512 * nt4: m * TH + 512 * (nt4 + 1)],
                                   ps[:, 0:512], AF.Gelu).then_inc(sv.sem, 1)

            @b.blk.vector
            def _(eng, n=n, tvh=tvh):
                for cnt2 in range(NTH * 2):
                    eng.wait_ge(tv.sem, tvh + 4 * (TH // 512) + cnt2 + 1)
                    i, dh = divmod(cnt2, 2)
                    ps = psC if cnt2 % 2 == 0 else psD
                    inst = eng.scalar_tensor_tensor(
                        yacc[:, 1024 * i + 512 * dh: 1024 * i + 512 * (dh + 1)],
                        ps[:, 0:512], masktok[:, 8 * i + n: 8 * i + n + 1],
                        yacc[:, 1024 * i + 512 * dh: 1024 * i + 512 * (dh + 1)],
                        ALU.mult, ALU.add)
                    inst.then_inc(vv.sem, 1)
                    if cnt2 == NTH * 2 - 1:
                        eng.wait_ge(vv.sem, vvh + NTH * 2)

        b.barrier()

        vvc = vv.n; vv.n += 2 * NTH
        svc = sv.n; sv.n += NTH
        ld0c = [c.n for c in lslot]
        st0c = [c.n for c in sslot]

        @b.blk.sync
        def _(eng, t0=t0, ld0c=ld0c, st0c=st0c):
            for i in range(NTH):
                j = i % 2
                if i >= 2:
                    eng.wait_ge(sslot[j].sem, st0c[j] + 16 * (i // 2))
                eng.dma_start(xsb[j][:], x_src[t0 + 128 * i: t0 + 128 * (i + 1), :]).then_inc(lslot[j].sem, 16)
                eng.dma_start(ssb[j][:], s_src[t0 + 128 * i: t0 + 128 * (i + 1), :]).then_inc(lslot[j].sem, 16)

        @b.blk.scalar
        def _(eng, svc=svc, vvc=vvc):
            for i in range(NTH):
                j = i % 2
                if i >= 2:
                    eng.wait_ge(vv.sem, vvc + 2 * (i - 2) + 2)
                eng.activation(gsb[j][:], yacc[:, 1024 * i:1024 * (i + 1)], AF.Silu).then_inc(sv.sem, 1)

        @b.blk.vector
        def _(eng, ld0c=ld0c, svc=svc):
            for i in range(NTH):
                j = i % 2
                eng.wait_ge(lslot[j].sem, ld0c[j] + 32 * (i // 2 + 1))
                eng.wait_ge(sv.sem, svc + i + 1)
                eng.tensor_tensor(gsb[j][:], gsb[j][:], ssb[j][:], ALU.mult).then_inc(vv.sem, 1)
                eng.wait_ge(vv.sem, vvc + 2 * i + 1)
                eng.tensor_tensor(x1sb[j][:], gsb[j][:], xsb[j][:], ALU.add).then_inc(vv.sem, 1)

        @b.blk.gpsimd
        def _(eng, t0=t0, vvc=vvc, st0c=st0c):
            for i in range(NTH):
                j = i % 2
                eng.wait_ge(vv.sem, vvc + 2 * i + 2)
                eng.dma_start(x1_dst[t0 + 128 * i: t0 + 128 * (i + 1), :], x1sb[j][:]).then_inc(sslot[j].sem, 16)
            for j in range(2):
                eng.wait_ge(sslot[j].sem, st0c[j] + 16 * len(range(j, NTH, 2)))
        for j in range(2):
            lslot[j].n += 32 * len(range(j, NTH, 2))
            sslot[j].n += 16 * len(range(j, NTH, 2))

        b.barrier()


def emit_l0attn(b: Bld, xn_bf, s0_dst):
    nc = b.nc
    W = b.dram
    SEG = 1024
    NSEG = T // SEG
    NTS = SEG // 128
    H = 16

    wl = b.cnt("a0_wl")
    ml = b.cnt("a0_ml")
    xfc = b.cnt("a0_xfc")
    tv = b.cnt("a0_t")
    vv = b.cnt("a0_v")
    sv = b.cnt("a0_sc")
    gv = b.cnt("a0_g")
    qkst = b.cnt("a0_qkst")
    qktr = [b.cnt(f"a0_qktr{j}") for j in range(2)]
    pst = b.cnt("a0_pst")
    ptr = [b.cnt(f"a0_ptr{j}") for j in range(2)]
    sst = [b.cnt(f"a0_sst{j}") for j in range(2)]

    phq_d = nc.dram_tensor("a0_phq", [SEG, 256], BF16, kind="Internal").ap()
    qk_d = nc.dram_tensor("a0_qkd", [SEG, 1152], BF16, kind="Internal").ap()

    xf = b.sbuf("a0_xf", [128, 8 * SEG], BF16)
    qk_tok = b.sbuf("a0_qk", [128, NTS * 1024], BF16)
    qf = b.sbuf("a0_qf", [128, 8 * SEG], BF16)
    kf = b.sbuf("a0_kf", [128, 8 * SEG], BF16)
    v_tok = b.sbuf("a0_v", [128, NTS * H * 65], BF16)
    phs = b.sbuf("a0_phs", [128, NTS * 256], BF16)
    phk_tok = b.sbuf("a0_phkt", [128, NTS * 256], BF16)
    phqf = b.sbuf("a0_phqf", [128, 2 * SEG], BF16)
    o_fm = b.sbuf("a0_ofm", [128, 8 * SEG], BF16)
    st32 = b.sbuf("a0_st32", [128, H * 130], F32)
    stbf = b.sbuf("a0_stbf", [128, H * 130], BF16)
    mask_su = b.sbuf("a0_mask", [128, 128], F32)
    ones1 = b.sbuf("a0_ones", [1, 128], BF16)
    rrow = b.sbuf("a0_rrow", [1, 128], F32)
    rrow_bf = b.sbuf("a0_rrowbf", [1, 128], BF16)
    obsb = b.sbuf("a0_obsb", [128, 128], F32)
    mtf = b.sbuf("a0_mtf", [128, 128], F32)
    mt_sb = b.sbuf("a0_mt", [128, 128], BF16)
    junkg = b.sbuf("a0_junkg", [128, 2], F32)
    ztile = b.sbuf("a0_ztile", [128, 128], BF16)
    s0sb = [b.sbuf(f"a0_s0{j}", [128, D], F32) for j in range(2)]
    wq = b.sbuf("a0_wq", [128, 8 * 256], BF16)
    wk = b.sbuf("a0_wk", [128, 8 * 256], BF16)
    wv = b.sbuf("a0_wv", [128, 8 * 1024], BF16)
    wo = b.sbuf("a0_wo", [128, 8 * 1024], BF16)

    psA = b.psum("a0_psA", [128, 512])
    psB = b.psum("a0_psB", [128, 512])
    psM = b.psum("a0_psM", [128, 128])
    psO = b.psum("a0_psO", [128, 128])
    psBC = b.psum("a0_psBC", [128, 128])
    psS = b.psum("a0_psS", [128, 130])

    EPS = 1e-5

    @b.blk.gpsimd
    def _(eng):
        eng.dma_start(wq[:].rearrange("p (kc f) -> p kc f", kc=8),
                      W["l0_wq"].rearrange("(kc p) f -> p kc f", p=128)).then_inc(wl.sem, 16)
        eng.dma_start(wk[:].rearrange("p (kc f) -> p kc f", kc=8),
                      W["l0_wk"].rearrange("(kc p) f -> p kc f", p=128)).then_inc(wl.sem, 16)
        eng.dma_start(wv[:].rearrange("p (kc f) -> p kc f", kc=8),
                      W["l0_wv"].rearrange("(kc p) f -> p kc f", p=128)).then_inc(wl.sem, 16)
        eng.dma_start(wo[:].rearrange("p (kc f) -> p kc f", kc=8),
                      W["l0_wo"].rearrange("(kc p) f -> p kc f", p=128)).then_inc(wl.sem, 16)
    wl.n += 64

    @b.blk.sync
    def _(eng):
        eng.dma_start(mask_su[:], W["mask_su"]).then_inc(ml.sem, 16)
        eng.wait_ge(b.cnt("a0_v").sem, 1)
        for t in range(SEG // 128):
            eng.dma_start(qk_d[128 * t:128 * (t + 1), 1024:1152], ztile[:]).then_inc(ml.sem, 16)
    ml.n += 16 * (1 + SEG // 128)

    @b.blk.vector
    def _(eng):
        eng.wait_ge(wl.sem, wl.n)
        eng.wait_ge(ml.sem, 16)
        eng.memset(st32[:], 0.0)
        eng.memset(stbf[:], 0.0)
        eng.memset(ones1[:], 1.0)
        eng.memset(qk_tok[:], 0.0)
        eng.memset(ztile[:], 0.0)
        b.fence(eng, "v", eng.memset(
            v_tok[:].rearrange("p (t h c) -> p t h c", t=NTS, h=H)[:, :, :, 64:65], 1.0))
        eng.nop().then_inc(vv.sem, 1)
    vv.n += 1
    b.barrier()

    for seg in range(NSEG):
        t0 = seg * SEG
        xf0 = xfc.n

        @b.blk.sync
        def _(eng, t0=t0):
            for r in range(8):
                eng.dma_start(xf[:, r * SEG:(r + 1) * SEG],
                              xn_bf[t0:t0 + SEG, r * 128:(r + 1) * 128], transpose=True).then_inc(xfc.sem, 16)
        xfc.n += 16 * 8

        tv0 = tv.n
        sv0 = sv.n

        @b.blk.tensor
        def _(eng, xf0=xf0, tv0=tv0, sv0=sv0):
            eng.wait_ge(xfc.sem, xf0 + 16 * 8)
            cnt = 0
            for i in range(NTS):
                ps = psA if cnt % 2 == 0 else psB
                if cnt >= 2:
                    eng.wait_ge(sv.sem, sv0 + 2 * (cnt - 1))
                for kc in range(8):
                    nc.tensor.matmul(ps[:, 0:256], xf[:, kc * SEG + 128 * i: kc * SEG + 128 * (i + 1)],
                                     wq[:, kc * 256:(kc + 1) * 256], start=(kc == 0), stop=False)
                for kc in range(8):
                    inst = nc.tensor.matmul(ps[:, 256:512], xf[:, kc * SEG + 128 * i: kc * SEG + 128 * (i + 1)],
                                            wk[:, kc * 256:(kc + 1) * 256], start=False, stop=(kc == 7))
                inst.then_inc(tv.sem, 1)
                cnt += 1
            for i in range(NTS):
                for nh in range(2):
                    ps = psA if cnt % 2 == 0 else psB
                    if cnt >= 2:
                        eng.wait_ge(sv.sem, sv0 + 2 * (cnt - 1))
                    for kc in range(8):
                        inst = nc.tensor.matmul(ps[:, 0:512], xf[:, kc * SEG + 128 * i: kc * SEG + 128 * (i + 1)],
                                                wv[:, kc * 1024 + 512 * nh: kc * 1024 + 512 * (nh + 1)],
                                                start=(kc == 0), stop=(kc == 7))
                    inst.then_inc(tv.sem, 1)
                    cnt += 1
        tv.n += 3 * NTS

        @b.blk.scalar
        def _(eng, tv0=tv0, sv0=sv0):
            cnt = 0
            for i in range(NTS):
                ps = psA if cnt % 2 == 0 else psB
                eng.wait_ge(tv.sem, tv0 + cnt + 1)
                qdst = qk_tok[:, 1024 * i: 1024 * i + 512].rearrange("p (h s) -> p h s", s=32)[:, :, 0:16]
                kdst = qk_tok[:, 1024 * i + 512: 1024 * (i + 1)].rearrange("p (h s) -> p h s", s=32)[:, :, 0:16]
                eng.activation(qdst, ps[:, 0:256].rearrange("p (h f) -> p h f", h=16), AF.Copy, scale=0.25)
                eng.copy(kdst, ps[:, 256:512].rearrange("p (h f) -> p h f", h=16)).then_inc(sv.sem, 2)
                cnt += 1
            for i in range(NTS):
                for nh in range(2):
                    ps = psA if cnt % 2 == 0 else psB
                    eng.wait_ge(tv.sem, tv0 + cnt + 1)
                    dst = v_tok[:, i * H * 65 + nh * 8 * 65: i * H * 65 + (nh * 8 + 8) * 65]
                    eng.copy(dst.rearrange("p (h c) -> p h c", h=8)[:, :, 0:64],
                             ps[:, 0:512].rearrange("p (h c) -> p h c", h=8)).then_inc(sv.sem, 2)
                    cnt += 1
        sv.n += 2 * 3 * NTS

        b.barrier()

        q0 = qkst.n
        qt0 = [c.n for c in qktr]

        @b.blk.sync
        def _(eng, q0=q0, qt0=qt0):
            eng.dma_start(qk_d[:, 0:1024].rearrange("(t p) f -> p t f", p=128),
                          qk_tok[:].rearrange("p (t f) -> p t f", t=NTS)).then_inc(qkst.sem, 16)
            eng.wait_ge(qkst.sem, q0 + 16)
            eng.wait_ge(ml.sem, 16 * (1 + SEG // 128))
            for g in range(8):
                srcq = 64 * g
                srck = 512 + 64 * g
                eng.dma_start(qf[:, g * SEG:(g + 1) * SEG],
                              qk_d[:, srcq:srcq + 128], transpose=True).then_inc(qktr[0].sem, 16)
                eng.dma_start(kf[:, g * SEG:(g + 1) * SEG],
                              qk_d[:, srck:srck + 128], transpose=True).then_inc(qktr[1].sem, 16)
        qkst.n += 16
        for c in qktr:
            c.n += 128
        qk_targets = [c.n for c in qktr]

        b.barrier()

        for h in range(H):
            vv0 = vv.n
            gv0 = gv.n

            @b.blk.vector
            def _(eng, h=h):
                for i in range(NTS):
                    qsl = qk_tok[:, 1024 * i + 32 * h: 1024 * i + 32 * h + 16]
                    in0 = qsl.rearrange("p (i one) -> p i one", one=1).broadcast_to([128, 16, 16])
                    in1 = qsl.rearrange("p (one j) -> p one j", one=1).broadcast_to([128, 16, 16])
                    inst = eng.tensor_tensor(phs[:, 256 * i:256 * (i + 1)].rearrange("p (i j) -> p i j", i=16),
                                             in0, in1, ALU.mult)
                    if i < NTS - 1:
                        b.ce("v", inst)
                inst.then_inc(vv.sem, 1)
                b.sems["chain_v"].n += 0
            vv.n += 1

            @b.blk.gpsimd
            def _(eng, h=h):
                for i in range(NTS):
                    ksl = qk_tok[:, 1024 * i + 512 + 32 * h: 1024 * i + 512 + 32 * h + 16]
                    in0 = ksl.rearrange("p (i one) -> p i one", one=1).broadcast_to([128, 16, 16])
                    in1 = ksl.rearrange("p (one j) -> p one j", one=1).broadcast_to([128, 16, 16])
                    inst = eng.tensor_tensor(phk_tok[:, 256 * i:256 * (i + 1)].rearrange("p (i j) -> p i j", i=16),
                                              in0, in1, ALU.mult)
                inst.then_inc(gv.sem, 1)
            gv.n += 1

            p0q = pst.n
            pt0 = [c.n for c in ptr]

            @b.blk.sync
            def _(eng, vv0=vv0, p0q=p0q):
                eng.wait_ge(vv.sem, vv0 + 1)
                eng.dma_start(phq_d.rearrange("(t p) f -> p t f", p=128),
                              phs[:].rearrange("p (t f) -> p t f", t=NTS)).then_inc(pst.sem, 16)
                eng.wait_ge(pst.sem, p0q + 16)
                eng.dma_start(phqf[:, 0:SEG], phq_d[:, 0:128], transpose=True).then_inc(ptr[0].sem, 16)
                eng.dma_start(phqf[:, SEG:2 * SEG], phq_d[:, 128:256], transpose=True).then_inc(ptr[1].sem, 16)
            pst.n += 16
            for c in ptr:
                c.n += 16
            ptr_targets = [c.n for c in ptr]
            gv_target = gv.n

            b.barrier()

            tvh = tv.n
            vvh = vv.n
            svh = sv.n
            n_c = NTS
            hb, ho = 32 * (h % 2), (h // 2) * SEG

            @b.blk.tensor
            def _(eng, h=h, tvh=tvh, vvh=vvh, svh=svh, ptr_targets=ptr_targets,
                  qk_targets=qk_targets, hb=hb, ho=ho):
                for j in range(2):
                    eng.wait_ge(ptr[j].sem, ptr_targets[j])
                    eng.wait_ge(qktr[j].sem, qk_targets[j])
                for c in range(n_c):
                    if c > 0:
                        eng.wait_ge(sv.sem, svh + 2 * (c - 1) + 1)
                    nc.tensor.matmul(psM[:, 0:128],
                                     kf[hb:hb + 16, ho + 128 * c: ho + 128 * (c + 1)],
                                     qf[hb:hb + 16, ho + 128 * c: ho + 128 * (c + 1)],
                                     start=True, stop=True).then_inc(tv.sem, 1)
                    eng.wait_ge(vv.sem, vvh + 4 * c + 2)
                    if c > 0:
                        eng.wait_ge(sv.sem, svh + 2 * (c - 1) + 2)
                        eng.wait_ge(vv.sem, vvh + 4 * (c - 1) + 3)
                        eng.wait_ge(vv.sem, vvh + 4 * (c - 1) + 4)
                    vt = v_tok[:, (c * H + h) * 65:(c * H + h) * 65 + 65]
                    nc.tensor.matmul(psO[0:65, 0:128], vt, mt_sb[:, 0:128], start=True, stop=False)
                    for fh in range(2):
                        inst = nc.tensor.matmul(psO[0:65, 0:128], stbf[:, h * 130 + 65 * fh: h * 130 + 65 * (fh + 1)],
                                                phqf[:, fh * SEG + 128 * c: fh * SEG + 128 * (c + 1)],
                                                start=False, stop=(fh == 1))
                    inst.then_inc(tv.sem, 1)
                    for fh in range(2):
                        inst = nc.tensor.matmul(psS[:, 65 * fh:65 * (fh + 1)],
                                                phk_tok[:, 256 * c + 128 * fh: 256 * c + 128 * (fh + 1)],
                                                vt, start=True, stop=True)
                    inst.then_inc(tv.sem, 1)
                    eng.wait_ge(vv.sem, vvh + 4 * c + 3)
                    nc.tensor.matmul(psBC[0:64, 0:128], ones1[0:1, 0:64], rrow_bf[0:1, 0:128],
                                     start=True, stop=True).then_inc(tv.sem, 1)
            tv.n += 4 * n_c

            @b.blk.scalar
            def _(eng, tvh=tvh, svh=svh, vvh=vvh, gv_target=gv_target):
                eng.wait_ge(gv.sem, gv_target)
                for c in range(n_c):
                    eng.wait_ge(tv.sem, tvh + 4 * c + 1)
                    if c > 0:
                        eng.wait_ge(vv.sem, vvh + 4 * (c - 1) + 1)
                    eng.activation(mtf[:, 0:128], psM[:, 0:128], AF.Square).then_inc(sv.sem, 1)
                    eng.wait_ge(tv.sem, tvh + 4 * c + 2)
                    if c > 0:
                        eng.wait_ge(vv.sem, vvh + 4 * (c - 1) + 4)
                    eng.copy(obsb[0:64, 0:128], psO[0:64, 0:128]).then_inc(sv.sem, 1)
            sv.n += 2 * n_c

            @b.blk.vector
            def _(eng, h=h, tvh=tvh, vvh=vvh, svh=svh):
                for c in range(n_c):
                    eng.wait_ge(sv.sem, svh + 2 * c + 1)
                    eng.tensor_tensor(mt_sb[:, 0:128], mtf[:, 0:128], mask_su[:], ALU.mult).then_inc(vv.sem, 2)
                    eng.wait_ge(tv.sem, tvh + 4 * c + 2)
                    b.fence(eng, "v", eng.tensor_scalar(rrow[0:1, 0:128], psO[64:65, 0:128],
                                                        EPS, EPS, ALU.add, ALU.max))
                    b.fence(eng, "v", eng.reciprocal(rrow[0:1, 0:128], rrow[0:1, 0:128]))
                    eng.tensor_copy(rrow_bf[0:1, 0:128], rrow[0:1, 0:128]).then_inc(vv.sem, 1)
                    eng.wait_ge(sv.sem, svh + 2 * c + 2)
                    eng.wait_ge(tv.sem, tvh + 4 * c + 4)
                    tile = h // 2
                    row0 = (h % 2) * 64
                    eng.tensor_tensor(o_fm[row0:row0 + 64, tile * SEG + 128 * c: tile * SEG + 128 * (c + 1)],
                                      obsb[0:64, 0:128], psBC[0:64, 0:128], ALU.mult)
                    eng.wait_ge(tv.sem, tvh + 4 * c + 3)
                    b.fence(eng, "v", eng.tensor_tensor(st32[:, h * 130:(h + 1) * 130],
                                                        st32[:, h * 130:(h + 1) * 130],
                                                        psS[:, 0:130], ALU.add))
                    eng.tensor_copy(stbf[:, h * 130:(h + 1) * 130],
                                    st32[:, h * 130:(h + 1) * 130]).then_inc(vv.sem, 1)
            vv.n += 4 * n_c

            b.barrier()

        tvp = tv.n
        vvp = vv.n
        st0s = [c.n for c in sst]

        @b.blk.tensor
        def _(eng, tvp=tvp, vvp=vvp):
            cnt = 0
            for i in range(NTS):
                for nh in range(2):
                    ps = psA if cnt % 2 == 0 else psB
                    if cnt >= 2:
                        eng.wait_ge(vv.sem, vvp + cnt - 1)
                    for kc in range(8):
                        inst = nc.tensor.matmul(ps[:, 0:512], o_fm[:, kc * SEG + 128 * i: kc * SEG + 128 * (i + 1)],
                                                wo[:, kc * 1024 + 512 * nh: kc * 1024 + 512 * (nh + 1)],
                                                start=(kc == 0), stop=(kc == 7))
                    inst.then_inc(tv.sem, 1)
                    cnt += 1
        tv.n += 2 * NTS

        @b.blk.vector
        def _(eng, tvp=tvp, st0s=st0s):
            cnt = 0
            for i in range(NTS):
                for nh in range(2):
                    j = i % 2
                    ps = psA if cnt % 2 == 0 else psB
                    eng.wait_ge(tv.sem, tvp + cnt + 1)
                    if cnt >= 4:
                        eng.wait_ge(sst[j].sem, st0s[j] + 16 * ((cnt - 4) // 4 + 1))
                    eng.tensor_copy(s0sb[j][:, 512 * nh:512 * (nh + 1)], ps[:, 0:512]).then_inc(vv.sem, 1)
                    cnt += 1
        vv.n += 2 * NTS

        @b.blk.gpsimd
        def _(eng, t0=t0, vvp=vvp, st0s=st0s):
            for i in range(NTS):
                j = i % 2
                eng.wait_ge(vv.sem, vvp + 2 * (i + 1))
                eng.dma_start(s0_dst[t0 + 128 * i:t0 + 128 * (i + 1), :], s0sb[j][:]).then_inc(sst[j].sem, 16)
            for j in range(2):
                eng.wait_ge(sst[j].sem, st0s[j] + 16 * len(range(j, NTS, 2)))
        for j in range(2):
            sst[j].n += 16 * len(range(j, NTS, 2))

        b.barrier()


def emit_l1attn(b: Bld, xn_bf, s1_dst):
    import contextlib
    nc = b.nc
    W = b.dram
    QT = 1024
    NQ = T // QT
    NTQ = QT // 128
    H, KD, VD = 8, 128, 256
    EPSR = 1e-5

    q_d = nc.dram_tensor("a1_q", [1024, T], BF16, kind="Internal").ap()
    k_d = nc.dram_tensor("a1_k", [1024, T], BF16, kind="Internal").ap()
    v_d = nc.dram_tensor("a1_v", [2048, T], BF16, kind="Internal").ap()
    gs_d = nc.dram_tensor("a1_gs", [2048, T], BF16, kind="Internal").ap()
    og_d = nc.dram_tensor("a1_og", [T, 2048], BF16, kind="Internal").ap()
    if L1A_SPILL:
        dbg_d = nc.dram_tensor("a1_dbg", [32 * 8, 4, 128, 256], F32, kind="Internal").ap()

    seq = Seq(b)
    wl = b.cnt("a1_wl")
    fl = b.cnt("a1_fl")

    bbuf = b.sbuf("a1_bbuf", [128, 32 * 8], F32)
    gbuf = b.sbuf("a1_gbuf", [128, 32 * 8], F32)
    negA = b.sbuf("a1_negA", [128, 8], F32)
    alog = b.sbuf("a1_alog", [1, 16], F32)
    onesc = b.sbuf("a1_onesc", [128, 1], F32)
    ones1 = b.sbuf("a1_ones1", [1, 128], BF16)
    onesf1 = b.sbuf("a1_onesf1", [1, 128], F32)
    cums = b.sbuf("a1_cums", [128, 128], F32)
    am_i = b.sbuf("a1_am_i", [128, 128], F32)
    am_sL = b.sbuf("a1_am_sL", [128, 128], F32)
    identf = b.sbuf("a1_identf", [128, 128], F32)
    bt0 = b.sbuf("a1_bt0", [128, 8], F32)
    bt1 = b.sbuf("a1_bt1", [128, 8], F32)
    bt2 = b.sbuf("a1_bt2", [128, 8], F32)
    bt3 = b.sbuf("a1_bt3", [128, 8], F32)
    albf = b.sbuf("a1_albf", [1, 16], BF16)

    ld0 = seq.dma("sync", lambda eng: [
        eng.dma_start(albf[:], W["alogdt_bf"]),
        eng.dma_start(alog[:], W["l1_alog_dt"]),
        eng.dma_start(cums[:], W["cums128"]),
        eng.dma_start(am_i[:], W["amask_i128"]),
        eng.dma_start(am_sL[:], W["amask_sL128"]),
        eng.dma_start(identf[:], W["ident"]),
    ], n=6)
    s_exp = seq.step("scalar", lambda eng: eng.activation(alog[0:1, 0:8], alog[0:1, 0:8], AF.Exp), deps=[ld0])
    v_ini = seq.step("vector", lambda eng: (
        eng.tensor_scalar(alog[0:1, 0:8], alog[0:1, 0:8], -1.0, None, ALU.mult),
        eng.memset(onesc[:], 1.0),
        eng.memset(ones1[:], 1.0),
        eng.memset(onesf1[:], 1.0))[-1], deps=[s_exp])
    ctxIN = contextlib.ExitStack()
    psIN = b.psum2("a1_psIN", [128, 128], F32, ctxIN)
    t_na = seq.step("tensor", lambda eng: nc.tensor.matmul(
        psIN[:, 0:8], onesf1[0:1, :], alog[0:1, 0:8], start=True, stop=True), deps=[v_ini])
    v_na = seq.step("vector", lambda eng: eng.tensor_copy(negA[:], psIN[:, 0:8]), deps=[t_na])

    ctxA = contextlib.ExitStack()
    xf = b.sbuf("a1_xf", [128, 8 * QT], BF16, ctxA)
    wq = b.sbuf("a1_wq", [128, 8 * 1024], BF16, ctxA)
    wk = b.sbuf("a1_wk", [128, 8 * 1024], BF16, ctxA)
    wv = b.sbuf("a1_wv", [128, 8 * 2048], BF16, ctxA)
    wg = b.sbuf("a1_wg", [128, 8 * 2048], BF16, ctxA)
    wab = b.sbuf("a1_wab", [128, 8 * 16], BF16, ctxA)
    cwq = b.sbuf("a1_cwq", [128, 8 * 4], F32, ctxA)
    cwk = b.sbuf("a1_cwk", [128, 8 * 4], F32, ctxA)
    cwv = b.sbuf("a1_cwv", [128, 16 * 4], F32, ctxA)
    raw = b.sbuf("a1_raw", [128, 1032], F32, ctxA)
    ysb = b.sbuf("a1_ysb", [128, 1024], F32, ctxA)
    sqb = b.sbuf("a1_sqb", [128, 1024], F32, ctxA)
    ssrow = b.sbuf("a1_ssrow", [1, 1024], F32, ctxA)
    rrow = b.sbuf("a1_rrow", [1, 1024], F32, ctxA)
    rrbf = b.sbuf("a1_rrbf", [1, 1024], BF16, ctxA)
    bnd = b.sbuf("a1_bnd", [128, 32 * 3], F32, ctxA)
    stg = b.sbuf("a1_stg", [128, 1024], BF16, ctxA)
    psQ = b.psum2("a1_psQ", [128, 512], F32, ctxA)
    psW = b.psum2("a1_psW", [128, 512], F32, ctxA)
    psE = b.psum2("a1_psE", [128, 512], F32, ctxA)
    psE2 = b.psum2("a1_psE2", [128, 512], F32, ctxA)
    psH8 = b.psum2("a1_psH8", [128, 16], F32, ctxA)

    wld = seq.dma("gpsimd_dma", lambda eng: [
        eng.dma_start(wq[:].rearrange("p (kc f) -> p kc f", kc=8),
                      W["l1_wq"].rearrange("(kc p) f -> p kc f", p=128)),
        eng.dma_start(wk[:].rearrange("p (kc f) -> p kc f", kc=8),
                      W["l1_wk"].rearrange("(kc p) f -> p kc f", p=128)),
        eng.dma_start(wv[:].rearrange("p (kc f) -> p kc f", kc=8),
                      W["l1_wv"].rearrange("(kc p) f -> p kc f", p=128)),
        eng.dma_start(wg[:].rearrange("p (kc f) -> p kc f", kc=8),
                      W["l1_wg"].rearrange("(kc p) f -> p kc f", p=128)),
        eng.dma_start(wab[:].rearrange("p (kc f) -> p kc f", kc=8),
                      W["l1_wab"].rearrange("(kc p) f -> p kc f", p=128)),
    ], n=5)
    cwl = seq.dma("sync", lambda eng: [
        eng.dma_start(cwq[:].rearrange("p (kc s) -> p kc s", kc=8),
                      W["l1_convq"].rearrange("(kc p) s -> p kc s", p=128)),
        eng.dma_start(cwk[:].rearrange("p (kc s) -> p kc s", kc=8),
                      W["l1_convk"].rearrange("(kc p) s -> p kc s", p=128)),
        eng.dma_start(cwv[:].rearrange("p (kc s) -> p kc s", kc=16),
                      W["l1_convv"].rearrange("(kc p) s -> p kc s", p=128)),
    ], n=3)
    v_bnd = seq.step("vector", lambda eng: eng.memset(bnd[:], 0.0), deps=[cwl])

    prev_projmm = None
    for qi in range(L1A_NQ):
        t0 = qi * QT
        xfl = seq.dma("sync", lambda eng, t0=t0: [
            eng.dma_start(xf[:, r * QT:(r + 1) * QT],
                          xn_bf[t0:t0 + QT, r * 128:(r + 1) * 128], transpose=True)
            for r in range(8)], deps=[wld, prev_projmm], n=8)

        prev_gv3 = None
        prev_gs2 = None
        prev_gs1 = None
        for i in range(NTQ):
            gcol = (qi * NTQ + i) * 8

            def abmm(eng, i=i):
                for kc in range(8):
                    nc.tensor.matmul(psH8[:, 0:16], xf[:, kc * QT + 128 * i: kc * QT + 128 * (i + 1)],
                                     wab[:, kc * 16:(kc + 1) * 16], start=(kc == 0), stop=False)
                return nc.tensor.matmul(psH8[:, 0:8], ones1[0:1, 0:128], albf[0:1, 8:16],
                                        start=False, stop=True)
            ab = seq.step("tensor", abmm, deps=[xfl, v_na, prev_gv3, prev_gs2, ld0, wld])
            gv1 = seq.step("vector", lambda eng: (
                b.fence(eng, "v", eng.tensor_scalar(bt0[:], psH8[:, 0:8], -1.0, None, ALU.mult)),
                eng.tensor_tensor(bt1[:], psH8[:, 0:8], bt0[:], ALU.max))[-1],
                deps=[ab, prev_gs1])
            gs1 = seq.step("scalar", lambda eng: (
                b.fence(eng, "s", eng.activation(bt2[:], bt1[:], AF.Exp, scale=-1.0)),
                eng.activation(bt2[:], bt2[:], AF.Ln, bias=1.0))[-1], deps=[gv1])
            prev_gs1 = gs1
            gv2 = seq.step("vector", lambda eng, gcol=gcol: (
                b.fence(eng, "v", eng.tensor_scalar(bt1[:], psH8[:, 0:8], 0.0, None, ALU.max)),
                b.fence(eng, "v", eng.tensor_tensor(bt2[:], bt2[:], bt1[:], ALU.add)),
                eng.tensor_tensor(gbuf[:, gcol:gcol + 8], bt2[:], negA[:], ALU.mult))[-1], deps=[gs1])
            gs2 = seq.step("scalar", lambda eng: eng.activation(bt3[:], psH8[:, 8:16], AF.Sigmoid), deps=[gv2])
            gv3 = seq.step("vector", lambda eng, gcol=gcol: eng.tensor_scalar(
                bbuf[:, gcol:gcol + 8], bt3[:], 2.0, None, ALU.mult), deps=[gs2])
            prev_gv3, prev_gs2 = gv3, gs2

        specs = [("q", 8, wq, cwq, True, q_d, 0), ("k", 8, wk, cwk, True, k_d, 8),
                 ("v", 16, wv, cwv, False, v_d, 16), ("g", 16, wg, None, False, gs_d, 0)]
        prev_ev = {}
        last_store = None
        for name, nm, wsb, cw, isqk, dst, bo in specs:
            for m in range(nm):
                def pjmm(eng, nm=nm, wsb=wsb, m=m):
                    for kc in range(8):
                        nc.tensor.matmul(psQ[:, 0:512],
                                         wsb[:, kc * (nm * 128) + 128 * m: kc * (nm * 128) + 128 * (m + 1)],
                                         xf[:, kc * QT: kc * QT + 512],
                                         start=(kc == 0), stop=(kc == 7))
                    for kc in range(8):
                        inst = nc.tensor.matmul(psQ[:, 0:512] if False else psW[:, 0:512],
                                                wsb[:, kc * (nm * 128) + 128 * m: kc * (nm * 128) + 128 * (m + 1)],
                                                xf[:, kc * QT + 512: kc * QT + 1024],
                                                start=(kc == 0), stop=(kc == 7))
                    return inst
                pj = seq.step("tensor", pjmm,
                              deps=[xfl, prev_ev.get("free_psQW"), (v_na if qi == 0 else None)])
                if name == "g":
                    sil = seq.step("scalar", lambda eng: (
                        eng.activation(stg[:, 0:512], psQ[:, 0:512], AF.Silu),
                        eng.activation(stg[:, 512:1024], psW[:, 0:512], AF.Silu))[-1],
                        deps=[pj, prev_ev.get("stg")])
                    prev_ev["free_psQW"] = sil
                    st = seq.dma("gpsimd_dma", lambda eng, dst=dst, m=m, t0=t0: [
                        eng.dma_start(dst[128 * m:128 * (m + 1), t0:t0 + QT], stg[:])], deps=[sil], n=1)
                    prev_ev["stg"] = st
                    last_store = st
                    continue
                bslot = bo + m

                def evconv(eng, cw=cw, m=m, bslot=bslot):
                    eng.tensor_copy(raw[:, 8:520], psQ[:, 0:512])
                    eng.tensor_copy(raw[:, 520:1032], psW[:, 0:512])
                    b.fence(eng, "v", eng.tensor_copy(raw[:, 5:8], bnd[:, 3 * bslot:3 * bslot + 3]))
                    b.fence(eng, "v", eng.tensor_scalar(ysb[:], raw[:, 8:1032], cw[:, 4 * m + 3:4 * m + 4], None, ALU.mult))
                    b.fence(eng, "v", eng.scalar_tensor_tensor(ysb[:], raw[:, 7:1031], cw[:, 4 * m + 2:4 * m + 3], ysb[:], ALU.mult, ALU.add))
                    b.fence(eng, "v", eng.scalar_tensor_tensor(ysb[:], raw[:, 6:1030], cw[:, 4 * m + 1:4 * m + 2], ysb[:], ALU.mult, ALU.add))
                    b.fence(eng, "v", eng.scalar_tensor_tensor(ysb[:], raw[:, 5:1029], cw[:, 4 * m + 0:4 * m + 1], ysb[:], ALU.mult, ALU.add))
                    return eng.tensor_copy(bnd[:, 3 * bslot:3 * bslot + 3], raw[:, 1029:1032])
                ev = seq.step("vector", evconv, deps=[pj, prev_ev.get("raw"), prev_ev.get("ysb")])
                prev_ev["raw"] = ev
                prev_ev["free_psQW"] = ev
                if not isqk:
                    silv = seq.step("scalar", lambda eng: eng.activation(stg[:], ysb[:], AF.Silu),
                                    deps=[ev, prev_ev.get("stg")])
                    prev_ev["ysb"] = silv
                    st = seq.dma("gpsimd_dma", lambda eng, dst=dst, m=m, t0=t0: [
                        eng.dma_start(dst[128 * m:128 * (m + 1), t0:t0 + QT], stg[:])], deps=[silv], n=1)
                    prev_ev["stg"] = st
                    last_store = st
                    continue
                sil = seq.step("scalar", lambda eng: (
                    b.fence(eng, "s", eng.activation(ysb[:], ysb[:], AF.Silu)),
                    eng.activation(sqb[:], ysb[:], AF.Square))[-1],
                    deps=[ev, prev_ev.get("sqb")])
                prev_ev["ysb"] = sil
                ssm = seq.step("tensor", lambda eng: (
                    nc.tensor.matmul(psE[0:1, 0:512], onesc[:], sqb[:, 0:512], start=True, stop=True),
                    nc.tensor.matmul(psE2[0:1, 0:512], onesc[:], sqb[:, 512:1024], start=True, stop=True))[-1],
                    deps=[sil, prev_ev.get("psE")])
                prev_ev["sqb"] = ssm
                nr1 = seq.step("vector", lambda eng: (
                    eng.tensor_copy(ssrow[0:1, 0:512], psE[0:1, 0:512]),
                    eng.tensor_copy(ssrow[0:1, 512:1024], psE2[0:1, 0:512]))[-1],
                    deps=[ssm, prev_ev.get("nr2")])
                nr2 = seq.step("scalar", lambda eng: (
                    b.fence(eng, "s", eng.activation(rrow[:], ssrow[:], AF.Ln)),
                    eng.activation(rrow[:], rrow[:], AF.Exp, scale=-0.5))[-1],
                    deps=[nr1, prev_ev.get("nr3")])
                nr3 = seq.step("vector", lambda eng: (
                    b.fence(eng, "v", eng.tensor_scalar(rrow[:], rrow[:], 1e6, None, ALU.min)),
                    eng.tensor_copy(rrbf[:], rrow[:]))[-1], deps=[nr2, prev_ev.get("bcm")])
                prev_ev["psE"] = nr1
                prev_ev["nr2"] = nr2
                prev_ev["nr3"] = nr3
                bcm = seq.step("tensor", lambda eng: (
                    nc.tensor.matmul(psQ[:, 0:512], ones1[0:1, :], rrbf[0:1, 0:512], start=True, stop=True),
                    nc.tensor.matmul(psW[:, 0:512], ones1[0:1, :], rrbf[0:1, 512:1024], start=True, stop=True))[-1],
                    deps=[nr3, ev])
                prev_ev["bcm"] = bcm
                fin = seq.step("vector", lambda eng: (
                    eng.tensor_tensor(stg[:, 0:512], ysb[:, 0:512], psQ[:, 0:512], ALU.mult),
                    eng.tensor_tensor(stg[:, 512:1024], ysb[:, 512:1024], psW[:, 0:512], ALU.mult))[-1],
                    deps=[bcm, sil, prev_ev.get("stg")])
                prev_ev["free_psQW"] = fin
                prev_ev["ysb"] = fin
                st = seq.dma("gpsimd_dma", lambda eng, dst=dst, m=m, t0=t0: [
                    eng.dma_start(dst[128 * m:128 * (m + 1), t0:t0 + QT], stg[:])], deps=[fin], n=1)
                prev_ev["stg"] = st
                last_store = st
        prev_projmm = last_store

    seq.barrier_all()
    ctxA.close()

    ctxIN.close()
    ctxB = contextlib.ExitStack()
    qfq = b.sbuf("a1_qfq", [128, 8 * QT], BF16, ctxB)
    kfq = b.sbuf("a1_kfq", [128, 8 * QT], BF16, ctxB)
    vtok = b.sbuf("a1_vtok", [128, 8 * 2048], BF16, ctxB)
    ktok = b.sbuf("a1_ktok", [128, 8 * 1024], BF16, ctxB)
    gstok = b.sbuf("a1_gstok", [128, 8 * 2048], BF16, ctxB)
    Sk = b.sbuf("a1_Sk", [128, 8 * 256], F32, ctxB)
    Skb = b.sbuf("a1_Skb", [128, 8 * 256], BF16, ctxB)
    lgsb = b.sbuf("a1_lgsb", [128, 8], F32, ctxB)
    gCrow = b.sbuf("a1_gCrow", [1, 8], F32, ctxB)
    gCcol = b.sbuf("a1_gCcol", [128, 8], F32, ctxB)
    gcol8 = b.sbuf("a1_gcol8", [128, 8], F32, ctxB)
    Jsb = b.sbuf("a1_Jsb", [128, 128], F32, ctxB)
    tE1 = b.sbuf("a1_tE1", [128, 128], F32, ctxB)
    tE2 = b.sbuf("a1_tE2", [128, 128], F32, ctxB)
    Ep = b.sbuf("a1_Ep", [128, 128], F32, ctxB)
    En = b.sbuf("a1_En", [128, 128], F32, ctxB)
    LamS = b.sbuf("a1_LamS", [128, 128], F32, ctxB)
    MpB = b.sbuf("a1_MpB", [128, 128], BF16, ctxB)
    Gp = b.sbuf("a1_Gp", [128, 128], F32, ctxB)
    Hp = b.sbuf("a1_Hp", [128, 128], F32, ctxB)
    Zs = b.sbuf("a1_Zs", [128, 128], F32, ctxB)
    Tpt = b.sbuf("a1_Tpt", [128, 128], F32, ctxB)
    xa_sb = b.sbuf("a1_xa", [128, 64], F32, ctxB)
    t7 = b.sbuf("a1_t7", [128, 256], F32, ctxB)
    RhsS = b.sbuf("a1_Rhs", [128, 256], F32, ctxB)
    Ub = b.sbuf("a1_Ub", [128, 256], BF16, ctxB)
    Osb = b.sbuf("a1_Osb", [128, 256], F32, ctxB)
    sqO = b.sbuf("a1_sqO", [128, 256], F32, ctxB)
    rmsc = b.sbuf("a1_rmsc", [128, 2], F32, ctxB)
    bgc = b.sbuf("a1_bgc", [128, 1], F32, ctxB)
    ktsc = b.sbuf("a1_ktsc", [128, 128], BF16, ctxB)
    OgB = b.sbuf("a1_OgB", [128, 256], BF16, ctxB)

    pM = b.psum2("a1_pM", [128, 128], F32, ctxB)
    pQ2 = b.psum2("a1_pQ2", [128, 128], F32, ctxB)
    pJ = b.psum2("a1_pJ", [128, 128], F32, ctxB)
    pJ2 = b.psum2("a1_pJ2", [128, 128], F32, ctxB)
    pZ = b.psum2("a1_pZ", [128, 128], F32, ctxB)
    pU = b.psum2("a1_pU", [128, 256], F32, ctxB)
    pP = b.psum2("a1_pP", [128, 256], F32, ctxB)
    pG8 = b.psum2("a1_pG8", [128, 16], F32, ctxB)

    vz = seq.step("vector", lambda eng: (
        eng.memset(Sk[:], 0.0), eng.memset(Skb[:], 0.0), eng.memset(Gp[:], 0.0),
        eng.memset(pJ[:], 0.0))[-1], deps=[])

    hold = {"skb": vz, "zero": vz}
    prev = {}
    for qi in range(L1A_NQ):
        t0 = qi * QT
        bload = seq.dma("sync", lambda eng, t0=t0: [
            eng.dma_start(qfq[:].rearrange("p (r t) -> p r t", r=8),
                          q_d.rearrange("(r p) t -> p r t", p=128)[:, :, t0:t0 + QT]),
            eng.dma_start(kfq[:].rearrange("p (r t) -> p r t", r=8),
                          k_d.rearrange("(r p) t -> p r t", p=128)[:, :, t0:t0 + QT]),
        ], deps=[prev.get("lastA2") or None, prev.get("q_done")], n=2)
        btr = seq.dma("sync", lambda eng, t0=t0: [
            eng.dma_start(vtok[:, tt * 2048:(tt + 1) * 2048],
                          v_d[:, t0 + 128 * tt: t0 + 128 * (tt + 1)], transpose=True)
            for tt in range(8)] + [
            eng.dma_start(ktok[:, tt * 1024:(tt + 1) * 1024],
                          k_d[:, t0 + 128 * tt: t0 + 128 * (tt + 1)], transpose=True)
            for tt in range(8)] + [
            eng.dma_start(gstok[:, tt * 2048:(tt + 1) * 2048],
                          gs_d[:, t0 + 128 * tt: t0 + 128 * (tt + 1)], transpose=True)
            for tt in range(8)], deps=[prev.get("tok_done")], n=24)
        og_last = []
        ncc_l = NTQ if L1A_DBG is None else L1A_DBG[0]
        for cc in range(ncc_l):
            gtile = qi * NTQ + cc
            gcol0 = gtile * 8
            lgm = seq.step("tensor", lambda eng, gcol0=gcol0: (
                nc.tensor.matmul(pG8[:, 0:8], cums[:], gbuf[:, gcol0:gcol0 + 8], start=True, stop=True),
                nc.tensor.matmul(pG8[0:1, 8:16], onesc[:], gbuf[:, gcol0:gcol0 + 8], start=True, stop=True))[-1],
                deps=[prev.get("pG8"), (v_na if qi == 0 and cc == 0 else None)])
            lgv = seq.step("vector", lambda eng: eng.tensor_copy(lgsb[:], pG8[:, 0:8]), deps=[lgm])
            gce = seq.step("scalar", lambda eng: (
                eng.activation(gCrow[:], pG8[0:1, 8:16], AF.Exp),
                eng.activation(gcol8[:], lgsb[:], AF.Exp))[-1], deps=[lgv, lgm])
            gcb = seq.step("tensor", lambda eng: nc.tensor.matmul(
                pG8[:, 8:16], onesf1[0:1, :], gCrow[0:1, :], start=True, stop=True), deps=[gce])
            gcv = seq.step("vector", lambda eng: eng.tensor_copy(gCcol[:], pG8[:, 8:16]), deps=[gcb])
            prev["pG8"] = gcv
            nh_l = H if L1A_DBG is None else L1A_DBG[1]
            for h in range(nh_l):
                ksl = kfq[:, h * QT + 128 * cc: h * QT + 128 * (cc + 1)]
                qsl = qfq[:, h * QT + 128 * cc: h * QT + 128 * (cc + 1)]
                vsl = vtok[:, cc * 2048 + 256 * h: cc * 2048 + 256 * (h + 1)]
                ktsl = ktok[:, cc * 1024 + 128 * h: cc * 1024 + 128 * (h + 1)]
                gssl = gstok[:, cc * 2048 + 256 * h: cc * 2048 + 256 * (h + 1)]
                bcol = bbuf[:, gcol0 + h: gcol0 + h + 1]
                skslab = slice(h * 256, (h + 1) * 256)

                akk = seq.step("tensor", lambda eng, ksl=ksl: nc.tensor.matmul(
                    pM[:], ksl, ksl, start=True, stop=True), deps=[bload, prev.get("pM")])
                akq = seq.step("tensor", lambda eng, ksl=ksl, qsl=qsl: nc.tensor.matmul(
                    pQ2[:], ksl, qsl, start=True, stop=True), deps=[bload, prev.get("pQ2")])
                jt = seq.step("tensor", lambda eng, h=h: nc.tensor.transpose(
                    pJ[:], lgsb[:, h:h + 1].broadcast_to([128, 128]), identf[:]),
                    deps=[lgv, prev.get("pJ"), hold.get("zero")])
                jev = seq.step("vector", lambda eng: eng.tensor_copy(Jsb[:], pJ[:]),
                               deps=[jt, prev.get("Jsb")])
                j2t = seq.step("tensor", lambda eng: nc.tensor.transpose(pJ2[:], Jsb[:], identf[:]),
                               deps=[jev, prev.get("pJ2")])
                ep1 = seq.step("vector", lambda eng: eng.tensor_tensor(tE1[:], Jsb[:], am_i[:], ALU.add),
                               deps=[jev, prev.get("tE1")])
                ep2 = seq.step("vector", lambda eng: eng.tensor_tensor(tE2[:], tE1[:], pJ2[:], ALU.subtract),
                               deps=[ep1, j2t, prev.get("tE2")])
                epx = seq.step("scalar", lambda eng: eng.activation(Ep[:], tE2[:], AF.Exp),
                               deps=[ep2, prev.get("Ep")])
                en1 = seq.step("vector", lambda eng: eng.tensor_tensor(tE1[:], pJ2[:], am_sL[:], ALU.add),
                               deps=[j2t, ep2])
                en2 = seq.step("vector", lambda eng: eng.tensor_tensor(tE2[:], tE1[:], Jsb[:], ALU.subtract),
                               deps=[en1, epx])
                enx = seq.step("scalar", lambda eng: eng.activation(En[:], tE2[:], AF.Exp),
                               deps=[en2, prev.get("En")])
                lam1 = seq.step("vector", lambda eng: eng.tensor_tensor(tE1[:], En[:], pM[:], ALU.mult),
                                deps=[enx, akk, en2])
                lam2 = seq.step("vector", lambda eng, bcol=bcol: eng.tensor_scalar(
                    LamS[:], tE1[:], bcol, None, ALU.mult), deps=[lam1, prev.get("LamS")])
                mp1 = seq.step("vector", lambda eng: eng.tensor_tensor(tE2[:], Ep[:], pQ2[:], ALU.mult),
                               deps=[epx, akq, enx])
                mp2 = seq.step("vector", lambda eng: eng.tensor_copy(MpB[:], tE2[:]),
                               deps=[mp1, prev.get("MpB")])
                gp1 = seq.step("vector", lambda eng: (
                    eng.tensor_copy(Gp[0:64, 0:64], LamS[0:64, 0:64]),
                    eng.tensor_copy(Gp[64:128, 64:128], LamS[64:128, 64:128]))[-1],
                    deps=[lam2, prev.get("Gp")])
                h0t = seq.step("tensor", lambda eng: nc.tensor.transpose(pZ[:], Gp[:], identf[:]),
                               deps=[gp1, prev.get("pZ")])
                h0e = seq.step("vector", lambda eng: eng.tensor_copy(Hp[:], pZ[:]), deps=[h0t, prev.get("Hp")])
                z0 = seq.step("vector", lambda eng: eng.tensor_tensor(Zs[:], identf[:], Gp[:], ALU.subtract),
                              deps=[gp1, prev.get("Zs")])
                gj, hj, zj = gp1, h0e, z0
                for lev in range(5):
                    sq1 = seq.step("tensor", lambda eng: nc.tensor.matmul(
                        pZ[:], Hp[:], Gp[:], start=True, stop=True), deps=[hj, gj, h0e, en1])
                    sq2 = seq.step("tensor", lambda eng: nc.tensor.matmul(
                        pJ2[:], Gp[:], Hp[:], start=True, stop=True), deps=[hj, gj, en1])
                    ge = seq.step("vector", lambda eng: eng.tensor_copy(Gp[:], pZ[:]),
                                  deps=[sq1, sq2, z0])
                    he = seq.step("vector", lambda eng: eng.tensor_copy(Hp[:], pJ2[:]), deps=[sq2])
                    zm = seq.step("tensor", lambda eng: nc.tensor.matmul(
                        pU[:, 0:128], Hp[:], Zs[:], start=True, stop=True), deps=[he, zj, prev.get("pU")])
                    za = seq.step("vector", lambda eng: eng.tensor_tensor(Zs[:], Zs[:], pU[:, 0:128], ALU.add),
                                  deps=[zm])
                    gj, hj, zj = ge, he, za
                    prev["pU"] = za
                tpt = seq.step("tensor", lambda eng: nc.tensor.transpose(pZ[:], Zs[:], identf[:]),
                               deps=[zj, gj])
                tpe = seq.step("vector", lambda eng: eng.tensor_copy(Tpt[:], pZ[:]), deps=[tpt, prev.get("Tpt")])
                sa = seq.step("tensor", lambda eng: nc.tensor.matmul(
                    pU[0:64, 0:64], LamS[64:128, 0:64], Tpt[64:128, 64:128], start=True, stop=True),
                    deps=[tpe, lam2, prev.get("pU")])
                sae = seq.step("vector", lambda eng: eng.tensor_copy(xa_sb[0:64, 0:64], pU[0:64, 0:64]),
                               deps=[sa])
                sb_ = seq.step("tensor", lambda eng: nc.tensor.matmul(
                    pU[0:64, 64:128], Zs[0:64, 0:64], xa_sb[0:64, 0:64], start=True, stop=True),
                    deps=[sae, zj])
                sbe = seq.step("vector", lambda eng: eng.tensor_scalar(
                    Tpt[0:64, 64:128], pU[0:64, 64:128], -1.0, None, ALU.mult), deps=[sb_, tpe])
                p3 = seq.step("tensor", lambda eng, ksl=ksl, skslab=skslab: nc.tensor.matmul(
                    pP[:], ksl, Skb[:, skslab], start=True, stop=True),
                    deps=[bload, hold.get("skb"), prev.get("pP")])
                bg = seq.step("vector", lambda eng, h=h, bcol=bcol: eng.tensor_tensor(
                    bgc[:], bcol, gcol8[:, h:h + 1], ALU.mult), deps=[gce, prev.get("bgc")])
                r1 = seq.step("vector", lambda eng: eng.tensor_scalar(t7[:], pP[:], bgc[:], None, ALU.mult),
                              deps=[p3, bg, prev.get("t7")])
                r2 = seq.step("vector", lambda eng, vsl=vsl, bcol=bcol: eng.scalar_tensor_tensor(
                    RhsS[:], vsl, bcol, t7[:], ALU.mult, ALU.subtract), deps=[r1, btr, prev.get("Rhs")])
                um = seq.step("tensor", lambda eng: nc.tensor.matmul(
                    pU[:, 0:256], Tpt[:], RhsS[:], start=True, stop=True), deps=[sbe, r2, sbe])
                ue = seq.step("vector", lambda eng: eng.tensor_copy(Ub[:], pU[:, 0:256]),
                              deps=[um, prev.get("Ub")])
                om = seq.step("tensor", lambda eng: nc.tensor.matmul(
                    pP[:], MpB[:], Ub[:], start=True, stop=True), deps=[ue, mp2, r1])
                p2 = seq.step("tensor", lambda eng, qsl=qsl, skslab=skslab: nc.tensor.matmul(
                    pM[:] if False else pU[:, 0:256], qsl, Skb[:, skslab], start=True, stop=True),
                    deps=[bload, hold.get("skb"), ue])
                oa1 = seq.step("vector", lambda eng, h=h: eng.tensor_scalar(
                    t7[:], pU[:, 0:256], gcol8[:, h:h + 1], None, ALU.mult), deps=[p2, r2])
                oa2 = seq.step("vector", lambda eng: eng.tensor_tensor(Osb[:], t7[:], pP[:], ALU.add),
                               deps=[oa1, om, prev.get("Osb")])
                prev["pP"] = oa2
                prev["pU"] = oa1
                dec = seq.step("vector", lambda eng, ktsl=ktsl: eng.tensor_scalar(
                    ktsc[:], ktsl, Ep[:, 127:128], None, ALU.mult), deps=[epx, btr, prev.get("ktsc")])
                sup = seq.step("tensor", lambda eng: nc.tensor.matmul(
                    pP[:], ktsc[:], Ub[:], start=True, stop=True), deps=[dec, ue, oa2])
                skbu = seq.step("vector", lambda eng, h=h, skslab=skslab: (
                    b.fence(eng, "v", eng.scalar_tensor_tensor(
                        Sk[:, skslab], Sk[:, skslab], gCcol[:, h:h + 1],
                        pP[:], ALU.mult, ALU.add)),
                    eng.tensor_copy(Skb[:, skslab], Sk[:, skslab]))[-1], deps=[sup, gcv])
                hold["skb"] = skbu
                prev["pP" ] = skbu
                rm1 = seq.step("scalar", lambda eng: eng.activation(sqO[:], Osb[:], AF.Square),
                               deps=[oa2, prev.get("sqO")])
                rm2 = seq.step("vector", lambda eng: (
                    b.fence(eng, "v", eng.reduce_sum(rmsc[:, 0:1], sqO[:], axis=AX.X)),
                    eng.tensor_scalar(rmsc[:, 0:1], rmsc[:, 0:1], 1.0 / 256, EPSR, ALU.mult, ALU.add))[-1],
                    deps=[rm1])
                rm3 = seq.step("scalar", lambda eng: eng.activation(rmsc[:, 1:2], rmsc[:, 0:1], AF.Sqrt),
                               deps=[rm2])
                rm4 = seq.step("vector", lambda eng: eng.reciprocal(rmsc[:, 0:1], rmsc[:, 1:2]), deps=[rm3])
                gat = seq.step("vector", lambda eng, gssl=gssl: (
                    b.fence(eng, "v", eng.tensor_scalar(t7[:], Osb[:], rmsc[:, 0:1], None, ALU.mult)),
                    eng.tensor_tensor(OgB[:], t7[:], gssl, ALU.mult))[-1],
                    deps=[rm4, btr, prev.get("OgB")])
                prev["t7"] = gat
                if L1A_SPILL:
                    idx = (qi * NTQ + cc) * 8 + h
                    sp0 = seq.dma("gpsimd_dma", lambda eng, idx=idx: [
                        eng.dma_start(dbg_d[idx, 0], Ub[:])], deps=[ue, om], n=1)
                    sp1 = seq.dma("gpsimd_dma", lambda eng, idx=idx: [
                        eng.dma_start(dbg_d[idx, 1], Osb[:])], deps=[oa2, gat], n=1)
                    sp2 = seq.dma("gpsimd_dma", lambda eng, idx=idx: [
                        eng.dma_start(dbg_d[idx, 2][:, 0:128], LamS[:])], deps=[lam2, sb_], n=1)
                    sp3 = seq.dma("gpsimd_dma", lambda eng, idx=idx: [
                        eng.dma_start(dbg_d[idx, 3], RhsS[:])], deps=[r2, um], n=1)
                    prev["_sp"] = (sp0, sp1, sp2, sp3)
                ogst = seq.dma("gpsimd_dma", lambda eng, t0=t0, cc=cc, h=h: [
                    eng.dma_start(og_d[t0 + 128 * cc: t0 + 128 * (cc + 1), 256 * h:256 * (h + 1)], OgB[:])],
                    deps=[gat], n=1)
                og_last.append(ogst)
                prev["OgB"] = ogst
                prev["pM"] = lam1
                prev["pQ2"] = mp1
                prev["pJ"] = ep2 if False else jev
                prev["pJ2"] = he
                prev["tE1"] = lam2
                prev["tE2"] = mp2
                prev["Ep"] = dec
                prev["En"] = lam1
                prev["LamS"] = sb_ if not L1A_SPILL else prev["_sp"][2]
                prev["MpB"] = om
                prev["Gp"] = sq1
                prev["Hp"] = sq2 if False else zm
                prev["Zs"] = sb_
                prev["Tpt"] = um
                prev["bgc"] = r1
                prev["Rhs"] = um if not L1A_SPILL else prev["_sp"][3]
                prev["Ub"] = sup if not L1A_SPILL else prev["_sp"][0]
                prev["Osb"] = gat if not L1A_SPILL else prev["_sp"][1]
                prev["sqO"] = rm2
                prev["ktsc"] = sup
                prev["pZ"] = tpe
                prev["Hp"] = he
                prev["Gp"] = ge
                prev["pJ"] = jev
                prev["Jsb"] = en2
        prev["q_done"] = og_last[-1]
        prev["tok_done"] = og_last[-1]
        prev["lastA2"] = og_last[-1]
    seq.barrier_all()
    ctxB.close()

    ctxC = contextlib.ExitStack()
    woS = b.sbuf("a1_woS", [128, 16 * 1024], BF16, ctxC)
    wof = b.sbuf("a1_wof", [128, 1024], F32, ctxC)
    onwc = b.sbuf("a1_onw", [128, 2], F32, ctxC)
    ogf = b.sbuf("a1_ogf", [128, 16 * 1024], BF16, ctxC)
    s1sb = b.sbuf("a1_s1sb", [128, 1024], F32, ctxC)
    psQ3 = b.psum2("a1_psQ3", [128, 512], F32, ctxC)
    psW3 = b.psum2("a1_psW3", [128, 512], F32, ctxC)

    onl = seq.dma("sync", lambda eng: [eng.dma_start(onwc[:], W["l1_onw"])], n=1)
    prev_sc = None
    for kc in range(16):
        wld2 = seq.dma("sync", lambda eng, kc=kc: [
            eng.dma_start(wof[:], W["l1_wo"][128 * kc:128 * (kc + 1), :])], deps=[prev_sc], n=1)
        sc = seq.step("vector", lambda eng, kc=kc: eng.tensor_scalar(
            woS[:, 1024 * kc:1024 * (kc + 1)], wof[:], onwc[:, kc % 2:kc % 2 + 1], None, ALU.mult),
            deps=[wld2, onl])
        prev_sc = sc
    wo_ready = prev_sc

    prev_pj = None
    prev_ev = None
    for qi in (range(L1A_NQ) if L1A_DBG is None else []):
        t0 = qi * QT
        ogl = seq.dma("sync", lambda eng, t0=t0: [
            eng.dma_start(ogf[:, kc * 1024:(kc + 1) * 1024],
                          og_d[t0:t0 + QT, 128 * kc:128 * (kc + 1)], transpose=True)
            for kc in range(16)], deps=[prev_pj], n=16)
        for i in range(NTQ):
            def womm(eng, i=i):
                for nh in range(2):
                    ps = psQ3 if nh == 0 else psW3
                    for kc in range(16):
                        inst = nc.tensor.matmul(ps[:, 0:512],
                                                ogf[:, kc * QT + 128 * i: kc * QT + 128 * (i + 1)],
                                                woS[:, kc * 1024 + 512 * nh: kc * 1024 + 512 * (nh + 1)],
                                                start=(kc == 0), stop=(kc == 15))
                return inst
            pj = seq.step("tensor", womm, deps=[ogl, wo_ready, prev_ev])
            ev = seq.step("vector", lambda eng: (
                eng.tensor_copy(s1sb[:, 0:512], psQ3[:, 0:512]),
                eng.tensor_copy(s1sb[:, 512:1024], psW3[:, 0:512]))[-1], deps=[pj])
            st = seq.dma("gpsimd_dma", lambda eng, t0=t0, i=i: [
                eng.dma_start(s1_dst[t0 + 128 * i: t0 + 128 * (i + 1), :], s1sb[:])], deps=[ev], n=1)
            prev_ev = st
            prev_pj = pj
    seq.barrier_all()
    ctxC.close()




def emit_l1ffn(b: Bld, xn1_bf, silg_d):
    nc = b.nc
    W = b.dram
    QT = 512
    NQ = F_NT
    knd = "ExternalOutput" if F_DBG else "Internal"
    if F_DBG:
        odbg_d = nc.dram_tensor("f_odbg", [512, T], BF16, kind=knd).ap()
        ddbg_d = nc.dram_tensor("f_ddbg", [8, T], F32, kind=knd).ap()
        pdbg_d = nc.dram_tensor("f_pdbg", [8 * 128, 512], BF16, kind=knd).ap()
        vdbg_d = nc.dram_tensor("f_vdbg", [128, 8 * 128], BF16, kind=knd).ap()
        podbg_d = nc.dram_tensor("f_podbg", [128, 512], F32, kind=knd).ap()
    qr_d = nc.dram_tensor("f_qr", [512, T], BF16, kind=knd).ap()
    kr_d = nc.dram_tensor("f_kr", [512, T], BF16, kind=knd).ap()
    v_d = nc.dram_tensor("f_vd", [512, T], BF16, kind=knd).ap()
    xatt_d = nc.dram_tensor("f_xatt", [1024, T], F32, kind=knd).ap()
    x2f_d = nc.dram_tensor("f_x2f", [1024, T], F32, kind=knd).ap()
    x2bf_d = nc.dram_tensor("f_x2bf", [1024, T], BF16, kind=knd).ap()

    gctx = ExitStack()
    onesc = b.sbuf("ff_onesc", [128, 1], F32, ctx=gctx)
    onesr = b.sbuf("ff_onesr", [1, 128], F32, ctx=gctx)

    for l in range(F_NL):
        ctxA = ExitStack()
        wq_sb = b.sbuf(f"ffa{l}_wqkv", [128, 8 * 1536], BF16, ctx=ctxA)
        wq2_sb = b.sbuf(f"ffa{l}_wqkv2", [128, 8 * 1024], BF16, ctx=ctxA)
        qscol = b.sbuf(f"ffa{l}_qs", [128, 4], F32, ctx=ctxA)
        tbl = [b.sbuf(f"ffa{l}_tbl{i}", [128, 4 * 512], F32, ctx=ctxA) for i in range(2)]
        xcur = [b.sbuf(f"ffa{l}_x{i}", [128, 8 * 512], BF16, ctx=ctxA) for i in range(2)]
        xsh = [b.sbuf(f"ffa{l}_xs{i}", [128, 4 * 512], BF16, ctx=ctxA) for i in range(2)]
        sqsb = [b.sbuf(f"ffa{l}_sq{i}", [128, 512], F32, ctx=ctxA) for i in range(2)]
        aob = [b.sbuf(f"ffa{l}_ob{i}", [128, 512], BF16, ctx=ctxA) for i in range(2)]
        tw = [b.sbuf(f"ffa{l}_t{i}", [128, 512], F32, ctx=ctxA) for i in range(4)]
        rsA = b.sbuf(f"ffa{l}_rA", [1, 512], F32, ctx=ctxA)
        rsB = b.sbuf(f"ffa{l}_rB", [1, 512], F32, ctx=ctxA)
        rsA2 = b.sbuf(f"ffa{l}_rA2", [1, 512], F32, ctx=ctxA)
        rsB2 = b.sbuf(f"ffa{l}_rB2", [1, 512], F32, ctx=ctxA)
        rsA3 = b.sbuf(f"ffa{l}_rA3", [1, 512], F32, ctx=ctxA)
        rsB3 = b.sbuf(f"ffa{l}_rB3", [1, 512], F32, ctx=ctxA)
        bbS = b.sbuf(f"ffa{l}_bbS", [128, 512], F32, ctx=ctxA)
        psP = [b.psum2(f"ffa{l}_pP{i}", [128, 512], ctx=ctxA) for i in range(2)]
        psP2 = [b.psum2(f"ffa{l}_pQ{i}", [128, 512], ctx=ctxA) for i in range(2)]
        psN = b.psum2(f"ffa{l}_pN", [1, 512], ctx=ctxA)
        psN2 = b.psum2(f"ffa{l}_pN2", [1, 512], ctx=ctxA)
        psBB = b.psum2(f"ffa{l}_pB", [128, 512], ctx=ctxA)

        seq = Seq(b)
        ini = None
        if l == 0:
            ini = seq.step("vector", lambda eng: (
                eng.memset(onesc[:], 1.0),
                eng.memset(onesr[:], 1.0))[-1])
        wld = seq.dma("sync", lambda eng, l=l: [
            eng.dma_start(wq_sb[:], W["f_wqkv"][l].rearrange("(kc p) f -> p kc f", p=128)),
            eng.dma_start(wq2_sb[:], W["f_wqkv2"][l].rearrange("(kc p) f -> p kc f", p=128))], n=2)
        qsld = seq.dma("sync", lambda eng, l=l: [
            eng.dma_start(qscol[:, 0:1], W["f_qs"][l]),
            eng.dma_start(qscol[:, 1:2], W["f_qs_sw"][l]),
            eng.dma_start(qscol[:, 2:3], W["f_ks"][l]),
            eng.dma_start(qscol[:, 3:4], W["f_ks_sw"][l])], n=4)

        pr_rhs = [None, None]
        pr_sh = None
        pr_tblread = [None, None]
        pr_ppc = [(), ()]
        pr_pp2c = [(), ()]
        pr_sqc = [(), ()]
        pr_rw = None
        pr_rv = None
        pr_bc = None
        pr_rotbb = None
        pr_bbread = None
        pr_aost = [None, None]
        gp = gp2 = gsq = gao = 0

        for t in range(NQ):
            t0 = QT * t
            tb = t % 2
            if l == 0:
                xld = seq.dma("sync", lambda eng, tb=tb, t0=t0: [
                    eng.dma_start(xcur[tb][:, 512 * c:512 * (c + 1)],
                                  xn1_bf[t0:t0 + QT, 128 * c:128 * (c + 1)], transpose=True)
                    for c in range(8)], deps=[pr_rhs[tb], pr_sh], n=8)

                def shmk(eng, tb=tb, pb=(t - 1) % 2, first=(t == 0)):
                    ii = None
                    for ci in range(4):
                        c = ci + 4
                        eng.tensor_copy(xsh[tb][:, 512 * ci + 1:512 * ci + 512],
                                        xcur[tb][:, 512 * c:512 * c + 511])
                        if first:
                            ii = eng.memset(xsh[tb][:, 512 * ci:512 * ci + 1], 0.0)
                        else:
                            ii = eng.tensor_copy(xsh[tb][:, 512 * ci:512 * ci + 1],
                                                 xcur[pb][:, 512 * c + 511:512 * c + 512])
                    return ii
                sh = seq.step("vector", shmk, deps=[xld, pr_rhs[tb]])
                pr_sh = sh
            else:
                sh = None
                if t == 0:
                    z0 = seq.step("vector", lambda eng, tb=tb: [
                        eng.memset(xcur[tb][:, 512 * c:512 * c + 1], 0.0) for c in range(4, 8)][-1],
                        deps=[pr_rhs[tb]])

                    def x0ld(eng, tb=tb):
                        out = []
                        for c in range(4):
                            out.append(eng.dma_start(xcur[tb][:, 512 * c:512 * (c + 1)],
                                                     x2bf_d[128 * c:128 * (c + 1), 0:QT]))
                        for c in range(4, 8):
                            out.append(eng.dma_start(xcur[tb][:, 512 * c + 1:512 * (c + 1)],
                                                     x2bf_d[128 * c:128 * (c + 1), 0:QT - 1]))
                        return out
                    xld = seq.dma("sync", x0ld, deps=[pr_rhs[tb], z0], n=8)
                else:
                    def xnld(eng, tb=tb, t0=t0):
                        out = []
                        for c in range(4):
                            out.append(eng.dma_start(xcur[tb][:, 512 * c:512 * (c + 1)],
                                                     x2bf_d[128 * c:128 * (c + 1), t0:t0 + QT]))
                        for c in range(4, 8):
                            out.append(eng.dma_start(xcur[tb][:, 512 * c:512 * (c + 1)],
                                                     x2bf_d[128 * c:128 * (c + 1), t0 - 1:t0 + QT - 1]))
                        return out
                    xld = seq.dma("sync", xnld, deps=[pr_rhs[tb]], n=8)
            tld = seq.dma("sync", lambda eng, tb=tb, t0=t0: [
                eng.dma_start(tbl[tb][:, 0:512], W["rot_csq"][:, t0:t0 + QT]),
                eng.dma_start(tbl[tb][:, 512:1024], W["rot_ssq"][:, t0:t0 + QT]),
                eng.dma_start(tbl[tb][:, 1024:1536], W["rot_csk"][:, t0:t0 + QT]),
                eng.dma_start(tbl[tb][:, 1536:2048], W["rot_ssk"][:, t0:t0 + QT])],
                deps=[pr_tblread[tb]], n=4)

            def rhs_c(kc, tb=tb):
                if l == 0 and kc >= 4:
                    return xsh[tb][:, 512 * (kc - 4):512 * (kc - 3)]
                return xcur[tb][:, 512 * kc:512 * (kc + 1)]

            last_mm = None
            last_rot = None
            for kind in range(3):
                for oc in range(4):
                    pp = gp % 2; gp += 1
                    wcol = 512 * kind + 128 * oc

                    def mmR(eng, pp=pp, wcol=wcol, tb=tb):
                        ii = None
                        for kc in range(8):
                            ii = nc.tensor.matmul(psP[pp][:], wq_sb[:, 1536 * kc + wcol:1536 * kc + wcol + 128],
                                                  rhs_c(kc, tb), start=(kc == 0), stop=(kc == 7))
                        return ii
                    mm_r = seq.step("tensor", mmR, deps=[xld, sh, wld] + list(pr_ppc[pp]))
                    last_mm = mm_r
                    if kind == 2:
                        ao = gao % 2; gao += 1
                        cpv = seq.step("scalar", lambda eng, ao=ao, pp=pp: eng.activation(
                            aob[ao][:], psP[pp][:], AF.Copy), deps=[mm_r, pr_aost[ao]])
                        sto = seq.dma("gpsimd_dma", lambda eng, ao=ao, oc=oc, t0=t0: eng.dma_start(
                            v_d[128 * oc:128 * (oc + 1), t0:t0 + QT], aob[ao][:]), deps=[cpv], n=1)
                        pr_ppc[pp] = (cpv,)
                        pr_aost[ao] = sto
                        continue
                    pp2 = gp2 % 2; gp2 += 1
                    w2col = 512 * kind + 128 * oc

                    def mmS(eng, pp2=pp2, w2col=w2col, tb=tb):
                        ii = None
                        for kc in range(8):
                            ii = nc.tensor.matmul(psP2[pp2][:], wq2_sb[:, 1024 * kc + w2col:1024 * kc + w2col + 128],
                                                  rhs_c(kc, tb), start=(kc == 0), stop=(kc == 7))
                        return ii
                    mm_s = seq.step("tensor", mmS, deps=[xld, sh, wld] + list(pr_pp2c[pp2]))
                    sb2 = gsq % 2; gsq += 1
                    sqs = seq.step("scalar", lambda eng, sb2=sb2, pp=pp: eng.activation(
                        sqsb[sb2][:], psP[pp][:], AF.Square), deps=[mm_r] + list(pr_sqc[sb2]))
                    nm1 = seq.step("tensor", lambda eng, sb2=sb2: nc.tensor.matmul(
                        psN[0:1, :], onesc[0:64, :], sqsb[sb2][0:64, :], start=True, stop=True),
                        deps=[sqs, pr_rw])
                    nm2 = seq.step("tensor", lambda eng, sb2=sb2: nc.tensor.matmul(
                        psN2[0:1, :], onesc[64:128, :], sqsb[sb2][64:128, :], start=True, stop=True),
                        deps=[sqs, pr_rw])
                    rw = seq.step("scalar", lambda eng: (
                        b.fence(eng, "s", eng.activation(rsA[:], psN[0:1, :], AF.Ln)),
                        b.fence(eng, "s", eng.activation(rsB[:], psN2[0:1, :], AF.Ln)),
                        eng.activation(rsA3[:], rsA[:], AF.Exp, scale=-0.5),
                        eng.activation(rsB3[:], rsB[:], AF.Exp, scale=-0.5))[-1],
                        deps=[nm1, nm2, pr_rv, pr_rw])
                    rv = seq.step("vector", lambda eng: (
                        eng.tensor_scalar(rsA2[:], rsA3[:], 1e12, None, ALU.min),
                        eng.tensor_scalar(rsB2[:], rsB3[:], 1e12, None, ALU.min))[-1], deps=[rw, pr_bc])
                    bc = seq.step("tensor", lambda eng: (
                        nc.tensor.matmul(psBB[0:64, :], onesr[0:1, 0:64], rsA2[:], start=True, stop=True),
                        nc.tensor.matmul(psBB[64:128, :], onesr[0:1, 0:64], rsB2[:], start=True, stop=True))[-1],
                        deps=[rv, pr_rotbb])
                    bbe = seq.step("scalar", lambda eng: eng.activation(
                        bbS[:], psBB[:], AF.Copy), deps=[bc, pr_bbread])
                    ao = gao % 2; gao += 1
                    scol = 2 * kind
                    toff = 1024 * kind

                    def rotk(eng, pp=pp, pp2=pp2, ao=ao, tb=tb, scol=scol, toff=toff):
                        b.fence(eng, "v", eng.scalar_tensor_tensor(
                            tw[0][:], psP[pp][:], qscol[:, scol:scol + 1], bbS[:], ALU.mult, ALU.mult))
                        b.fence(eng, "v", eng.scalar_tensor_tensor(
                            tw[1][:], psP2[pp2][:], qscol[:, scol + 1:scol + 2], bbS[:], ALU.mult, ALU.mult))
                        b.fence(eng, "v", eng.tensor_tensor(
                            tw[2][:], tw[0][:], tbl[tb][:, toff:toff + 512], ALU.mult))
                        b.fence(eng, "v", eng.tensor_tensor(
                            tw[3][:], tw[1][:], tbl[tb][:, toff + 512:toff + 1024], ALU.mult))
                        return eng.tensor_tensor(aob[ao][:], tw[2][:], tw[3][:], ALU.add)
                    rot = seq.step("vector", rotk, deps=[bbe, mm_r, mm_s, tld, qsld, ini, pr_aost[ao]])
                    last_rot = rot
                    dst = qr_d if kind == 0 else kr_d
                    sto = seq.dma("gpsimd_dma", lambda eng, ao=ao, oc=oc, t0=t0, dst=dst: eng.dma_start(
                        dst[128 * oc:128 * (oc + 1), t0:t0 + QT], aob[ao][:]), deps=[rot], n=1)
                    pr_ppc[pp] = (sqs, rot)
                    pr_pp2c[pp2] = (rot,)
                    pr_sqc[sb2] = (nm1, nm2)
                    pr_rw = rw
                    pr_rv = rv
                    pr_bc = bc
                    pr_rotbb = bbe
                    pr_bbread = rot
                    pr_aost[ao] = sto
            pr_rhs[tb] = last_mm
            pr_tblread[tb] = last_rot
        seq.barrier_all()
        ctxA.close()

        ctxB = ExitStack()
        maskb = b.sbuf(f"ffb{l}_mask", [128, 8 * 512], BF16, ctx=ctxB)
        wo_sb = b.sbuf(f"ffb{l}_wo", [128, 4 * 1024], BF16, ctx=ctxB)
        qt = [b.sbuf(f"ffb{l}_q{i}", [64, 512], BF16, ctx=ctxB) for i in range(2)]
        kt = [b.sbuf(f"ffb{l}_k{i}", [64, 1024], BF16, ctx=ctxB) for i in range(2)]
        vpt = [b.sbuf(f"ffb{l}_v{i}", [128, 8 * 128], BF16, ctx=ctxB) for i in range(2)]
        onescb = b.sbuf(f"ffb{l}_ones", [128, 1], BF16, ctx=ctxB)
        clsb = [b.sbuf(f"ffb{l}_cl{i}", [128, 512], F32, ctx=ctxB) for i in range(4)]
        ebf = [b.sbuf(f"ffb{l}_e{i}", [128, 512], BF16, ctx=ctxB) for i in range(4)]
        psb = [b.sbuf(f"ffb{l}_p{i}", [128, 512], BF16, ctx=ctxB) for i in range(4)]
        rc = [b.sbuf(f"ffb{l}_rcv{i}", [1, 512], F32, ctx=ctxB) for i in range(2)]
        rc0 = b.sbuf(f"ffb{l}_rc0", [1, 512], F32, ctx=ctxB)
        rbS = b.sbuf(f"ffb{l}_rb", [64, 512], F32, ctx=ctxB)
        if F_DBG:
            podbg_sb = b.sbuf(f"ffb{l}_podbg", [128, 512], F32, ctx=ctxB)
        o_f = [b.sbuf(f"ffb{l}_of{i}", [128, 4 * 512], BF16, ctx=ctxB) for i in range(2)]
        if l == 0:
            xres = [b.sbuf(f"ffb{l}_xr{i}", [128, 8 * 512], BF16, ctx=ctxB) for i in range(2)]
        else:
            xres = [b.sbuf(f"ffb{l}_xr{i}", [128, 8 * 512], F32, ctx=ctxB) for i in range(2)]
        xo_sb = [b.sbuf(f"ffb{l}_xo{i}", [128, 512], F32, ctx=ctxB) for i in range(2)]
        psS = [b.psum2(f"ffb{l}_pS{i}", [128, 512], ctx=ctxB) for i in range(2)]
        psO = [b.psum2(f"ffb{l}_pO{i}", [128, 512], ctx=ctxB) for i in range(2)]
        psD = b.psum2(f"ffb{l}_pD", [1, 512], ctx=ctxB)
        psB = b.psum2(f"ffb{l}_pBc", [128, 512], ctx=ctxB)
        psX = [b.psum2(f"ffb{l}_pX{i}", [128, 512], ctx=ctxB) for i in range(2)]

        seq = Seq(b)
        vini = seq.step("vector", lambda eng: eng.memset(onescb[:], 1.0))
        mld = seq.dma("sync", lambda eng: eng.dma_start(maskb[:], W["fmask"]), n=1)
        wold = seq.dma("sync", lambda eng, l=l: eng.dma_start(
            wo_sb[:], W["f_wo"][l].rearrange("(kc p) f -> p kc f", p=128)), n=1)

        pr_sim_hb = [None, None]
        pr_vpt = [None, None]
        pr_clamp = [None, None]
        pr_exp = [None] * 4
        pr_mask = [None] * 4
        pr_mm2b = [None] * 4
        pr_divO = [None, None]
        pr_bcmO = [None, None]
        pr_bcm = None
        pr_bce = None
        pr_rbread = None
        pr_dnst = None
        pr_rec = None
        pr_pdbg = [None, None]
        pr_odbg = [None, None]
        pr_ofuse = [None, None]
        pr_resread = [None, None]
        pr_resp = [None, None]
        pr_stoxo = [None, None]
        ga = 0
        gam = 0
        gfo = 0
        for w in range(NQ):
            q0 = QT * w
            na = 4 if w == 0 else 8
            kw0 = max(0, q0 - 512)
            wb = w % 2
            last_div = None
            pend = None

            def emit_tail(ob, h, mm2, rec, pb, cur_vld):
                nonlocal pr_bcm, pr_bce, pr_rbread, last_div
                bcm = seq.step("tensor", lambda eng: nc.tensor.matmul(
                    psB[0:64, :], onesr[0:1, 0:64], rc[ob][:], start=True, stop=True),
                    deps=[rec, pr_bce])
                bce = seq.step("scalar", lambda eng: eng.activation(
                    rbS[:], psB[0:64, :], AF.Copy), deps=[bcm, pr_rbread])
                div = seq.step("vector", lambda eng, h=h, wb=wb, ob=ob: eng.tensor_tensor(
                    o_f[wb][64 * (h % 2):64 * (h % 2) + 64, 512 * (h // 2):512 * (h // 2 + 1)],
                    psO[ob][0:64, :], rbS[:], ALU.mult), deps=[bce, mm2, pr_ofuse[wb], pr_odbg[wb]])
                pr_divO[ob] = div
                pr_bcmO[ob] = bcm
                pr_bcm = bcm
                pr_bce = bce
                pr_rbread = div
                last_div = div

            for h in range(8):
                hb = (w * 8 + h) % 2
                gpr = h // 2
                pb = (w * 4 + gpr) % 2
                ob = h % 2
                qld = seq.dma("gpsimd_dma", lambda eng, hb=hb, h=h, q0=q0: eng.dma_start(
                    qt[hb][:], qr_d[64 * h:64 * (h + 1), q0:q0 + QT]), deps=[pr_sim_hb[hb]], n=1)
                kld = seq.dma("gpsimd_dma", lambda eng, hb=hb, h=h, kw0=kw0, na=na: eng.dma_start(
                    kt[hb][:, 0:128 * na], kr_d[64 * h:64 * (h + 1), kw0:kw0 + 128 * na]),
                    deps=[pr_sim_hb[hb]], n=1)
                if h % 2 == 0:
                    cur_vld = seq.dma("sync", lambda eng, pb=pb, g=gpr, kw0=kw0, na=na: [
                        eng.dma_start(vpt[pb][:, 128 * a:128 * (a + 1)],
                                      v_d[128 * g:128 * (g + 1), kw0 + 128 * a:kw0 + 128 * (a + 1)],
                                      transpose=True) for a in range(na)],
                        deps=[pr_vpt[pb]], n=na)
                mm2 = None
                mmq = []

                def emit_mm2(a, am, msk, pb=pb, ob=ob, na=na, hoff=64 * (h % 2)):
                    def mm2k(eng, am=am, pb=pb, a=a, na=na, hoff=hoff, ob=ob):
                        nc.tensor.matmul(psO[ob][0:64, :], vpt[pb][:, 128 * a + hoff:128 * a + hoff + 64],
                                         psb[am][:], start=(a == 0), stop=(a == na - 1))
                        return nc.tensor.matmul(psD[0:1, :], onescb[:], psb[am][:],
                                                start=(a == 0), stop=(a == na - 1))
                    st = seq.step("tensor", mm2k,
                                  deps=[msk, cur_vld, vini]
                                  + ([pr_divO[ob], pr_rec] if a == 0 else []))
                    pr_mm2b[am] = st
                    return st

                for a in range(na):
                    ma = a + 4 if w == 0 else a
                    ab = ga % 2; ga += 1
                    am = gam % 4; gam += 1
                    sim = seq.step("tensor", lambda eng, ab=ab, hb=hb, a=a: nc.tensor.matmul(
                        psS[ab][:], kt[hb][:, 128 * a:128 * (a + 1)], qt[hb][:],
                        start=True, stop=True), deps=[qld, kld, pr_clamp[ab]])
                    if w == 0 or a >= 3:
                        clamp = seq.step("vector", lambda eng, ab=ab, am=am: eng.tensor_scalar(
                            clsb[am][:], psS[ab][:], 10.0, None, ALU.min), deps=[sim, pr_exp[am]])
                        exps = seq.step("scalar", lambda eng, am=am: eng.activation(
                            ebf[am][:], clsb[am][:], AF.Exp, scale=8.0), deps=[clamp, pr_mask[am]])
                    else:
                        exps = seq.step("scalar", lambda eng, ab=ab, am=am: eng.activation(
                            ebf[am][:], psS[ab][:], AF.Exp, scale=8.0), deps=[sim, pr_mask[am]])
                        clamp = exps
                    msk = seq.step("gpsimd", lambda eng, am=am, ma=ma: eng.tensor_tensor(
                        psb[am][:], ebf[am][:], maskb[:, 512 * ma:512 * (ma + 1)], ALU.mult),
                        deps=[exps, mld, pr_mm2b[am], pr_pdbg[am % 2]])
                    if F_DBG and l == 0 and w == 0 and h == 0:
                        pr_pdbg[am % 2] = seq.dma("gpsimd_dma", lambda eng, am=am, a=a: eng.dma_start(
                            pdbg_d[128 * a:128 * (a + 1), :], psb[am][:]), deps=[msk], n=1)
                    mmq.append((a, am, msk))
                    if len(mmq) > 3:
                        mm2 = emit_mm2(*mmq.pop(0))
                    pr_clamp[ab] = clamp
                    pr_exp[am] = exps
                    pr_mask[am] = msk
                while mmq:
                    mm2 = emit_mm2(*mmq.pop(0))
                pr_sim_hb[hb] = sim
                pr_vpt[pb] = mm2
                pr_po = None
                if F_DBG and l == 0 and w == 0 and h == 0:
                    poev = seq.step("scalar", lambda eng, ob=ob: eng.activation(
                        podbg_sb[:], psO[ob][:], AF.Copy), deps=[mm2])
                    pr_po = seq.dma("gpsimd_dma", lambda eng, pb=pb: [
                        eng.dma_start(vdbg_d, vpt[pb][:]),
                        eng.dma_start(podbg_d, podbg_sb[:])], deps=[poev, cur_vld], n=2)
                rec = seq.step("scalar", lambda eng, ob=ob: (
                    b.fence(eng, "s", eng.activation(rc0[:], psD[0:1, :], AF.Ln)),
                    eng.activation(rc[ob][:], rc0[:], AF.Exp, scale=-1.0))[-1],
                    deps=[mm2, pr_bcmO[ob], pr_dnst, pr_po, pr_rec])
                if F_DBG and l == 0:
                    pr_dnst = seq.dma("gpsimd_dma", lambda eng, h=h, q0=q0, ob=ob: eng.dma_start(
                        ddbg_d[h:h + 1, q0:q0 + QT], rc[ob][:]), deps=[rec], n=1)
                pr_rec = rec
                if pend is not None:
                    emit_tail(*pend)
                pend = (ob, h, mm2, rec, pb, cur_vld)
            emit_tail(*pend)
            pend = None
            if l == 0:
                xrl = seq.dma("sync", lambda eng, wb=wb, q0=q0: [
                    eng.dma_start(xres[wb][:, 512 * c:512 * (c + 1)],
                                  xn1_bf[q0:q0 + QT, 128 * c:128 * (c + 1)], transpose=True)
                    for c in range(8)], deps=[pr_resread[wb]], n=8)
            else:
                xrl = seq.dma("sync", lambda eng, wb=wb, q0=q0: [
                    eng.dma_start(xres[wb][:, 512 * c:512 * (c + 1)],
                                  x2f_d[128 * c:128 * (c + 1), q0:q0 + QT])
                    for c in range(8)], deps=[pr_resread[wb]], n=8)
            if F_DBG and l == 0:
                pr_odbg[wb] = seq.dma("gpsimd_dma", lambda eng, wb=wb, q0=q0: [
                    eng.dma_start(odbg_d[128 * ic:128 * (ic + 1), q0:q0 + QT],
                                  o_f[wb][:, 512 * ic:512 * (ic + 1)]) for ic in range(4)],
                    deps=[last_div], n=4)
            last_wom = None
            last_res = None
            for fo in range(8):
                fb = gfo % 2; gfo += 1

                def womk(eng, fb=fb, fo=fo, wb=wb):
                    ii = None
                    for ic in range(4):
                        ii = nc.tensor.matmul(psX[fb][:], wo_sb[:, 1024 * ic + 128 * fo:1024 * ic + 128 * fo + 128],
                                              o_f[wb][:, 512 * ic:512 * (ic + 1)],
                                              start=(ic == 0), stop=(ic == 3))
                    return ii
                wom = seq.step("tensor", womk, deps=[last_div, wold, pr_resp[fb]])
                res = seq.step("vector", lambda eng, fb=fb, fo=fo, wb=wb: eng.tensor_tensor(
                    xo_sb[fb][:], psX[fb][:], xres[wb][:, 512 * fo:512 * (fo + 1)], ALU.add),
                    deps=[wom, xrl, pr_stoxo[fb]])
                sto = seq.dma("gpsimd_dma", lambda eng, fb=fb, fo=fo, q0=q0: eng.dma_start(
                    xatt_d[128 * fo:128 * (fo + 1), q0:q0 + QT], xo_sb[fb][:]), deps=[res], n=1)
                pr_resp[fb] = res
                pr_stoxo[fb] = sto
                last_wom = wom
                last_res = res
            pr_ofuse[wb] = last_wom
            pr_resread[wb] = last_res
        seq.barrier_all()
        ctxB.close()

        ctxC = ExitStack()
        w1sb = b.sbuf(f"ffc{l}_w1", [128, 8 * 5632], BF16, ctx=ctxC)
        w2sb = b.sbuf(f"ffc{l}_w2", [128, 22 * 1024], BF16, ctx=ctxC)
        b1c = b.sbuf(f"ffc{l}_b1", [128, 44], F32, ctx=ctxC)
        b2c = b.sbuf(f"ffc{l}_b2", [128, 8], F32, ctx=ctxC)
        lgc = b.sbuf(f"ffc{l}_lg", [128, 8], F32, ctx=ctxC)
        lbc = b.sbuf(f"ffc{l}_lb", [128, 8], F32, ctx=ctxC)
        xa03 = b.sbuf(f"ffc{l}_x03", [128, 4 * 512], F32, ctx=ctxC)
        xa47 = b.sbuf(f"ffc{l}_x47", [128, 4 * 513], F32, ctx=ctxC)
        hn = b.sbuf(f"ffc{l}_hn", [128, 8 * 512], BF16, ctx=ctxC)
        msb = b.sbuf(f"ffc{l}_m", [128, 22 * 512], BF16, ctx=ctxC)
        sqc = [b.sbuf(f"ffc{l}_sq{i}", [128, 512], F32, ctx=ctxC) for i in range(2)]
        glb = [b.sbuf(f"ffc{l}_gl{i}", [128, 512], BF16, ctx=ctxC) for i in range(2)]
        x2sb = [b.sbuf(f"ffc{l}_x2{i}", [128, 512], F32, ctx=ctxC) for i in range(2)]
        obf = [b.sbuf(f"ffc{l}_ob{i}", [128, 512], BF16, ctx=ctxC) for i in range(2)]
        tc1 = b.sbuf(f"ffc{l}_tc1", [128, 512], F32, ctx=ctxC)
        tc2 = b.sbuf(f"ffc{l}_tc2", [128, 512], F32, ctx=ctxC)
        muR = b.sbuf(f"ffc{l}_mu", [1, 512], F32, ctx=ctxC)
        mu2R = b.sbuf(f"ffc{l}_mu2", [1, 512], F32, ctx=ctxC)
        vrR = b.sbuf(f"ffc{l}_vr", [1, 512], F32, ctx=ctxC)
        sdR = b.sbuf(f"ffc{l}_sd", [1, 512], F32, ctx=ctxC)
        rsR = b.sbuf(f"ffc{l}_rs", [1, 512], F32, ctx=ctxC)
        psA = [b.psum2(f"ffc{l}_pA{i}", [128, 512], ctx=ctxC) for i in range(2)]
        psG = [b.psum2(f"ffc{l}_pG{i}", [128, 512], ctx=ctxC) for i in range(2)]
        psY = [b.psum2(f"ffc{l}_pY{i}", [128, 512], ctx=ctxC) for i in range(2)]
        psMuT = b.psum2(f"ffc{l}_pM", [128, 512], ctx=ctxC)
        psSqT = b.psum2(f"ffc{l}_pQ", [128, 512], ctx=ctxC)

        seq = Seq(b)
        wld1 = seq.dma("sync", lambda eng, l=l: eng.dma_start(
            w1sb[:], W["f_w1"][l].rearrange("(kc p) f -> p kc f", p=128)), n=1)
        wld2 = seq.dma("sync", lambda eng, l=l: eng.dma_start(
            w2sb[:], W["f_w2"][l].rearrange("(kc p) f -> p kc f", p=128)), n=1)
        bld = seq.dma("sync", lambda eng, l=l: [
            eng.dma_start(b1c[:], W["f_b1"][l]),
            eng.dma_start(b2c[:], W["f_b2"][l])], n=2)
        lld = seq.dma("sync", lambda eng, l=l: [
            eng.dma_start(lgc[:], W["f_lng"][l]),
            eng.dma_start(lbc[:], W["f_lnb"][l])], n=2)

        pr_res_last = None
        pr_gel_last = None
        pr_mmY_last = None
        pr_hn_last = None
        pr_st2c = [None, None]
        pr_mTTp = [None, None]
        pr_gelp = [None, None]
        pr_resy = [None, None]
        pr_sty = [None, None]
        pr_sty2 = [None, None]
        pr_bcrd = None
        pr_bc2rd = None

        def uview(c):
            if c < 4:
                return xa03[:, 512 * c:512 * (c + 1)]
            return xa47[:, 513 * (c - 4) + 1:513 * (c - 4) + 513]

        def sview(c):
            if c < 4:
                return xa03[:, 512 * c:512 * (c + 1)]
            return xa47[:, 513 * (c - 4):513 * (c - 4) + 512]

        for t in range(NQ):
            t0 = QT * t
            xdep = [pr_res_last, pr_gel_last, pr_mmY_last]
            xld1 = seq.dma("sync", lambda eng, t0=t0: [
                eng.dma_start(xa03[:, 512 * c:512 * (c + 1)],
                              xatt_d[128 * c:128 * (c + 1), t0:t0 + QT]) for c in range(4)],
                deps=xdep, n=4)
            if t == 0:
                zc = seq.step("vector", lambda eng: [
                    eng.memset(xa47[:, 513 * ci:513 * ci + 1], 0.0) for ci in range(4)][-1],
                    deps=xdep)
                xld2 = seq.dma("sync", lambda eng: [
                    eng.dma_start(xa47[:, 513 * ci + 1:513 * ci + 513],
                                  xatt_d[128 * (ci + 4):128 * (ci + 5), 0:QT]) for ci in range(4)],
                    deps=xdep + [zc], n=4)
            else:
                xld2 = seq.dma("sync", lambda eng, t0=t0: [
                    eng.dma_start(xa47[:, 513 * ci:513 * ci + 513],
                                  xatt_d[128 * (ci + 4):128 * (ci + 5), t0 - 1:t0 + QT]) for ci in range(4)],
                    deps=xdep, n=4)
            stm = st2 = None
            for c in range(8):
                stm = seq.step("tensor", lambda eng, c=c: nc.tensor.matmul(
                    psMuT[0:1, :], onesc[:], sview(c), start=(c == 0), stop=(c == 7)),
                    deps=[xld1, xld2] + ([pr_hn_last] if c == 0 else []))
                sqs = seq.step("scalar", lambda eng, c=c: eng.activation(
                    sqc[c % 2][:], sview(c), AF.Square), deps=[xld1, xld2, pr_st2c[c % 2]])
                st2 = seq.step("tensor", lambda eng, c=c: nc.tensor.matmul(
                    psSqT[0:1, :], onesc[:], sqc[c % 2][:], start=(c == 0), stop=(c == 7)),
                    deps=[sqs] + ([pr_hn_last] if c == 0 else []))
                pr_st2c[c % 2] = st2
            r1 = seq.step("vector", lambda eng: eng.tensor_scalar(
                muR[:], psMuT[0:1, :], 1.0 / 1024.0, None, ALU.mult), deps=[stm, pr_bcrd])
            r2 = seq.step("scalar", lambda eng: eng.activation(mu2R[:], muR[:], AF.Square), deps=[r1])
            r3 = seq.step("vector", lambda eng: (
                b.fence(eng, "v", eng.tensor_scalar(vrR[:], psSqT[0:1, :], 1.0 / 1024.0, None, ALU.mult)),
                eng.tensor_tensor(vrR[:], vrR[:], mu2R[:], ALU.subtract))[-1], deps=[st2, r2, pr_bc2rd])
            r4 = seq.step("scalar", lambda eng: (
                b.fence(eng, "s", eng.activation(sdR[:], vrR[:], AF.Ln, bias=1e-5)),
                eng.activation(rsR[:], sdR[:], AF.Exp, scale=-0.5))[-1], deps=[r3])
            bcm = seq.step("tensor", lambda eng: nc.tensor.matmul(
                psMuT[:], onesr[0:1, :], muR[:], start=True, stop=True), deps=[r1])
            bcs = seq.step("tensor", lambda eng: nc.tensor.matmul(
                psSqT[:], onesr[0:1, :], rsR[:], start=True, stop=True), deps=[r4])
            hns = None
            for c in range(8):
                hns = seq.step("vector", lambda eng, c=c: (
                    b.fence(eng, "v", eng.tensor_tensor(tc1[:], sview(c), psMuT[:], ALU.subtract)),
                    b.fence(eng, "v", eng.tensor_tensor(tc2[:], tc1[:], psSqT[:], ALU.mult)),
                    eng.scalar_tensor_tensor(hn[:, 512 * c:512 * (c + 1)], tc2[:], lgc[:, c:c + 1],
                                             lbc[:, c:c + 1].broadcast_to([128, 512]),
                                             ALU.mult, ALU.add))[-1],
                    deps=[bcm, bcs, lld, pr_hn_last])
            pr_bcrd = hns
            pr_bc2rd = hns
            last_mTT = None
            for pc in range(22):
                pb2 = pc % 2

                def mmAk(eng, pb2=pb2, pc=pc):
                    ii = None
                    for kc in range(8):
                        ii = nc.tensor.matmul(psA[pb2][:], w1sb[:, 5632 * kc + 128 * pc:5632 * kc + 128 * pc + 128],
                                              hn[:, 512 * kc:512 * (kc + 1)], start=(kc == 0), stop=(kc == 7))
                    return ii
                mmA_ = seq.step("tensor", mmAk, deps=[hns, wld1, pr_mTTp[pb2]])

                def mmGk(eng, pb2=pb2, pc=pc):
                    ii = None
                    for kc in range(8):
                        ii = nc.tensor.matmul(psG[pb2][:],
                                              w1sb[:, 5632 * kc + 2816 + 128 * pc:5632 * kc + 2816 + 128 * pc + 128],
                                              hn[:, 512 * kc:512 * (kc + 1)], start=(kc == 0), stop=(kc == 7))
                    return ii
                mmG_ = seq.step("tensor", mmGk, deps=[hns, wld1, pr_gelp[pb2]])
                gel = seq.step("scalar", lambda eng, pb2=pb2, pc=pc: eng.activation(
                    glb[pb2][:], psG[pb2][:], AF.Gelu, bias=b1c[:, 22 + pc:23 + pc]),
                    deps=[mmG_, bld, pr_mTTp[pb2]])
                mTT = seq.step("vector", lambda eng, pb2=pb2, pc=pc: eng.scalar_tensor_tensor(
                    msb[:, 512 * pc:512 * (pc + 1)], psA[pb2][:], b1c[:, pc:pc + 1], glb[pb2][:],
                    ALU.add, ALU.mult), deps=[gel, mmA_, bld, pr_mmY_last])
                pr_mTTp[pb2] = mTT
                pr_gelp[pb2] = gel
                last_mTT = mTT
            for yc in range(8):
                yb = yc % 2

                def mmYk(eng, yb=yb, yc=yc):
                    ii = None
                    for pc in range(22):
                        ii = nc.tensor.matmul(psY[yb][:], w2sb[:, 1024 * pc + 128 * yc:1024 * pc + 128 * yc + 128],
                                              msb[:, 512 * pc:512 * (pc + 1)], start=(pc == 0), stop=(pc == 21))
                    return ii
                mmY_ = seq.step("tensor", mmYk, deps=[last_mTT, wld2, pr_resy[yb]])
                res = seq.step("vector", lambda eng, yb=yb, yc=yc: eng.scalar_tensor_tensor(
                    x2sb[yb][:], psY[yb][:], b2c[:, yc:yc + 1], uview(yc), ALU.add, ALU.add),
                    deps=[mmY_, bld, pr_sty[yb]])
                if l == 0:
                    cpb = seq.step("scalar", lambda eng, yb=yb: eng.activation(
                        obf[yb][:], x2sb[yb][:], AF.Copy), deps=[res, pr_sty2[yb]])
                    sto = seq.dma("gpsimd_dma", lambda eng, yb=yb, yc=yc, t0=t0: [
                        eng.dma_start(x2f_d[128 * yc:128 * (yc + 1), t0:t0 + QT], x2sb[yb][:]),
                        eng.dma_start(x2bf_d[128 * yc:128 * (yc + 1), t0:t0 + QT], obf[yb][:])],
                        deps=[res, cpb], n=2)
                    pr_sty2[yb] = sto
                else:
                    cpb = seq.step("scalar", lambda eng, yb=yb: eng.activation(
                        obf[yb][:], x2sb[yb][:], AF.Silu), deps=[res, pr_sty2[yb]])
                    sto = seq.dma("gpsimd_dma", lambda eng, yb=yb, yc=yc, t0=t0: eng.dma_start(
                        silg_d[128 * yc:128 * (yc + 1), t0:t0 + QT], obf[yb][:]), deps=[cpb], n=1)
                    pr_sty2[yb] = sto
                pr_sty[yb] = sto
                pr_resy[yb] = res
                pr_res_last = res
                pr_mmY_last = mmY_
            pr_gel_last = gel
            pr_hn_last = res
        seq.barrier_all()
        ctxC.close()
    gctx.close()


def emit_final_combine(b: Bld, x1, s1, silg, out):
    ctx = ExitStack()
    sgt = [b.sbuf(f"fc_sg{i}", [128, 1024], BF16, ctx=ctx) for i in range(2)]
    x1t = [b.sbuf(f"fc_x1{i}", [128, 1024], F32, ctx=ctx) for i in range(2)]
    s1t = [b.sbuf(f"fc_s1{i}", [128, 1024], F32, ctx=ctx) for i in range(2)]
    tmp = b.sbuf("fc_tmp", [128, 1024], F32, ctx=ctx)
    ot = [b.sbuf(f"fc_o{i}", [128, 1024], F32, ctx=ctx) for i in range(2)]
    seq = Seq(b)
    pr_v1 = [None, None]
    pr_sto = [None, None]
    pr_v1_any = None
    for tt in range(32):
        tq0 = 128 * tt
        bb = tt % 2
        sgld = seq.dma("sync", lambda eng, bb=bb, tq0=tq0: eng.dma_start(
            sgt[bb][:], silg[:, tq0:tq0 + 128], transpose=True),
            deps=[pr_v1[bb]], n=1)
        xld = seq.dma("sync", lambda eng, bb=bb, tq0=tq0: [
            eng.dma_start(x1t[bb][:], x1[tq0:tq0 + 128, :]),
            eng.dma_start(s1t[bb][:], s1[tq0:tq0 + 128, :])], deps=[pr_v1[bb]], n=2)
        v1 = seq.step("vector", lambda eng, bb=bb: (
            b.fence(eng, "v", eng.tensor_tensor(tmp[:], s1t[bb][:], sgt[bb][:], ALU.mult)),
            eng.tensor_tensor(ot[bb][:], x1t[bb][:], tmp[:], ALU.add))[-1],
            deps=[sgld, xld, pr_sto[bb], pr_v1_any])
        pr_v1_any = v1
        sto = seq.dma("gpsimd_dma", lambda eng, bb=bb, tq0=tq0: eng.dma_start(
            out[tq0:tq0 + 128, :], ot[bb][:]), deps=[v1], n=1)
        pr_v1[bb] = v1
        pr_sto[bb] = sto
    seq.barrier_all()
    ctx.close()


SCRATCH = {
    "xn_bf": ([T, D], BF16), "s0": ([T, D], F32), "x1": ([T, D], F32),
    "xn1_bf": ([T, D], BF16), "xn1_f32": ([T, D], F32), "s1": ([T, D], F32),
    "silg": ([D, T], BF16), "out": ([T, D], F32),
}

PHASE_IO = {
    "ln0": (["x"], ["xn_bf"]),
    "l0attn": (["xn_bf"], ["s0"]),
    "moe": (["xn_bf", "x", "s0"], ["x1"]),
    "ln1": (["x1"], ["xn1_bf", "xn1_f32"]),
    "l1attn": (["xn1_bf"], ["s1"]),
    "l1ffn": (["xn1_bf"], ["silg"]),
    "combine": (["x1", "s1", "silg"], ["out"]),
}

PHASE_WEIGHTS = {
    "ln0": [],
    "l0attn": ["l0_wq", "l0_wk", "l0_wv", "l0_wo", "mask_su"],
    "moe": ["l0_rw", "l0_rb", "l0_w1", "l0_b1", "l0_w2", "l0_b2"],
    "ln1": [],
    "l1attn": ["l1_wq", "l1_wk", "l1_wv", "l1_wg", "l1_wo", "l1_wab", "l1_alog_dt",
               "l1_convq", "l1_convk", "l1_convv", "l1_onw", "alogdt_bf",
               "cums128", "amask_i128", "amask_sL128"],
    "l1ffn": ["f_wqkv", "f_wqkv2", "f_qs", "f_qs_sw", "f_ks", "f_ks_sw", "f_wo",
              "f_lng", "f_lnb", "f_w1", "f_b1", "f_w2", "f_b2",
              "rot_csq", "rot_ssq", "rot_csk", "rot_ssk", "fmask"],
    "combine": [],
}


def group_io(phases):
    produced = set()
    sin = set()
    needs_x = False
    for ph in phases:
        rd, wr = PHASE_IO[ph]
        for r in rd:
            if r == "x":
                needs_x = True
            elif r not in produced:
                sin.add(r)
        produced.update(wr)
    weights = ["ident", "constcols"]
    for ph in phases:
        weights.extend(PHASE_WEIGHTS[ph])
    return sorted(sin), sorted(produced), needs_x, weights


def build_program(phases, host_map, extern_in, outs):
    nc = bass.Bass("TRN2", target_bir_lowering=False)
    from concourse.bass import get_kernel_semaphore_range
    kr = get_kernel_semaphore_range()
    clear_range = range(kr.start + 3, kr.stop)
    nc.all_engine_barrier()
    nc.gpsimd.dma_reset(clear_range)
    nc.gpsimd.sem_clear(clear_range)
    nc.all_engine_barrier()
    ctx = ExitStack()
    aps = {}
    for name, arr in host_map.items():
        dt = BF16 if arr.dtype == BF16NP else F32
        aps[name] = nc.dram_tensor(name, list(arr.shape), dt, kind="ExternalInput").ap()
    for name, (shape, dt) in SCRATCH.items():
        if name in aps:
            continue
        if name in extern_in:
            kind = "ExternalInput"
        elif name in outs:
            kind = "ExternalOutput"
        else:
            kind = "Internal"
        aps[name] = nc.dram_tensor(name, shape, dt, kind=kind).ap()

    with ctx:
        blk = ctx.enter_context(nc.Block())
        b = Bld(nc, blk, ctx)
        b.dram = aps
        b.sb["ident_bf"] = b.sbuf("g_ident_bf", [128, 128], BF16)
        b.sb["ident_f32"] = b.sbuf("g_ident_f32", [128, 128], F32)
        b.sb["constcols"] = b.sbuf("g_constcols", [128, 6], F32)
        nc.const_aps.register(F32, 0.25, b.sb["constcols"][:, 0:1])
        nc.const_aps.register(F32, 8.0, b.sb["constcols"][:, 1:2])
        nc.const_aps.register(F32, 2.0, b.sb["constcols"][:, 2:3])
        nc.const_aps.register(F32, 0.5, b.sb["constcols"][:, 3:4])
        nc.const_aps.register(F32, 1e-5, b.sb["constcols"][:, 4:5])
        nc.const_aps.register(F32, -0.5, b.sb["constcols"][:, 5:6])
        ic = b.cnt("g_init")

        @blk.sync
        def _(eng):
            eng.dma_start(b.sb["ident_f32"][:], aps["ident"]).then_inc(ic.sem, 16)
            eng.dma_start(b.sb["constcols"][:], aps["constcols"]).then_inc(ic.sem, 16)
        ic.n += 32

        @blk.vector
        def _(eng):
            eng.wait_ge(ic.sem, 32)
            eng.tensor_copy(b.sb["ident_bf"][:], b.sb["ident_f32"][:]).then_inc(ic.sem, 1)
        ic.n += 1
        b.barrier()

        for ph in phases:
            pctx = ExitStack()
            outer = b.ctx
            b.ctx = pctx
            emit_phase(b, ph)
            b.barrier()
            b.ctx = outer
            pctx.close()
    return nc


def emit_phase(b, ph):
    aps = b.dram
    if ph == "ln0":
        emit_ln(b, aps["x"], aps["xn_bf"], None, tag="ln0")
    elif ph == "ln1":
        emit_ln(b, aps["x1"], aps["xn1_bf"], aps["xn1_f32"], tag="ln1")
    elif ph == "moe":
        emit_moe_combine(b, aps["xn_bf"], aps["x"], aps["s0"], aps["x1"])
    elif ph == "l0attn":
        emit_l0attn(b, aps["xn_bf"], aps["s0"])
    elif ph == "l1attn":
        emit_l1attn(b, aps["xn1_bf"], aps["s1"])
    elif ph == "l1ffn":
        emit_l1ffn(b, aps["xn1_bf"], aps["silg"])
    elif ph == "combine":
        emit_final_combine(b, aps["x1"], aps["s1"], aps["silg"], aps["out"])
    else:
        raise ValueError(ph)


LAUNCH_GROUPS = [["ln0", "l0attn", "moe", "ln1", "l1attn", "l1ffn", "combine"]]


def run_groups(x_full, inputs, groups):
    wmap = prep_weights(inputs)
    wmap.update(host_consts())
    shards = [np.ascontiguousarray(np.asarray(x_full[i], np.float32)) for i in range(8)]
    live = [dict() for _ in range(8)]
    final = None
    for group in groups:
        sin, souts, needs_x, weights = group_io(group)
        host_map = {w: wmap[w] for w in dict.fromkeys(weights)}
        if needs_x or any(ph in ("ln0", "moe") for ph in group):
            host_map["x"] = shards[0]
        nc = build_program(group, host_map, sin, souts)
        in_maps = []
        for core in range(8):
            m = {k: v for k, v in host_map.items() if k != "x"}
            if "x" in host_map:
                m["x"] = shards[core]
            for sname in sin:
                m[sname] = live[core][sname]
            in_maps.append(m)
        res = run_bass_kernel_spmd(nc, in_maps, core_ids=list(range(8)))
        for core in range(8):
            for sname in souts:
                live[core][sname] = np.ascontiguousarray(np.asarray(res.results[core][sname]))
        if "out" in souts:
            final = np.stack([live[core]["out"] for core in range(8)], axis=0)
    return final


def kernel(**inputs):
    x_full = np.ascontiguousarray(np.asarray(inputs["L0_attn_hidden_states"], dtype=np.float32))
    return run_groups(x_full, inputs, LAUNCH_GROUPS)


# version 15 (final)
# speedup vs baseline: 1.0194x; 1.0111x over previous
import numpy as np
import ml_dtypes
from contextlib import ExitStack
import concourse.bass as bass
import concourse.mybir as mybir
from concourse.bass_utils import run_bass_kernel_spmd

F32 = mybir.dt.float32
BF16 = mybir.dt.bfloat16
AF = mybir.ActivationFunctionType
ALU = mybir.AluOpType
AX = mybir.AxisListType

T, D = 4096, 1024
NT = T // 128
L1A_NQ = 4
L1A_DBG = None
L1A_SPILL = False
F_NT = 8
F_DBG = False
F_NL = 2
EPS_LN = 1e-5
NEG = np.float32(-1e30)
BF16NP = ml_dtypes.bfloat16



def host_consts():
    c = {}
    c["ident"] = np.eye(128, dtype=np.float32)
    c["constcols"] = np.tile(np.array([[0.25, 8.0, 2.0, 0.5, 1e-5, -0.5]], np.float32), (128, 1))
    c["mask_su"] = np.triu(np.ones((128, 128), np.float32))
    c["cums128"] = np.triu(np.ones((128, 128), np.float32))
    NEGC = np.float32(-1e30)
    c["amask_i128"] = np.where(np.triu(np.ones((128, 128), np.float32)) > 0, np.float32(0), NEGC)
    c["amask_sL128"] = np.where(np.tril(np.ones((128, 128), np.float32), -1) > 0, np.float32(0), NEGC)
    half = np.arange(0, 64, 2, dtype=np.float64)
    inv = 1.0 / 10000.0 ** (half / 64.0)
    tpos = np.arange(T, dtype=np.float64)
    fr = tpos[:, None] * inv[None, :]
    fr = np.concatenate([fr, fr], -1)
    base = (half + 0.4 * 64) / (1.4 * 64)
    power = (tpos - T // 2) / 256.0
    sc = base[None, :] ** power[:, None]
    sc = np.concatenate([sc, sc], -1)
    cos, sin = np.cos(fr), np.sin(fr)
    sgn = np.where(np.arange(64) < 32, -1.0, 1.0)[:, None]

    def tile2(a):
        return np.ascontiguousarray(np.tile(a.astype(np.float32), (2, 1)))
    c["rot_csq"] = tile2((cos * sc).T)
    c["rot_ssq"] = tile2((sin * sc).T * sgn)
    c["rot_csk"] = tile2((cos / sc).T)
    c["rot_ssk"] = tile2((sin / sc).T * sgn)
    jj = np.arange(128)[:, None]
    qq = np.arange(512)[None, :]
    fm = np.zeros((128, 8 * 512), np.float32)
    for a in range(8):
        if a < 4:
            m = (jj >= qq - 128 * a)
        else:
            m = (jj <= qq - 128 * (a - 4))
        fm[:, 512 * a:512 * (a + 1)] = m
    c["fmask"] = fm.astype(BF16NP)
    return c


def prep_weights(ins):
    w = {}
    w["l0_wq"] = np.ascontiguousarray(ins["L0_attn_Wq"])
    w["l0_wk"] = np.ascontiguousarray(ins["L0_attn_Wk"])
    w["l0_wv"] = np.ascontiguousarray(ins["L0_attn_Wv"])
    w["l0_wo"] = np.ascontiguousarray(ins["L0_attn_Wo"])
    w["l0_rw"] = np.ascontiguousarray(ins["L0_ffn_router_w"])
    w["l0_rb"] = np.ascontiguousarray(ins["L0_ffn_router_b"]).reshape(1, 8)
    w["l0_w1"] = np.ascontiguousarray(ins["L0_ffn_W1"])
    w["l0_b1"] = np.ascontiguousarray(ins["L0_ffn_b1"]).reshape(1, 8 * 512)
    w["l0_w2"] = np.ascontiguousarray(ins["L0_ffn_W2"])
    w["l0_b2"] = np.ascontiguousarray(ins["L0_ffn_b2"]).reshape(1, 8 * 1024)
    w["l1_wq"] = np.ascontiguousarray(ins["L1_attn_Wq"])
    w["l1_wk"] = np.ascontiguousarray(ins["L1_attn_Wk"])
    w["l1_wv"] = np.ascontiguousarray(ins["L1_attn_Wv"])
    w["l1_wg"] = np.ascontiguousarray(ins["L1_attn_Wg"])
    w["l1_wo"] = np.ascontiguousarray(ins["L1_attn_Wo"])
    w["l1_wab"] = np.concatenate([ins["L1_attn_Wa"], ins["L1_attn_Wb"]], axis=1)
    w["l1_alog_dt"] = np.concatenate(
        [np.asarray(ins["L1_attn_A_log"]).reshape(1, 8), np.asarray(ins["L1_attn_dt_bias"]).reshape(1, 8)], axis=1)
    w["l1_convq"] = np.ascontiguousarray(ins["L1_attn_conv_q"])
    w["l1_convk"] = np.ascontiguousarray(ins["L1_attn_conv_k"])
    w["l1_convv"] = np.ascontiguousarray(ins["L1_attn_conv_v"])
    w["l1_onw"] = np.ascontiguousarray(np.asarray(ins["L1_attn_o_norm_w"], np.float32).reshape(2, 128).T)
    w["alogdt_bf"] = np.ascontiguousarray(
        np.concatenate([np.asarray(ins["L1_attn_A_log"]).reshape(1, 8),
                        np.asarray(ins["L1_attn_dt_bias"]).reshape(1, 8)], axis=1).astype(BF16NP))
    wqkv = np.asarray(ins["L1_ffn_wqkv"], np.float32)
    w["f_wqkv"] = np.ascontiguousarray(wqkv.astype(BF16NP))
    perm = np.concatenate([64 * h + (np.arange(64) + 32) % 64 for h in range(8)])
    w["f_wqkv2"] = np.ascontiguousarray(np.concatenate(
        [wqkv[:, :, 0:512][:, :, perm], wqkv[:, :, 512:1024][:, :, perm]], axis=2).astype(BF16NP))
    qs = np.asarray(ins["L1_ffn_q_scale"], np.float32)
    ks = np.asarray(ins["L1_ffn_k_scale"], np.float32)
    w["f_qs"] = np.ascontiguousarray(np.tile(qs, (1, 2)).reshape(2, 128, 1))
    w["f_qs_sw"] = np.ascontiguousarray(np.tile(np.roll(qs, -32, axis=1), (1, 2)).reshape(2, 128, 1))
    w["f_ks"] = np.ascontiguousarray(np.tile(ks, (1, 2)).reshape(2, 128, 1))
    w["f_ks_sw"] = np.ascontiguousarray(np.tile(np.roll(ks, -32, axis=1), (1, 2)).reshape(2, 128, 1))
    w["f_wo"] = np.ascontiguousarray(np.asarray(ins["L1_ffn_wo"], np.float32).astype(BF16NP))
    w["f_lng"] = np.ascontiguousarray(
        np.asarray(ins["L1_ffn_ln_g"], np.float32).reshape(2, 8, 128).transpose(0, 2, 1))
    w["f_lnb"] = np.ascontiguousarray(
        np.asarray(ins["L1_ffn_ln_b"], np.float32).reshape(2, 8, 128).transpose(0, 2, 1))
    HID = 2730
    w1 = np.asarray(ins["L1_ffn_w1"], np.float32); b1 = np.asarray(ins["L1_ffn_b1"], np.float32)
    w1p = np.zeros((2, 1024, 5632), np.float32)
    w1p[:, :, 0:HID] = w1[:, :, 0:HID]
    w1p[:, :, 2816:2816 + HID] = w1[:, :, HID:2 * HID]
    b1p = np.zeros((2, 1, 5632), np.float32)
    b1p[:, 0, 0:HID] = b1[:, 0:HID]
    b1p[:, 0, 2816:2816 + HID] = b1[:, HID:2 * HID]
    w["f_w1"] = np.ascontiguousarray(w1p.astype(BF16NP))
    w["f_b1"] = np.ascontiguousarray(b1p.reshape(2, 44, 128).transpose(0, 2, 1).astype(np.float32))
    w2 = np.asarray(ins["L1_ffn_w2"], np.float32)
    w2p = np.zeros((2, 2816, 1024), np.float32)
    w2p[:, 0:HID, :] = w2
    w["f_w2"] = np.ascontiguousarray(w2p.astype(BF16NP))
    w["f_b2"] = np.ascontiguousarray(
        np.asarray(ins["L1_ffn_b2"], np.float32).reshape(2, 8, 128).transpose(0, 2, 1))
    return w



class Cnt:
    __slots__ = ("sem", "n")

    def __init__(self, sem):
        self.sem = sem
        self.n = 0


class SlotDma:

    def __init__(self, b, name, nslots):
        self.cnts = [b.cnt(f"{name}{j}") for j in range(nslots)]

    def issue(self, eng, j, out, in_, **kw):
        c = self.cnts[j]
        eng.dma_start(out, in_, **kw).then_inc(c.sem, 16)
        c.n += 16
        return c.n

    def wait_total(self, eng, j):
        c = self.cnts[j]
        eng.wait_ge(c.sem, c.n)


class Bld:
    def __init__(self, nc, blk, ctx):
        self.nc = nc
        self.blk = blk
        self.ctx = ctx
        self.sem_ctx = ctx
        self.phase = 0
        self.sems = {}
        self.dram = {}
        self.sb = {}

    def cnt(self, name):
        if name not in self.sems:
            sem = self.sem_ctx.enter_context(self.nc.semaphore(name))
            self.sems[name] = Cnt(sem)
        return self.sems[name]

    def sbuf(self, name, shape, dtype, ctx=None):
        t = (ctx or self.ctx).enter_context(self.nc.sbuf_tensor(name, shape, dtype))
        return t

    def psum2(self, name, shape, dtype=F32, ctx=None):
        return (ctx or self.ctx).enter_context(self.nc.psum_tensor(name, shape, dtype))

    def psum(self, name, shape, dtype=F32):
        return self.ctx.enter_context(self.nc.psum_tensor(name, shape, dtype))

    def ce(self, key, inst, k=1):
        c = self.cnt("chain_" + key)
        inst.then_inc(c.sem, k)
        c.n += k
        return c.n

    def cew(self, eng, key):
        c = self.cnt("chain_" + key)
        eng.wait_ge(c.sem, c.n)

    def fence(self, eng, key, inst):
        self.ce(key, inst)
        self.cew(eng, key)
        return inst

    def barrier(self):
        self.phase += 1
        ph = self.cnt("ph")
        target = 5 * self.phase
        for name in ("sync", "tensor", "vector", "scalar", "gpsimd"):
            sec = getattr(self.blk, name)

            def body(eng, target=target, ph=ph):
                eng.nop().then_inc(ph.sem, 1)
                eng.wait_ge(ph.sem, target)

            sec(body)
        ph.n = target



ENG_SEMNAME = {"tensor": "sq_t", "vector": "sq_v", "scalar": "sq_s", "gpsimd": "sq_g"}
DMA_QUEUES = {"sync": "sq_dS", "gpsimd_dma": "sq_dG", "scalar_dma": "sq_dA"}
DMA_POOL = 10


class Step:
    __slots__ = ("sem", "val")

    def __init__(self, sem, val):
        self.sem = sem
        self.val = val


class Seq:

    def __init__(self, b):
        self.b = b
        self.steps = []
        self.dma_pool = {}
        self.dma_rr = {}

    def step(self, engkey, thunk, deps=(), ninc=1):
        c = self.b.cnt(ENG_SEMNAME[engkey])
        c.n += ninc
        st = Step(c.sem, c.n)
        self.steps.append((engkey, tuple(d for d in deps if d is not None), thunk, st, ninc))
        return st

    def dma(self, queue, thunk, deps=(), n=1):
        if queue not in self.dma_pool:
            self.dma_pool[queue] = [self.b.cnt(f"{DMA_QUEUES[queue]}{j}") for j in range(DMA_POOL)]
            self.dma_rr[queue] = 0
        j = self.dma_rr[queue] % DMA_POOL
        self.dma_rr[queue] += 1
        c = self.dma_pool[queue][j]
        prev_total = c.n
        c.n += 16 * n
        st = Step(c.sem, c.n)
        self.steps.append(("__dma_" + queue, tuple(d for d in deps if d is not None) + (("__slot", c, prev_total),),
                           thunk, st, 16 * n))
        return st

    def emit(self):
        by_sec = {}
        for engkey, deps, thunk, st, ninc in self.steps:
            if engkey.startswith("__dma_"):
                queue = engkey[len("__dma_"):]
                secname = "sync" if queue == "sync" else ("gpsimd" if queue == "gpsimd_dma" else "scalar")
            else:
                secname = engkey
            by_sec.setdefault(secname, []).append((engkey, deps, thunk, st, ninc))
        blk = self.b.blk
        for secname, items in by_sec.items():
            sec = getattr(blk, secname)

            def body(eng, items=items):
                last_wait = {}
                for engkey, deps, thunk, st, ninc in items:
                    for d in deps:
                        if isinstance(d, tuple) and d and d[0] == "__slot":
                            _, c, prev_total = d
                            if prev_total > 0:
                                eng.wait_ge(c.sem, prev_total)
                            continue
                        if last_wait.get(d.sem, -1) >= d.val:
                            continue
                        eng.wait_ge(d.sem, d.val)
                        last_wait[d.sem] = d.val
                    inst = thunk(eng)
                    if engkey.startswith("__dma_"):
                        insts = inst if isinstance(inst, (list, tuple)) else [inst]
                        assert 16 * len(insts) == ninc
                        for ii in insts:
                            ii.then_inc(st.sem, 16)
                    else:
                        inst.then_inc(st.sem, ninc)

            sec(body)
        self.steps = []

    def drain_dmas(self):
        blk = self.b.blk
        for queue, cnts in self.dma_pool.items():
            secname = "sync" if queue == "sync" else ("gpsimd" if queue == "gpsimd_dma" else "scalar")
            sec = getattr(blk, secname)

            def body(eng, cnts=list(cnts)):
                for c in cnts:
                    if c.n > 0:
                        eng.wait_ge(c.sem, c.n)

            sec(body)

    def barrier_all(self):
        self.emit()
        self.drain_dmas()
        self.b.barrier()



def emit_ln(b: Bld, src, dst_bf, dst_f32=None, tag="ln"):
    nc = b.nc
    NB = 3
    lds = [b.cnt(tag + f"_ld{j}") for j in range(NB)]
    sts = [b.cnt(tag + f"_st{j}") for j in range(NB)]
    vv = b.cnt(tag + "_v"); sv = b.cnt(tag + "_s")
    per = 3 if dst_f32 is not None else 2
    nst = 2 if dst_f32 is not None else 1
    xa = [b.sbuf(f"{tag}_x{i}", [128, D], F32) for i in range(NB)]
    xc = [b.sbuf(f"{tag}_c{i}", [128, D], F32) for i in range(NB)]
    ob = [b.sbuf(f"{tag}_o{i}", [128, D], BF16) for i in range(NB)]
    of = [b.sbuf(f"{tag}_f{i}", [128, D], F32) for i in range(NB)] if dst_f32 is not None else None
    sq = b.sbuf(tag + "_sq", [128, D], F32)
    stat = b.sbuf(tag + "_stat", [128, 8], F32)
    junk = b.sbuf(tag + "_junk", [128, 2], F32)
    vv0, sv0 = vv.n, sv.n
    ld0 = [c.n for c in lds]
    st0 = [c.n for c in sts]

    @b.blk.sync
    def _(eng):
        for i in range(NT):
            j = i % NB
            if i >= NB:
                eng.wait_ge(sts[j].sem, st0[j] + 16 * nst * (i // NB))
            eng.dma_start(xa[j][:], src[128 * i:128 * (i + 1), :]).then_inc(lds[j].sem, 16)
    for j in range(NB):
        lds[j].n += 16 * len(range(j, NT, NB))

    @b.blk.vector
    def _(eng):
        for i in range(NT):
            j = i % NB
            eng.wait_ge(lds[j].sem, ld0[j] + 16 * (i // NB + 1))
            b.fence(eng, "v", eng.reduce_sum(stat[:, 0:1], xa[j][:], axis=AX.X))
            b.fence(eng, "v", eng.tensor_scalar(stat[:, 1:2], stat[:, 0:1], 1.0 / D, None, ALU.mult))
            b.fence(eng, "v", eng.tensor_scalar(xc[j][:], xa[j][:], stat[:, 1:2], None, ALU.subtract))
            b.fence(eng, "v", eng.tensor_tensor(sq[:], xc[j][:], xc[j][:], ALU.mult))
            b.fence(eng, "v", eng.reduce_sum(stat[:, 2:3], sq[:], axis=AX.X))
            b.fence(eng, "v", eng.tensor_scalar(stat[:, 3:4], stat[:, 2:3], 1.0 / D, EPS_LN, ALU.mult, ALU.add))
            eng.nop().then_inc(vv.sem, 1)
            eng.wait_ge(sv.sem, sv0 + i + 1)
            b.fence(eng, "v", eng.reciprocal(stat[:, 5:6], stat[:, 4:5]))
            eng.tensor_scalar(ob[j][:], xc[j][:], stat[:, 5:6], None, ALU.mult).then_inc(vv.sem, 1)
            if dst_f32 is not None:
                eng.tensor_scalar(of[j][:], xc[j][:], stat[:, 5:6], None, ALU.mult).then_inc(vv.sem, 1)
    vv.n += per * NT

    @b.blk.scalar
    def _(eng):
        for i in range(NT):
            eng.wait_ge(vv.sem, vv0 + per * i + 1)
            eng.activation(stat[:, 4:5], stat[:, 3:4], AF.Sqrt).then_inc(sv.sem, 1)
    sv.n += NT

    @b.blk.gpsimd
    def _(eng):
        for i in range(NT):
            j = i % NB
            eng.wait_ge(vv.sem, vv0 + per * i + 2)
            eng.dma_start(dst_bf[128 * i:128 * (i + 1), :], ob[j][:]).then_inc(sts[j].sem, 16)
            if dst_f32 is not None:
                eng.wait_ge(vv.sem, vv0 + per * i + 3)
                eng.dma_start(dst_f32[128 * i:128 * (i + 1), :], of[j][:]).then_inc(sts[j].sem, 16)
        for j in range(NB):
            eng.wait_ge(sts[j].sem, st0[j] + 16 * nst * len(range(j, NT, NB)))
    for j in range(NB):
        sts[j].n += 16 * nst * len(range(j, NT, NB))


def emit_moe_combine(b: Bld, xn_bf, x_src, s_src, x1_dst):
    nc = b.nc
    W = b.dram
    TH = 1024
    NTH = TH // 128
    wls = b.cnt("moe_wls")
    wle = [b.cnt(f"moe_wle{j}") for j in range(2)]
    lslot = [b.cnt(f"moe_l{j}") for j in range(2)]
    sslot = [b.cnt(f"moe_s{j}") for j in range(2)]
    tv = b.cnt("moe_t"); vv = b.cnt("moe_v"); sv = b.cnt("moe_sc")
    xfc = b.cnt("moe_xfc")

    xf = b.sbuf("moe_xf", [128, 8 * TH], BF16)
    w1sb = [b.sbuf(f"moe_w1_{i}", [128, 8 * 512], BF16) for i in range(2)]
    w2sb = [b.sbuf(f"moe_w2_{i}", [128, 4 * 1024], BF16) for i in range(2)]
    rwsb = b.sbuf("moe_rw", [128, 8 * 8], BF16)
    rbsb = b.sbuf("moe_rb", [1, 8], BF16)
    b1sb = b.sbuf("moe_b1", [1, 8 * 512], BF16)
    b2sb = b.sbuf("moe_b2", [1, 8 * 1024], BF16)
    ones1 = b.sbuf("moe_ones1", [1, 512], BF16)
    masktok = b.sbuf("moe_masktok", [128, NTH * 8], F32)
    rstat = b.sbuf("moe_rstat", [128, 16], F32)
    junkv = b.sbuf("moe_junkv", [128, 2], F32)
    rtile = b.sbuf("moe_rtile", [128, 8], F32)
    h1sb = [b.sbuf(f"moe_h1_{i}", [128, 4 * TH], BF16) for i in range(2)]
    yacc = b.sbuf("moe_yacc", [128, NTH * 1024], F32)
    xsb = [b.sbuf(f"moe_xs{i}", [128, D], F32) for i in range(2)]
    ssb = [b.sbuf(f"moe_ss{i}", [128, D], F32) for i in range(2)]
    gsb = [b.sbuf(f"moe_gs{i}", [128, D], F32) for i in range(2)]
    x1sb = [b.sbuf(f"moe_x1{i}", [128, D], F32) for i in range(2)]

    psA = b.psum("moe_psA", [128, 512])
    psB = b.psum("moe_psB", [128, 512])
    psC = b.psum("moe_psC", [128, 512])
    psD = b.psum("moe_psD", [128, 512])
    psR = b.psum("moe_psR", [128, 512])

    @b.blk.gpsimd
    def _(eng):
        eng.dma_start(rwsb[:].rearrange("p (kc e) -> p kc e", kc=8),
                      W["l0_rw"].rearrange("(kc p) e -> p kc e", p=128)).then_inc(wls.sem, 16)
        eng.dma_start(rbsb[:], W["l0_rb"]).then_inc(wls.sem, 16)
        eng.dma_start(b1sb[:], W["l0_b1"]).then_inc(wls.sem, 16)
        eng.dma_start(b2sb[:], W["l0_b2"]).then_inc(wls.sem, 16)
    wls.n += 16 * 4
    wl_static = wls.n

    @b.blk.vector
    def _(eng):
        eng.memset(ones1[:], 1.0).then_inc(vv.sem, 1)
    vv.n += 1
    ones_done = vv.n

    for half in range(4):
        t0 = half * TH
        xf0 = xfc.n

        @b.blk.sync
        def _(eng, t0=t0):
            for r in range(8):
                eng.dma_start(xf[:, r * TH:(r + 1) * TH],
                              xn_bf[t0:t0 + TH, r * 128:(r + 1) * 128], transpose=True).then_inc(xfc.sem, 16)
        xfc.n += 16 * 8

        tvr = tv.n; tv.n += NTH
        svr = sv.n; sv.n += NTH
        vvr = vv.n; vv.n += 3 * NTH

        @b.blk.tensor
        def _(eng, xf0=xf0, tvr=tvr, vvr=vvr, half=half):
            eng.wait_ge(xfc.sem, xf0 + 16 * 8)
            if half == 0:
                eng.wait_ge(wls.sem, wl_static)
                eng.wait_ge(vv.sem, ones_done)
            for i in range(NTH):
                if i > 0:
                    eng.wait_ge(vv.sem, vvr + 3 * (i - 1) + 1)
                for kc in range(8):
                    nc.tensor.matmul(psR[:, 0:8], xf[:, kc * TH + 128 * i: kc * TH + 128 * (i + 1)],
                                     rwsb[:, kc * 8:(kc + 1) * 8], start=(kc == 0), stop=False)
                nc.tensor.matmul(psR[:, 0:8], ones1[0:1, 0:128], rbsb[0:1, :],
                                 start=False, stop=True).then_inc(tv.sem, 1)

        @b.blk.vector
        def _(eng, tvr=tvr, vvr=vvr, svr=svr):
            for i in range(NTH):
                eng.wait_ge(tv.sem, tvr + (i + 1))
                b.fence(eng, "v", eng.reduce_max(rstat[:, 0:1], psR[:, 0:8], axis=AX.X))
                b.fence(eng, "v", eng.tensor_scalar(rtile[:], psR[:, 0:8], rstat[:, 0:1], None, ALU.subtract))
                eng.nop().then_inc(vv.sem, 1)
                eng.wait_ge(sv.sem, svr + i + 1)
                b.fence(eng, "v", eng.reduce_sum(rstat[:, 1:2], rtile[:], axis=AX.X))
                b.fence(eng, "v", eng.reciprocal(rstat[:, 2:3], rstat[:, 1:2]))
                b.fence(eng, "v", eng.tensor_scalar(rtile[:], rtile[:], rstat[:, 2:3], None, ALU.mult))
                b.fence(eng, "v", eng.reduce_max(rstat[:, 3:4], rtile[:], axis=AX.X))
                b.fence(eng, "v", eng.tensor_scalar(rstat[:, 8:16], rtile[:], rstat[:, 3:4], None, ALU.is_ge))
                b.fence(eng, "v", eng.scalar_tensor_tensor(rstat[:, 8:16], rstat[:, 8:16], -1e30, rtile[:], ALU.mult, ALU.add))
                b.fence(eng, "v", eng.reduce_max(rstat[:, 4:5], rstat[:, 8:16], axis=AX.X))
                b.fence(eng, "v", eng.tensor_tensor(rstat[:, 5:6], rstat[:, 3:4], rstat[:, 4:5], ALU.add))
                b.fence(eng, "v", eng.reciprocal(rstat[:, 6:7], rstat[:, 5:6]))
                b.fence(eng, "v", eng.tensor_scalar(rstat[:, 8:16], rtile[:], rstat[:, 4:5], None, ALU.is_ge))
                b.fence(eng, "v", eng.tensor_tensor(rstat[:, 8:16], rstat[:, 8:16], rtile[:], ALU.mult))
                eng.tensor_scalar(masktok[:, 8 * i:8 * (i + 1)], rstat[:, 8:16], rstat[:, 6:7],
                                  None, ALU.mult).then_inc(vv.sem, 2)

        @b.blk.scalar
        def _(eng, vvr=vvr, svr=svr):
            for i in range(NTH):
                eng.wait_ge(vv.sem, vvr + 3 * i + 1)
                eng.activation(rtile[:], rtile[:], AF.Exp).then_inc(sv.sem, 1)

        @b.blk.vector
        def _(eng):
            for i in range(NTH):
                inst = eng.memset(yacc[:, 1024 * i:1024 * (i + 1)], 0.0)
            b.fence(eng, "v", inst)
            eng.nop().then_inc(vv.sem, 1)
        vv.n += 1

        y_done_tv = {}
        NT512 = TH // 512
        n_h1 = 4 * NT512
        n_y = NTH * 2
        for n in range(8):
            buf = n % 2
            wl_tag = wle[buf].n; wle[buf].n += 32
            tvh = tv.n; tv.n += n_h1 + n_y
            svh = sv.n; sv.n += n_h1
            vvh = vv.n; vv.n += n_y

            @b.blk.gpsimd
            def _(eng, n=n, buf=buf, gate=(y_done_tv.get(n - 2))):
                if gate is not None:
                    eng.wait_ge(tv.sem, gate)
                eng.dma_start(w1sb[buf][:].rearrange("p (kc f) -> p kc f", kc=8),
                              W["l0_w1"][n].rearrange("(kc p) f -> p kc f", p=128)).then_inc(wle[buf].sem, 16)
                eng.dma_start(w2sb[buf][:].rearrange("p (kc f) -> p kc f", kc=4),
                              W["l0_w2"][n].rearrange("(kc p) f -> p kc f", p=128)).then_inc(wle[buf].sem, 16)

            @b.blk.tensor
            def _(eng, n=n, buf=buf, wl_tag=wl_tag, tvh=tvh, svh=svh, vvh=vvh):
                eng.wait_ge(wle[buf].sem, wl_tag + 32)
                cnt = 0
                for m in range(4):
                    for nt4 in range(TH // 512):
                        if svh + cnt - 1 > 0:
                            eng.wait_ge(sv.sem, svh + cnt - 1)
                        ps = psA if cnt % 2 == 0 else psB
                        for kc in range(8):
                            nc.tensor.matmul(ps[:, 0:512],
                                             w1sb[buf][:, kc * 512 + 128 * m: kc * 512 + 128 * (m + 1)],
                                             xf[:, kc * TH + 512 * nt4: kc * TH + 512 * (nt4 + 1)],
                                             start=(kc == 0), stop=False)
                        nc.tensor.matmul(ps[:, 0:512],
                                         b1sb[0:1, n * 512 + 128 * m: n * 512 + 128 * (m + 1)],
                                         ones1[0:1, 0:512],
                                         start=False, stop=True).then_inc(tv.sem, 1)
                        cnt += 1
                eng.wait_ge(sv.sem, svh + 4 * (TH // 512))
                cnt2 = 0
                for i in range(NTH):
                    for dh in range(2):
                        if vvh + cnt2 - 1 > 0:
                            eng.wait_ge(vv.sem, vvh + cnt2 - 1)
                        ps = psC if cnt2 % 2 == 0 else psD
                        for kc in range(4):
                            nc.tensor.matmul(ps[:, 0:512],
                                             h1sb[buf][:, kc * TH + 128 * i: kc * TH + 128 * (i + 1)],
                                             w2sb[buf][:, kc * 1024 + 512 * dh: kc * 1024 + 512 * (dh + 1)],
                                             start=(kc == 0), stop=False)
                        nc.tensor.matmul(ps[:, 0:512], ones1[0:1, 0:128],
                                         b2sb[0:1, n * 1024 + 512 * dh: n * 1024 + 512 * (dh + 1)],
                                         start=False, stop=True).then_inc(tv.sem, 1)
                        cnt2 += 1
            y_done_tv[n] = tv.n

            @b.blk.scalar
            def _(eng, buf=buf, tvh=tvh):
                for cnt in range(4 * (TH // 512)):
                    eng.wait_ge(tv.sem, tvh + cnt + 1)
                    m, nt4 = divmod(cnt, TH // 512)
                    ps = psA if cnt % 2 == 0 else psB
                    eng.activation(h1sb[buf][:, m * TH + 512 * nt4: m * TH + 512 * (nt4 + 1)],
                                   ps[:, 0:512], AF.Gelu).then_inc(sv.sem, 1)

            @b.blk.vector
            def _(eng, n=n, tvh=tvh):
                for cnt2 in range(NTH * 2):
                    eng.wait_ge(tv.sem, tvh + 4 * (TH // 512) + cnt2 + 1)
                    i, dh = divmod(cnt2, 2)
                    ps = psC if cnt2 % 2 == 0 else psD
                    inst = eng.scalar_tensor_tensor(
                        yacc[:, 1024 * i + 512 * dh: 1024 * i + 512 * (dh + 1)],
                        ps[:, 0:512], masktok[:, 8 * i + n: 8 * i + n + 1],
                        yacc[:, 1024 * i + 512 * dh: 1024 * i + 512 * (dh + 1)],
                        ALU.mult, ALU.add)
                    inst.then_inc(vv.sem, 1)
                    if cnt2 == NTH * 2 - 1:
                        eng.wait_ge(vv.sem, vvh + NTH * 2)

        b.barrier()

        vvc = vv.n; vv.n += 2 * NTH
        svc = sv.n; sv.n += NTH
        ld0c = [c.n for c in lslot]
        st0c = [c.n for c in sslot]

        @b.blk.sync
        def _(eng, t0=t0, ld0c=ld0c, st0c=st0c):
            for i in range(NTH):
                j = i % 2
                if i >= 2:
                    eng.wait_ge(sslot[j].sem, st0c[j] + 16 * (i // 2))
                eng.dma_start(xsb[j][:], x_src[t0 + 128 * i: t0 + 128 * (i + 1), :]).then_inc(lslot[j].sem, 16)
                eng.dma_start(ssb[j][:], s_src[t0 + 128 * i: t0 + 128 * (i + 1), :]).then_inc(lslot[j].sem, 16)

        @b.blk.scalar
        def _(eng, svc=svc, vvc=vvc):
            for i in range(NTH):
                j = i % 2
                if i >= 2:
                    eng.wait_ge(vv.sem, vvc + 2 * (i - 2) + 2)
                eng.activation(gsb[j][:], yacc[:, 1024 * i:1024 * (i + 1)], AF.Silu).then_inc(sv.sem, 1)

        @b.blk.vector
        def _(eng, ld0c=ld0c, svc=svc):
            for i in range(NTH):
                j = i % 2
                eng.wait_ge(lslot[j].sem, ld0c[j] + 32 * (i // 2 + 1))
                eng.wait_ge(sv.sem, svc + i + 1)
                eng.tensor_tensor(gsb[j][:], gsb[j][:], ssb[j][:], ALU.mult).then_inc(vv.sem, 1)
                eng.wait_ge(vv.sem, vvc + 2 * i + 1)
                eng.tensor_tensor(x1sb[j][:], gsb[j][:], xsb[j][:], ALU.add).then_inc(vv.sem, 1)

        @b.blk.gpsimd
        def _(eng, t0=t0, vvc=vvc, st0c=st0c):
            for i in range(NTH):
                j = i % 2
                eng.wait_ge(vv.sem, vvc + 2 * i + 2)
                eng.dma_start(x1_dst[t0 + 128 * i: t0 + 128 * (i + 1), :], x1sb[j][:]).then_inc(sslot[j].sem, 16)
            for j in range(2):
                eng.wait_ge(sslot[j].sem, st0c[j] + 16 * len(range(j, NTH, 2)))
        for j in range(2):
            lslot[j].n += 32 * len(range(j, NTH, 2))
            sslot[j].n += 16 * len(range(j, NTH, 2))

        b.barrier()


def emit_l0attn(b: Bld, xn_bf, s0_dst):
    nc = b.nc
    W = b.dram
    SEG = 1024
    NSEG = T // SEG
    NTS = SEG // 128
    H = 16

    wl = b.cnt("a0_wl")
    ml = b.cnt("a0_ml")
    xfc = b.cnt("a0_xfc")
    tv = b.cnt("a0_t")
    vv = b.cnt("a0_v")
    sv = b.cnt("a0_sc")
    gv = b.cnt("a0_g")
    qkst = b.cnt("a0_qkst")
    qktr = [b.cnt(f"a0_qktr{j}") for j in range(2)]
    pst = b.cnt("a0_pst")
    ptr = [b.cnt(f"a0_ptr{j}") for j in range(2)]
    sst = [b.cnt(f"a0_sst{j}") for j in range(2)]

    phq_d = nc.dram_tensor("a0_phq", [SEG, 256], BF16, kind="Internal").ap()
    qk_d = nc.dram_tensor("a0_qkd", [SEG, 1152], BF16, kind="Internal").ap()

    xf = b.sbuf("a0_xf", [128, 8 * SEG], BF16)
    qk_tok = b.sbuf("a0_qk", [128, NTS * 1024], BF16)
    qf = b.sbuf("a0_qf", [128, 8 * SEG], BF16)
    kf = b.sbuf("a0_kf", [128, 8 * SEG], BF16)
    v_tok = b.sbuf("a0_v", [128, NTS * H * 65], BF16)
    phs = b.sbuf("a0_phs", [128, NTS * 256], BF16)
    phk_tok = b.sbuf("a0_phkt", [128, NTS * 256], BF16)
    phqf = b.sbuf("a0_phqf", [128, 2 * SEG], BF16)
    o_fm = b.sbuf("a0_ofm", [128, 8 * SEG], BF16)
    st32 = b.sbuf("a0_st32", [128, H * 130], F32)
    stbf = b.sbuf("a0_stbf", [128, H * 130], BF16)
    mask_su = b.sbuf("a0_mask", [128, 128], F32)
    ones1 = b.sbuf("a0_ones", [1, 128], BF16)
    rrow = b.sbuf("a0_rrow", [1, 128], F32)
    rrow_bf = b.sbuf("a0_rrowbf", [1, 128], BF16)
    obsb = b.sbuf("a0_obsb", [128, 128], F32)
    mtf = b.sbuf("a0_mtf", [128, 128], F32)
    mt_sb = b.sbuf("a0_mt", [128, 128], BF16)
    junkg = b.sbuf("a0_junkg", [128, 2], F32)
    ztile = b.sbuf("a0_ztile", [128, 128], BF16)
    s0sb = [b.sbuf(f"a0_s0{j}", [128, D], F32) for j in range(2)]
    wq = b.sbuf("a0_wq", [128, 8 * 256], BF16)
    wk = b.sbuf("a0_wk", [128, 8 * 256], BF16)
    wv = b.sbuf("a0_wv", [128, 8 * 1024], BF16)
    wo = b.sbuf("a0_wo", [128, 8 * 1024], BF16)

    psA = b.psum("a0_psA", [128, 512])
    psB = b.psum("a0_psB", [128, 512])
    psM = b.psum("a0_psM", [128, 128])
    psO = b.psum("a0_psO", [128, 128])
    psBC = b.psum("a0_psBC", [128, 128])
    psS = b.psum("a0_psS", [128, 130])

    EPS = 1e-5

    @b.blk.gpsimd
    def _(eng):
        eng.dma_start(wq[:].rearrange("p (kc f) -> p kc f", kc=8),
                      W["l0_wq"].rearrange("(kc p) f -> p kc f", p=128)).then_inc(wl.sem, 16)
        eng.dma_start(wk[:].rearrange("p (kc f) -> p kc f", kc=8),
                      W["l0_wk"].rearrange("(kc p) f -> p kc f", p=128)).then_inc(wl.sem, 16)
        eng.dma_start(wv[:].rearrange("p (kc f) -> p kc f", kc=8),
                      W["l0_wv"].rearrange("(kc p) f -> p kc f", p=128)).then_inc(wl.sem, 16)
        eng.dma_start(wo[:].rearrange("p (kc f) -> p kc f", kc=8),
                      W["l0_wo"].rearrange("(kc p) f -> p kc f", p=128)).then_inc(wl.sem, 16)
    wl.n += 64

    @b.blk.sync
    def _(eng):
        eng.dma_start(mask_su[:], W["mask_su"]).then_inc(ml.sem, 16)
        eng.wait_ge(b.cnt("a0_v").sem, 1)
        for t in range(SEG // 128):
            eng.dma_start(qk_d[128 * t:128 * (t + 1), 1024:1152], ztile[:]).then_inc(ml.sem, 16)
    ml.n += 16 * (1 + SEG // 128)

    @b.blk.vector
    def _(eng):
        eng.wait_ge(wl.sem, wl.n)
        eng.wait_ge(ml.sem, 16)
        eng.memset(st32[:], 0.0)
        eng.memset(stbf[:], 0.0)
        eng.memset(ones1[:], 1.0)
        eng.memset(qk_tok[:], 0.0)
        eng.memset(ztile[:], 0.0)
        b.fence(eng, "v", eng.memset(
            v_tok[:].rearrange("p (t h c) -> p t h c", t=NTS, h=H)[:, :, :, 64:65], 1.0))
        eng.nop().then_inc(vv.sem, 1)
    vv.n += 1
    b.barrier()

    for seg in range(NSEG):
        t0 = seg * SEG
        xf0 = xfc.n

        @b.blk.sync
        def _(eng, t0=t0):
            for r in range(8):
                eng.dma_start(xf[:, r * SEG:(r + 1) * SEG],
                              xn_bf[t0:t0 + SEG, r * 128:(r + 1) * 128], transpose=True).then_inc(xfc.sem, 16)
        xfc.n += 16 * 8

        tv0 = tv.n
        sv0 = sv.n

        @b.blk.tensor
        def _(eng, xf0=xf0, tv0=tv0, sv0=sv0):
            eng.wait_ge(xfc.sem, xf0 + 16 * 8)
            cnt = 0
            for i in range(NTS):
                ps = psA if cnt % 2 == 0 else psB
                if cnt >= 2:
                    eng.wait_ge(sv.sem, sv0 + 2 * (cnt - 1))
                for kc in range(8):
                    nc.tensor.matmul(ps[:, 0:256], xf[:, kc * SEG + 128 * i: kc * SEG + 128 * (i + 1)],
                                     wq[:, kc * 256:(kc + 1) * 256], start=(kc == 0), stop=False)
                for kc in range(8):
                    inst = nc.tensor.matmul(ps[:, 256:512], xf[:, kc * SEG + 128 * i: kc * SEG + 128 * (i + 1)],
                                            wk[:, kc * 256:(kc + 1) * 256], start=False, stop=(kc == 7))
                inst.then_inc(tv.sem, 1)
                cnt += 1
            for i in range(NTS):
                for nh in range(2):
                    ps = psA if cnt % 2 == 0 else psB
                    if cnt >= 2:
                        eng.wait_ge(sv.sem, sv0 + 2 * (cnt - 1))
                    for kc in range(8):
                        inst = nc.tensor.matmul(ps[:, 0:512], xf[:, kc * SEG + 128 * i: kc * SEG + 128 * (i + 1)],
                                                wv[:, kc * 1024 + 512 * nh: kc * 1024 + 512 * (nh + 1)],
                                                start=(kc == 0), stop=(kc == 7))
                    inst.then_inc(tv.sem, 1)
                    cnt += 1
        tv.n += 3 * NTS

        @b.blk.scalar
        def _(eng, tv0=tv0, sv0=sv0):
            cnt = 0
            for i in range(NTS):
                ps = psA if cnt % 2 == 0 else psB
                eng.wait_ge(tv.sem, tv0 + cnt + 1)
                qdst = qk_tok[:, 1024 * i: 1024 * i + 512].rearrange("p (h s) -> p h s", s=32)[:, :, 0:16]
                kdst = qk_tok[:, 1024 * i + 512: 1024 * (i + 1)].rearrange("p (h s) -> p h s", s=32)[:, :, 0:16]
                eng.activation(qdst, ps[:, 0:256].rearrange("p (h f) -> p h f", h=16), AF.Copy, scale=0.25)
                eng.copy(kdst, ps[:, 256:512].rearrange("p (h f) -> p h f", h=16)).then_inc(sv.sem, 2)
                cnt += 1
            for i in range(NTS):
                for nh in range(2):
                    ps = psA if cnt % 2 == 0 else psB
                    eng.wait_ge(tv.sem, tv0 + cnt + 1)
                    dst = v_tok[:, i * H * 65 + nh * 8 * 65: i * H * 65 + (nh * 8 + 8) * 65]
                    eng.copy(dst.rearrange("p (h c) -> p h c", h=8)[:, :, 0:64],
                             ps[:, 0:512].rearrange("p (h c) -> p h c", h=8)).then_inc(sv.sem, 2)
                    cnt += 1
        sv.n += 2 * 3 * NTS

        b.barrier()

        q0 = qkst.n
        qt0 = [c.n for c in qktr]

        @b.blk.sync
        def _(eng, q0=q0, qt0=qt0):
            eng.dma_start(qk_d[:, 0:1024].rearrange("(t p) f -> p t f", p=128),
                          qk_tok[:].rearrange("p (t f) -> p t f", t=NTS)).then_inc(qkst.sem, 16)
            eng.wait_ge(qkst.sem, q0 + 16)
            eng.wait_ge(ml.sem, 16 * (1 + SEG // 128))
            for g in range(8):
                srcq = 64 * g
                srck = 512 + 64 * g
                eng.dma_start(qf[:, g * SEG:(g + 1) * SEG],
                              qk_d[:, srcq:srcq + 128], transpose=True).then_inc(qktr[0].sem, 16)
                eng.dma_start(kf[:, g * SEG:(g + 1) * SEG],
                              qk_d[:, srck:srck + 128], transpose=True).then_inc(qktr[1].sem, 16)
        qkst.n += 16
        for c in qktr:
            c.n += 128
        qk_targets = [c.n for c in qktr]

        b.barrier()

        for h in range(H):
            vv0 = vv.n
            gv0 = gv.n

            @b.blk.vector
            def _(eng, h=h):
                for i in range(NTS):
                    qsl = qk_tok[:, 1024 * i + 32 * h: 1024 * i + 32 * h + 16]
                    in0 = qsl.rearrange("p (i one) -> p i one", one=1).broadcast_to([128, 16, 16])
                    in1 = qsl.rearrange("p (one j) -> p one j", one=1).broadcast_to([128, 16, 16])
                    inst = eng.tensor_tensor(phs[:, 256 * i:256 * (i + 1)].rearrange("p (i j) -> p i j", i=16),
                                             in0, in1, ALU.mult)
                    if i < NTS - 1:
                        b.ce("v", inst)
                inst.then_inc(vv.sem, 1)
                b.sems["chain_v"].n += 0
            vv.n += 1

            @b.blk.gpsimd
            def _(eng, h=h):
                for i in range(NTS):
                    ksl = qk_tok[:, 1024 * i + 512 + 32 * h: 1024 * i + 512 + 32 * h + 16]
                    in0 = ksl.rearrange("p (i one) -> p i one", one=1).broadcast_to([128, 16, 16])
                    in1 = ksl.rearrange("p (one j) -> p one j", one=1).broadcast_to([128, 16, 16])
                    inst = eng.tensor_tensor(phk_tok[:, 256 * i:256 * (i + 1)].rearrange("p (i j) -> p i j", i=16),
                                              in0, in1, ALU.mult)
                inst.then_inc(gv.sem, 1)
            gv.n += 1

            p0q = pst.n
            pt0 = [c.n for c in ptr]

            @b.blk.sync
            def _(eng, vv0=vv0, p0q=p0q):
                eng.wait_ge(vv.sem, vv0 + 1)
                eng.dma_start(phq_d.rearrange("(t p) f -> p t f", p=128),
                              phs[:].rearrange("p (t f) -> p t f", t=NTS)).then_inc(pst.sem, 16)
                eng.wait_ge(pst.sem, p0q + 16)
                eng.dma_start(phqf[:, 0:SEG], phq_d[:, 0:128], transpose=True).then_inc(ptr[0].sem, 16)
                eng.dma_start(phqf[:, SEG:2 * SEG], phq_d[:, 128:256], transpose=True).then_inc(ptr[1].sem, 16)
            pst.n += 16
            for c in ptr:
                c.n += 16
            ptr_targets = [c.n for c in ptr]
            gv_target = gv.n

            b.barrier()

            tvh = tv.n
            vvh = vv.n
            svh = sv.n
            n_c = NTS
            hb, ho = 32 * (h % 2), (h // 2) * SEG

            @b.blk.tensor
            def _(eng, h=h, tvh=tvh, vvh=vvh, svh=svh, ptr_targets=ptr_targets,
                  qk_targets=qk_targets, hb=hb, ho=ho):
                for j in range(2):
                    eng.wait_ge(ptr[j].sem, ptr_targets[j])
                    eng.wait_ge(qktr[j].sem, qk_targets[j])
                for c in range(n_c):
                    if c > 0:
                        eng.wait_ge(sv.sem, svh + 2 * (c - 1) + 1)
                    nc.tensor.matmul(psM[:, 0:128],
                                     kf[hb:hb + 16, ho + 128 * c: ho + 128 * (c + 1)],
                                     qf[hb:hb + 16, ho + 128 * c: ho + 128 * (c + 1)],
                                     start=True, stop=True).then_inc(tv.sem, 1)
                    eng.wait_ge(vv.sem, vvh + 4 * c + 2)
                    if c > 0:
                        eng.wait_ge(sv.sem, svh + 2 * (c - 1) + 2)
                        eng.wait_ge(vv.sem, vvh + 4 * (c - 1) + 3)
                        eng.wait_ge(vv.sem, vvh + 4 * (c - 1) + 4)
                    vt = v_tok[:, (c * H + h) * 65:(c * H + h) * 65 + 65]
                    nc.tensor.matmul(psO[0:65, 0:128], vt, mt_sb[:, 0:128], start=True, stop=False)
                    for fh in range(2):
                        inst = nc.tensor.matmul(psO[0:65, 0:128], stbf[:, h * 130 + 65 * fh: h * 130 + 65 * (fh + 1)],
                                                phqf[:, fh * SEG + 128 * c: fh * SEG + 128 * (c + 1)],
                                                start=False, stop=(fh == 1))
                    inst.then_inc(tv.sem, 1)
                    for fh in range(2):
                        inst = nc.tensor.matmul(psS[:, 65 * fh:65 * (fh + 1)],
                                                phk_tok[:, 256 * c + 128 * fh: 256 * c + 128 * (fh + 1)],
                                                vt, start=True, stop=True)
                    inst.then_inc(tv.sem, 1)
                    eng.wait_ge(vv.sem, vvh + 4 * c + 3)
                    nc.tensor.matmul(psBC[0:64, 0:128], ones1[0:1, 0:64], rrow_bf[0:1, 0:128],
                                     start=True, stop=True).then_inc(tv.sem, 1)
            tv.n += 4 * n_c

            @b.blk.scalar
            def _(eng, tvh=tvh, svh=svh, vvh=vvh, gv_target=gv_target):
                eng.wait_ge(gv.sem, gv_target)
                for c in range(n_c):
                    eng.wait_ge(tv.sem, tvh + 4 * c + 1)
                    if c > 0:
                        eng.wait_ge(vv.sem, vvh + 4 * (c - 1) + 1)
                    eng.activation(mtf[:, 0:128], psM[:, 0:128], AF.Square).then_inc(sv.sem, 1)
                    eng.wait_ge(tv.sem, tvh + 4 * c + 2)
                    if c > 0:
                        eng.wait_ge(vv.sem, vvh + 4 * (c - 1) + 4)
                    eng.copy(obsb[0:64, 0:128], psO[0:64, 0:128]).then_inc(sv.sem, 1)
            sv.n += 2 * n_c

            @b.blk.vector
            def _(eng, h=h, tvh=tvh, vvh=vvh, svh=svh):
                for c in range(n_c):
                    eng.wait_ge(sv.sem, svh + 2 * c + 1)
                    eng.tensor_tensor(mt_sb[:, 0:128], mtf[:, 0:128], mask_su[:], ALU.mult).then_inc(vv.sem, 2)
                    eng.wait_ge(tv.sem, tvh + 4 * c + 2)
                    b.fence(eng, "v", eng.tensor_scalar(rrow[0:1, 0:128], psO[64:65, 0:128],
                                                        EPS, EPS, ALU.add, ALU.max))
                    b.fence(eng, "v", eng.reciprocal(rrow[0:1, 0:128], rrow[0:1, 0:128]))
                    eng.tensor_copy(rrow_bf[0:1, 0:128], rrow[0:1, 0:128]).then_inc(vv.sem, 1)
                    eng.wait_ge(sv.sem, svh + 2 * c + 2)
                    eng.wait_ge(tv.sem, tvh + 4 * c + 4)
                    tile = h // 2
                    row0 = (h % 2) * 64
                    eng.tensor_tensor(o_fm[row0:row0 + 64, tile * SEG + 128 * c: tile * SEG + 128 * (c + 1)],
                                      obsb[0:64, 0:128], psBC[0:64, 0:128], ALU.mult)
                    eng.wait_ge(tv.sem, tvh + 4 * c + 3)
                    b.fence(eng, "v", eng.tensor_tensor(st32[:, h * 130:(h + 1) * 130],
                                                        st32[:, h * 130:(h + 1) * 130],
                                                        psS[:, 0:130], ALU.add))
                    eng.tensor_copy(stbf[:, h * 130:(h + 1) * 130],
                                    st32[:, h * 130:(h + 1) * 130]).then_inc(vv.sem, 1)
            vv.n += 4 * n_c

            b.barrier()

        tvp = tv.n
        vvp = vv.n
        st0s = [c.n for c in sst]

        @b.blk.tensor
        def _(eng, tvp=tvp, vvp=vvp):
            cnt = 0
            for i in range(NTS):
                for nh in range(2):
                    ps = psA if cnt % 2 == 0 else psB
                    if cnt >= 2:
                        eng.wait_ge(vv.sem, vvp + cnt - 1)
                    for kc in range(8):
                        inst = nc.tensor.matmul(ps[:, 0:512], o_fm[:, kc * SEG + 128 * i: kc * SEG + 128 * (i + 1)],
                                                wo[:, kc * 1024 + 512 * nh: kc * 1024 + 512 * (nh + 1)],
                                                start=(kc == 0), stop=(kc == 7))
                    inst.then_inc(tv.sem, 1)
                    cnt += 1
        tv.n += 2 * NTS

        @b.blk.vector
        def _(eng, tvp=tvp, st0s=st0s):
            cnt = 0
            for i in range(NTS):
                for nh in range(2):
                    j = i % 2
                    ps = psA if cnt % 2 == 0 else psB
                    eng.wait_ge(tv.sem, tvp + cnt + 1)
                    if cnt >= 4:
                        eng.wait_ge(sst[j].sem, st0s[j] + 16 * ((cnt - 4) // 4 + 1))
                    eng.tensor_copy(s0sb[j][:, 512 * nh:512 * (nh + 1)], ps[:, 0:512]).then_inc(vv.sem, 1)
                    cnt += 1
        vv.n += 2 * NTS

        @b.blk.gpsimd
        def _(eng, t0=t0, vvp=vvp, st0s=st0s):
            for i in range(NTS):
                j = i % 2
                eng.wait_ge(vv.sem, vvp + 2 * (i + 1))
                eng.dma_start(s0_dst[t0 + 128 * i:t0 + 128 * (i + 1), :], s0sb[j][:]).then_inc(sst[j].sem, 16)
            for j in range(2):
                eng.wait_ge(sst[j].sem, st0s[j] + 16 * len(range(j, NTS, 2)))
        for j in range(2):
            sst[j].n += 16 * len(range(j, NTS, 2))

        b.barrier()


def emit_l1attn(b: Bld, xn_bf, s1_dst):
    import contextlib
    nc = b.nc
    W = b.dram
    QT = 1024
    NQ = T // QT
    NTQ = QT // 128
    H, KD, VD = 8, 128, 256
    EPSR = 1e-5

    q_d = nc.dram_tensor("a1_q", [1024, T], BF16, kind="Internal").ap()
    k_d = nc.dram_tensor("a1_k", [1024, T], BF16, kind="Internal").ap()
    v_d = nc.dram_tensor("a1_v", [2048, T], BF16, kind="Internal").ap()
    gs_d = nc.dram_tensor("a1_gs", [2048, T], BF16, kind="Internal").ap()
    og_d = nc.dram_tensor("a1_og", [T, 2048], BF16, kind="Internal").ap()
    if L1A_SPILL:
        dbg_d = nc.dram_tensor("a1_dbg", [32 * 8, 4, 128, 256], F32, kind="Internal").ap()

    seq = Seq(b)
    wl = b.cnt("a1_wl")
    fl = b.cnt("a1_fl")

    bbuf = b.sbuf("a1_bbuf", [128, 32 * 8], F32)
    gbuf = b.sbuf("a1_gbuf", [128, 32 * 8], F32)
    negA = b.sbuf("a1_negA", [128, 8], F32)
    alog = b.sbuf("a1_alog", [1, 16], F32)
    onesc = b.sbuf("a1_onesc", [128, 1], F32)
    ones1 = b.sbuf("a1_ones1", [1, 128], BF16)
    onesf1 = b.sbuf("a1_onesf1", [1, 128], F32)
    cums = b.sbuf("a1_cums", [128, 128], F32)
    am_i = b.sbuf("a1_am_i", [128, 128], F32)
    am_sL = b.sbuf("a1_am_sL", [128, 128], F32)
    identf = b.sbuf("a1_identf", [128, 128], F32)
    bt0 = b.sbuf("a1_bt0", [128, 8], F32)
    bt1 = b.sbuf("a1_bt1", [128, 8], F32)
    bt2 = b.sbuf("a1_bt2", [128, 8], F32)
    bt3 = b.sbuf("a1_bt3", [128, 8], F32)
    albf = b.sbuf("a1_albf", [1, 16], BF16)

    ld0 = seq.dma("sync", lambda eng: [
        eng.dma_start(albf[:], W["alogdt_bf"]),
        eng.dma_start(alog[:], W["l1_alog_dt"]),
        eng.dma_start(cums[:], W["cums128"]),
        eng.dma_start(am_i[:], W["amask_i128"]),
        eng.dma_start(am_sL[:], W["amask_sL128"]),
        eng.dma_start(identf[:], W["ident"]),
    ], n=6)
    s_exp = seq.step("scalar", lambda eng: eng.activation(alog[0:1, 0:8], alog[0:1, 0:8], AF.Exp), deps=[ld0])
    v_ini = seq.step("vector", lambda eng: (
        eng.tensor_scalar(alog[0:1, 0:8], alog[0:1, 0:8], -1.0, None, ALU.mult),
        eng.memset(onesc[:], 1.0),
        eng.memset(ones1[:], 1.0),
        eng.memset(onesf1[:], 1.0))[-1], deps=[s_exp])
    ctxIN = contextlib.ExitStack()
    psIN = b.psum2("a1_psIN", [128, 128], F32, ctxIN)
    t_na = seq.step("tensor", lambda eng: nc.tensor.matmul(
        psIN[:, 0:8], onesf1[0:1, :], alog[0:1, 0:8], start=True, stop=True), deps=[v_ini])
    v_na = seq.step("vector", lambda eng: eng.tensor_copy(negA[:], psIN[:, 0:8]), deps=[t_na])

    ctxA = contextlib.ExitStack()
    xf = b.sbuf("a1_xf", [128, 8 * QT], BF16, ctxA)
    wq = b.sbuf("a1_wq", [128, 8 * 1024], BF16, ctxA)
    wk = b.sbuf("a1_wk", [128, 8 * 1024], BF16, ctxA)
    wv = b.sbuf("a1_wv", [128, 8 * 2048], BF16, ctxA)
    wg = b.sbuf("a1_wg", [128, 8 * 2048], BF16, ctxA)
    wab = b.sbuf("a1_wab", [128, 8 * 16], BF16, ctxA)
    cwq = b.sbuf("a1_cwq", [128, 8 * 4], F32, ctxA)
    cwk = b.sbuf("a1_cwk", [128, 8 * 4], F32, ctxA)
    cwv = b.sbuf("a1_cwv", [128, 16 * 4], F32, ctxA)
    raw = b.sbuf("a1_raw", [128, 1032], F32, ctxA)
    ysb = b.sbuf("a1_ysb", [128, 1024], F32, ctxA)
    sqb = b.sbuf("a1_sqb", [128, 1024], F32, ctxA)
    ssrow = b.sbuf("a1_ssrow", [1, 1024], F32, ctxA)
    rrow = b.sbuf("a1_rrow", [1, 1024], F32, ctxA)
    rrbf = b.sbuf("a1_rrbf", [1, 1024], BF16, ctxA)
    bnd = b.sbuf("a1_bnd", [128, 32 * 3], F32, ctxA)
    stg = b.sbuf("a1_stg", [128, 1024], BF16, ctxA)
    psQ = b.psum2("a1_psQ", [128, 512], F32, ctxA)
    psW = b.psum2("a1_psW", [128, 512], F32, ctxA)
    psE = b.psum2("a1_psE", [128, 512], F32, ctxA)
    psE2 = b.psum2("a1_psE2", [128, 512], F32, ctxA)
    psH8 = b.psum2("a1_psH8", [128, 16], F32, ctxA)

    wld = seq.dma("gpsimd_dma", lambda eng: [
        eng.dma_start(wq[:].rearrange("p (kc f) -> p kc f", kc=8),
                      W["l1_wq"].rearrange("(kc p) f -> p kc f", p=128)),
        eng.dma_start(wk[:].rearrange("p (kc f) -> p kc f", kc=8),
                      W["l1_wk"].rearrange("(kc p) f -> p kc f", p=128)),
        eng.dma_start(wv[:].rearrange("p (kc f) -> p kc f", kc=8),
                      W["l1_wv"].rearrange("(kc p) f -> p kc f", p=128)),
        eng.dma_start(wg[:].rearrange("p (kc f) -> p kc f", kc=8),
                      W["l1_wg"].rearrange("(kc p) f -> p kc f", p=128)),
        eng.dma_start(wab[:].rearrange("p (kc f) -> p kc f", kc=8),
                      W["l1_wab"].rearrange("(kc p) f -> p kc f", p=128)),
    ], n=5)
    cwl = seq.dma("sync", lambda eng: [
        eng.dma_start(cwq[:].rearrange("p (kc s) -> p kc s", kc=8),
                      W["l1_convq"].rearrange("(kc p) s -> p kc s", p=128)),
        eng.dma_start(cwk[:].rearrange("p (kc s) -> p kc s", kc=8),
                      W["l1_convk"].rearrange("(kc p) s -> p kc s", p=128)),
        eng.dma_start(cwv[:].rearrange("p (kc s) -> p kc s", kc=16),
                      W["l1_convv"].rearrange("(kc p) s -> p kc s", p=128)),
    ], n=3)
    v_bnd = seq.step("vector", lambda eng: eng.memset(bnd[:], 0.0), deps=[cwl])

    prev_projmm = None
    for qi in range(L1A_NQ):
        t0 = qi * QT
        xfl = seq.dma("sync", lambda eng, t0=t0: [
            eng.dma_start(xf[:, r * QT:(r + 1) * QT],
                          xn_bf[t0:t0 + QT, r * 128:(r + 1) * 128], transpose=True)
            for r in range(8)], deps=[wld, prev_projmm], n=8)

        prev_gv3 = None
        prev_gs2 = None
        prev_gs1 = None
        for i in range(NTQ):
            gcol = (qi * NTQ + i) * 8

            def abmm(eng, i=i):
                for kc in range(8):
                    nc.tensor.matmul(psH8[:, 0:16], xf[:, kc * QT + 128 * i: kc * QT + 128 * (i + 1)],
                                     wab[:, kc * 16:(kc + 1) * 16], start=(kc == 0), stop=False)
                return nc.tensor.matmul(psH8[:, 0:8], ones1[0:1, 0:128], albf[0:1, 8:16],
                                        start=False, stop=True)
            ab = seq.step("tensor", abmm, deps=[xfl, v_na, prev_gv3, prev_gs2, ld0, wld])
            gv1 = seq.step("vector", lambda eng: (
                b.fence(eng, "v", eng.tensor_scalar(bt0[:], psH8[:, 0:8], -1.0, None, ALU.mult)),
                eng.tensor_tensor(bt1[:], psH8[:, 0:8], bt0[:], ALU.max))[-1],
                deps=[ab, prev_gs1])
            gs1 = seq.step("scalar", lambda eng: (
                b.fence(eng, "s", eng.activation(bt2[:], bt1[:], AF.Exp, scale=-1.0)),
                eng.activation(bt2[:], bt2[:], AF.Ln, bias=1.0))[-1], deps=[gv1])
            prev_gs1 = gs1
            gv2 = seq.step("vector", lambda eng, gcol=gcol: (
                b.fence(eng, "v", eng.tensor_scalar(bt1[:], psH8[:, 0:8], 0.0, None, ALU.max)),
                b.fence(eng, "v", eng.tensor_tensor(bt2[:], bt2[:], bt1[:], ALU.add)),
                eng.tensor_tensor(gbuf[:, gcol:gcol + 8], bt2[:], negA[:], ALU.mult))[-1], deps=[gs1])
            gs2 = seq.step("scalar", lambda eng: eng.activation(bt3[:], psH8[:, 8:16], AF.Sigmoid), deps=[gv2])
            gv3 = seq.step("vector", lambda eng, gcol=gcol: eng.tensor_scalar(
                bbuf[:, gcol:gcol + 8], bt3[:], 2.0, None, ALU.mult), deps=[gs2])
            prev_gv3, prev_gs2 = gv3, gs2

        specs = [("q", 8, wq, cwq, True, q_d, 0), ("k", 8, wk, cwk, True, k_d, 8),
                 ("v", 16, wv, cwv, False, v_d, 16), ("g", 16, wg, None, False, gs_d, 0)]
        prev_ev = {}
        last_store = None
        for name, nm, wsb, cw, isqk, dst, bo in specs:
            for m in range(nm):
                def pjmm(eng, nm=nm, wsb=wsb, m=m):
                    for kc in range(8):
                        nc.tensor.matmul(psQ[:, 0:512],
                                         wsb[:, kc * (nm * 128) + 128 * m: kc * (nm * 128) + 128 * (m + 1)],
                                         xf[:, kc * QT: kc * QT + 512],
                                         start=(kc == 0), stop=(kc == 7))
                    for kc in range(8):
                        inst = nc.tensor.matmul(psQ[:, 0:512] if False else psW[:, 0:512],
                                                wsb[:, kc * (nm * 128) + 128 * m: kc * (nm * 128) + 128 * (m + 1)],
                                                xf[:, kc * QT + 512: kc * QT + 1024],
                                                start=(kc == 0), stop=(kc == 7))
                    return inst
                pj = seq.step("tensor", pjmm,
                              deps=[xfl, prev_ev.get("free_psQW"), (v_na if qi == 0 else None)])
                if name == "g":
                    sil = seq.step("scalar", lambda eng: (
                        eng.activation(stg[:, 0:512], psQ[:, 0:512], AF.Silu),
                        eng.activation(stg[:, 512:1024], psW[:, 0:512], AF.Silu))[-1],
                        deps=[pj, prev_ev.get("stg")])
                    prev_ev["free_psQW"] = sil
                    st = seq.dma("gpsimd_dma", lambda eng, dst=dst, m=m, t0=t0: [
                        eng.dma_start(dst[128 * m:128 * (m + 1), t0:t0 + QT], stg[:])], deps=[sil], n=1)
                    prev_ev["stg"] = st
                    last_store = st
                    continue
                bslot = bo + m

                def evconv(eng, cw=cw, m=m, bslot=bslot):
                    eng.tensor_copy(raw[:, 8:520], psQ[:, 0:512])
                    eng.tensor_copy(raw[:, 520:1032], psW[:, 0:512])
                    b.fence(eng, "v", eng.tensor_copy(raw[:, 5:8], bnd[:, 3 * bslot:3 * bslot + 3]))
                    b.fence(eng, "v", eng.tensor_scalar(ysb[:], raw[:, 8:1032], cw[:, 4 * m + 3:4 * m + 4], None, ALU.mult))
                    b.fence(eng, "v", eng.scalar_tensor_tensor(ysb[:], raw[:, 7:1031], cw[:, 4 * m + 2:4 * m + 3], ysb[:], ALU.mult, ALU.add))
                    b.fence(eng, "v", eng.scalar_tensor_tensor(ysb[:], raw[:, 6:1030], cw[:, 4 * m + 1:4 * m + 2], ysb[:], ALU.mult, ALU.add))
                    b.fence(eng, "v", eng.scalar_tensor_tensor(ysb[:], raw[:, 5:1029], cw[:, 4 * m + 0:4 * m + 1], ysb[:], ALU.mult, ALU.add))
                    return eng.tensor_copy(bnd[:, 3 * bslot:3 * bslot + 3], raw[:, 1029:1032])
                ev = seq.step("vector", evconv, deps=[pj, prev_ev.get("raw"), prev_ev.get("ysb")])
                prev_ev["raw"] = ev
                prev_ev["free_psQW"] = ev
                if not isqk:
                    silv = seq.step("scalar", lambda eng: eng.activation(stg[:], ysb[:], AF.Silu),
                                    deps=[ev, prev_ev.get("stg")])
                    prev_ev["ysb"] = silv
                    st = seq.dma("gpsimd_dma", lambda eng, dst=dst, m=m, t0=t0: [
                        eng.dma_start(dst[128 * m:128 * (m + 1), t0:t0 + QT], stg[:])], deps=[silv], n=1)
                    prev_ev["stg"] = st
                    last_store = st
                    continue
                sil = seq.step("scalar", lambda eng: (
                    b.fence(eng, "s", eng.activation(ysb[:], ysb[:], AF.Silu)),
                    eng.activation(sqb[:], ysb[:], AF.Square))[-1],
                    deps=[ev, prev_ev.get("sqb")])
                prev_ev["ysb"] = sil
                ssm = seq.step("tensor", lambda eng: (
                    nc.tensor.matmul(psE[0:1, 0:512], onesc[:], sqb[:, 0:512], start=True, stop=True),
                    nc.tensor.matmul(psE2[0:1, 0:512], onesc[:], sqb[:, 512:1024], start=True, stop=True))[-1],
                    deps=[sil, prev_ev.get("psE")])
                prev_ev["sqb"] = ssm
                nr1 = seq.step("vector", lambda eng: (
                    eng.tensor_copy(ssrow[0:1, 0:512], psE[0:1, 0:512]),
                    eng.tensor_copy(ssrow[0:1, 512:1024], psE2[0:1, 0:512]))[-1],
                    deps=[ssm, prev_ev.get("nr2")])
                nr2 = seq.step("scalar", lambda eng: (
                    b.fence(eng, "s", eng.activation(rrow[:], ssrow[:], AF.Ln)),
                    eng.activation(rrow[:], rrow[:], AF.Exp, scale=-0.5))[-1],
                    deps=[nr1, prev_ev.get("nr3")])
                nr3 = seq.step("vector", lambda eng: (
                    b.fence(eng, "v", eng.tensor_scalar(rrow[:], rrow[:], 1e6, None, ALU.min)),
                    eng.tensor_copy(rrbf[:], rrow[:]))[-1], deps=[nr2, prev_ev.get("bcm")])
                prev_ev["psE"] = nr1
                prev_ev["nr2"] = nr2
                prev_ev["nr3"] = nr3
                bcm = seq.step("tensor", lambda eng: (
                    nc.tensor.matmul(psQ[:, 0:512], ones1[0:1, :], rrbf[0:1, 0:512], start=True, stop=True),
                    nc.tensor.matmul(psW[:, 0:512], ones1[0:1, :], rrbf[0:1, 512:1024], start=True, stop=True))[-1],
                    deps=[nr3, ev])
                prev_ev["bcm"] = bcm
                fin = seq.step("vector", lambda eng: (
                    eng.tensor_tensor(stg[:, 0:512], ysb[:, 0:512], psQ[:, 0:512], ALU.mult),
                    eng.tensor_tensor(stg[:, 512:1024], ysb[:, 512:1024], psW[:, 0:512], ALU.mult))[-1],
                    deps=[bcm, sil, prev_ev.get("stg")])
                prev_ev["free_psQW"] = fin
                prev_ev["ysb"] = fin
                st = seq.dma("gpsimd_dma", lambda eng, dst=dst, m=m, t0=t0: [
                    eng.dma_start(dst[128 * m:128 * (m + 1), t0:t0 + QT], stg[:])], deps=[fin], n=1)
                prev_ev["stg"] = st
                last_store = st
        prev_projmm = last_store

    seq.barrier_all()
    ctxA.close()

    ctxIN.close()
    ctxB = contextlib.ExitStack()
    qfq = b.sbuf("a1_qfq", [128, 8 * QT], BF16, ctxB)
    kfq = b.sbuf("a1_kfq", [128, 8 * QT], BF16, ctxB)
    vtok = b.sbuf("a1_vtok", [128, 8 * 2048], BF16, ctxB)
    ktok = b.sbuf("a1_ktok", [128, 8 * 1024], BF16, ctxB)
    gstok = b.sbuf("a1_gstok", [128, 8 * 2048], BF16, ctxB)
    Sk = b.sbuf("a1_Sk", [128, 8 * 256], F32, ctxB)
    Skb = b.sbuf("a1_Skb", [128, 8 * 256], BF16, ctxB)
    lgsb = b.sbuf("a1_lgsb", [128, 8], F32, ctxB)
    gCrow = b.sbuf("a1_gCrow", [1, 8], F32, ctxB)
    gCcol = b.sbuf("a1_gCcol", [128, 8], F32, ctxB)
    gcol8 = b.sbuf("a1_gcol8", [128, 8], F32, ctxB)
    Jsb = b.sbuf("a1_Jsb", [128, 128], F32, ctxB)
    tE1 = b.sbuf("a1_tE1", [128, 128], F32, ctxB)
    tE2 = b.sbuf("a1_tE2", [128, 128], F32, ctxB)
    Ep = b.sbuf("a1_Ep", [128, 128], F32, ctxB)
    En = b.sbuf("a1_En", [128, 128], F32, ctxB)
    LamS = b.sbuf("a1_LamS", [128, 128], F32, ctxB)
    MpB = b.sbuf("a1_MpB", [128, 128], BF16, ctxB)
    Gp = b.sbuf("a1_Gp", [128, 128], F32, ctxB)
    Hp = b.sbuf("a1_Hp", [128, 128], F32, ctxB)
    Zs = b.sbuf("a1_Zs", [128, 128], F32, ctxB)
    Tpt = b.sbuf("a1_Tpt", [128, 128], F32, ctxB)
    xa_sb = b.sbuf("a1_xa", [128, 64], F32, ctxB)
    t7 = b.sbuf("a1_t7", [128, 256], F32, ctxB)
    RhsS = b.sbuf("a1_Rhs", [128, 256], F32, ctxB)
    Ub = b.sbuf("a1_Ub", [128, 256], BF16, ctxB)
    Osb = b.sbuf("a1_Osb", [128, 256], F32, ctxB)
    sqO = b.sbuf("a1_sqO", [128, 256], F32, ctxB)
    rmsc = b.sbuf("a1_rmsc", [128, 2], F32, ctxB)
    bgc = b.sbuf("a1_bgc", [128, 1], F32, ctxB)
    ktsc = b.sbuf("a1_ktsc", [128, 128], BF16, ctxB)
    OgB = b.sbuf("a1_OgB", [128, 256], BF16, ctxB)

    pM = b.psum2("a1_pM", [128, 128], F32, ctxB)
    pQ2 = b.psum2("a1_pQ2", [128, 128], F32, ctxB)
    pJ = b.psum2("a1_pJ", [128, 128], F32, ctxB)
    pJ2 = b.psum2("a1_pJ2", [128, 128], F32, ctxB)
    pZ = b.psum2("a1_pZ", [128, 128], F32, ctxB)
    pU = b.psum2("a1_pU", [128, 256], F32, ctxB)
    pP = b.psum2("a1_pP", [128, 256], F32, ctxB)
    pG8 = b.psum2("a1_pG8", [128, 16], F32, ctxB)

    vz = seq.step("vector", lambda eng: (
        eng.memset(Sk[:], 0.0), eng.memset(Skb[:], 0.0), eng.memset(Gp[:], 0.0),
        eng.memset(pJ[:], 0.0))[-1], deps=[])

    hold = {"skb": vz, "zero": vz}
    prev = {}
    for qi in range(L1A_NQ):
        t0 = qi * QT
        bload = seq.dma("sync", lambda eng, t0=t0: [
            eng.dma_start(qfq[:].rearrange("p (r t) -> p r t", r=8),
                          q_d.rearrange("(r p) t -> p r t", p=128)[:, :, t0:t0 + QT]),
            eng.dma_start(kfq[:].rearrange("p (r t) -> p r t", r=8),
                          k_d.rearrange("(r p) t -> p r t", p=128)[:, :, t0:t0 + QT]),
        ], deps=[prev.get("lastA2") or None, prev.get("q_done")], n=2)
        btr = seq.dma("sync", lambda eng, t0=t0: [
            eng.dma_start(vtok[:, tt * 2048:(tt + 1) * 2048],
                          v_d[:, t0 + 128 * tt: t0 + 128 * (tt + 1)], transpose=True)
            for tt in range(8)] + [
            eng.dma_start(ktok[:, tt * 1024:(tt + 1) * 1024],
                          k_d[:, t0 + 128 * tt: t0 + 128 * (tt + 1)], transpose=True)
            for tt in range(8)] + [
            eng.dma_start(gstok[:, tt * 2048:(tt + 1) * 2048],
                          gs_d[:, t0 + 128 * tt: t0 + 128 * (tt + 1)], transpose=True)
            for tt in range(8)], deps=[prev.get("tok_done")], n=24)
        og_last = []
        ncc_l = NTQ if L1A_DBG is None else L1A_DBG[0]
        for cc in range(ncc_l):
            gtile = qi * NTQ + cc
            gcol0 = gtile * 8
            lgm = seq.step("tensor", lambda eng, gcol0=gcol0: (
                nc.tensor.matmul(pG8[:, 0:8], cums[:], gbuf[:, gcol0:gcol0 + 8], start=True, stop=True),
                nc.tensor.matmul(pG8[0:1, 8:16], onesc[:], gbuf[:, gcol0:gcol0 + 8], start=True, stop=True))[-1],
                deps=[prev.get("pG8"), (v_na if qi == 0 and cc == 0 else None)])
            lgv = seq.step("vector", lambda eng: eng.tensor_copy(lgsb[:], pG8[:, 0:8]), deps=[lgm])
            gce = seq.step("scalar", lambda eng: (
                eng.activation(gCrow[:], pG8[0:1, 8:16], AF.Exp),
                eng.activation(gcol8[:], lgsb[:], AF.Exp))[-1], deps=[lgv, lgm])
            gcb = seq.step("tensor", lambda eng: nc.tensor.matmul(
                pG8[:, 8:16], onesf1[0:1, :], gCrow[0:1, :], start=True, stop=True), deps=[gce])
            gcv = seq.step("vector", lambda eng: eng.tensor_copy(gCcol[:], pG8[:, 8:16]), deps=[gcb])
            prev["pG8"] = gcv
            nh_l = H if L1A_DBG is None else L1A_DBG[1]
            for h in range(nh_l):
                ksl = kfq[:, h * QT + 128 * cc: h * QT + 128 * (cc + 1)]
                qsl = qfq[:, h * QT + 128 * cc: h * QT + 128 * (cc + 1)]
                vsl = vtok[:, cc * 2048 + 256 * h: cc * 2048 + 256 * (h + 1)]
                ktsl = ktok[:, cc * 1024 + 128 * h: cc * 1024 + 128 * (h + 1)]
                gssl = gstok[:, cc * 2048 + 256 * h: cc * 2048 + 256 * (h + 1)]
                bcol = bbuf[:, gcol0 + h: gcol0 + h + 1]
                skslab = slice(h * 256, (h + 1) * 256)

                akk = seq.step("tensor", lambda eng, ksl=ksl: nc.tensor.matmul(
                    pM[:], ksl, ksl, start=True, stop=True), deps=[bload, prev.get("pM")])
                akq = seq.step("tensor", lambda eng, ksl=ksl, qsl=qsl: nc.tensor.matmul(
                    pQ2[:], ksl, qsl, start=True, stop=True), deps=[bload, prev.get("pQ2")])
                jt = seq.step("tensor", lambda eng, h=h: nc.tensor.transpose(
                    pJ[:], lgsb[:, h:h + 1].broadcast_to([128, 128]), identf[:]),
                    deps=[lgv, prev.get("pJ"), hold.get("zero")])
                jev = seq.step("vector", lambda eng: eng.tensor_copy(Jsb[:], pJ[:]),
                               deps=[jt, prev.get("Jsb")])
                j2t = seq.step("tensor", lambda eng: nc.tensor.transpose(pJ2[:], Jsb[:], identf[:]),
                               deps=[jev, prev.get("pJ2")])
                ep1 = seq.step("vector", lambda eng: eng.tensor_tensor(tE1[:], Jsb[:], am_i[:], ALU.add),
                               deps=[jev, prev.get("tE1")])
                ep2 = seq.step("vector", lambda eng: eng.tensor_tensor(tE2[:], tE1[:], pJ2[:], ALU.subtract),
                               deps=[ep1, j2t, prev.get("tE2")])
                epx = seq.step("scalar", lambda eng: eng.activation(Ep[:], tE2[:], AF.Exp),
                               deps=[ep2, prev.get("Ep")])
                en1 = seq.step("vector", lambda eng: eng.tensor_tensor(tE1[:], pJ2[:], am_sL[:], ALU.add),
                               deps=[j2t, ep2])
                en2 = seq.step("vector", lambda eng: eng.tensor_tensor(tE2[:], tE1[:], Jsb[:], ALU.subtract),
                               deps=[en1, epx])
                enx = seq.step("scalar", lambda eng: eng.activation(En[:], tE2[:], AF.Exp),
                               deps=[en2, prev.get("En")])
                lam1 = seq.step("vector", lambda eng: eng.tensor_tensor(tE1[:], En[:], pM[:], ALU.mult),
                                deps=[enx, akk, en2])
                lam2 = seq.step("vector", lambda eng, bcol=bcol: eng.tensor_scalar(
                    LamS[:], tE1[:], bcol, None, ALU.mult), deps=[lam1, prev.get("LamS")])
                mp1 = seq.step("vector", lambda eng: eng.tensor_tensor(tE2[:], Ep[:], pQ2[:], ALU.mult),
                               deps=[epx, akq, enx])
                mp2 = seq.step("vector", lambda eng: eng.tensor_copy(MpB[:], tE2[:]),
                               deps=[mp1, prev.get("MpB")])
                gp1 = seq.step("vector", lambda eng: (
                    eng.tensor_copy(Gp[0:64, 0:64], LamS[0:64, 0:64]),
                    eng.tensor_copy(Gp[64:128, 64:128], LamS[64:128, 64:128]))[-1],
                    deps=[lam2, prev.get("Gp")])
                h0t = seq.step("tensor", lambda eng: nc.tensor.transpose(pZ[:], Gp[:], identf[:]),
                               deps=[gp1, prev.get("pZ")])
                h0e = seq.step("vector", lambda eng: eng.tensor_copy(Hp[:], pZ[:]), deps=[h0t, prev.get("Hp")])
                z0 = seq.step("vector", lambda eng: eng.tensor_tensor(Zs[:], identf[:], Gp[:], ALU.subtract),
                              deps=[gp1, prev.get("Zs")])
                gj, hj, zj = gp1, h0e, z0
                for lev in range(5):
                    sq1 = seq.step("tensor", lambda eng: nc.tensor.matmul(
                        pZ[:], Hp[:], Gp[:], start=True, stop=True), deps=[hj, gj, h0e, en1])
                    sq2 = seq.step("tensor", lambda eng: nc.tensor.matmul(
                        pJ2[:], Gp[:], Hp[:], start=True, stop=True), deps=[hj, gj, en1])
                    ge = seq.step("vector", lambda eng: eng.tensor_copy(Gp[:], pZ[:]),
                                  deps=[sq1, sq2, z0])
                    he = seq.step("vector", lambda eng: eng.tensor_copy(Hp[:], pJ2[:]), deps=[sq2])
                    zm = seq.step("tensor", lambda eng: nc.tensor.matmul(
                        pU[:, 0:128], Hp[:], Zs[:], start=True, stop=True), deps=[he, zj, prev.get("pU")])
                    za = seq.step("vector", lambda eng: eng.tensor_tensor(Zs[:], Zs[:], pU[:, 0:128], ALU.add),
                                  deps=[zm])
                    gj, hj, zj = ge, he, za
                    prev["pU"] = za
                tpt = seq.step("tensor", lambda eng: nc.tensor.transpose(pZ[:], Zs[:], identf[:]),
                               deps=[zj, gj])
                tpe = seq.step("vector", lambda eng: eng.tensor_copy(Tpt[:], pZ[:]), deps=[tpt, prev.get("Tpt")])
                sa = seq.step("tensor", lambda eng: nc.tensor.matmul(
                    pU[0:64, 0:64], LamS[64:128, 0:64], Tpt[64:128, 64:128], start=True, stop=True),
                    deps=[tpe, lam2, prev.get("pU")])
                sae = seq.step("vector", lambda eng: eng.tensor_copy(xa_sb[0:64, 0:64], pU[0:64, 0:64]),
                               deps=[sa])
                sb_ = seq.step("tensor", lambda eng: nc.tensor.matmul(
                    pU[0:64, 64:128], Zs[0:64, 0:64], xa_sb[0:64, 0:64], start=True, stop=True),
                    deps=[sae, zj])
                sbe = seq.step("vector", lambda eng: eng.tensor_scalar(
                    Tpt[0:64, 64:128], pU[0:64, 64:128], -1.0, None, ALU.mult), deps=[sb_, tpe])
                p3 = seq.step("tensor", lambda eng, ksl=ksl, skslab=skslab: nc.tensor.matmul(
                    pP[:], ksl, Skb[:, skslab], start=True, stop=True),
                    deps=[bload, hold.get("skb"), prev.get("pP")])
                bg = seq.step("vector", lambda eng, h=h, bcol=bcol: eng.tensor_tensor(
                    bgc[:], bcol, gcol8[:, h:h + 1], ALU.mult), deps=[gce, prev.get("bgc")])
                r1 = seq.step("vector", lambda eng: eng.tensor_scalar(t7[:], pP[:], bgc[:], None, ALU.mult),
                              deps=[p3, bg, prev.get("t7")])
                r2 = seq.step("vector", lambda eng, vsl=vsl, bcol=bcol: eng.scalar_tensor_tensor(
                    RhsS[:], vsl, bcol, t7[:], ALU.mult, ALU.subtract), deps=[r1, btr, prev.get("Rhs")])
                um = seq.step("tensor", lambda eng: nc.tensor.matmul(
                    pU[:, 0:256], Tpt[:], RhsS[:], start=True, stop=True), deps=[sbe, r2, sbe])
                ue = seq.step("vector", lambda eng: eng.tensor_copy(Ub[:], pU[:, 0:256]),
                              deps=[um, prev.get("Ub")])
                om = seq.step("tensor", lambda eng: nc.tensor.matmul(
                    pP[:], MpB[:], Ub[:], start=True, stop=True), deps=[ue, mp2, r1])
                p2 = seq.step("tensor", lambda eng, qsl=qsl, skslab=skslab: nc.tensor.matmul(
                    pM[:] if False else pU[:, 0:256], qsl, Skb[:, skslab], start=True, stop=True),
                    deps=[bload, hold.get("skb"), ue])
                oa1 = seq.step("vector", lambda eng, h=h: eng.tensor_scalar(
                    t7[:], pU[:, 0:256], gcol8[:, h:h + 1], None, ALU.mult), deps=[p2, r2])
                oa2 = seq.step("vector", lambda eng: eng.tensor_tensor(Osb[:], t7[:], pP[:], ALU.add),
                               deps=[oa1, om, prev.get("Osb")])
                prev["pP"] = oa2
                prev["pU"] = oa1
                dec = seq.step("vector", lambda eng, ktsl=ktsl: eng.tensor_scalar(
                    ktsc[:], ktsl, Ep[:, 127:128], None, ALU.mult), deps=[epx, btr, prev.get("ktsc")])
                sup = seq.step("tensor", lambda eng: nc.tensor.matmul(
                    pP[:], ktsc[:], Ub[:], start=True, stop=True), deps=[dec, ue, oa2])
                skbu = seq.step("vector", lambda eng, h=h, skslab=skslab: (
                    b.fence(eng, "v", eng.scalar_tensor_tensor(
                        Sk[:, skslab], Sk[:, skslab], gCcol[:, h:h + 1],
                        pP[:], ALU.mult, ALU.add)),
                    eng.tensor_copy(Skb[:, skslab], Sk[:, skslab]))[-1], deps=[sup, gcv])
                hold["skb"] = skbu
                prev["pP" ] = skbu
                rm1 = seq.step("scalar", lambda eng: eng.activation(sqO[:], Osb[:], AF.Square),
                               deps=[oa2, prev.get("sqO")])
                rm2 = seq.step("vector", lambda eng: (
                    b.fence(eng, "v", eng.reduce_sum(rmsc[:, 0:1], sqO[:], axis=AX.X)),
                    eng.tensor_scalar(rmsc[:, 0:1], rmsc[:, 0:1], 1.0 / 256, EPSR, ALU.mult, ALU.add))[-1],
                    deps=[rm1])
                rm3 = seq.step("scalar", lambda eng: eng.activation(rmsc[:, 1:2], rmsc[:, 0:1], AF.Sqrt),
                               deps=[rm2])
                rm4 = seq.step("vector", lambda eng: eng.reciprocal(rmsc[:, 0:1], rmsc[:, 1:2]), deps=[rm3])
                gat = seq.step("vector", lambda eng, gssl=gssl: (
                    b.fence(eng, "v", eng.tensor_scalar(t7[:], Osb[:], rmsc[:, 0:1], None, ALU.mult)),
                    eng.tensor_tensor(OgB[:], t7[:], gssl, ALU.mult))[-1],
                    deps=[rm4, btr, prev.get("OgB")])
                prev["t7"] = gat
                if L1A_SPILL:
                    idx = (qi * NTQ + cc) * 8 + h
                    sp0 = seq.dma("gpsimd_dma", lambda eng, idx=idx: [
                        eng.dma_start(dbg_d[idx, 0], Ub[:])], deps=[ue, om], n=1)
                    sp1 = seq.dma("gpsimd_dma", lambda eng, idx=idx: [
                        eng.dma_start(dbg_d[idx, 1], Osb[:])], deps=[oa2, gat], n=1)
                    sp2 = seq.dma("gpsimd_dma", lambda eng, idx=idx: [
                        eng.dma_start(dbg_d[idx, 2][:, 0:128], LamS[:])], deps=[lam2, sb_], n=1)
                    sp3 = seq.dma("gpsimd_dma", lambda eng, idx=idx: [
                        eng.dma_start(dbg_d[idx, 3], RhsS[:])], deps=[r2, um], n=1)
                    prev["_sp"] = (sp0, sp1, sp2, sp3)
                ogst = seq.dma("gpsimd_dma", lambda eng, t0=t0, cc=cc, h=h: [
                    eng.dma_start(og_d[t0 + 128 * cc: t0 + 128 * (cc + 1), 256 * h:256 * (h + 1)], OgB[:])],
                    deps=[gat], n=1)
                og_last.append(ogst)
                prev["OgB"] = ogst
                prev["pM"] = lam1
                prev["pQ2"] = mp1
                prev["pJ"] = ep2 if False else jev
                prev["pJ2"] = he
                prev["tE1"] = lam2
                prev["tE2"] = mp2
                prev["Ep"] = dec
                prev["En"] = lam1
                prev["LamS"] = sb_ if not L1A_SPILL else prev["_sp"][2]
                prev["MpB"] = om
                prev["Gp"] = sq1
                prev["Hp"] = sq2 if False else zm
                prev["Zs"] = sb_
                prev["Tpt"] = um
                prev["bgc"] = r1
                prev["Rhs"] = um if not L1A_SPILL else prev["_sp"][3]
                prev["Ub"] = sup if not L1A_SPILL else prev["_sp"][0]
                prev["Osb"] = gat if not L1A_SPILL else prev["_sp"][1]
                prev["sqO"] = rm2
                prev["ktsc"] = sup
                prev["pZ"] = tpe
                prev["Hp"] = he
                prev["Gp"] = ge
                prev["pJ"] = jev
                prev["Jsb"] = en2
        prev["q_done"] = og_last[-1]
        prev["tok_done"] = og_last[-1]
        prev["lastA2"] = og_last[-1]
    seq.barrier_all()
    ctxB.close()

    ctxC = contextlib.ExitStack()
    woS = b.sbuf("a1_woS", [128, 16 * 1024], BF16, ctxC)
    wof = b.sbuf("a1_wof", [128, 1024], F32, ctxC)
    onwc = b.sbuf("a1_onw", [128, 2], F32, ctxC)
    ogf = b.sbuf("a1_ogf", [128, 16 * 1024], BF16, ctxC)
    s1sb = b.sbuf("a1_s1sb", [128, 1024], F32, ctxC)
    psQ3 = b.psum2("a1_psQ3", [128, 512], F32, ctxC)
    psW3 = b.psum2("a1_psW3", [128, 512], F32, ctxC)

    onl = seq.dma("sync", lambda eng: [eng.dma_start(onwc[:], W["l1_onw"])], n=1)
    prev_sc = None
    for kc in range(16):
        wld2 = seq.dma("sync", lambda eng, kc=kc: [
            eng.dma_start(wof[:], W["l1_wo"][128 * kc:128 * (kc + 1), :])], deps=[prev_sc], n=1)
        sc = seq.step("vector", lambda eng, kc=kc: eng.tensor_scalar(
            woS[:, 1024 * kc:1024 * (kc + 1)], wof[:], onwc[:, kc % 2:kc % 2 + 1], None, ALU.mult),
            deps=[wld2, onl])
        prev_sc = sc
    wo_ready = prev_sc

    prev_pj = None
    prev_ev = None
    for qi in (range(L1A_NQ) if L1A_DBG is None else []):
        t0 = qi * QT
        ogl = seq.dma("sync", lambda eng, t0=t0: [
            eng.dma_start(ogf[:, kc * 1024:(kc + 1) * 1024],
                          og_d[t0:t0 + QT, 128 * kc:128 * (kc + 1)], transpose=True)
            for kc in range(16)], deps=[prev_pj], n=16)
        for i in range(NTQ):
            def womm(eng, i=i):
                for nh in range(2):
                    ps = psQ3 if nh == 0 else psW3
                    for kc in range(16):
                        inst = nc.tensor.matmul(ps[:, 0:512],
                                                ogf[:, kc * QT + 128 * i: kc * QT + 128 * (i + 1)],
                                                woS[:, kc * 1024 + 512 * nh: kc * 1024 + 512 * (nh + 1)],
                                                start=(kc == 0), stop=(kc == 15))
                return inst
            pj = seq.step("tensor", womm, deps=[ogl, wo_ready, prev_ev])
            ev = seq.step("vector", lambda eng: (
                eng.tensor_copy(s1sb[:, 0:512], psQ3[:, 0:512]),
                eng.tensor_copy(s1sb[:, 512:1024], psW3[:, 0:512]))[-1], deps=[pj])
            st = seq.dma("gpsimd_dma", lambda eng, t0=t0, i=i: [
                eng.dma_start(s1_dst[t0 + 128 * i: t0 + 128 * (i + 1), :], s1sb[:])], deps=[ev], n=1)
            prev_ev = st
            prev_pj = pj
    seq.barrier_all()
    ctxC.close()




def emit_l1ffn(b: Bld, xn1_bf, silg_d):
    nc = b.nc
    W = b.dram
    QT = 512
    NQ = F_NT
    knd = "ExternalOutput" if F_DBG else "Internal"
    if F_DBG:
        odbg_d = nc.dram_tensor("f_odbg", [512, T], BF16, kind=knd).ap()
        ddbg_d = nc.dram_tensor("f_ddbg", [8, T], F32, kind=knd).ap()
        pdbg_d = nc.dram_tensor("f_pdbg", [8 * 128, 512], BF16, kind=knd).ap()
        vdbg_d = nc.dram_tensor("f_vdbg", [128, 8 * 128], BF16, kind=knd).ap()
        podbg_d = nc.dram_tensor("f_podbg", [128, 512], F32, kind=knd).ap()
    qr_d = nc.dram_tensor("f_qr", [512, T], BF16, kind=knd).ap()
    kr_d = nc.dram_tensor("f_kr", [512, T], BF16, kind=knd).ap()
    v_d = nc.dram_tensor("f_vd", [512, T], BF16, kind=knd).ap()
    xatt_d = nc.dram_tensor("f_xatt", [1024, T], F32, kind=knd).ap()
    x2f_d = nc.dram_tensor("f_x2f", [1024, T], F32, kind=knd).ap()
    x2bf_d = nc.dram_tensor("f_x2bf", [1024, T], BF16, kind=knd).ap()

    gctx = ExitStack()
    onesc = b.sbuf("ff_onesc", [128, 1], F32, ctx=gctx)
    onesr = b.sbuf("ff_onesr", [1, 128], F32, ctx=gctx)

    for l in range(F_NL):
        ctxA = ExitStack()
        wq_sb = b.sbuf(f"ffa{l}_wqkv", [128, 8 * 1536], BF16, ctx=ctxA)
        wq2_sb = b.sbuf(f"ffa{l}_wqkv2", [128, 8 * 1024], BF16, ctx=ctxA)
        qscol = b.sbuf(f"ffa{l}_qs", [128, 4], F32, ctx=ctxA)
        tbl = [b.sbuf(f"ffa{l}_tbl{i}", [128, 4 * 512], F32, ctx=ctxA) for i in range(2)]
        xcur = [b.sbuf(f"ffa{l}_x{i}", [128, 8 * 512], BF16, ctx=ctxA) for i in range(2)]
        xsh = [b.sbuf(f"ffa{l}_xs{i}", [128, 4 * 512], BF16, ctx=ctxA) for i in range(2)]
        sqsb = [b.sbuf(f"ffa{l}_sq{i}", [128, 512], F32, ctx=ctxA) for i in range(2)]
        aob = [b.sbuf(f"ffa{l}_ob{i}", [128, 512], BF16, ctx=ctxA) for i in range(2)]
        tw = [b.sbuf(f"ffa{l}_t{i}", [128, 512], F32, ctx=ctxA) for i in range(4)]
        rsA = b.sbuf(f"ffa{l}_rA", [1, 512], F32, ctx=ctxA)
        rsB = b.sbuf(f"ffa{l}_rB", [1, 512], F32, ctx=ctxA)
        rsA2 = b.sbuf(f"ffa{l}_rA2", [1, 512], F32, ctx=ctxA)
        rsB2 = b.sbuf(f"ffa{l}_rB2", [1, 512], F32, ctx=ctxA)
        rsA3 = b.sbuf(f"ffa{l}_rA3", [1, 512], F32, ctx=ctxA)
        rsB3 = b.sbuf(f"ffa{l}_rB3", [1, 512], F32, ctx=ctxA)
        bbS = b.sbuf(f"ffa{l}_bbS", [128, 512], F32, ctx=ctxA)
        psP = [b.psum2(f"ffa{l}_pP{i}", [128, 512], ctx=ctxA) for i in range(2)]
        psP2 = [b.psum2(f"ffa{l}_pQ{i}", [128, 512], ctx=ctxA) for i in range(2)]
        psN = b.psum2(f"ffa{l}_pN", [1, 512], ctx=ctxA)
        psN2 = b.psum2(f"ffa{l}_pN2", [1, 512], ctx=ctxA)
        psBB = b.psum2(f"ffa{l}_pB", [128, 512], ctx=ctxA)

        seq = Seq(b)
        ini = None
        if l == 0:
            ini = seq.step("vector", lambda eng: (
                eng.memset(onesc[:], 1.0),
                eng.memset(onesr[:], 1.0))[-1])
        wld = seq.dma("sync", lambda eng, l=l: [
            eng.dma_start(wq_sb[:], W["f_wqkv"][l].rearrange("(kc p) f -> p kc f", p=128)),
            eng.dma_start(wq2_sb[:], W["f_wqkv2"][l].rearrange("(kc p) f -> p kc f", p=128))], n=2)
        qsld = seq.dma("sync", lambda eng, l=l: [
            eng.dma_start(qscol[:, 0:1], W["f_qs"][l]),
            eng.dma_start(qscol[:, 1:2], W["f_qs_sw"][l]),
            eng.dma_start(qscol[:, 2:3], W["f_ks"][l]),
            eng.dma_start(qscol[:, 3:4], W["f_ks_sw"][l])], n=4)

        pr_rhs = [None, None]
        pr_sh = None
        pr_tblread = [None, None]
        pr_ppc = [(), ()]
        pr_pp2c = [(), ()]
        pr_sqc = [(), ()]
        pr_rw = None
        pr_rv = None
        pr_bc = None
        pr_rotbb = None
        pr_bbread = None
        pr_aost = [None, None]
        gp = gp2 = gsq = gao = 0

        for t in range(NQ):
            t0 = QT * t
            tb = t % 2
            if l == 0:
                xld = seq.dma("sync", lambda eng, tb=tb, t0=t0: [
                    eng.dma_start(xcur[tb][:, 512 * c:512 * (c + 1)],
                                  xn1_bf[t0:t0 + QT, 128 * c:128 * (c + 1)], transpose=True)
                    for c in range(8)], deps=[pr_rhs[tb], pr_sh], n=8)

                def shmk(eng, tb=tb, pb=(t - 1) % 2, first=(t == 0)):
                    ii = None
                    for ci in range(4):
                        c = ci + 4
                        eng.tensor_copy(xsh[tb][:, 512 * ci + 1:512 * ci + 512],
                                        xcur[tb][:, 512 * c:512 * c + 511])
                        if first:
                            ii = eng.memset(xsh[tb][:, 512 * ci:512 * ci + 1], 0.0)
                        else:
                            ii = eng.tensor_copy(xsh[tb][:, 512 * ci:512 * ci + 1],
                                                 xcur[pb][:, 512 * c + 511:512 * c + 512])
                    return ii
                sh = seq.step("vector", shmk, deps=[xld, pr_rhs[tb]])
                pr_sh = sh
            else:
                sh = None
                if t == 0:
                    z0 = seq.step("vector", lambda eng, tb=tb: [
                        eng.memset(xcur[tb][:, 512 * c:512 * c + 1], 0.0) for c in range(4, 8)][-1],
                        deps=[pr_rhs[tb]])

                    def x0ld(eng, tb=tb):
                        out = []
                        for c in range(4):
                            out.append(eng.dma_start(xcur[tb][:, 512 * c:512 * (c + 1)],
                                                     x2bf_d[128 * c:128 * (c + 1), 0:QT]))
                        for c in range(4, 8):
                            out.append(eng.dma_start(xcur[tb][:, 512 * c + 1:512 * (c + 1)],
                                                     x2bf_d[128 * c:128 * (c + 1), 0:QT - 1]))
                        return out
                    xld = seq.dma("sync", x0ld, deps=[pr_rhs[tb], z0], n=8)
                else:
                    def xnld(eng, tb=tb, t0=t0):
                        out = []
                        for c in range(4):
                            out.append(eng.dma_start(xcur[tb][:, 512 * c:512 * (c + 1)],
                                                     x2bf_d[128 * c:128 * (c + 1), t0:t0 + QT]))
                        for c in range(4, 8):
                            out.append(eng.dma_start(xcur[tb][:, 512 * c:512 * (c + 1)],
                                                     x2bf_d[128 * c:128 * (c + 1), t0 - 1:t0 + QT - 1]))
                        return out
                    xld = seq.dma("sync", xnld, deps=[pr_rhs[tb]], n=8)
            tld = seq.dma("sync", lambda eng, tb=tb, t0=t0: [
                eng.dma_start(tbl[tb][:, 0:512], W["rot_csq"][:, t0:t0 + QT]),
                eng.dma_start(tbl[tb][:, 512:1024], W["rot_ssq"][:, t0:t0 + QT]),
                eng.dma_start(tbl[tb][:, 1024:1536], W["rot_csk"][:, t0:t0 + QT]),
                eng.dma_start(tbl[tb][:, 1536:2048], W["rot_ssk"][:, t0:t0 + QT])],
                deps=[pr_tblread[tb]], n=4)

            def rhs_c(kc, tb=tb):
                if l == 0 and kc >= 4:
                    return xsh[tb][:, 512 * (kc - 4):512 * (kc - 3)]
                return xcur[tb][:, 512 * kc:512 * (kc + 1)]

            last_mm = None
            last_rot = None
            for kind in range(3):
                for oc in range(4):
                    pp = gp % 2; gp += 1
                    wcol = 512 * kind + 128 * oc

                    def mmR(eng, pp=pp, wcol=wcol, tb=tb):
                        ii = None
                        for kc in range(8):
                            ii = nc.tensor.matmul(psP[pp][:], wq_sb[:, 1536 * kc + wcol:1536 * kc + wcol + 128],
                                                  rhs_c(kc, tb), start=(kc == 0), stop=(kc == 7))
                        return ii
                    mm_r = seq.step("tensor", mmR, deps=[xld, sh, wld] + list(pr_ppc[pp]))
                    last_mm = mm_r
                    if kind == 2:
                        ao = gao % 2; gao += 1
                        cpv = seq.step("scalar", lambda eng, ao=ao, pp=pp: eng.activation(
                            aob[ao][:], psP[pp][:], AF.Copy), deps=[mm_r, pr_aost[ao]])
                        sto = seq.dma("gpsimd_dma", lambda eng, ao=ao, oc=oc, t0=t0: eng.dma_start(
                            v_d[128 * oc:128 * (oc + 1), t0:t0 + QT], aob[ao][:]), deps=[cpv], n=1)
                        pr_ppc[pp] = (cpv,)
                        pr_aost[ao] = sto
                        continue
                    pp2 = gp2 % 2; gp2 += 1
                    w2col = 512 * kind + 128 * oc

                    def mmS(eng, pp2=pp2, w2col=w2col, tb=tb):
                        ii = None
                        for kc in range(8):
                            ii = nc.tensor.matmul(psP2[pp2][:], wq2_sb[:, 1024 * kc + w2col:1024 * kc + w2col + 128],
                                                  rhs_c(kc, tb), start=(kc == 0), stop=(kc == 7))
                        return ii
                    mm_s = seq.step("tensor", mmS, deps=[xld, sh, wld] + list(pr_pp2c[pp2]))
                    sb2 = gsq % 2; gsq += 1
                    sqs = seq.step("scalar", lambda eng, sb2=sb2, pp=pp: eng.activation(
                        sqsb[sb2][:], psP[pp][:], AF.Square), deps=[mm_r] + list(pr_sqc[sb2]))
                    nm1 = seq.step("tensor", lambda eng, sb2=sb2: nc.tensor.matmul(
                        psN[0:1, :], onesc[0:64, :], sqsb[sb2][0:64, :], start=True, stop=True),
                        deps=[sqs, pr_rw])
                    nm2 = seq.step("tensor", lambda eng, sb2=sb2: nc.tensor.matmul(
                        psN2[0:1, :], onesc[64:128, :], sqsb[sb2][64:128, :], start=True, stop=True),
                        deps=[sqs, pr_rw])
                    rw = seq.step("scalar", lambda eng: (
                        b.fence(eng, "s", eng.activation(rsA[:], psN[0:1, :], AF.Ln)),
                        b.fence(eng, "s", eng.activation(rsB[:], psN2[0:1, :], AF.Ln)),
                        eng.activation(rsA3[:], rsA[:], AF.Exp, scale=-0.5),
                        eng.activation(rsB3[:], rsB[:], AF.Exp, scale=-0.5))[-1],
                        deps=[nm1, nm2, pr_rv, pr_rw])
                    rv = seq.step("vector", lambda eng: (
                        eng.tensor_scalar(rsA2[:], rsA3[:], 1e12, None, ALU.min),
                        eng.tensor_scalar(rsB2[:], rsB3[:], 1e12, None, ALU.min))[-1], deps=[rw, pr_bc])
                    bc = seq.step("tensor", lambda eng: (
                        nc.tensor.matmul(psBB[0:64, :], onesr[0:1, 0:64], rsA2[:], start=True, stop=True),
                        nc.tensor.matmul(psBB[64:128, :], onesr[0:1, 0:64], rsB2[:], start=True, stop=True))[-1],
                        deps=[rv, pr_rotbb])
                    bbe = seq.step("scalar", lambda eng: eng.activation(
                        bbS[:], psBB[:], AF.Copy), deps=[bc, pr_bbread])
                    ao = gao % 2; gao += 1
                    scol = 2 * kind
                    toff = 1024 * kind

                    def rotk(eng, pp=pp, pp2=pp2, ao=ao, tb=tb, scol=scol, toff=toff):
                        b.fence(eng, "v", eng.scalar_tensor_tensor(
                            tw[0][:], psP[pp][:], qscol[:, scol:scol + 1], bbS[:], ALU.mult, ALU.mult))
                        b.fence(eng, "v", eng.scalar_tensor_tensor(
                            tw[1][:], psP2[pp2][:], qscol[:, scol + 1:scol + 2], bbS[:], ALU.mult, ALU.mult))
                        b.fence(eng, "v", eng.tensor_tensor(
                            tw[2][:], tw[0][:], tbl[tb][:, toff:toff + 512], ALU.mult))
                        b.fence(eng, "v", eng.tensor_tensor(
                            tw[3][:], tw[1][:], tbl[tb][:, toff + 512:toff + 1024], ALU.mult))
                        return eng.tensor_tensor(aob[ao][:], tw[2][:], tw[3][:], ALU.add)
                    rot = seq.step("vector", rotk, deps=[bbe, mm_r, mm_s, tld, qsld, ini, pr_aost[ao]])
                    last_rot = rot
                    dst = qr_d if kind == 0 else kr_d
                    sto = seq.dma("gpsimd_dma", lambda eng, ao=ao, oc=oc, t0=t0, dst=dst: eng.dma_start(
                        dst[128 * oc:128 * (oc + 1), t0:t0 + QT], aob[ao][:]), deps=[rot], n=1)
                    pr_ppc[pp] = (sqs, rot)
                    pr_pp2c[pp2] = (rot,)
                    pr_sqc[sb2] = (nm1, nm2)
                    pr_rw = rw
                    pr_rv = rv
                    pr_bc = bc
                    pr_rotbb = bbe
                    pr_bbread = rot
                    pr_aost[ao] = sto
            pr_rhs[tb] = last_mm
            pr_tblread[tb] = last_rot
        seq.barrier_all()
        ctxA.close()

        ctxB = ExitStack()
        maskb = b.sbuf(f"ffb{l}_mask", [128, 8 * 512], BF16, ctx=ctxB)
        wo_sb = b.sbuf(f"ffb{l}_wo", [128, 4 * 1024], BF16, ctx=ctxB)
        qt = [b.sbuf(f"ffb{l}_q{i}", [64, 512], BF16, ctx=ctxB) for i in range(4)]
        kt = [b.sbuf(f"ffb{l}_k{i}", [64, 1024], BF16, ctx=ctxB) for i in range(4)]
        vpt = [b.sbuf(f"ffb{l}_v{i}", [128, 8 * 128], BF16, ctx=ctxB) for i in range(3)]
        onescb = b.sbuf(f"ffb{l}_ones", [128, 1], BF16, ctx=ctxB)
        clsb = [b.sbuf(f"ffb{l}_cl{i}", [128, 512], F32, ctx=ctxB) for i in range(4)]
        ebf = [b.sbuf(f"ffb{l}_e{i}", [128, 512], BF16, ctx=ctxB) for i in range(4)]
        psb = [b.sbuf(f"ffb{l}_p{i}", [128, 512], BF16, ctx=ctxB) for i in range(4)]
        rc = [b.sbuf(f"ffb{l}_rcv{i}", [1, 512], F32, ctx=ctxB) for i in range(2)]
        rc0 = b.sbuf(f"ffb{l}_rc0", [1, 512], F32, ctx=ctxB)
        rbS = b.sbuf(f"ffb{l}_rb", [64, 512], F32, ctx=ctxB)
        if F_DBG:
            podbg_sb = b.sbuf(f"ffb{l}_podbg", [128, 512], F32, ctx=ctxB)
        o_f = [b.sbuf(f"ffb{l}_of{i}", [128, 4 * 512], BF16, ctx=ctxB) for i in range(2)]
        if l == 0:
            xres = [b.sbuf(f"ffb{l}_xr{i}", [128, 8 * 512], BF16, ctx=ctxB) for i in range(2)]
        else:
            xres = [b.sbuf(f"ffb{l}_xr{i}", [128, 8 * 512], F32, ctx=ctxB) for i in range(2)]
        xo_sb = [b.sbuf(f"ffb{l}_xo{i}", [128, 512], F32, ctx=ctxB) for i in range(2)]
        psS = [b.psum2(f"ffb{l}_pS{i}", [128, 512], ctx=ctxB) for i in range(2)]
        psO = [b.psum2(f"ffb{l}_pO{i}", [128, 512], ctx=ctxB) for i in range(2)]
        psD = b.psum2(f"ffb{l}_pD", [1, 512], ctx=ctxB)
        psB = b.psum2(f"ffb{l}_pBc", [128, 512], ctx=ctxB)
        psX = [b.psum2(f"ffb{l}_pX{i}", [128, 512], ctx=ctxB) for i in range(2)]

        seq = Seq(b)
        vini = seq.step("vector", lambda eng: eng.memset(onescb[:], 1.0))
        mld = seq.dma("sync", lambda eng: eng.dma_start(maskb[:], W["fmask"]), n=1)
        wold = seq.dma("sync", lambda eng, l=l: eng.dma_start(
            wo_sb[:], W["f_wo"][l].rearrange("(kc p) f -> p kc f", p=128)), n=1)

        pr_sim_hb = [None] * 4
        pr_vpt = [None] * 3
        pr_clamp = [None, None]
        pr_exp = [None] * 4
        pr_mask = [None] * 4
        pr_mm2b = [None] * 4
        pr_divO = [None, None]
        pr_bcmO = [None, None]
        pr_bcm = None
        pr_bce = None
        pr_rbread = None
        pr_dnst = None
        pr_rec = None
        pr_pdbg = [None, None]
        pr_odbg = [None, None]
        pr_ofuse = [None, None]
        pr_resread = [None, None]
        pr_resp = [None, None]
        pr_stoxo = [None, None]
        ga = 0
        gam = 0
        gfo = 0
        for w in range(NQ):
            q0 = QT * w
            na = 4 if w == 0 else 8
            kw0 = max(0, q0 - 512)
            wb = w % 2
            last_div = None
            pend = None

            def emit_tail(ob, h, mm2, rec, pb, cur_vld):
                nonlocal pr_bcm, pr_bce, pr_rbread, last_div
                bcm = seq.step("tensor", lambda eng: nc.tensor.matmul(
                    psB[0:64, :], onesr[0:1, 0:64], rc[ob][:], start=True, stop=True),
                    deps=[rec, pr_bce])
                bce = seq.step("scalar", lambda eng: eng.activation(
                    rbS[:], psB[0:64, :], AF.Copy), deps=[bcm, pr_rbread])
                div = seq.step("vector", lambda eng, h=h, wb=wb, ob=ob: eng.tensor_tensor(
                    o_f[wb][64 * (h % 2):64 * (h % 2) + 64, 512 * (h // 2):512 * (h // 2 + 1)],
                    psO[ob][0:64, :], rbS[:], ALU.mult), deps=[bce, mm2, pr_ofuse[wb], pr_odbg[wb]])
                pr_divO[ob] = div
                pr_bcmO[ob] = bcm
                pr_bcm = bcm
                pr_bce = bce
                pr_rbread = div
                last_div = div

            for h in range(8):
                hb = (w * 8 + h) % 4
                gpr = h // 2
                pb = (w * 4 + gpr) % 3
                ob = h % 2
                qld = seq.dma("sync", lambda eng, hb=hb, h=h, q0=q0: eng.dma_start(
                    qt[hb][:], qr_d[64 * h:64 * (h + 1), q0:q0 + QT]), deps=[pr_sim_hb[hb]], n=1)
                kld = seq.dma("sync", lambda eng, hb=hb, h=h, kw0=kw0, na=na: eng.dma_start(
                    kt[hb][:, 0:128 * na], kr_d[64 * h:64 * (h + 1), kw0:kw0 + 128 * na]),
                    deps=[pr_sim_hb[hb]], n=1)
                if h % 2 == 0:
                    cur_vld = seq.dma("sync", lambda eng, pb=pb, g=gpr, kw0=kw0, na=na: [
                        eng.dma_start(vpt[pb][:, 128 * a:128 * (a + 1)],
                                      v_d[128 * g:128 * (g + 1), kw0 + 128 * a:kw0 + 128 * (a + 1)],
                                      transpose=True) for a in range(na)],
                        deps=[pr_vpt[pb]], n=na)
                mm2 = None
                mmq = []

                def emit_mm2(a, am, msk, pb=pb, ob=ob, na=na, hoff=64 * (h % 2)):
                    def mm2k(eng, am=am, pb=pb, a=a, na=na, hoff=hoff, ob=ob):
                        nc.tensor.matmul(psO[ob][0:64, :], vpt[pb][:, 128 * a + hoff:128 * a + hoff + 64],
                                         psb[am][:], start=(a == 0), stop=(a == na - 1))
                        return nc.tensor.matmul(psD[0:1, :], onescb[:], psb[am][:],
                                                start=(a == 0), stop=(a == na - 1))
                    st = seq.step("tensor", mm2k,
                                  deps=[msk, cur_vld, vini]
                                  + ([pr_divO[ob], pr_rec] if a == 0 else []))
                    pr_mm2b[am] = st
                    return st

                for a in range(na):
                    ma = a + 4 if w == 0 else a
                    ab = ga % 2; ga += 1
                    am = gam % 4; gam += 1
                    sim = seq.step("tensor", lambda eng, ab=ab, hb=hb, a=a: nc.tensor.matmul(
                        psS[ab][:], kt[hb][:, 128 * a:128 * (a + 1)], qt[hb][:],
                        start=True, stop=True), deps=[qld, kld, pr_clamp[ab]])
                    if w == 0 or a >= 3:
                        clamp = seq.step("vector", lambda eng, ab=ab, am=am: eng.tensor_scalar(
                            clsb[am][:], psS[ab][:], 10.0, None, ALU.min), deps=[sim, pr_exp[am]])
                        exps = seq.step("scalar", lambda eng, am=am: eng.activation(
                            ebf[am][:], clsb[am][:], AF.Exp, scale=8.0), deps=[clamp, pr_mask[am]])
                    else:
                        exps = seq.step("scalar", lambda eng, ab=ab, am=am: eng.activation(
                            ebf[am][:], psS[ab][:], AF.Exp, scale=8.0), deps=[sim, pr_mask[am]])
                        clamp = exps
                    msk = seq.step("gpsimd", lambda eng, am=am, ma=ma: eng.tensor_tensor(
                        psb[am][:], ebf[am][:], maskb[:, 512 * ma:512 * (ma + 1)], ALU.mult),
                        deps=[exps, mld, pr_mm2b[am], pr_pdbg[am % 2]])
                    if F_DBG and l == 0 and w == 0 and h == 0:
                        pr_pdbg[am % 2] = seq.dma("gpsimd_dma", lambda eng, am=am, a=a: eng.dma_start(
                            pdbg_d[128 * a:128 * (a + 1), :], psb[am][:]), deps=[msk], n=1)
                    mmq.append((a, am, msk))
                    if len(mmq) > 3:
                        mm2 = emit_mm2(*mmq.pop(0))
                    pr_clamp[ab] = clamp
                    pr_exp[am] = exps
                    pr_mask[am] = msk
                while mmq:
                    mm2 = emit_mm2(*mmq.pop(0))
                pr_sim_hb[hb] = sim
                pr_vpt[pb] = mm2
                pr_po = None
                if F_DBG and l == 0 and w == 0 and h == 0:
                    poev = seq.step("scalar", lambda eng, ob=ob: eng.activation(
                        podbg_sb[:], psO[ob][:], AF.Copy), deps=[mm2])
                    pr_po = seq.dma("gpsimd_dma", lambda eng, pb=pb: [
                        eng.dma_start(vdbg_d, vpt[pb][:]),
                        eng.dma_start(podbg_d, podbg_sb[:])], deps=[poev, cur_vld], n=2)
                rec = seq.step("scalar", lambda eng, ob=ob: (
                    b.fence(eng, "s", eng.activation(rc0[:], psD[0:1, :], AF.Ln)),
                    eng.activation(rc[ob][:], rc0[:], AF.Exp, scale=-1.0))[-1],
                    deps=[mm2, pr_bcmO[ob], pr_dnst, pr_po, pr_rec])
                if F_DBG and l == 0:
                    pr_dnst = seq.dma("gpsimd_dma", lambda eng, h=h, q0=q0, ob=ob: eng.dma_start(
                        ddbg_d[h:h + 1, q0:q0 + QT], rc[ob][:]), deps=[rec], n=1)
                pr_rec = rec
                if pend is not None:
                    emit_tail(*pend)
                pend = (ob, h, mm2, rec, pb, cur_vld)
            emit_tail(*pend)
            pend = None
            if l == 0:
                xrl = seq.dma("sync", lambda eng, wb=wb, q0=q0: [
                    eng.dma_start(xres[wb][:, 512 * c:512 * (c + 1)],
                                  xn1_bf[q0:q0 + QT, 128 * c:128 * (c + 1)], transpose=True)
                    for c in range(8)], deps=[pr_resread[wb]], n=8)
            else:
                xrl = seq.dma("sync", lambda eng, wb=wb, q0=q0: [
                    eng.dma_start(xres[wb][:, 512 * c:512 * (c + 1)],
                                  x2f_d[128 * c:128 * (c + 1), q0:q0 + QT])
                    for c in range(8)], deps=[pr_resread[wb]], n=8)
            if F_DBG and l == 0:
                pr_odbg[wb] = seq.dma("gpsimd_dma", lambda eng, wb=wb, q0=q0: [
                    eng.dma_start(odbg_d[128 * ic:128 * (ic + 1), q0:q0 + QT],
                                  o_f[wb][:, 512 * ic:512 * (ic + 1)]) for ic in range(4)],
                    deps=[last_div], n=4)
            last_wom = None
            last_res = None
            for fo in range(8):
                fb = gfo % 2; gfo += 1

                def womk(eng, fb=fb, fo=fo, wb=wb):
                    ii = None
                    for ic in range(4):
                        ii = nc.tensor.matmul(psX[fb][:], wo_sb[:, 1024 * ic + 128 * fo:1024 * ic + 128 * fo + 128],
                                              o_f[wb][:, 512 * ic:512 * (ic + 1)],
                                              start=(ic == 0), stop=(ic == 3))
                    return ii
                wom = seq.step("tensor", womk, deps=[last_div, wold, pr_resp[fb]])
                res = seq.step("vector", lambda eng, fb=fb, fo=fo, wb=wb: eng.tensor_tensor(
                    xo_sb[fb][:], psX[fb][:], xres[wb][:, 512 * fo:512 * (fo + 1)], ALU.add),
                    deps=[wom, xrl, pr_stoxo[fb]])
                sto = seq.dma("gpsimd_dma", lambda eng, fb=fb, fo=fo, q0=q0: eng.dma_start(
                    xatt_d[128 * fo:128 * (fo + 1), q0:q0 + QT], xo_sb[fb][:]), deps=[res], n=1)
                pr_resp[fb] = res
                pr_stoxo[fb] = sto
                last_wom = wom
                last_res = res
            pr_ofuse[wb] = last_wom
            pr_resread[wb] = last_res
        seq.barrier_all()
        ctxB.close()

        ctxC = ExitStack()
        w1sb = b.sbuf(f"ffc{l}_w1", [128, 8 * 5632], BF16, ctx=ctxC)
        w2sb = b.sbuf(f"ffc{l}_w2", [128, 22 * 1024], BF16, ctx=ctxC)
        b1c = b.sbuf(f"ffc{l}_b1", [128, 44], F32, ctx=ctxC)
        b2c = b.sbuf(f"ffc{l}_b2", [128, 8], F32, ctx=ctxC)
        lgc = b.sbuf(f"ffc{l}_lg", [128, 8], F32, ctx=ctxC)
        lbc = b.sbuf(f"ffc{l}_lb", [128, 8], F32, ctx=ctxC)
        xa03 = b.sbuf(f"ffc{l}_x03", [128, 4 * 512], F32, ctx=ctxC)
        xa47 = b.sbuf(f"ffc{l}_x47", [128, 4 * 513], F32, ctx=ctxC)
        hn = b.sbuf(f"ffc{l}_hn", [128, 8 * 512], BF16, ctx=ctxC)
        msb = b.sbuf(f"ffc{l}_m", [128, 22 * 512], BF16, ctx=ctxC)
        sqc = [b.sbuf(f"ffc{l}_sq{i}", [128, 512], F32, ctx=ctxC) for i in range(2)]
        glb = [b.sbuf(f"ffc{l}_gl{i}", [128, 512], BF16, ctx=ctxC) for i in range(2)]
        x2sb = [b.sbuf(f"ffc{l}_x2{i}", [128, 512], F32, ctx=ctxC) for i in range(2)]
        obf = [b.sbuf(f"ffc{l}_ob{i}", [128, 512], BF16, ctx=ctxC) for i in range(2)]
        tc1 = b.sbuf(f"ffc{l}_tc1", [128, 512], F32, ctx=ctxC)
        tc2 = b.sbuf(f"ffc{l}_tc2", [128, 512], F32, ctx=ctxC)
        muR = b.sbuf(f"ffc{l}_mu", [1, 512], F32, ctx=ctxC)
        mu2R = b.sbuf(f"ffc{l}_mu2", [1, 512], F32, ctx=ctxC)
        vrR = b.sbuf(f"ffc{l}_vr", [1, 512], F32, ctx=ctxC)
        sdR = b.sbuf(f"ffc{l}_sd", [1, 512], F32, ctx=ctxC)
        rsR = b.sbuf(f"ffc{l}_rs", [1, 512], F32, ctx=ctxC)
        psA = [b.psum2(f"ffc{l}_pA{i}", [128, 512], ctx=ctxC) for i in range(2)]
        psG = [b.psum2(f"ffc{l}_pG{i}", [128, 512], ctx=ctxC) for i in range(2)]
        psY = [b.psum2(f"ffc{l}_pY{i}", [128, 512], ctx=ctxC) for i in range(2)]
        psMuT = b.psum2(f"ffc{l}_pM", [128, 512], ctx=ctxC)
        psSqT = b.psum2(f"ffc{l}_pQ", [128, 512], ctx=ctxC)

        seq = Seq(b)
        wld1 = seq.dma("sync", lambda eng, l=l: eng.dma_start(
            w1sb[:], W["f_w1"][l].rearrange("(kc p) f -> p kc f", p=128)), n=1)
        wld2 = seq.dma("sync", lambda eng, l=l: eng.dma_start(
            w2sb[:], W["f_w2"][l].rearrange("(kc p) f -> p kc f", p=128)), n=1)
        bld = seq.dma("sync", lambda eng, l=l: [
            eng.dma_start(b1c[:], W["f_b1"][l]),
            eng.dma_start(b2c[:], W["f_b2"][l])], n=2)
        lld = seq.dma("sync", lambda eng, l=l: [
            eng.dma_start(lgc[:], W["f_lng"][l]),
            eng.dma_start(lbc[:], W["f_lnb"][l])], n=2)

        pr_res_last = None
        pr_gel_last = None
        pr_mmY_last = None
        pr_hn_last = None
        pr_st2c = [None, None]
        pr_mTTp = [None, None]
        pr_gelp = [None, None]
        pr_resy = [None, None]
        pr_sty = [None, None]
        pr_sty2 = [None, None]
        pr_bcrd = None
        pr_bc2rd = None

        def uview(c):
            if c < 4:
                return xa03[:, 512 * c:512 * (c + 1)]
            return xa47[:, 513 * (c - 4) + 1:513 * (c - 4) + 513]

        def sview(c):
            if c < 4:
                return xa03[:, 512 * c:512 * (c + 1)]
            return xa47[:, 513 * (c - 4):513 * (c - 4) + 512]

        for t in range(NQ):
            t0 = QT * t
            xdep = [pr_res_last, pr_gel_last, pr_mmY_last]
            xld1 = seq.dma("sync", lambda eng, t0=t0: [
                eng.dma_start(xa03[:, 512 * c:512 * (c + 1)],
                              xatt_d[128 * c:128 * (c + 1), t0:t0 + QT]) for c in range(4)],
                deps=xdep, n=4)
            if t == 0:
                zc = seq.step("vector", lambda eng: [
                    eng.memset(xa47[:, 513 * ci:513 * ci + 1], 0.0) for ci in range(4)][-1],
                    deps=xdep)
                xld2 = seq.dma("sync", lambda eng: [
                    eng.dma_start(xa47[:, 513 * ci + 1:513 * ci + 513],
                                  xatt_d[128 * (ci + 4):128 * (ci + 5), 0:QT]) for ci in range(4)],
                    deps=xdep + [zc], n=4)
            else:
                xld2 = seq.dma("sync", lambda eng, t0=t0: [
                    eng.dma_start(xa47[:, 513 * ci:513 * ci + 513],
                                  xatt_d[128 * (ci + 4):128 * (ci + 5), t0 - 1:t0 + QT]) for ci in range(4)],
                    deps=xdep, n=4)
            stm = st2 = None
            for c in range(8):
                stm = seq.step("tensor", lambda eng, c=c: nc.tensor.matmul(
                    psMuT[0:1, :], onesc[:], sview(c), start=(c == 0), stop=(c == 7)),
                    deps=[xld1, xld2] + ([pr_hn_last] if c == 0 else []))
                sqs = seq.step("scalar", lambda eng, c=c: eng.activation(
                    sqc[c % 2][:], sview(c), AF.Square), deps=[xld1, xld2, pr_st2c[c % 2]])
                st2 = seq.step("tensor", lambda eng, c=c: nc.tensor.matmul(
                    psSqT[0:1, :], onesc[:], sqc[c % 2][:], start=(c == 0), stop=(c == 7)),
                    deps=[sqs] + ([pr_hn_last] if c == 0 else []))
                pr_st2c[c % 2] = st2
            r1 = seq.step("vector", lambda eng: eng.tensor_scalar(
                muR[:], psMuT[0:1, :], 1.0 / 1024.0, None, ALU.mult), deps=[stm, pr_bcrd])
            r2 = seq.step("scalar", lambda eng: eng.activation(mu2R[:], muR[:], AF.Square), deps=[r1])
            r3 = seq.step("vector", lambda eng: (
                b.fence(eng, "v", eng.tensor_scalar(vrR[:], psSqT[0:1, :], 1.0 / 1024.0, None, ALU.mult)),
                eng.tensor_tensor(vrR[:], vrR[:], mu2R[:], ALU.subtract))[-1], deps=[st2, r2, pr_bc2rd])
            r4 = seq.step("scalar", lambda eng: (
                b.fence(eng, "s", eng.activation(sdR[:], vrR[:], AF.Ln, bias=1e-5)),
                eng.activation(rsR[:], sdR[:], AF.Exp, scale=-0.5))[-1], deps=[r3])
            bcm = seq.step("tensor", lambda eng: nc.tensor.matmul(
                psMuT[:], onesr[0:1, :], muR[:], start=True, stop=True), deps=[r1])
            bcs = seq.step("tensor", lambda eng: nc.tensor.matmul(
                psSqT[:], onesr[0:1, :], rsR[:], start=True, stop=True), deps=[r4])
            hns = None
            for c in range(8):
                hns = seq.step("vector", lambda eng, c=c: (
                    b.fence(eng, "v", eng.tensor_tensor(tc1[:], sview(c), psMuT[:], ALU.subtract)),
                    b.fence(eng, "v", eng.tensor_tensor(tc2[:], tc1[:], psSqT[:], ALU.mult)),
                    eng.scalar_tensor_tensor(hn[:, 512 * c:512 * (c + 1)], tc2[:], lgc[:, c:c + 1],
                                             lbc[:, c:c + 1].broadcast_to([128, 512]),
                                             ALU.mult, ALU.add))[-1],
                    deps=[bcm, bcs, lld, pr_hn_last])
            pr_bcrd = hns
            pr_bc2rd = hns
            last_mTT = None
            for pc in range(22):
                pb2 = pc % 2

                def mmAk(eng, pb2=pb2, pc=pc):
                    ii = None
                    for kc in range(8):
                        ii = nc.tensor.matmul(psA[pb2][:], w1sb[:, 5632 * kc + 128 * pc:5632 * kc + 128 * pc + 128],
                                              hn[:, 512 * kc:512 * (kc + 1)], start=(kc == 0), stop=(kc == 7))
                    return ii
                mmA_ = seq.step("tensor", mmAk, deps=[hns, wld1, pr_mTTp[pb2]])

                def mmGk(eng, pb2=pb2, pc=pc):
                    ii = None
                    for kc in range(8):
                        ii = nc.tensor.matmul(psG[pb2][:],
                                              w1sb[:, 5632 * kc + 2816 + 128 * pc:5632 * kc + 2816 + 128 * pc + 128],
                                              hn[:, 512 * kc:512 * (kc + 1)], start=(kc == 0), stop=(kc == 7))
                    return ii
                mmG_ = seq.step("tensor", mmGk, deps=[hns, wld1, pr_gelp[pb2]])
                gel = seq.step("scalar", lambda eng, pb2=pb2, pc=pc: eng.activation(
                    glb[pb2][:], psG[pb2][:], AF.Gelu, bias=b1c[:, 22 + pc:23 + pc]),
                    deps=[mmG_, bld, pr_mTTp[pb2]])
                mTT = seq.step("vector", lambda eng, pb2=pb2, pc=pc: eng.scalar_tensor_tensor(
                    msb[:, 512 * pc:512 * (pc + 1)], psA[pb2][:], b1c[:, pc:pc + 1], glb[pb2][:],
                    ALU.add, ALU.mult), deps=[gel, mmA_, bld, pr_mmY_last])
                pr_mTTp[pb2] = mTT
                pr_gelp[pb2] = gel
                last_mTT = mTT
            for yc in range(8):
                yb = yc % 2

                def mmYk(eng, yb=yb, yc=yc):
                    ii = None
                    for pc in range(22):
                        ii = nc.tensor.matmul(psY[yb][:], w2sb[:, 1024 * pc + 128 * yc:1024 * pc + 128 * yc + 128],
                                              msb[:, 512 * pc:512 * (pc + 1)], start=(pc == 0), stop=(pc == 21))
                    return ii
                mmY_ = seq.step("tensor", mmYk, deps=[last_mTT, wld2, pr_resy[yb]])
                res = seq.step("vector", lambda eng, yb=yb, yc=yc: eng.scalar_tensor_tensor(
                    x2sb[yb][:], psY[yb][:], b2c[:, yc:yc + 1], uview(yc), ALU.add, ALU.add),
                    deps=[mmY_, bld, pr_sty[yb]])
                if l == 0:
                    cpb = seq.step("scalar", lambda eng, yb=yb: eng.activation(
                        obf[yb][:], x2sb[yb][:], AF.Copy), deps=[res, pr_sty2[yb]])
                    sto = seq.dma("gpsimd_dma", lambda eng, yb=yb, yc=yc, t0=t0: [
                        eng.dma_start(x2f_d[128 * yc:128 * (yc + 1), t0:t0 + QT], x2sb[yb][:]),
                        eng.dma_start(x2bf_d[128 * yc:128 * (yc + 1), t0:t0 + QT], obf[yb][:])],
                        deps=[res, cpb], n=2)
                    pr_sty2[yb] = sto
                else:
                    cpb = seq.step("scalar", lambda eng, yb=yb: eng.activation(
                        obf[yb][:], x2sb[yb][:], AF.Silu), deps=[res, pr_sty2[yb]])
                    sto = seq.dma("gpsimd_dma", lambda eng, yb=yb, yc=yc, t0=t0: eng.dma_start(
                        silg_d[128 * yc:128 * (yc + 1), t0:t0 + QT], obf[yb][:]), deps=[cpb], n=1)
                    pr_sty2[yb] = sto
                pr_sty[yb] = sto
                pr_resy[yb] = res
                pr_res_last = res
                pr_mmY_last = mmY_
            pr_gel_last = gel
            pr_hn_last = res
        seq.barrier_all()
        ctxC.close()
    gctx.close()


def emit_final_combine(b: Bld, x1, s1, silg, out):
    ctx = ExitStack()
    sgt = [b.sbuf(f"fc_sg{i}", [128, 1024], BF16, ctx=ctx) for i in range(2)]
    x1t = [b.sbuf(f"fc_x1{i}", [128, 1024], F32, ctx=ctx) for i in range(2)]
    s1t = [b.sbuf(f"fc_s1{i}", [128, 1024], F32, ctx=ctx) for i in range(2)]
    tmp = b.sbuf("fc_tmp", [128, 1024], F32, ctx=ctx)
    ot = [b.sbuf(f"fc_o{i}", [128, 1024], F32, ctx=ctx) for i in range(2)]
    seq = Seq(b)
    pr_v1 = [None, None]
    pr_sto = [None, None]
    pr_v1_any = None
    for tt in range(32):
        tq0 = 128 * tt
        bb = tt % 2
        sgld = seq.dma("sync", lambda eng, bb=bb, tq0=tq0: eng.dma_start(
            sgt[bb][:], silg[:, tq0:tq0 + 128], transpose=True),
            deps=[pr_v1[bb]], n=1)
        xld = seq.dma("sync", lambda eng, bb=bb, tq0=tq0: [
            eng.dma_start(x1t[bb][:], x1[tq0:tq0 + 128, :]),
            eng.dma_start(s1t[bb][:], s1[tq0:tq0 + 128, :])], deps=[pr_v1[bb]], n=2)
        v1 = seq.step("vector", lambda eng, bb=bb: (
            b.fence(eng, "v", eng.tensor_tensor(tmp[:], s1t[bb][:], sgt[bb][:], ALU.mult)),
            eng.tensor_tensor(ot[bb][:], x1t[bb][:], tmp[:], ALU.add))[-1],
            deps=[sgld, xld, pr_sto[bb], pr_v1_any])
        pr_v1_any = v1
        sto = seq.dma("gpsimd_dma", lambda eng, bb=bb, tq0=tq0: eng.dma_start(
            out[tq0:tq0 + 128, :], ot[bb][:]), deps=[v1], n=1)
        pr_v1[bb] = v1
        pr_sto[bb] = sto
    seq.barrier_all()
    ctx.close()


SCRATCH = {
    "xn_bf": ([T, D], BF16), "s0": ([T, D], F32), "x1": ([T, D], F32),
    "xn1_bf": ([T, D], BF16), "xn1_f32": ([T, D], F32), "s1": ([T, D], F32),
    "silg": ([D, T], BF16), "out": ([T, D], F32),
}

PHASE_IO = {
    "ln0": (["x"], ["xn_bf"]),
    "l0attn": (["xn_bf"], ["s0"]),
    "moe": (["xn_bf", "x", "s0"], ["x1"]),
    "ln1": (["x1"], ["xn1_bf", "xn1_f32"]),
    "l1attn": (["xn1_bf"], ["s1"]),
    "l1ffn": (["xn1_bf"], ["silg"]),
    "combine": (["x1", "s1", "silg"], ["out"]),
}

PHASE_WEIGHTS = {
    "ln0": [],
    "l0attn": ["l0_wq", "l0_wk", "l0_wv", "l0_wo", "mask_su"],
    "moe": ["l0_rw", "l0_rb", "l0_w1", "l0_b1", "l0_w2", "l0_b2"],
    "ln1": [],
    "l1attn": ["l1_wq", "l1_wk", "l1_wv", "l1_wg", "l1_wo", "l1_wab", "l1_alog_dt",
               "l1_convq", "l1_convk", "l1_convv", "l1_onw", "alogdt_bf",
               "cums128", "amask_i128", "amask_sL128"],
    "l1ffn": ["f_wqkv", "f_wqkv2", "f_qs", "f_qs_sw", "f_ks", "f_ks_sw", "f_wo",
              "f_lng", "f_lnb", "f_w1", "f_b1", "f_w2", "f_b2",
              "rot_csq", "rot_ssq", "rot_csk", "rot_ssk", "fmask"],
    "combine": [],
}


def group_io(phases):
    produced = set()
    sin = set()
    needs_x = False
    for ph in phases:
        rd, wr = PHASE_IO[ph]
        for r in rd:
            if r == "x":
                needs_x = True
            elif r not in produced:
                sin.add(r)
        produced.update(wr)
    weights = ["ident", "constcols"]
    for ph in phases:
        weights.extend(PHASE_WEIGHTS[ph])
    return sorted(sin), sorted(produced), needs_x, weights


def build_program(phases, host_map, extern_in, outs):
    nc = bass.Bass("TRN2", target_bir_lowering=False)
    from concourse.bass import get_kernel_semaphore_range
    kr = get_kernel_semaphore_range()
    clear_range = range(kr.start + 3, kr.stop)
    nc.all_engine_barrier()
    nc.gpsimd.dma_reset(clear_range)
    nc.gpsimd.sem_clear(clear_range)
    nc.all_engine_barrier()
    ctx = ExitStack()
    aps = {}
    for name, arr in host_map.items():
        dt = BF16 if arr.dtype == BF16NP else F32
        aps[name] = nc.dram_tensor(name, list(arr.shape), dt, kind="ExternalInput").ap()
    for name, (shape, dt) in SCRATCH.items():
        if name in aps:
            continue
        if name in extern_in:
            kind = "ExternalInput"
        elif name in outs:
            kind = "ExternalOutput"
        else:
            kind = "Internal"
        aps[name] = nc.dram_tensor(name, shape, dt, kind=kind).ap()

    with ctx:
        blk = ctx.enter_context(nc.Block())
        b = Bld(nc, blk, ctx)
        b.dram = aps
        b.sb["ident_bf"] = b.sbuf("g_ident_bf", [128, 128], BF16)
        b.sb["ident_f32"] = b.sbuf("g_ident_f32", [128, 128], F32)
        b.sb["constcols"] = b.sbuf("g_constcols", [128, 6], F32)
        nc.const_aps.register(F32, 0.25, b.sb["constcols"][:, 0:1])
        nc.const_aps.register(F32, 8.0, b.sb["constcols"][:, 1:2])
        nc.const_aps.register(F32, 2.0, b.sb["constcols"][:, 2:3])
        nc.const_aps.register(F32, 0.5, b.sb["constcols"][:, 3:4])
        nc.const_aps.register(F32, 1e-5, b.sb["constcols"][:, 4:5])
        nc.const_aps.register(F32, -0.5, b.sb["constcols"][:, 5:6])
        ic = b.cnt("g_init")

        @blk.sync
        def _(eng):
            eng.dma_start(b.sb["ident_f32"][:], aps["ident"]).then_inc(ic.sem, 16)
            eng.dma_start(b.sb["constcols"][:], aps["constcols"]).then_inc(ic.sem, 16)
        ic.n += 32

        @blk.vector
        def _(eng):
            eng.wait_ge(ic.sem, 32)
            eng.tensor_copy(b.sb["ident_bf"][:], b.sb["ident_f32"][:]).then_inc(ic.sem, 1)
        ic.n += 1
        b.barrier()

        for ph in phases:
            pctx = ExitStack()
            outer = b.ctx
            b.ctx = pctx
            emit_phase(b, ph)
            b.barrier()
            b.ctx = outer
            pctx.close()
    return nc


def emit_phase(b, ph):
    aps = b.dram
    if ph == "ln0":
        emit_ln(b, aps["x"], aps["xn_bf"], None, tag="ln0")
    elif ph == "ln1":
        emit_ln(b, aps["x1"], aps["xn1_bf"], aps["xn1_f32"], tag="ln1")
    elif ph == "moe":
        emit_moe_combine(b, aps["xn_bf"], aps["x"], aps["s0"], aps["x1"])
    elif ph == "l0attn":
        emit_l0attn(b, aps["xn_bf"], aps["s0"])
    elif ph == "l1attn":
        emit_l1attn(b, aps["xn1_bf"], aps["s1"])
    elif ph == "l1ffn":
        emit_l1ffn(b, aps["xn1_bf"], aps["silg"])
    elif ph == "combine":
        emit_final_combine(b, aps["x1"], aps["s1"], aps["silg"], aps["out"])
    else:
        raise ValueError(ph)


LAUNCH_GROUPS = [["ln0", "l0attn", "moe", "ln1", "l1attn", "l1ffn", "combine"]]


def run_groups(x_full, inputs, groups):
    wmap = prep_weights(inputs)
    wmap.update(host_consts())
    shards = [np.ascontiguousarray(np.asarray(x_full[i], np.float32)) for i in range(8)]
    live = [dict() for _ in range(8)]
    final = None
    for group in groups:
        sin, souts, needs_x, weights = group_io(group)
        host_map = {w: wmap[w] for w in dict.fromkeys(weights)}
        if needs_x or any(ph in ("ln0", "moe") for ph in group):
            host_map["x"] = shards[0]
        nc = build_program(group, host_map, sin, souts)
        in_maps = []
        for core in range(8):
            m = {k: v for k, v in host_map.items() if k != "x"}
            if "x" in host_map:
                m["x"] = shards[core]
            for sname in sin:
                m[sname] = live[core][sname]
            in_maps.append(m)
        res = run_bass_kernel_spmd(nc, in_maps, core_ids=list(range(8)))
        for core in range(8):
            for sname in souts:
                live[core][sname] = np.ascontiguousarray(np.asarray(res.results[core][sname]))
        if "out" in souts:
            final = np.stack([live[core]["out"] for core in range(8)], axis=0)
    return final


def kernel(**inputs):
    x_full = np.ascontiguousarray(np.asarray(inputs["L0_attn_hidden_states"], dtype=np.float32))
    return run_groups(x_full, inputs, LAUNCH_GROUPS)
